# version 08
# baseline (speedup 1.0000x reference)
import numpy as np
import ml_dtypes
import concourse.bass as bass
import concourse.mybir as mybir
from concourse.alu_op_type import AluOpType
from contextlib import ExitStack

F32 = mybir.dt.float32
BF16 = mybir.dt.bfloat16
AF = mybir.ActivationFunctionType
AX = mybir.AxisListType
OP = AluOpType

ENGINES = ("tensor", "vector", "scalar", "gpsimd", "sync")


class Cell:
    __slots__ = ("writer", "readers", "name")

    def __init__(self, name=""):
        self.writer = None
        self.readers = []
        self.name = name


class P:
    def __init__(self, name="k", ncores=8):
        self.nc = bass.Bass("TRN2", target_bir_lowering=False)
        self.ncores = ncores
        self.stack = ExitStack()
        self.ops = {e: [] for e in ENGINES}
        self.sems = {}
        self.sem_count = {}
        self.waited = {e: {} for e in ENGINES}
        self.cells = {}
        self._sbuf_bytes = 0
        self._uid = 0
        self.io = {}
        self.gbanks = None
        self.gbank_cells = None
        self._scope = None
        self._keep = []
        names = ["sv", "ss", "st", "sg", "sq", "cc"]
        self.dma_sems = {"sync": [f"ds{i}" for i in range(4)],
                         "scalar": [f"dc{i}" for i in range(2)],
                         "gpsimd": [f"dg{i}" for i in range(4)],
                         "vector": ["dv0"]}
        for q, ls in self.dma_sems.items():
            names.extend(ls)
        for s in names:
            self.sems[s] = self.stack.enter_context(self.nc.semaphore(s))
            self.sem_count[s] = 0
        self.eng_sem = {"tensor": "st", "vector": "sv", "scalar": "ss",
                        "gpsimd": "sg", "sync": "sq"}
        self._dma_rr = {q: 0 for q in self.dma_sems}

    def dram(self, name, shape, dtype=F32, kind=None, addr_space=None):
        kw = {}
        if kind is not None:
            kw["kind"] = kind
        else:
            self._uid += 1
            name = f"{name}_d{self._uid}"
        if addr_space is not None:
            kw["addr_space"] = addr_space
        t = self.nc.dram_tensor(name, list(shape), dtype, **kw)
        self._keep.append(t)
        return t

    def inp(self, name, shape, dtype=F32):
        if name in self.io:
            return self.io[name]
        if not hasattr(self, "ext_inputs"):
            self.ext_inputs = set()
        self.ext_inputs.add(name)
        return self.dram(name, shape, dtype, kind="ExternalInput").ap()

    def outp(self, name, shape, dtype=F32):
        if name in self.io:
            return self.io[name]
        return self.dram(name, shape, dtype, kind="ExternalOutput").ap()

    def sbuf(self, name, shape, dtype=F32):
        self._uid += 1
        arena = self._scope if self._scope is not None else self.stack
        t = arena.enter_context(
            self.nc.sbuf_tensor(f"{name}_u{self._uid}", list(shape), dtype))
        self._keep.append(t)
        return t

    def barrier(self):
        toks = [(s, c) for s, c in self.sem_count.items() if c > 0]
        for eng in ENGINES:
            need = [t for t in toks if self.waited[eng].get(t[0], 0) < t[1]]
            if need:
                self.wait_all(eng, need)

    def phase(self):
        import contextlib

        @contextlib.contextmanager
        def _cm():
            self.barrier()
            prev = self._scope
            self._scope = ExitStack()
            try:
                yield
            finally:
                self.barrier()
                self._scope.close()
                self._scope = prev
        return _cm()

    def psum(self, name, shape, dtype=F32):
        self._uid += 1
        t = self.stack.enter_context(
            self.nc.psum_tensor(f"{name}_u{self._uid}", list(shape), dtype))
        return t

    def init_gbanks(self):
        if self.gbanks is None:
            self.gbanks = [self.psum(f"gb{i}", [128, 512]) for i in range(7)]
            self.gbank_cells = [self.cell(b) for b in self.gbanks]
            self.ptr_bf = self.psum("gptrb", [128, 512], BF16)
            self.ptr_f32 = self.ptr_bf[0:64, 256:512].bitcast(F32)
            c7 = self.cell(self.ptr_bf)
            self.ptr_cell = c7

    def cell(self, obj, key=None):
        k = (id(obj), key)
        c = self.cells.get(k)
        if c is None:
            c = Cell(f"{getattr(obj, 'name', obj)}:{key}")
            self.cells[k] = c
        return c

    def emit(self, eng, build, reads=(), writes=(), sem=None, inc=1,
             tag="", extra_waits=()):
        waits = {}
        for s, v in extra_waits:
            if v > 0 and self.waited[eng].get(s, 0) < v:
                waits[s] = max(waits.get(s, 0), v)

        def add_wait(tok):
            if tok is None:
                return
            s, v = tok
            if self.waited[eng].get(s, 0) >= v:
                return
            if waits.get(s, 0) < v:
                waits[s] = v

        for c in reads:
            add_wait(c.writer)
        for c in writes:
            add_wait(c.writer)
            for t in c.readers:
                add_wait(t)

        sem = sem or self.eng_sem[eng]
        self.sem_count[sem] += inc
        tok = (sem, self.sem_count[sem])
        semh = self.sems[sem]
        wl = sorted(waits.items())
        for s, v in wl:
            if self.waited[eng].get(s, 0) < v:
                self.waited[eng][s] = v

        def fn(e, wl=wl, build=build, semh=semh, inc=inc):
            for s, v in wl:
                e.wait_ge(self.sems[s], v)
            instr = build(e)
            if instr is not None:
                instr.then_inc(semh, inc)

        self.ops[eng].append(fn)
        for c in writes:
            c.writer = tok
            c.readers = []
        for c in reads:
            c.readers.append(tok)
        return tok

    def wait_all(self, eng, toks):
        def fn(e, toks=tuple(toks)):
            for s, v in toks:
                e.wait_ge(self.sems[s], v)
        for s, v in toks:
            if self.waited[eng].get(s, 0) < v:
                self.waited[eng][s] = v
        self.ops[eng].append(fn)

    def barrier_tok(self, cells):
        return [c.writer for c in cells if c.writer is not None]

    def _dma_emit(self, q, build, reads, writes):
        ls = self.dma_sems[q]
        sem = ls[self._dma_rr[q] % len(ls)]
        self._dma_rr[q] += 1
        prev = self.sem_count[sem]
        tok = self.emit(q, build, reads=reads, writes=writes, sem=sem, inc=16,
                        extra_waits=[(sem, prev)])
        return tok

    def drain_q(self, q):
        self.wait_all(q, [(s, self.sem_count[s]) for s in self.dma_sems[q]
                          if self.sem_count[s] > 0])

    def dma(self, q, out_ap, in_ap, reads=(), writes=(), **kw):
        return self._dma_emit(
            q, lambda e: e.dma_start(out=out_ap, in_=in_ap, **kw), reads, writes)

    def dma_tp(self, q, out_ap, in_ap, reads=(), writes=()):
        assert q in ("sync", "scalar")
        return self._dma_emit(
            q, lambda e: e.dma_start(out=out_ap, in_=in_ap, transpose=True),
            reads, writes)

    def tt(self, out, a, b, op, reads=(), writes=(), eng="vector"):
        return self.emit(eng, lambda e: e.tensor_tensor(out, a, b, op),
                         reads=reads, writes=writes)

    def ts(self, out, a, s1, s2=None, op0=OP.mult, op1=None, accum_out=None,
           reads=(), writes=(), eng="vector"):
        if op1 is None:
            op1 = OP.bypass
        return self.emit(
            eng, lambda e: e.tensor_scalar(out, a, s1, s2, op0, op1,
                                           accum_out=accum_out),
            reads=reads, writes=writes)

    def stt(self, out, in0, scalar, in1, op0, op1, accum_out=None,
            reads=(), writes=()):
        return self.emit(
            "vector",
            lambda e: e.scalar_tensor_tensor(out, in0, scalar, in1, op0, op1,
                                             accum_out=accum_out),
            reads=reads, writes=writes)

    def scan(self, out, d0, d1, init, reads=(), writes=()):
        return self.emit(
            "vector",
            lambda e: e.tensor_tensor_scan(out, d0, d1, init, OP.mult, OP.add),
            reads=reads, writes=writes)

    def act(self, out, in_, func, bias=None, scale=None, accum_out=None,
            reads=(), writes=()):
        kw = {}
        if bias is not None:
            kw["bias"] = bias
        if scale is not None:
            kw["scale"] = scale
        if accum_out is not None:
            kw["accum_out"] = accum_out
        return self.emit(
            "scalar", lambda e: e.activation(out, in_, func, **kw),
            reads=reads, writes=writes)

    def red(self, out, in_, op=OP.add, axis=AX.X, negate=None,
            reads=(), writes=(), eng="vector"):
        return self.emit(
            eng, lambda e: e.tensor_reduce(out, in_, axis, op, negate=negate),
            reads=reads, writes=writes)

    def copy(self, out, in_, reads=(), writes=(), eng="vector"):
        return self.emit(eng, lambda e: e.tensor_copy(out, in_),
                         reads=reads, writes=writes)

    def memset(self, ap, val, reads=(), writes=(), eng="vector"):
        return self.emit(eng, lambda e: e.memset(ap, val),
                         reads=reads, writes=writes)

    def mm(self, out, lhsT, rhs, start=True, stop=True, reads=(), writes=()):
        return self.emit(
            "tensor",
            lambda e: e.matmul(out, lhsT, rhs, start=start, stop=stop,
                               skip_group_check=True),
            reads=reads, writes=writes)

    def trans(self, out, in_, ident, reads=(), writes=()):
        return self.emit(
            "tensor", lambda e: e.transpose(out, in_, ident),
            reads=reads, writes=writes)

    def recip(self, out, in_, reads=(), writes=()):
        return self.emit("vector", lambda e: e.reciprocal(out, in_),
                         reads=reads, writes=writes)

    def collective(self, kind, op, ins, outs, reads=(), writes=()):
        def build(e):
            if kind == "AllToAll":
                from concourse.aot_env import allow_ici_all_to_all
                with allow_ici_all_to_all():
                    return e.collective_compute(
                        kind, op, replica_groups=[list(range(self.ncores))],
                        ins=ins, outs=outs)
            return e.collective_compute(
                kind, op, replica_groups=[list(range(self.ncores))],
                ins=ins, outs=outs)
        return self.emit("gpsimd", build, reads=reads, writes=writes,
                         sem="cc", inc=1)

    def finish(self):
        with self.nc.Block() as block:
            for eng in ENGINES:
                if not self.ops[eng]:
                    continue
                dec = getattr(block, eng)

                def make(eng):
                    def body(e):
                        for fn in self.ops[eng]:
                            fn(e)
                    return body
                dec(make(eng))
        self.stack.close()
        return self.nc


def ceil_div(a, b):
    return (a + b - 1) // b


def mm_group(p, items, reads, write_cell, item_reads=None):
    n = len(items)
    assert n >= 1
    eng = "tensor"

    def mk_waits(cells, extra=()):
        waits = {}

        def add_wait(tok):
            if tok is None:
                return
            s, v = tok
            if p.waited[eng].get(s, 0) >= v:
                return
            if waits.get(s, 0) < v:
                waits[s] = v

        for c in cells:
            add_wait(c.writer)
        for t in extra:
            add_wait(t)
        wl = sorted(waits.items())
        for s, v in wl:
            if p.waited[eng].get(s, 0) < v:
                p.waited[eng][s] = v
        return wl

    extra = [write_cell.writer] + list(write_cell.readers)
    first = list(reads) + (list(item_reads[0]) if item_reads else [])
    wl0 = mk_waits(first, [t for t in extra if t is not None])
    iwls = [()] * n
    if item_reads is not None:
        assert len(item_reads) == n
        for i in range(1, n):
            iwls[i] = mk_waits(item_reads[i])
    sem = p.eng_sem[eng]
    p.sem_count[sem] += 1
    tok = (sem, p.sem_count[sem])
    semh = p.sems[sem]

    def fn(e, wl0=wl0, iwls=tuple(iwls), items=tuple(items), semh=semh, n=n):
        for s, v in wl0:
            e.wait_ge(p.sems[s], v)
        for i, (out_ap, lhsT, rhs) in enumerate(items):
            for s, v in iwls[i]:
                e.wait_ge(p.sems[s], v)
            instr = e.matmul(out_ap, lhsT, rhs, start=(i == 0), stop=(i == n - 1))
            if i == n - 1:
                instr.then_inc(semh, 1)

    p.ops[eng].append(fn)
    write_cell.writer = tok
    write_cell.readers = []
    for c in reads:
        c.readers.append(tok)
    if item_reads is not None:
        for cl in item_reads:
            for c in cl:
                c.readers.append(tok)
    return tok


class PsumPool:

    def __init__(self, p, n=8, name="pb", off=0):
        self.p = p
        if p.gbanks is not None:
            n = min(n, len(p.gbanks) - off)
            self.banks = p.gbanks[off:off + n]
            self.cells = p.gbank_cells[off:off + n]
        else:
            self.banks = [p.psum(f"{name}{i}", [128, 512]) for i in range(n)]
            self.cells = [p.cell(b) for b in self.banks]
        self.i = 0

    def next(self):
        j = self.i % len(self.banks)
        self.i += 1
        return self.banks[j], self.cells[j]


def load_w_bf16(p, name, w_dram_ap, K, RO, q="gpsimd", dtype=BF16):
    nk = ceil_div(K, 128)
    sb = p.sbuf(name, [128, nk * RO], dtype)
    cl = [p.cell(sb, f"t{ik}") for ik in range(nk)]
    for ik in range(nk):
        kk = min(128, K - ik * 128)
        p.dma(q, sb[0:kk, ik * RO:(ik + 1) * RO],
              w_dram_ap[ik * 128:ik * 128 + kk, :], writes=[cl[ik]])
    return sb, cl, nk


def gemm(p, pool, wt, cwt, nk, RO, K, xf, cxf, T, TG, evac,
         ro_list=None, x_col=None, kk_last=128):
    ro_list = ro_list if ro_list is not None else range(ceil_div(RO, 128))
    wl = cwt if isinstance(cwt, list) else [cwt] * nk
    xl = cxf if isinstance(cxf, list) else [cxf] * nk
    for ro in ro_list:
        rw = min(128, RO - ro * 128)
        for c0 in range(0, TG, 512):
            cw = min(512, TG - c0)
            pb, cb = pool.next()
            items = []
            for ik in range(nk):
                kk = kk_last if ik == nk - 1 else 128
                lhsT = wt[0:kk, ik * RO + ro * 128: ik * RO + ro * 128 + rw]
                rhs = (x_col(ik, c0, cw) if x_col is not None
                       else xf[0:kk, ik * TG + c0: ik * TG + c0 + cw])
                items.append((pb[0:rw, 0:cw], lhsT, rhs))
            mm_group(p, items, [], cb,
                     item_reads=[[wl[ik], xl[ik]] for ik in range(nk)])
            evac(ro, c0, cw, pb, cb)


"""Phase builders for the mk_recurrent_d2 kernel (stage A: one launch each)."""


TT_SHAPES = {
    "in": (32, 64, 32, 64),
    "xp": (32, 8, 64, 12),
    "dt": (8, 32, 8, 64),
    "out": (32, 32, 64, 32),
}
TT_CHUNK = {"in": 512, "xp": 384, "dt": 512, "out": 512}


def host_tt_inputs(ins, prefix, key):
    g1 = np.asarray(ins[f"{prefix}_{key}_g1"], np.float32)
    g2 = np.asarray(ins[f"{prefix}_{key}_g2"], np.float32)
    m1, n1, m2, n2 = TT_SHAPES[key]
    g1T = np.ascontiguousarray(np.transpose(g1, (2, 0, 1)).reshape(16, m1 * n1))
    g2f = np.ascontiguousarray(g2.reshape(16, m2 * n2))
    return {f"{key}_g1T": g1T, f"{key}_g2f": g2f}


def build_prep(p, plan=None):
    pool = PsumPool(p, 4)
    ev = p.sbuf("ev", [128, 512])
    cev = p.cell(ev)
    out_toks = []
    for key in ("in", "xp", "dt", "out"):
        m1, n1, m2, n2 = (plan or TT_SHAPES)[key]
        g1T = p.inp(f"{key}_g1T", [16, m1 * n1])
        g2f = p.inp(f"{key}_g2f", [16, m2 * n2])
        W = p.outp(f"W_{key}", [m1 * m2, n1 * n2])
        W4 = W.rearrange("(i j) (u v) -> i u j v", i=m1, u=n1)
        g1s = p.sbuf(f"{key}_g1s", [16, m1 * n1])
        g2s = p.sbuf(f"{key}_g2s", [16, m2 * n2])
        c1, c2 = p.cell(g1s), p.cell(g2s)
        p.dma("sync", g1s[:, :], g1T, writes=[c1])
        p.dma("sync", g2s[:, :], g2f, writes=[c2])
        AR = m1 * n1
        CW = TT_CHUNK[key]
        for rt in range(ceil_div(AR, 128)):
            rw = min(128, AR - rt * 128)
            for c0 in range(0, m2 * n2, CW):
                cw = min(CW, m2 * n2 - c0)
                pb, cb = pool.next()
                mm_group(p, [(pb[0:rw, 0:cw],
                              g1s[:, rt * 128: rt * 128 + rw],
                              g2s[:, c0:c0 + cw])], [c1, c2], cb)
                p.copy(ev[0:rw, 0:cw], pb[0:rw, 0:cw], reads=[cb], writes=[cev])
                ni = rw // n1 if n1 <= 128 else 1
                j0 = c0 // n2
                nj = cw // n2
                for il in range(ni):
                    i_abs = (rt * 128) // n1 + il
                    tok = p.dma("sync",
                                W4[i_abs, :, j0:j0 + nj, :],
                                ev[il * n1:(il + 1) * n1, 0:cw],
                                reads=[cev])
                    out_toks.append(tok)
    p.drain_q("sync")
    return p


def build_l0_inproj(p, T=16384, TG=2048):
    xnT = p.inp("xnT", [1024, T])
    W = p.inp("W_sel", [1024, 512])
    b = p.inp("b_sel", [512, 1])
    xcz = p.outp("xcz", [512, T])
    pool = PsumPool(p, 8)
    wt, cwt, nk = load_w_bf16(p, "wt", W, 1024, 512)
    bsl = p.sbuf("bsl", [128, 4])
    cb = p.cell(bsl)
    for ro in range(4):
        p.dma("sync", bsl[:, ro:ro + 1], b[ro * 128:(ro + 1) * 128, :], writes=[cb])
    xf = p.sbuf("xf", [128, nk * TG], BF16)
    cxf = [p.cell(xf, f"t{i}") for i in range(nk)]
    ev = [p.sbuf(f"ev{i}", [128, 512]) for i in range(2)]
    cev = [p.cell(t) for t in ev]
    for t0 in range(0, T, TG):
        for ik in range(nk):
            p.dma("gpsimd", xf[:, ik * TG:(ik + 1) * TG],
                  xnT[ik * 128:(ik + 1) * 128, t0:t0 + TG], writes=[cxf[ik]])

        def evac(ro, c0, cw, pb, cbk, t0=t0):
            i = pool.i % 2
            p.ts(ev[i][:, 0:cw], pb[:, 0:cw], bsl[:, ro:ro + 1], None, OP.add,
                 reads=[cbk, cb], writes=[cev[i]])
            p.dma("sync", xcz[ro * 128:(ro + 1) * 128, t0 + c0:t0 + c0 + cw],
                  ev[i][:, 0:cw], reads=[cev[i]])
        gemm(p, pool, wt, cwt, nk, 512, 1024, xf, cxf, TG, TG, evac)
    p.drain_q("sync")
    return p


def build_l0_conv(p, T=16384, TG=2048, LSEQ=8192):
    xcz = p.inp("xcz", [512, T])
    cwT = p.inp("conv_wT_sel", [256, 4])
    cbv = p.inp("conv_b_sel", [256, 1])
    Wxp = p.inp("Wxp_rows", [256, 96])
    u_out = p.outp("u_out", [256, T])
    xp_o = p.outp("xp_part", [96, T])
    pool = PsumPool(p, 8)
    wxp, cwxp, nkxp = load_w_bf16(p, "wxp", Wxp, 256, 96)
    wcv = p.sbuf("wcv", [128, 8])
    cb_s = p.sbuf("cbs", [128, 2])
    cwc, ccb = p.cell(wcv), p.cell(cb_s)
    for di in range(2):
        p.dma("sync", wcv[:, di * 4:(di + 1) * 4], cwT[di * 128:(di + 1) * 128, :], writes=[cwc])
        p.dma("sync", cb_s[:, di:di + 1], cbv[di * 128:(di + 1) * 128, :], writes=[ccb])
    xct = p.sbuf("xct", [128, TG + 3])
    zt = p.sbuf("zt", [128, TG])
    acc = p.sbuf("acc", [128, TG])
    sg = p.sbuf("sg", [128, TG])
    ut = p.sbuf("ut", [128, TG])
    ubf = p.sbuf("ubf", [128, 2 * TG], BF16)
    ev = p.sbuf("evp", [96, 512])
    cxc, czt, cac, csg, cut, cub, cev = (p.cell(t) for t in (xct, zt, acc, sg, ut, ubf, ev))
    for t0 in range(0, T, TG):
        left = (t0 % LSEQ) == 0
        right = ((t0 + TG) % LSEQ) == 0
        seg0 = (t0 // LSEQ) * LSEQ
        seg1 = seg0 + LSEQ
        vlo = max(t0 - 1, seg0)
        vhi = min(t0 + TG + 2, seg1)
        off = vlo - (t0 - 1)
        for di in range(2):
            if left:
                p.memset(xct[:, 0:1], 0.0, writes=[cxc])
            if right:
                p.memset(xct[:, TG + 1:TG + 3], 0.0, writes=[cxc])
            p.dma("sync", xct[:, off:off + (vhi - vlo)],
                  xcz[di * 128:(di + 1) * 128, vlo:vhi], writes=[cxc])
            p.dma("sync", zt[:, :], xcz[256 + di * 128: 256 + (di + 1) * 128, t0:t0 + TG],
                  writes=[czt])
            p.ts(acc[:, :], xct[:, 0:TG], wcv[:, di * 4:di * 4 + 1], None, OP.mult,
                 reads=[cxc, cwc], writes=[cac])
            for k in range(1, 4):
                p.stt(acc[:, :], xct[:, k:k + TG], wcv[:, di * 4 + k:di * 4 + k + 1],
                      acc[:, :], OP.mult, OP.add, reads=[cxc, cwc, cac], writes=[cac])
            p.act(sg[:, :], acc[:, :], AF.Silu, bias=cb_s[:, di:di + 1],
                  reads=[cac, ccb], writes=[csg])
            p.tt(ut[:, :], sg[:, :], zt[:, :], OP.mult, reads=[csg, czt], writes=[cut])
            p.dma("sync", u_out[di * 128:(di + 1) * 128, t0:t0 + TG], ut[:, :],
                  reads=[cut])
            p.copy(ubf[:, di * TG:(di + 1) * TG], ut[:, :], reads=[cut], writes=[cub])

        def evac(ro, c0, cw, pb, cbk, t0=t0):
            p.copy(ev[0:96, 0:cw], pb[0:96, 0:cw], reads=[cbk], writes=[cev])
            p.dma("sync", xp_o[:, t0 + c0:t0 + c0 + cw], ev[0:96, 0:cw], reads=[cev])
        gemm(p, pool, wxp, cwxp, nkxp, 96, 256, ubf, cub, TG, TG, evac)
    p.drain_q("sync")
    return p


def build_l0_scan(p, T=16384, TB=2048, LSEQ=8192):
    u_in = p.inp("u_in", [256, T])
    xp_all = p.inp("xp_all", [768, T])
    xp_b = p.inp("xp_b", [96, 1])
    Wdt = p.inp("Wdt_sel", [64, 256])
    dt_b = p.inp("dt_b_sel", [256, 1])
    A_log = p.inp("A_log_sel", [256, 16])
    D_sel = p.inp("D_sel", [256, 1])
    sel = p.inp("sel", [32, 4096])
    ident = p.inp("ident", [128, 128], BF16)
    ones_col = p.inp("ones_col", [128, 1])
    ssm_o = p.outp("ssm", [256, T])
    hsum_o = p.outp("hsum", [1, 1])

    if p.gbanks is not None:
        ybanks = p.gbanks[0:4]
        ycells = p.gbank_cells[0:4]
        pool = PsumPool(p, 3, name="pm", off=4)
    else:
        ybanks = [p.psum(f"yb{i}", [128, 512]) for i in range(4)]
        ycells = [p.cell(b) for b in ybanks]
        pool = PsumPool(p, 4, name="pm")

    dtbc = p.sbuf("dtbc", [96, TB])
    bc = p.sbuf("bc", [32, TB])
    sel_s = p.sbuf("sel_s", [32, 4096])
    id_s = p.sbuf("id_s", [128, 128], BF16)
    one_s = p.sbuf("one_s", [128, 1])
    wdt_s = p.sbuf("wdt_s", [64, 256])
    dtb_s = p.sbuf("dtb_s", [128, 2])
    D_s = p.sbuf("D_s", [128, 2])
    Alog_s = p.sbuf("Alog_s", [128, 32])
    negA = p.sbuf("negA", [128, 32])
    tmp96 = p.sbuf("tmp96", [96, TB])
    dt2 = p.sbuf("dt2", [128, 2 * TB])
    dtu2 = p.sbuf("dtu2", [128, 2 * TB])
    ublk = p.sbuf("ublk", [128, 2 * TB])
    dA2_ = [p.sbuf(f"dA{i}", [128, TB // 2]) for i in range(4)]
    wts2_ = [p.sbuf(f"wts{i}", [128, TB // 2]) for i in range(4)]
    h2_ = [p.sbuf(f"h_s{i}", [128, TB // 2]) for i in range(4)]
    hC2_ = [p.sbuf(f"hC{i}", [128, TB // 2], BF16) for i in range(4)]
    evb = p.sbuf("evb", [128, TB])
    sp1 = p.sbuf("sp1", [128, 512])
    sp2 = p.sbuf("sp2", [128, 512])
    sp3 = p.sbuf("sp3", [128, 512])
    carry = p.sbuf("carry", [128, 32])
    hacc = p.sbuf("hacc", [128, 1])
    hrow = p.sbuf("hrow", [128, 1])
    hfin = p.sbuf("hfin", [1, 1])

    (cdtbc, cbc, csel, cid, cone, cwdt, cdtb, cD, cAl, cnA, ctmp, cdt2, cdtu,
     cu, cev, c1, c2, c3, ccar, chac, chrw, chfin) = (
        p.cell(t) for t in (dtbc, bc, sel_s, id_s, one_s, wdt_s, dtb_s, D_s,
                            Alog_s, negA, tmp96, dt2, dtu2, ublk,
                            evb, sp1, sp2, sp3, carry, hacc, hrow, hfin))
    cdA2 = [p.cell(t) for t in dA2_]
    cw2 = [p.cell(t) for t in wts2_]
    ch2 = [p.cell(t) for t in h2_]
    chC2 = [p.cell(t) for t in hC2_]

    p.dma("sync", sel_s[:, :], sel, writes=[csel])
    p.dma("sync", id_s[:, :], ident, writes=[cid])
    p.dma("sync", one_s[:, :], ones_col, writes=[cone])
    p.dma("sync", wdt_s[:, :], Wdt, writes=[cwdt])
    for di in range(2):
        p.dma("sync", dtb_s[:, di:di + 1], dt_b[di * 128:(di + 1) * 128, :], writes=[cdtb])
        p.dma("sync", D_s[:, di:di + 1], D_sel[di * 128:(di + 1) * 128, :], writes=[cD])
        p.dma("sync", Alog_s[:, di * 16:(di + 1) * 16], A_log[di * 128:(di + 1) * 128, :],
              writes=[cAl])
    p.act(negA[:, :], Alog_s[:, :], AF.Exp, reads=[cAl], writes=[cnA])
    p.ts(negA[:, :], negA[:, :], -1.0, reads=[cnA], writes=[cnA])
    p.memset(hacc[:, :], 0.0, writes=[chac])

    xpb_s = p.sbuf("xpb_s", [96, 1])
    cxpb = p.cell(xpb_s)
    p.dma("sync", xpb_s[:, :], xp_b, writes=[cxpb])

    nblk = T // TB
    for bi in range(nblk):
        t0 = bi * TB
        if t0 % LSEQ == 0:
            p.memset(carry[:, :], 0.0, writes=[ccar])
        for di in range(2):
            p.dma("sync", ublk[:, di * TB:(di + 1) * TB],
                  u_in[di * 128:(di + 1) * 128, t0:t0 + TB], writes=[cu])
        p.dma("sync", dtbc[:, :], xp_all[0:96, t0:t0 + TB], writes=[cdtbc])
        for k in range(1, 8):
            p.dma("sync", tmp96[:, :], xp_all[96 * k:96 * (k + 1), t0:t0 + TB],
                  writes=[ctmp])
            p.tt(dtbc[:, :], dtbc[:, :], tmp96[:, :], OP.add,
                 reads=[ctmp, cdtbc], writes=[cdtbc])
        p.ts(dtbc[:, :], dtbc[:, :], xpb_s[:, :], None, OP.add,
             reads=[cxpb, cdtbc], writes=[cdtbc])
        p.copy(bc[:, :], dtbc[64:96, :], reads=[cdtbc], writes=[cbc])
        for di in range(2):
            for c0 in range(0, TB, 512):
                pb, cpb = pool.next()
                mm_group(p, [(pb[:, 0:512], wdt_s[:, di * 128:(di + 1) * 128],
                              dtbc[0:64, c0:c0 + 512])], [cwdt, cdtbc], cpb)
                p.ts(sp1[:, :], pb[:, 0:512], dtb_s[:, di:di + 1], None, OP.add,
                     reads=[cpb, cdtb], writes=[c1])
                p.act(sp2[:, :], sp1[:, :], AF.Abs, reads=[c1], writes=[c2])
                p.act(sp2[:, :], sp2[:, :], AF.Exp, scale=-1.0, reads=[c2], writes=[c2])
                p.act(sp2[:, :], sp2[:, :], AF.Ln, bias=1.0, reads=[c2], writes=[c2])
                p.act(sp3[:, :], sp1[:, :], AF.Relu, reads=[c1], writes=[c3])
                p.tt(dt2[:, di * TB + c0: di * TB + c0 + 512], sp3[:, :], sp2[:, :],
                     OP.add, reads=[c2, c3], writes=[cdt2])
            p.tt(dtu2[:, di * TB:(di + 1) * TB], dt2[:, di * TB:(di + 1) * TB],
                 ublk[:, di * TB:(di + 1) * TB], OP.mult, reads=[cdt2, cu], writes=[cdtu])
        for half in range(2):
            o0 = half * 1024
            for n in range(16):
                par = n % 2
                pbB = []
                for c0 in (0, 512):
                    pb, cpb = pool.next()
                    mm_group(p, [(pb[:, 0:512], sel_s[:, n * 128:(n + 1) * 128],
                                  bc[0:32, o0 + c0:o0 + c0 + 512])], [csel, cbc],
                             cpb)
                    pbB.append((pb, cpb))
                for di in range(2):
                    bi4 = di * 2 + par
                    dA, cdA = dA2_[bi4], cdA2[bi4]
                    wts, cw = wts2_[bi4], cw2[bi4]
                    p.act(dA[:, :], dt2[:, di * TB + o0: di * TB + o0 + 1024],
                          AF.Exp, scale=negA[:, di * 16 + n:di * 16 + n + 1],
                          reads=[cdt2, cnA], writes=[cdA])
                    for j, c0 in enumerate((0, 512)):
                        pb, cpb = pbB[j]
                        p.tt(wts[:, c0:c0 + 512],
                             dtu2[:, di * TB + o0 + c0: di * TB + o0 + c0 + 512],
                             pb[:, 0:512], OP.mult, reads=[cdtu, cpb], writes=[cw])
                for di in range(2):
                    bi4 = di * 2 + par
                    p.scan(h2_[bi4][:, :], dA2_[bi4][:, :], wts2_[bi4][:, :],
                           carry[:, di * 16 + n:di * 16 + n + 1],
                           reads=[cdA2[bi4], cw2[bi4], ccar], writes=[ch2[bi4]])
                    p.copy(carry[:, di * 16 + n:di * 16 + n + 1],
                           h2_[bi4][:, 1023:1024], reads=[ch2[bi4]], writes=[ccar])
                    p.red(hrow[:, :], h2_[bi4][:, :], reads=[ch2[bi4]],
                          writes=[chrw])
                    p.tt(hacc[:, :], hacc[:, :], hrow[:, :], OP.add,
                         reads=[chrw, chac], writes=[chac])
                pbC = []
                for c0 in (0, 512):
                    pb, cpb = pool.next()
                    mm_group(p, [(pb[:, 0:512],
                                  sel_s[:, (16 + n) * 128:(17 + n) * 128],
                                  bc[0:32, o0 + c0:o0 + c0 + 512])], [csel, cbc],
                             cpb)
                    pbC.append((pb, cpb))
                for di in range(2):
                    bi4 = di * 2 + par
                    hC, chC = hC2_[bi4], chC2[bi4]
                    for j, c0 in enumerate((0, 512)):
                        pb, cpb = pbC[j]
                        p.tt(hC[:, c0:c0 + 512], h2_[bi4][:, c0:c0 + 512],
                             pb[:, 0:512], OP.mult, reads=[ch2[bi4], cpb],
                             writes=[chC])
                        yi = di * 2 + j
                        p.mm(ybanks[yi][:, :], id_s[:, :], hC[:, c0:c0 + 512],
                             start=(n == 0), stop=(n == 15),
                             reads=[cid, chC], writes=[ycells[yi]])
            for di in range(2):
                for j in range(2):
                    yi = di * 2 + j
                    p.stt(evb[:, di * 1024 + j * 512: di * 1024 + (j + 1) * 512],
                          ublk[:, di * TB + o0 + j * 512:
                               di * TB + o0 + (j + 1) * 512],
                          D_s[:, di:di + 1], ybanks[yi][:, :], OP.mult, OP.add,
                          reads=[cu, cD, ycells[yi]], writes=[cev])
                p.dma("sync", ssm_o[di * 128:(di + 1) * 128, t0 + o0:t0 + o0 + 1024],
                      evb[:, di * 1024: (di + 1) * 1024], reads=[cev])
    pb, cpb = pool.next()
    mm_group(p, [(pb[0:1, 0:1], hacc[:, :], one_s[:, :])], [chac, cone], cpb)
    p.copy(hfin[:, :], pb[0:1, 0:1], reads=[cpb], writes=[chfin])
    p.dma("sync", hsum_o, hfin[:, :], reads=[chfin])
    p.drain_q("sync")
    return p


def build_l0_outproj(p, T=2048):
    ssmT = p.inp("ssmT_sl", [2048, T])
    W = p.inp("W_out_f", [2048, 1024])
    ob = p.inp("out_b", [1024, 1])
    hsums = p.inp("hsums", [8, 1])
    ones8 = p.inp("ones8", [8, 1])
    ones1 = p.inp("ones1", [1, 128])
    onesc_bf = p.inp("onesc_bf", [128, 1], BF16)
    s_o = p.outp("s_fm", [1024, T])
    pool = PsumPool(p, 8)
    wt, cwt, nk = load_w_bf16(p, "wto", W, 2048, 1024)
    xf = p.sbuf("xfo", [128, 16 * T], BF16)
    cxf = [p.cell(xf, f"t{i}") for i in range(nk)]
    for ik in range(nk):
        p.dma("gpsimd", xf[:, ik * T:(ik + 1) * T], ssmT[ik * 128:(ik + 1) * 128, :],
              writes=[cxf[ik]])
    hs_s = p.sbuf("hs_s", [8, 1])
    o8_s = p.sbuf("o8_s", [8, 1])
    o1_s = p.sbuf("o1_s", [1, 128])
    ocb_s = p.sbuf("ocb_s", [128, 1], BF16)
    cscal = p.sbuf("cscal", [1, 1])
    cbc = p.sbuf("cbc128", [128, 1])
    wc = p.sbuf("wc", [128, 8])
    bias2 = p.sbuf("bias2", [128, 8])
    obs = p.sbuf("obs", [128, 8])
    (chs, co8, co1, cocb, ccs, ccb, cwc2, cb2, cob) = (
        p.cell(t) for t in (hs_s, o8_s, o1_s, ocb_s, cscal, cbc, wc, bias2, obs))
    p.dma("sync", hs_s[:, :], hsums, writes=[chs])
    p.dma("sync", o8_s[:, :], ones8, writes=[co8])
    p.dma("sync", o1_s[:, :], ones1, writes=[co1])
    p.dma("sync", ocb_s[:, :], onesc_bf, writes=[cocb])
    for ro in range(8):
        p.dma("sync", obs[:, ro:ro + 1], ob[ro * 128:(ro + 1) * 128, :], writes=[cob])
    pb, cpb = pool.next()
    mm_group(p, [(pb[0:1, 0:1], hs_s[:, :], o8_s[:, :])], [chs, co8], cpb)
    p.ts(cscal[:, :], pb[0:1, 0:1], 1e-8 / (16384.0 * 2048.0 * 16.0), None, OP.mult,
         reads=[cpb], writes=[ccs])
    pb2, cpb2 = pool.next()
    mm_group(p, [(pb2[:, 0:1], o1_s[:, :], cscal[:, :])], [co1, ccs], cpb2)
    p.copy(cbc[:, :], pb2[:, 0:1], reads=[cpb2], writes=[ccb])
    for ro in range(8):
        pb3, cpb3 = pool.next()
        items = [(pb3[:, 0:1], wt[:, ik * 1024 + ro * 128: ik * 1024 + ro * 128 + 128],
                  ocb_s[:, :]) for ik in range(nk)]
        mm_group(p, items, list(cwt) + [cocb], cpb3)
        p.copy(wc[:, ro:ro + 1], pb3[:, 0:1], reads=[cpb3], writes=[cwc2])
    p.ts(bias2[:, :], wc[:, :], cbc[:, 0:1], None, OP.mult, reads=[cwc2, ccb],
         writes=[cb2])
    p.tt(bias2[:, :], bias2[:, :], obs[:, :], OP.add, reads=[cb2, cob], writes=[cb2])
    ev = p.sbuf("evo", [128, 512])
    cev = p.cell(ev)

    def evac(ro, c0, cw, pb, cbk):
        p.ts(ev[:, 0:cw], pb[:, 0:cw], bias2[:, ro:ro + 1], None, OP.add,
             reads=[cbk, cb2], writes=[cev])
        p.dma("sync", s_o[ro * 128:(ro + 1) * 128, c0:c0 + cw], ev[:, 0:cw],
              reads=[cev])
    gemm(p, pool, wt, cwt, nk, 1024, 2048, xf, cxf, T, T, evac)
    p.drain_q("sync")
    return p


def build_l0_gate(p, T=16384, TG=2048):
    xnT = p.inp("xnT_g", [1024, T])
    gw = p.inp("gate_w", [1024, 8])
    sel8 = p.inp("sel8", [128, 8])
    score_o = p.outp("score_row", [1, T])
    mask_o = p.outp("mask_row", [1, T])
    pool = PsumPool(p, 8)
    gw_s = p.sbuf("gw_s", [128, 8 * 8])
    sel_s = p.sbuf("sel8_s", [128, 8])
    cgw, csel = p.cell(gw_s), p.cell(sel_s)
    for ik in range(8):
        p.dma("sync", gw_s[:, ik * 8:(ik + 1) * 8], gw[ik * 128:(ik + 1) * 128, :],
              writes=[cgw])
    p.dma("sync", sel_s[:, :], sel8, writes=[csel])
    xg = p.sbuf("xg", [128, 8 * TG])
    cxg = p.cell(xg)
    mx = p.sbuf("mx", [128, 1])
    sc = p.sbuf("sc", [128, 1])
    m8 = p.sbuf("m8", [128, 8])
    me = p.sbuf("me", [128, 1])
    cmx, csc, cm8, cme = (p.cell(t) for t in (mx, sc, m8, me))
    for t0 in range(0, T, TG):
        for ik in range(8):
            p.dma("sync", xg[:, ik * TG:(ik + 1) * TG],
                  xnT[ik * 128:(ik + 1) * 128, t0:t0 + TG], writes=[cxg])
        for tb in range(TG // 128):
            pb, cpb = pool.next()
            items = [(pb[0:128, 0:8],
                      xg[:, ik * TG + tb * 128: ik * TG + (tb + 1) * 128],
                      gw_s[:, ik * 8:(ik + 1) * 8]) for ik in range(8)]
            mm_group(p, items, [cxg, cgw], cpb)
            p.red(mx[:, :], pb[:, 0:8], op=OP.max, reads=[cpb], writes=[cmx])
            p.act(sc[:, :], mx[:, :], AF.Sigmoid, reads=[cmx], writes=[csc])
            p.tt(m8[:, :], pb[:, 0:8], mx[:, 0:1].broadcast_to([128, 8]),
                 OP.is_equal, reads=[cpb, cmx], writes=[cm8])
            p.tt(m8[:, :], m8[:, :], sel_s[:, :], OP.mult, reads=[cm8, csel],
                 writes=[cm8])
            p.red(me[:, :], m8[:, :], op=OP.max, reads=[cm8], writes=[cme])
            p.dma("sync",
                  score_o[0:1, t0 + tb * 128: t0 + (tb + 1) * 128].rearrange("one t -> t one"),
                  sc[:, :], reads=[csc])
            p.dma("sync",
                  mask_o[0:1, t0 + tb * 128: t0 + (tb + 1) * 128].rearrange("one t -> t one"),
                  me[:, :], reads=[cme])
    p.drain_q("sync")
    return p


def _swiglu_gemm(p, pool, act, cact, sg, csg, wt_g, cwg, wt_u, cwu, wt_d, cwd,
                 xs, cxs, TG, out_cb):
    for ro in range(16):
        for c0 in range(0, TG, 512):
            cw = min(512, TG - c0)
            pg, cpg = pool.next()
            items = [(pg[:, 0:cw], wt_g[:, ik * 2048 + ro * 128: ik * 2048 + ro * 128 + 128],
                      xs[:, ik * TG + c0: ik * TG + c0 + cw]) for ik in range(8)]
            mm_group(p, items, [cxs], cpg,
                     item_reads=[[cwg[ik]] for ik in range(8)])
            pu, cpu = pool.next()
            items = [(pu[:, 0:cw], wt_u[:, ik * 2048 + ro * 128: ik * 2048 + ro * 128 + 128],
                      xs[:, ik * TG + c0: ik * TG + c0 + cw]) for ik in range(8)]
            mm_group(p, items, [cxs], cpu,
                     item_reads=[[cwu[ik]] for ik in range(8)])
            p.act(sg[:, 0:cw], pg[:, 0:cw], AF.Silu, reads=[cpg], writes=[csg])
            p.tt(act[:, ro * TG + c0: ro * TG + c0 + cw], sg[:, 0:cw], pu[:, 0:cw],
                 OP.mult, reads=[csg, cpu], writes=[cact])
    for ro in range(8):
        for c0 in range(0, TG, 512):
            cw = min(512, TG - c0)
            pd, cpd = pool.next()
            items = [(pd[:, 0:cw], wt_d[:, ik * 1024 + ro * 128: ik * 1024 + ro * 128 + 128],
                      act[:, ik * TG + c0: ik * TG + c0 + cw]) for ik in range(16)]
            mm_group(p, items, [cact], cpd,
                     item_reads=[[cwd[ik]] for ik in range(16)])
            out_cb(ro, c0, cw, pd, cpd)


def build_l0_expert(p, T=16384, TG=1024):
    xnT = p.inp("xnT_e", [1024, T])
    xnT_my = p.inp("xnT_my", [1024, 2048])
    score = p.inp("score_row_i", [1, T])
    score_my = p.inp("score_my", [1, 2048])
    mask = p.inp("mask_row_i", [1, T])
    rt_g = p.inp("rt_g", [1024, 2048])
    rt_u = p.inp("rt_u", [1024, 2048])
    rt_d = p.inp("rt_d", [2048, 1024])
    sh_g = p.inp("sh_g", [1024, 2048])
    sh_u = p.inp("sh_u", [1024, 2048])
    sh_d = p.inp("sh_d", [2048, 1024])
    ones1 = p.inp("ones1_e", [1, 128])
    routed_o = p.outp("routed_m", [1024, T])
    shared_o = p.outp("shared_fm", [1024, 2048])
    pool = PsumPool(p, 8)
    o1_s = p.sbuf("o1s", [1, 128])
    co1s = p.cell(o1_s)
    p.dma("sync", o1_s[:, :], ones1, writes=[co1s])
    wt_g, cwg, _ = load_w_bf16(p, "wtg", rt_g, 1024, 2048)
    wt_u, cwu, _ = load_w_bf16(p, "wtu", rt_u, 1024, 2048)
    wt_d, cwd, _ = load_w_bf16(p, "wtd", rt_d, 2048, 1024)
    xs = p.sbuf("xs", [128, 8 * TG], BF16)
    cxs = p.cell(xs)
    srow = p.sbuf("srow", [1, TG])
    csrow = p.cell(srow)
    mrow = p.sbuf("mrow", [1, TG])
    cmrow = p.cell(mrow)
    mrow_g = p.sbuf("mrow_g", [128, TG])
    cmrg = p.cell(mrow_g)
    evr = p.sbuf("evr", [128, 512])
    cevr = p.cell(evr)
    act_s = p.sbuf("act_s", [128, 16 * TG], BF16)
    cact_s = p.cell(act_s)
    sg_s = p.sbuf("sg_s", [128, 512], BF16)
    csg_s = p.cell(sg_s)

    def make_xs(src_xnT, src_score, t0):
        for ik in range(8):
            p.dma("gpsimd", xs[:, ik * TG:(ik + 1) * TG],
                  src_xnT[ik * 128:(ik + 1) * 128, t0:t0 + TG], writes=[cxs])
        if src_score is None:
            return
        p.dma("sync", srow[:, :], src_score[0:1, t0:t0 + TG], writes=[csrow])
        for c0 in range(0, TG, 512):
            psc, cpsc = pool.next()
            mm_group(p, [(psc[:, 0:512], o1_s[:, :], srow[0:1, c0:c0 + 512])],
                     [co1s, csrow], cpsc)
            for ik in range(8):
                p.tt(xs[:, ik * TG + c0: ik * TG + c0 + 512],
                     xs[:, ik * TG + c0: ik * TG + c0 + 512], psc[:, 0:512],
                     OP.mult, reads=[cpsc, cxs], writes=[cxs])

    for t0 in range(0, T, TG):
        make_xs(xnT, score, t0)
        p.dma("sync", mrow[:, :], mask[0:1, t0:t0 + TG], writes=[cmrow])
        for c0 in range(0, TG, 512):
            pmb, cpmb = pool.next()
            mm_group(p, [(pmb[:, 0:512], o1_s[:, :], mrow[0:1, c0:c0 + 512])],
                     [co1s, cmrow], cpmb)
            p.copy(mrow_g[:, c0:c0 + 512], pmb[:, 0:512], reads=[cpmb], writes=[cmrg])

        def out_cb(ro, c0, cw, pd, cpd, t0=t0):
            p.tt(evr[:, 0:cw], pd[:, 0:cw], mrow_g[:, c0:c0 + cw], OP.mult,
                 reads=[cpd, cmrg], writes=[cevr])
            p.dma("sync", routed_o[ro * 128:(ro + 1) * 128, t0 + c0:t0 + c0 + cw],
                  evr[:, 0:cw], reads=[cevr])
        _swiglu_gemm(p, pool, act_s, cact_s, sg_s, csg_s, wt_g, cwg, wt_u, cwu,
                     wt_d, cwd, xs, cxs, TG, out_cb)
    for wsb, csb, src, K, RO in ((wt_g, cwg, sh_g, 1024, 2048),
                                 (wt_u, cwu, sh_u, 1024, 2048),
                                 (wt_d, cwd, sh_d, 2048, 1024)):
        for ik in range(K // 128):
            p.dma("gpsimd", wsb[:, ik * RO:(ik + 1) * RO],
                  src[ik * 128:(ik + 1) * 128, :], writes=[csb[ik]])
    for t0 in range(0, 2048, TG):
        make_xs(xnT_my, None, t0)

        def out_cb_sh(ro, c0, cw, pd, cpd, t0=t0):
            p.copy(evr[:, 0:cw], pd[:, 0:cw], reads=[cpd], writes=[cevr])
            p.dma("sync", shared_o[ro * 128:(ro + 1) * 128, t0 + c0:t0 + c0 + cw],
                  evr[:, 0:cw], reads=[cevr])
        _swiglu_gemm(p, pool, act_s, cact_s, sg_s, csg_s, wt_g, cwg, wt_u, cwu,
                     wt_d, cwd, xs, cxs, TG, out_cb_sh)
    p.drain_q("sync")
    return p


def emit_ln(p, out_ap, in_ap, sc, cs_, rms=False, eps=1e-5, D=1024):
    junk, s1, s2, m, v, r = (sc[k] for k in ("junk", "s1", "s2", "m", "v", "r"))
    cj, c1, c2, cm, cv, cr = (cs_[k] for k in ("junk", "s1", "s2", "m", "v", "r"))
    cin = cs_["in"]
    if not rms:
        p.red(s1[:, :], in_ap, reads=[cin], writes=[c1])
    p.stt(junk[:, :], in_ap, 1.0, in_ap, OP.bypass, OP.mult,
          accum_out=s2[:, :], reads=[cin], writes=[cj, c2])
    if not rms:
        p.ts(m[:, :], s1[:, :], 1.0 / D, reads=[c1], writes=[cm])
        p.ts(v[:, :], s2[:, :], 1.0 / D, reads=[c2], writes=[cv])
        p.stt(v[:, :], m[:, :], m[:, :], v[:, :], OP.mult, OP.subtract,
              reads=[cm, cv], writes=[cv])
        p.ts(v[:, :], v[:, :], -1.0, eps, OP.mult, OP.add, reads=[cv], writes=[cv])
    else:
        p.ts(v[:, :], s2[:, :], 1.0 / D, eps, OP.mult, OP.add, reads=[c2], writes=[cv])
    p.act(r[:, :], v[:, :], AF.Ln, reads=[cv], writes=[cr])
    p.act(r[:, :], r[:, :], AF.Exp, scale=-0.5, reads=[cr], writes=[cr])
    if not rms:
        p.ts(m[:, :], m[:, :], -1.0, reads=[cm], writes=[cm])
        p.ts(out_ap, in_ap, m[:, :], r[:, :], OP.add, OP.mult,
             reads=[cin, cm, cr], writes=[cs_["out"]])
    else:
        p.ts(out_ap, in_ap, r[:, :], None, OP.mult,
             reads=[cin, cr], writes=[cs_["out"]])


def mk_ln_scratch(p, tag, D=1024):
    sc = {"junk": p.sbuf(f"lnj{tag}", [128, D]),
          "s1": p.sbuf(f"ln1{tag}", [128, 1]),
          "s2": p.sbuf(f"ln2{tag}", [128, 1]),
          "m": p.sbuf(f"lnm{tag}", [128, 1]),
          "v": p.sbuf(f"lnv{tag}", [128, 1]),
          "r": p.sbuf(f"lnr{tag}", [128, 1])}
    cs_ = {k: p.cell(t) for k, t in sc.items()}
    return sc, cs_


def build_l0_combine(p, T=2048):
    x_tm = p.inp("x_tm", [T, 1024])
    s_fm = p.inp("s_fm_i", [1024, T])
    sh_fm = p.inp("shared_fm_i", [1024, T])
    r_all = p.inp("routed_all", [8192, T])
    identf = p.inp("ident_f", [128, 128])
    x1_o = p.outp("x1_tm", [T, 1024])
    xn1_o = p.outp("xn1_tm", [T, 1024])
    pool = PsumPool(p, 8)
    idf = p.sbuf("idf", [128, 128])
    cidf = p.cell(idf)
    p.dma("sync", idf[:, :], identf, writes=[cidf])
    accf = p.sbuf("accf", [128, T])
    tmpf = p.sbuf("tmpf", [128, T])
    sfm = p.sbuf("sfm", [128, T])
    cacc, ctmp, csfm = p.cell(accf), p.cell(tmpf), p.cell(sfm)
    nt = T // 128
    xout = p.sbuf("xout", [128, nt * 1024 // 16])
    xout = p.sbuf("xout2", [128, 16 * 1024])
    cxo = p.cell(xout)
    for ti in range(nt):
        p.dma("sync", xout[:, ti * 1024:(ti + 1) * 1024],
              x_tm[ti * 128:(ti + 1) * 128, :], writes=[cxo])
    for ro in range(8):
        p.dma("sync", accf[:, :], sh_fm[ro * 128:(ro + 1) * 128, :], writes=[cacc])
        for e in range(8):
            p.dma("sync", tmpf[:, :], r_all[e * 1024 + ro * 128: e * 1024 + (ro + 1) * 128, :],
                  writes=[ctmp])
            p.tt(accf[:, :], accf[:, :], tmpf[:, :], OP.add, reads=[ctmp, cacc],
                 writes=[cacc])
        p.act(accf[:, :], accf[:, :], AF.Silu, reads=[cacc], writes=[cacc])
        p.dma("sync", sfm[:, :], s_fm[ro * 128:(ro + 1) * 128, :], writes=[csfm])
        p.tt(accf[:, :], accf[:, :], sfm[:, :], OP.mult, reads=[csfm, cacc],
             writes=[cacc])
        for ti in range(nt):
            pb, cpb = pool.next()
            p.trans(pb[:, 0:128], accf[:, ti * 128:(ti + 1) * 128], idf[:, :],
                    reads=[cacc, cidf], writes=[cpb])
            p.tt(xout[:, ti * 1024 + ro * 128: ti * 1024 + (ro + 1) * 128],
                 xout[:, ti * 1024 + ro * 128: ti * 1024 + (ro + 1) * 128],
                 pb[:, 0:128], OP.add, reads=[cpb, cxo], writes=[cxo])
    sc, cs_ = mk_ln_scratch(p, "c")
    xn_t = p.sbuf("xn_t", [128, 1024])
    cs_["in"] = cxo
    cs_["out"] = p.cell(xn_t)
    for ti in range(nt):
        p.dma("sync", x1_o[ti * 128:(ti + 1) * 128, :],
              xout[:, ti * 1024:(ti + 1) * 1024], reads=[cxo])
        emit_ln(p, xn_t[:, :], xout[:, ti * 1024:(ti + 1) * 1024], sc, cs_)
        p.dma("sync", xn1_o[ti * 128:(ti + 1) * 128, :], xn_t[:, :],
              reads=[cs_["out"]])
    p.drain_q("sync")
    return p


def _row_bcast(p, pool, dst, cdst, row_sb, crow, o1_s, co1s, W=1024):
    for c0 in range(0, W, 512):
        cw = min(512, W - c0)
        pb, cpb = pool.next()
        mm_group(p, [(pb[:, 0:cw], o1_s[:, :], row_sb[0:1, c0:c0 + cw])],
                 [co1s, crow], cpb)
        p.copy(dst[:, c0:c0 + cw], pb[:, 0:cw], reads=[cpb], writes=[cdst])


def build_l1_lnchan(p, T=2048, D=1024):
    h_tm = p.inp("h_tm", [T, D])
    wrow = p.inp("ln_w_row", [1, D])
    brow = p.inp("ln_b_row", [1, D])
    ones1 = p.inp("ones1_l", [1, 128])
    z_o = p.outp("z_tm", [T, D])
    pool = PsumPool(p, 4)
    o1_s = p.sbuf("o1l", [1, 128])
    wr_s = p.sbuf("wrs", [1, D])
    br_s = p.sbuf("brs", [1, D])
    wb = p.sbuf("wb", [128, D])
    bb = p.sbuf("bb", [128, D])
    co1, cwr, cbr, cwb, cbb = (p.cell(t) for t in (o1_s, wr_s, br_s, wb, bb))
    p.dma("sync", o1_s[:, :], ones1, writes=[co1])
    p.dma("sync", wr_s[:, :], wrow, writes=[cwr])
    p.dma("sync", br_s[:, :], brow, writes=[cbr])
    _row_bcast(p, pool, wb, cwb, wr_s, cwr, o1_s, co1, D)
    _row_bcast(p, pool, bb, cbb, br_s, cbr, o1_s, co1, D)
    xt = p.sbuf("xtl", [128, D])
    ot = p.sbuf("otl", [128, D])
    cxt, cot = p.cell(xt), p.cell(ot)
    sc, cs_ = mk_ln_scratch(p, "lc", D)
    cs_["in"] = cxt
    cs_["out"] = cot
    for ti in range(T // 128):
        p.dma("sync", xt[:, :], h_tm[ti * 128:(ti + 1) * 128, :], writes=[cxt])
        emit_ln(p, ot[:, :], xt[:, :], sc, cs_, D=D)
        p.tt(ot[:, :], ot[:, :], wb[:, :], OP.mult, reads=[cwb, cot], writes=[cot])
        p.tt(ot[:, :], ot[:, :], bb[:, :], OP.add, reads=[cbb, cot], writes=[cot])
        p.dma("sync", z_o[ti * 128:(ti + 1) * 128, :], ot[:, :], reads=[cot])
    p.drain_q("sync")
    return p


def build_l1_wout(p, T=2048):
    ygT = p.inp("ygT_sl", [1024, T])
    WoT = p.inp("WoutT", [1024, 2048])
    bo = p.inp("b_out_i", [2048, 1])
    h_tm = p.inp("h_tm_w", [T, 1024])
    identf = p.inp("ident_fw", [128, 128])
    h1_o = p.outp("h1_tm", [T, 1024])
    pool = PsumPool(p, 8)
    wt, cwt, nk = load_w_bf16(p, "wtw", WoT, 1024, 2048)
    idf = p.sbuf("idfw", [128, 128])
    cidf = p.cell(idf)
    p.dma("sync", idf[:, :], identf, writes=[cidf])
    bos = p.sbuf("bos", [128, 16])
    cbo = p.cell(bos)
    for ro in range(16):
        p.dma("sync", bos[:, ro:ro + 1], bo[ro * 128:(ro + 1) * 128, :], writes=[cbo])
    xf = p.sbuf("xfw", [128, 8 * T], BF16)
    cxf = [p.cell(xf, f"t{i}") for i in range(nk)]
    for ik in range(nk):
        p.dma("gpsimd", xf[:, ik * T:(ik + 1) * T], ygT[ik * 128:(ik + 1) * 128, :],
              writes=[cxf[ik]])
    sigb = p.sbuf("sigb", [128, 8 * T], BF16)
    csig = p.cell(sigb)
    def evac_g(ro, c0, cw, pb, cbk):
        rr = ro - 8
        p.act(sigb[:, rr * T + c0: rr * T + c0 + cw], pb[:, 0:cw], AF.Sigmoid,
              bias=bos[:, ro:ro + 1], reads=[cbk, cbo], writes=[csig])
    gemm(p, pool, wt, cwt, nk, 2048, 1024, xf, cxf, T, T, evac_g,
         ro_list=range(8, 16))
    xout = p.sbuf("xoutw", [128, (T // 128) * 1024])
    cxo = p.cell(xout)
    for ti in range(T // 128):
        p.dma("sync", xout[:, ti * 1024:(ti + 1) * 1024],
              h_tm[ti * 128:(ti + 1) * 128, :], writes=[cxo])
    glu = p.sbuf("glu", [128, T])
    cglu = p.cell(glu)

    def evac_v(ro, c0, cw, pb, cbk):
        p.ts(glu[:, c0:c0 + cw], pb[:, 0:cw], bos[:, ro:ro + 1], None, OP.add,
             reads=[cbk, cbo], writes=[cglu])
        p.tt(glu[:, c0:c0 + cw], glu[:, c0:c0 + cw],
             sigb[:, ro * T + c0: ro * T + c0 + cw], OP.mult,
             reads=[csig, cglu], writes=[cglu])
        if c0 + cw == T:
            for ti in range(T // 128):
                pb2, cpb2 = pool.next()
                p.trans(pb2[:, 0:128], glu[:, ti * 128:(ti + 1) * 128], idf[:, :],
                        reads=[cglu, cidf], writes=[cpb2])
                p.tt(xout[:, ti * 1024 + ro * 128: ti * 1024 + (ro + 1) * 128],
                     xout[:, ti * 1024 + ro * 128: ti * 1024 + (ro + 1) * 128],
                     pb2[:, 0:128], OP.add, reads=[cpb2, cxo], writes=[cxo])
    gemm(p, pool, wt, cwt, nk, 2048, 1024, xf, cxf, T, T, evac_v,
         ro_list=range(8))
    for ti in range(T // 128):
        p.dma("sync", h1_o[ti * 128:(ti + 1) * 128, :],
              xout[:, ti * 1024:(ti + 1) * 1024], reads=[cxo])
    p.drain_q("sync")
    return p


def custom_ap(base_ap, dims, offset):
    import bass_rust as br
    c = base_ap.copy()
    c.ap = br.VecI64Pair([list(d) for d in dims])
    c.offset = offset
    return c


def build_l1_conv(p, L=8192, NB=2, Q=128):
    NCH = L // Q
    zT = p.inp("zT_sl", [128, NB * L])
    ldt = p.inp("log_dt_sel", [128, 1])
    arl = p.inp("A_re_log_sel", [128, 32])
    aim = p.inp("A_im_sel", [128, 32])
    cre = p.inp("C_re_sel", [128, 32])
    cim = p.inp("C_im_sel", [128, 32])
    dsk = p.inp("D_sel", [128, 1])
    idb = p.inp("ident_b", [128, 128], BF16)
    swp = p.inp("swap64", [64, 64])
    idf32_i = p.inp("ident_f32c", [128, 128])
    yg_o = p.outp("yg_sl", [128, NB * L])
    kpad_d = p.dram("kpad_d", [128, 256])
    q_d = p.dram("q_d", [2, 128, 32, 128])
    pool = PsumPool(p, 6)
    ckpd = p.cell(kpad_d)
    cqd = p.cell(q_d)
    if p.gbanks is not None:
        _ptr_bf = p.ptr_bf
        _ptr_f = p.ptr_f32
        _cptr = p.ptr_cell
    else:
        _ptr_bf = None

    P_ = p.sbuf("Ptab", [128, 8192])
    tmp = p.sbuf("tmpc", [128, 4096])
    sm = p.sbuf("smalls", [128, 512])
    toep = p.sbuf("toep", [128, 128 * 128], BF16)
    m1st = p.sbuf("m1st", [128, 8192], BF16)
    m2st = p.sbuf("m2st", [64, 128 * 128], BF16)
    lamA = p.sbuf("lamA", [64, 128])
    lamB = p.sbuf("lamB", [64, 128])
    swp_s = p.sbuf("swp_s", [64, 64])
    ztst = p.sbuf("ztst", [128, 8192], BF16)
    stbf = p.sbuf("stbf", [64, 8192], BF16)
    scur = p.sbuf("scur", [64, 128])

    ytst = p.sbuf("ytst", [128, 8192], BF16)
    zbf = ytst[:, 0:2048]
    yblk = p.sbuf("yblkc", [128, 2048])
    zldc = p.sbuf("zldc", [128, 2048])
    g1t = p.sbuf("g1tc", [128, 2048])
    g2t = zldc
    kt = g1t[:, 0:256]

    ids = p.sbuf("idsc", [128, 128], BF16)
    (cP, ctmp, csm, ctoep, cm1, cm2, clA, clB, cswp, czt, cst, cscur, cyt,
     cyb, czl, cg1, cid) = (p.cell(t) for t in (
        P_, tmp, sm, toep, m1st, m2st, lamA, lamB, swp_s, ztst, stbf, scur,
        ytst, yblk, zldc, g1t, ids))
    cg2 = czl
    czb = cyt
    ckt = cg1
    ctoep4 = [p.cell(toep, f"s{j}") for j in range(4)]
    cin = [p.cell(zT, "in")]
    idf32 = p.sbuf("idf32c", [128, 128])
    cidf = p.cell(idf32)
    p.dma("sync", ids[:, :], idb, writes=[cid])
    p.dma("sync", swp_s[:, :], swp, writes=[cswp])
    p.dma("sync", idf32[:, :], idf32_i, writes=[cidf])

    S = lambda i: sm[:, i * 32:(i + 1) * 32]
    csl = csm
    d_in = [p.cell(ldt), p.cell(arl), p.cell(aim), p.cell(cre), p.cell(cim),
            p.cell(dsk)]
    ldt_s = p.sbuf("ldt_s", [128, 1])
    dsk_s = p.sbuf("dsk_s", [128, 1])
    arl_s = p.sbuf("arl_s", [128, 32])
    aim_s = p.sbuf("aim_s", [128, 32])
    cre_s = p.sbuf("cre_s", [128, 32])
    cim_s = p.sbuf("cim_s", [128, 32])
    cldt, cdsk, carl, caim, ccre, ccim = (p.cell(t) for t in
                                          (ldt_s, dsk_s, arl_s, aim_s, cre_s, cim_s))
    p.dma("sync", ldt_s[:, :], ldt, writes=[cldt])
    p.dma("sync", dsk_s[:, :], dsk, writes=[cdsk])
    p.dma("sync", arl_s[:, :], arl, writes=[carl])
    p.dma("sync", aim_s[:, :], aim, writes=[caim])
    p.dma("sync", cre_s[:, :], cre, writes=[ccre])
    p.dma("sync", cim_s[:, :], cim, writes=[ccim])
    PI = float(np.pi)

    dtc = p.sbuf("dtc", [128, 1])
    cdt = p.cell(dtc)
    p.act(dtc[:, :], ldt_s[:, :], AF.Exp, reads=[cldt], writes=[cdt])
    p.act(S(0), arl_s[:, :], AF.Exp, reads=[carl], writes=[csl])
    p.ts(S(1), S(0), dtc[:, :], None, OP.mult, reads=[csl, cdt], writes=[csl])
    p.ts(S(1), S(1), -1.0, None, OP.mult, reads=[csl], writes=[csl])
    p.ts(S(2), aim_s[:, :], dtc[:, :], None, OP.mult, reads=[caim, cdt, csl],
         writes=[csl])
    p.act(S(3), S(1), AF.Exp, reads=[csl], writes=[csl])
    p.copy(S(4), S(2), reads=[csl], writes=[csl], eng="vector")
    for _ in range(2):
        p.ts(S(13), S(4), PI, None, OP.is_gt, reads=[csl], writes=[csl])
        p.stt(S(4), S(13), -2.0 * PI, S(4), OP.mult, OP.add, reads=[csl],
              writes=[csl])
    p.act(S(6), S(4), AF.Sin, reads=[csl], writes=[csl])
    p.ts(S(13), S(4), -1.0, PI / 2, OP.mult, OP.add, reads=[csl], writes=[csl])
    p.ts(S(14), S(13), PI, None, OP.is_gt, reads=[csl], writes=[csl])
    p.stt(S(13), S(14), -2.0 * PI, S(13), OP.mult, OP.add, reads=[csl], writes=[csl])
    p.act(S(5), S(13), AF.Sin, reads=[csl], writes=[csl])
    p.tt(S(7), S(3), S(5), OP.mult, reads=[csl], writes=[csl])
    p.tt(S(8), S(3), S(6), OP.mult, reads=[csl], writes=[csl])
    p.ts(S(13), S(7), 1.0, None, OP.subtract, reads=[csl], writes=[csl])
    p.tt(S(14), S(0), S(0), OP.mult, reads=[csl], writes=[csl])
    p.stt(S(14), aim_s[:, :], 1.0, S(14), OP.bypass, OP.bypass, reads=[caim, csl],
          writes=[csl]) if False else None
    p.stt(S(14), aim_s[:, :], 1.0, S(14), OP.mult, OP.add, reads=[caim, csl],
          writes=[csl]) if False else None
    p.tt(S(15), aim_s[:, :], aim_s[:, :], OP.mult, reads=[caim, csl], writes=[csl])
    p.tt(S(14), S(14), S(15), OP.add, reads=[csl], writes=[csl])
    p.recip(S(14), S(14), reads=[csl], writes=[csl])
    p.tt(S(9), S(13), S(0), OP.mult, reads=[csl], writes=[csl])
    p.ts(S(9), S(9), -1.0, None, OP.mult, reads=[csl], writes=[csl])
    p.tt(S(15), S(8), aim_s[:, :], OP.mult, reads=[caim, csl], writes=[csl])
    p.tt(S(9), S(9), S(15), OP.add, reads=[csl], writes=[csl])
    p.tt(S(9), S(9), S(14), OP.mult, reads=[csl], writes=[csl])
    p.tt(S(10), S(8), S(0), OP.mult, reads=[csl], writes=[csl])
    p.ts(S(10), S(10), -1.0, None, OP.mult, reads=[csl], writes=[csl])
    p.tt(S(15), S(13), aim_s[:, :], OP.mult, reads=[caim, csl], writes=[csl])
    p.tt(S(10), S(10), S(15), OP.subtract, reads=[csl], writes=[csl])
    p.tt(S(10), S(10), S(14), OP.mult, reads=[csl], writes=[csl])
    p.tt(S(11), cre_s[:, :], S(9), OP.mult, reads=[ccre, csl], writes=[csl])
    p.tt(S(15), cim_s[:, :], S(10), OP.mult, reads=[ccim, csl], writes=[csl])
    p.tt(S(11), S(11), S(15), OP.subtract, reads=[csl], writes=[csl])
    p.tt(S(12), cre_s[:, :], S(10), OP.mult, reads=[ccre, csl], writes=[csl])
    p.tt(S(15), cim_s[:, :], S(9), OP.mult, reads=[ccim, csl], writes=[csl])
    p.tt(S(12), S(12), S(15), OP.add, reads=[csl], writes=[csl])

    import os as _os
    _dbg = _os.environ.get("DBG_CONV", "0") == "1"
    if _dbg:
        dbg_sm = p.outp("dbg_sm", [128, 512])
        dbg_P = p.outp("dbg_P", [128, 8192])
    Pre = P_[:, 0:4096]
    Pim = P_[:, 4096:8192]
    n32 = lambda ap, n0, w: ap

    def bview(slot_ap, w):
        return slot_ap.unsqueeze(2).broadcast_to([128, 32, w])

    def pview(half, c0, w):
        base = 0 if half == "re" else 4096
        return P_[:, base:base + 4096].rearrange("p (n d) -> p n d", n=32)[:, :, c0:c0 + w]

    p.memset(Pre[:, :], 0.0, writes=[cP])
    p.memset(Pim[:, :], 0.0, writes=[cP])
    p.memset(pview("re", 0, 1), 1.0, writes=[cP])
    p.copy(pview("re", 1, 1), S(7).unsqueeze(2), reads=[csl], writes=[cP])
    p.copy(pview("im", 1, 1), S(8).unsqueeze(2), reads=[csl], writes=[cP])
    t1 = tmp[:, 0:2048]
    t2 = tmp[:, 2048:4096]

    def t1v(w):
        return tmp[:, 0:32 * w].rearrange("p (n d) -> p n d", n=32)

    m = 2
    while m < 128:
        p.tt(t1v(1), pview("re", m - 1, 1), S(7).unsqueeze(2), OP.mult,
             reads=[cP, csl], writes=[ctmp])
        p.tt(pview("re", m, 1), pview("im", m - 1, 1), S(8).unsqueeze(2),
             OP.mult, reads=[cP, csl], writes=[cP])
        p.tt(t2[:, 0:32].rearrange("p (n d) -> p n d", n=32),
             pview("im", m - 1, 1), S(7).unsqueeze(2), OP.mult,
             reads=[cP, csl], writes=[ctmp])
        p.tt(pview("im", m, 1), pview("re", m - 1, 1), S(8).unsqueeze(2),
             OP.mult, reads=[cP, csl], writes=[cP])
        p.tt(pview("im", m, 1), pview("im", m, 1),
             t2[:, 0:32].rearrange("p (n d) -> p n d", n=32), OP.add,
             reads=[cP, ctmp], writes=[cP])
        p.tt(pview("re", m, 1), t1v(1), pview("re", m, 1), OP.subtract,
             reads=[ctmp, cP], writes=[cP])
        w = m - 1
        if w > 0:
            rb = pview("re", m, 1).broadcast_to([128, 32, w])
            ib = pview("im", m, 1).broadcast_to([128, 32, w])
            ta = t1v(w)
            t2v = t2[:, 0:32 * w].rearrange("p (n d) -> p n d", n=32)
            p.tt(ta, pview("re", 1, w), rb, OP.mult, reads=[cP], writes=[ctmp])
            p.tt(t2v, pview("im", 1, w), rb, OP.mult, reads=[cP], writes=[ctmp])
            p.tt(pview("im", m + 1, w), pview("re", 1, w), ib, OP.mult,
                 reads=[cP], writes=[cP])
            p.tt(pview("im", m + 1, w), pview("im", m + 1, w), t2v, OP.add,
                 reads=[cP, ctmp], writes=[cP])
            p.tt(pview("re", m + 1, w), pview("im", 1, w), ib, OP.mult,
                 reads=[cP], writes=[cP])
            p.tt(pview("re", m + 1, w), ta, pview("re", m + 1, w), OP.subtract,
                 reads=[ctmp, cP], writes=[cP])
        m *= 2

    if _dbg:
        p.dma("sync", dbg_sm, sm[:, :], reads=[csl])
        p.dma("sync", dbg_P, P_[:, :], reads=[cP])
    p.tt(t1v(128), pview("re", 0, 128), bview(S(11), 128), OP.mult,
         reads=[cP, csl], writes=[ctmp])
    p.tt(t2[:, 0:2048].rearrange("p (n d) -> p n d", n=32) if False else
         tmp[:, 2048:2048 + 32 * 128].rearrange("p (n d) -> p n d", n=32),
         pview("im", 0, 128), bview(S(12), 128), OP.mult,
         reads=[cP, csl], writes=[ctmp]) if False else None
    p.memset(kt[:, :], 0.0, writes=[ckt])
    for half in range(2):
        n0 = half * 16
        tva = tmp[:, 0:16 * 128].rearrange("p (n d) -> p n d", n=16)
        tvb = tmp[:, 2048:2048 + 16 * 128].rearrange("p (n d) -> p n d", n=16)
        Prh = P_[:, n0 * 128:(n0 + 16) * 128].rearrange("p (n d) -> p n d", n=16)
        Pih = P_[:, 4096 + n0 * 128:4096 + (n0 + 16) * 128].rearrange(
            "p (n d) -> p n d", n=16)
        Cr = S(11)[:, n0:n0 + 16].unsqueeze(2).broadcast_to([128, 16, 128])
        Ci = S(12)[:, n0:n0 + 16].unsqueeze(2).broadcast_to([128, 16, 128])
        p.tt(tva, Prh, Cr, OP.mult, reads=[cP, csl], writes=[ctmp])
        p.tt(tvb, Pih, Ci, OP.mult, reads=[cP, csl], writes=[ctmp])
        p.tt(tva, tva, tvb, OP.subtract, reads=[ctmp], writes=[ctmp])
        red_in = tmp[:, 0:16 * 128].rearrange("p (n d) -> p d n", n=16)
        p.red(kt[:, 128 + half:0] if False else kt[:, 0:128], red_in, op=OP.add,
              reads=[ctmp], writes=[ckt]) if half == 0 else None
        if half == 0:
            pass
        else:
            p.red(kt[:, 128:256], red_in, op=OP.add, reads=[ctmp], writes=[ckt])
    p.tt(kt[:, 0:128], kt[:, 0:128], kt[:, 128:256], OP.add, reads=[ckt],
         writes=[ckt])
    p.ts(kt[:, 0:128], kt[:, 0:128], 2.0, None, OP.mult, reads=[ckt], writes=[ckt])
    p.memset(kt[:, 128:256], 0.0, writes=[ckt])
    p.dma("sync", kpad_d.ap()[:, 127:256], kt[:, 0:129], reads=[ckt], writes=[ckpd])
    p.dma("sync", kpad_d.ap()[:, 0:127], kt[:, 128:255], reads=[ckt], writes=[ckpd])

    for half in range(2):
        n0 = half * 16
        tva = tmp[:, 0:16 * 128].rearrange("p (n d) -> p n d", n=16)
        tvb = tmp[:, 2048:2048 + 16 * 128].rearrange("p (n d) -> p n d", n=16)
        Prh = P_[:, n0 * 128:(n0 + 16) * 128].rearrange("p (n d) -> p n d", n=16)
        Pih = P_[:, 4096 + n0 * 128:4096 + (n0 + 16) * 128].rearrange(
            "p (n d) -> p n d", n=16)
        lr = S(7)[:, n0:n0 + 16].unsqueeze(2).broadcast_to([128, 16, 128])
        li = S(8)[:, n0:n0 + 16].unsqueeze(2).broadcast_to([128, 16, 128])
        Cr = S(11)[:, n0:n0 + 16].unsqueeze(2).broadcast_to([128, 16, 128])
        Ci = S(12)[:, n0:n0 + 16].unsqueeze(2).broadcast_to([128, 16, 128])
        p.tt(tva, Prh, lr, OP.mult, reads=[cP, csl], writes=[ctmp])
        p.tt(tvb, Pih, li, OP.mult, reads=[cP, csl], writes=[ctmp])
        p.tt(tva, tva, tvb, OP.subtract, reads=[ctmp], writes=[ctmp])
        p.tt(tvb, Prh, li, OP.mult, reads=[cP, csl], writes=[ctmp])
        ex = g1t[:, 0:16 * 128].rearrange("p (n d) -> p n d", n=16)
        p.tt(ex, Pih, lr, OP.mult, reads=[cP, csl], writes=[cg1])
        p.tt(tvb, tvb, ex, OP.add, reads=[ctmp, cg1], writes=[ctmp])
        p.tt(ex, tva, Cr, OP.mult, reads=[ctmp, csl], writes=[cg1])
        ex2 = g2t[:, 0:16 * 128].rearrange("p (n d) -> p n d", n=16)
        p.tt(ex2, tvb, Ci, OP.mult, reads=[ctmp, csl], writes=[cg2])
        p.tt(ex, ex, ex2, OP.subtract, reads=[cg1, cg2], writes=[cg1])
        p.ts(g1t[:, 0:2048], g1t[:, 0:2048], 2.0, None, OP.mult, reads=[cg1],
             writes=[cg1])
        p.dma("sync", q_d.ap()[0, :, n0:n0 + 16, :], g1t[:, 0:2048], reads=[cg1],
              writes=[cqd])
        p.tt(ex, tvb, Cr, OP.mult, reads=[ctmp, csl], writes=[cg1])
        p.tt(ex2, tva, Ci, OP.mult, reads=[ctmp, csl], writes=[cg2])
        p.tt(ex, ex, ex2, OP.add, reads=[cg1, cg2], writes=[cg1])
        p.ts(g1t[:, 0:2048], g1t[:, 0:2048], -2.0, None, OP.mult, reads=[cg1],
             writes=[cg1])
        p.dma("sync", q_d.ap()[1, :, n0:n0 + 16, :], g1t[:, 0:2048], reads=[cg1],
              writes=[cqd])

    p.copy(S(13), pview("re", 127, 1).squeeze(2), reads=[cP], writes=[csl])
    p.copy(S(14), pview("im", 127, 1).squeeze(2), reads=[cP], writes=[csl])
    p.tt(S(0), S(13), S(7), OP.mult, reads=[csl], writes=[csl])
    p.tt(S(15), S(14), S(8), OP.mult, reads=[csl], writes=[csl])
    p.tt(S(0), S(0), S(15), OP.subtract, reads=[csl], writes=[csl])
    p.tt(S(1), S(13), S(8), OP.mult, reads=[csl], writes=[csl])
    p.tt(S(15), S(14), S(7), OP.mult, reads=[csl], writes=[csl])
    p.tt(S(1), S(1), S(15), OP.add, reads=[csl], writes=[csl])
    l128b = p.sbuf("l128b", [128, 64])
    cl128 = p.cell(l128b)
    if _ptr_bf is not None:
        ptr = _ptr_bf
        cptr = _cptr
        ptrfv = _ptr_f
        cptrf = _cptr
    else:
        ptr = p.psum("ptrc", [128, 128], BF16)
        cptr = p.cell(ptr)
        _ptrf_t = p.psum("ptrf", [64, 128])
        ptrfv = _ptrf_t[0:64, 0:128]
        cptrf = p.cell(_ptrf_t)
    p.copy(l128b[:, 0:32], S(0), reads=[csl], writes=[cl128])
    p.copy(l128b[:, 32:64], S(0), reads=[csl], writes=[cl128])
    p.trans(ptrfv, l128b[:, :], idf32[:, :], reads=[cl128, cidf],
            writes=[cptrf])
    p.copy(lamA[:, :], ptrfv, reads=[cptrf], writes=[clA])
    p.ts(S(15), S(1), -1.0, None, OP.mult, reads=[csl], writes=[csl])
    p.copy(l128b[:, 0:32], S(15), reads=[csl], writes=[cl128])
    p.copy(l128b[:, 32:64], S(1), reads=[csl], writes=[cl128])
    p.trans(ptrfv, l128b[:, :], idf32[:, :], reads=[cl128, cidf],
            writes=[cptrf])
    p.copy(lamB[:, :], ptrfv, reads=[cptrf], writes=[clB])

    p.tt(S(15), S(3), S(3), OP.mult, reads=[csl], writes=[csl])
    p.recip(S(15), S(15), reads=[csl], writes=[csl])
    p.tt(S(9), S(7), S(15), OP.mult, reads=[csl], writes=[csl])
    p.tt(S(10), S(8), S(15), OP.mult, reads=[csl], writes=[csl])
    p.ts(S(10), S(10), -1.0, None, OP.mult, reads=[csl], writes=[csl])
    p.memset(Pre[:, :], 0.0, reads=[cP], writes=[cP])
    p.memset(Pim[:, :], 0.0, reads=[cP], writes=[cP])
    p.memset(pview("re", 0, 1), 1.0, writes=[cP])
    p.copy(pview("re", 1, 1), S(9).unsqueeze(2), reads=[csl], writes=[cP])
    p.copy(pview("im", 1, 1), S(10).unsqueeze(2), reads=[csl], writes=[cP])
    m = 2
    while m < 128:
        p.tt(t1v(1), pview("re", m - 1, 1), S(9).unsqueeze(2), OP.mult,
             reads=[cP, csl], writes=[ctmp])
        p.tt(pview("re", m, 1), pview("im", m - 1, 1), S(10).unsqueeze(2),
             OP.mult, reads=[cP, csl], writes=[cP])
        p.tt(t2[:, 0:32].rearrange("p (n d) -> p n d", n=32),
             pview("im", m - 1, 1), S(9).unsqueeze(2), OP.mult,
             reads=[cP, csl], writes=[ctmp])
        p.tt(pview("im", m, 1), pview("re", m - 1, 1), S(10).unsqueeze(2),
             OP.mult, reads=[cP, csl], writes=[cP])
        p.tt(pview("im", m, 1), pview("im", m, 1),
             t2[:, 0:32].rearrange("p (n d) -> p n d", n=32), OP.add,
             reads=[cP, ctmp], writes=[cP])
        p.tt(pview("re", m, 1), t1v(1), pview("re", m, 1), OP.subtract,
             reads=[ctmp, cP], writes=[cP])
        w = m - 1
        if w > 0:
            rb = pview("re", m, 1).broadcast_to([128, 32, w])
            ib = pview("im", m, 1).broadcast_to([128, 32, w])
            ta = t1v(w)
            t2v = t2[:, 0:32 * w].rearrange("p (n d) -> p n d", n=32)
            p.tt(ta, pview("re", 1, w), rb, OP.mult, reads=[cP], writes=[ctmp])
            p.tt(t2v, pview("im", 1, w), rb, OP.mult, reads=[cP], writes=[ctmp])
            p.tt(pview("im", m + 1, w), pview("re", 1, w), ib, OP.mult,
                 reads=[cP], writes=[cP])
            p.tt(pview("im", m + 1, w), pview("im", m + 1, w), t2v, OP.add,
                 reads=[cP, ctmp], writes=[cP])
            p.tt(pview("re", m + 1, w), pview("im", 1, w), ib, OP.mult,
                 reads=[cP], writes=[cP])
            p.tt(pview("re", m + 1, w), ta, pview("re", m + 1, w), OP.subtract,
                 reads=[ctmp, cP], writes=[cP])
        m *= 2
    prevb = ztst[:, 0:4096]
    cprev = czt
    for r in range(2):
        for half in range(2):
            n0 = half * 16
            tva = tmp[:, 0:16 * 128].rearrange("p (n d) -> p n d", n=16)
            tvb = tmp[:, 2048:4096].rearrange("p (n d) -> p n d", n=16)
            Prh = P_[:, n0 * 128:(n0 + 16) * 128].rearrange("p (n d) -> p n d", n=16)
            Pih = P_[:, 4096 + n0 * 128:4096 + (n0 + 16) * 128].rearrange(
                "p (n d) -> p n d", n=16)
            l7r = S(13)[:, n0:n0 + 16].unsqueeze(2).broadcast_to([128, 16, 128])
            l7i = S(14)[:, n0:n0 + 16].unsqueeze(2).broadcast_to([128, 16, 128])
            if r == 0:
                p.tt(tva, Prh, l7r, OP.mult, reads=[cP, csl], writes=[ctmp])
                p.tt(tvb, Pih, l7i, OP.mult, reads=[cP, csl], writes=[ctmp])
                p.tt(tva, tva, tvb, OP.subtract, reads=[ctmp], writes=[ctmp])
            else:
                p.tt(tva, Prh, l7i, OP.mult, reads=[cP, csl], writes=[ctmp])
                p.tt(tvb, Pih, l7r, OP.mult, reads=[cP, csl], writes=[ctmp])
                p.tt(tva, tva, tvb, OP.add, reads=[ctmp], writes=[ctmp])
            p.copy(prevb[:, n0 * 128:(n0 + 16) * 128], tmp[:, 0:2048],
                   reads=[ctmp], writes=[cprev])
        for n in range(32):
            p.trans(ptr[:, 0:128], prevb[:, n * 128:(n + 1) * 128], ids[:, :],
                    reads=[cprev, cid], writes=[cptr])
            dst = m1st[:, :].rearrange("t (h c) -> t h c", h=128)[:, :, r * 32 + n:
                                                                 r * 32 + n + 1]
            p.copy(dst, ptr[:, 0:128].unsqueeze(2), reads=[cptr], writes=[cm1])

    for r in range(2):
        src = q_d.ap()[r].rearrange("h n d -> n (h d)") if False else None
        src = custom_ap(q_d.ap(), [[128, 32], [4096, 128], [1, 128]],
                        r * 128 * 32 * 128)
        p.dma("gpsimd", m2st[r * 32:(r + 1) * 32, :], src, reads=[cqd], writes=[cm2])

    for tau in range(128):
        srcap = custom_ap(kpad_d.ap(), [[1, 1], [256, 128], [1, 128]], 127 - tau)
        p.dma("gpsimd", toep[tau:tau + 1, :].rearrange("one (h t) -> one h t", h=128),
              srcap, reads=[ckpd], writes=[ctoep4[tau % 4]])

    zT3 = zT
    for b in range(NB):
        for blk in range(4):
            p.dma("gpsimd", zbf[:, :], zT[0:128, b * L + blk * 2048:
                                          b * L + (blk + 1) * 2048], writes=[czb])
            for cc in range(16):
                c_abs = blk * 16 + cc
                p.trans(ptr[:, 0:128], zbf[:, cc * 128:(cc + 1) * 128], ids[:, :],
                        reads=[czb, cid], writes=[cptr])
                dst = ztst[:, :].rearrange("t (h c) -> t h c", h=128)[:, :, c_abs:
                                                                      c_abs + 1]
                p.copy(dst, ptr[:, 0:128].unsqueeze(2), reads=[cptr], writes=[czt])
        agg = P_[0:64, 0:8192]
        for h in range(128):
            pb, cpb = pool.next()
            mm_group(p, [(pb[0:64, 0:64], m1st[:, h * 64:(h + 1) * 64],
                          ztst[:, h * 64:(h + 1) * 64])], [cm1, czt], cpb)
            p.copy(agg[:, h * 64:(h + 1) * 64], pb[0:64, 0:64], reads=[cpb],
                   writes=[cP])
        p.memset(scur[:, :], 0.0, writes=[cscur])
        rec1 = tmp[0:64, 0:128]
        cr1 = ctmp
        for c in range(NCH):
            dstc = stbf[:, :].rearrange("s (h c) -> s h c", h=128)[:, :, c:c + 1]
            p.copy(dstc, scur[:, :].unsqueeze(2), reads=[cscur], writes=[cst])
            aggc = agg[:, :].rearrange("s (h c) -> s h c", h=128)[:, :, c]
            pb, cpb = pool.next()
            mm_group(p, [(pb[0:64, 0:128], swp_s[:, :], scur[:, :])],
                     [cswp, cscur], cpb)
            p.tt(rec1, pb[0:64, 0:128], lamB[:, :], OP.mult,
                 reads=[cpb, clB], writes=[cr1])
            p.tt(scur[:, :], scur[:, :], lamA[:, :], OP.mult,
                 reads=[clA, cscur], writes=[cscur])
            p.tt(scur[:, :], scur[:, :], rec1, OP.add,
                 reads=[cr1, cscur], writes=[cscur])
            p.tt(scur[:, :], scur[:, :], aggc, OP.add,
                 reads=[cP, cscur], writes=[cscur])
        for h in range(128):
            pb, cpb = pool.next()
            mm_group(p, [
                (pb[:, 0:64], toep[:, h * 128:(h + 1) * 128],
                 ztst[:, h * 64:(h + 1) * 64]),
                (pb[:, 0:64], m2st[:, h * 128:(h + 1) * 128],
                 stbf[:, h * 64:(h + 1) * 64]),
            ], ctoep4 + [cm1, cm2, czt, cst], cpb)
            dst = ytst[:, :].rearrange("t (c h) -> t c h", c=64)[:, :, h]
            p.copy(dst, pb[:, 0:64], reads=[cpb], writes=[cyt])
        for blk in range(4):
            for cc in range(16):
                c_abs = blk * 16 + cc
                p.trans(ptr[:, 0:128], ytst[:, c_abs * 128:(c_abs + 1) * 128],
                        ids[:, :], reads=[cyt, cid], writes=[cptr])
                p.copy(yblk[:, cc * 128:(cc + 1) * 128], ptr[:, 0:128],
                       reads=[cptr], writes=[cyb])
            p.dma("sync", zldc[:, :], zT[0:128, b * L + blk * 2048:
                                         b * L + (blk + 1) * 2048], writes=[czl])
            p.stt(yblk[:, :], zldc[:, :], dsk_s[:, :], yblk[:, :], OP.mult,
                  OP.add, reads=[czl, cdsk, cyb], writes=[cyb])
            p.act(g1t[:, :], yblk[:, :], AF.Square, reads=[cyb], writes=[cg1])
            p.ts(g1t[:, :], g1t[:, :], 0.7978845608 * 0.044715, 0.7978845608,
                 OP.mult, OP.add, reads=[cg1], writes=[cg1])
            p.tt(g1t[:, :], g1t[:, :], yblk[:, :], OP.mult, reads=[cyb, cg1],
                 writes=[cg1])
            p.act(g1t[:, :], g1t[:, :], AF.Tanh, reads=[cg1], writes=[cg1])
            p.ts(g1t[:, :], g1t[:, :], 0.5, 0.5, OP.mult, OP.add, reads=[cg1],
                 writes=[cg1])
            p.tt(g1t[:, :], g1t[:, :], yblk[:, :], OP.mult, reads=[cyb, cg1],
                 writes=[cg1])
            p.dma("sync", yg_o[0:128, b * L + blk * 2048:b * L + (blk + 1) * 2048],
                  g1t[:, :], reads=[cg1])
    import os
    if os.environ.get("DBG_CONV", "0") == "1":
        dbg_k = p.outp("dbg_k", [128, 256])
        dbg_lamA = p.outp("dbg_lamA", [64, 128])
        dbg_lamB = p.outp("dbg_lamB", [64, 128])
        dbg_toep = p.outp("dbg_toep", [128, 16384], BF16)
        dbg_m1 = p.outp("dbg_m1", [128, 8192], BF16)
        dbg_m2 = p.outp("dbg_m2", [64, 16384], BF16)
        dbg_zt = p.outp("dbg_zt", [128, 8192], BF16)
        dbg_st = p.outp("dbg_st", [64, 8192], BF16)
        dbg_agg = p.outp("dbg_agg", [64, 8192])
        dbg_yt = p.outp("dbg_yt", [128, 8192], BF16)
        p.dma("sync", dbg_k, g1t[:, 0:256], reads=[cg1])
        p.dma("sync", dbg_lamA, lamA[:, :], reads=[clA])
        p.dma("sync", dbg_lamB, lamB[:, :], reads=[clB])
        p.dma("sync", dbg_toep, toep[:, :], reads=ctoep4)
        p.dma("sync", dbg_m1, m1st[:, :], reads=[cm1])
        p.dma("sync", dbg_m2, m2st[:, :], reads=[cm2])
        p.dma("sync", dbg_zt, ztst[:, :], reads=[czt])
        p.dma("sync", dbg_st, stbf[:, :], reads=[cst])
        p.dma("sync", dbg_agg, P_[0:64, :], reads=[cP])
        p.dma("sync", dbg_yt, ytst[:, :], reads=[cyt])
    p.drain_q("sync")
    p.drain_q("gpsimd")
    return p


def build_l1_qkv(p, T=2048):
    xT = p.inp("xn1T_sl", [1024, T])

    Wq = p.inp("Wq_f", [1024, 1024])
    Wkc = p.inp("Wk_c_f", [1024, 256])
    Wk = p.inp("Wk_f", [256, 1024])
    Wvc = p.inp("Wv_c_f", [1024, 256])
    Wv = p.inp("Wv_f", [256, 1024])
    sin_i = p.inp("sin_my", [T, 32])
    cos_i = p.inp("cos_my", [T, 32])
    idf = p.inp("ident_fq", [128, 128])
    q_o = p.outp("q_tm", [T, 1024])
    k_o = p.outp("k_tm", [T, 1024])
    v_o = p.outp("v_tm", [T, 1024])
    pool = PsumPool(p, 7)
    ids = p.sbuf("idsq", [128, 128])
    cid = p.cell(ids)
    p.dma("sync", ids[:, :], idf, writes=[cid])
    xf = p.sbuf("xfq", [128, 8 * T], BF16)
    cxf = [p.cell(xf, f"t{i}") for i in range(8)]
    for ik in range(8):
        p.dma("gpsimd", xf[:, ik * T:(ik + 1) * T], xT[ik * 128:(ik + 1) * 128, :],
              writes=[cxf[ik]])
    wtq, cwq, _ = load_w_bf16(p, "wtq", Wq, 1024, 1024)
    res_fm = p.sbuf("res_fm", [128, 8 * T], BF16)
    cres = p.cell(res_fm)
    kc_fm = p.sbuf("kc_fm", [128, 2 * T], BF16)
    ckc = p.cell(kc_fm)
    tm = p.sbuf("tmq", [128, (T // 128) * 1024])
    ctm = p.cell(tm)
    sin_s = p.sbuf("sin_s", [128, (T // 128) * 32])
    cos_s = p.sbuf("cos_s", [128, (T // 128) * 32])
    csin, ccos = p.cell(sin_s), p.cell(cos_s)
    nt = T // 128
    for ti in range(nt):
        p.dma("sync", sin_s[:, ti * 32:(ti + 1) * 32], sin_i[ti * 128:(ti + 1) * 128, :],
              writes=[csin])
        p.dma("sync", cos_s[:, ti * 32:(ti + 1) * 32], cos_i[ti * 128:(ti + 1) * 128, :],
              writes=[ccos])
    rt = p.sbuf("rtq", [128, 1024])
    crt = p.cell(rt)
    idsb = p.sbuf("idsbq", [128, 128], BF16)
    cidb = p.cell(idsb)
    p.copy(idsb[:, :], ids[:, :], reads=[cid], writes=[cidb])
    if p.gbanks is not None:
        ptq = p.ptr_bf
        cptq = p.ptr_cell
    else:
        ptq = p.psum("ptq", [128, 128], BF16)
        cptq = p.cell(ptq)

    def fm_to_tm(cfm, fm):
        for ti in range(nt):
            for ro in range(8):
                p.trans(ptq[:, 0:128], fm[:, ro * T + ti * 128: ro * T + (ti + 1) * 128],
                        idsb[:, :], reads=[cfm, cidb], writes=[cptq])
                p.copy(tm[:, ti * 1024 + ro * 128: ti * 1024 + (ro + 1) * 128],
                       ptq[:, 0:128], reads=[cptq], writes=[ctm])

    def rope_and_out(out_dram, do_rope=True):
        for ti in range(nt):
            src = tm[:, ti * 1024:(ti + 1) * 1024]
            if do_rope:
                x1 = src.rearrange("p (h two j) -> p h two j", h=16, two=2)[:, :, 0, :] \
                    if False else src.rearrange("p (h j two) -> p h j two", h=16,
                                                two=2)[:, :, :, 0]
                x2 = src.rearrange("p (h j two) -> p h j two", h=16, two=2)[:, :, :, 1]
                cosb = cos_s[:, ti * 32:(ti + 1) * 32].unsqueeze(1).broadcast_to(
                    [128, 16, 32]).rearrange("p h j -> p h j") if False else \
                    cos_s[:, ti * 32:(ti + 1) * 32].unsqueeze(0).broadcast_to(
                        [128, 16, 32]) if False else None
                cosb = cos_s[:, ti * 32:(ti + 1) * 32].rearrange(
                    "p j -> p j").unsqueeze(1).broadcast_to([128, 16, 32])
                sinb = sin_s[:, ti * 32:(ti + 1) * 32].unsqueeze(1).broadcast_to(
                    [128, 16, 32])
                r1 = rt.rearrange("p (h j two) -> p h j two", h=16, two=2)[:, :, :, 0]
                r2 = rt.rearrange("p (h j two) -> p h j two", h=16, two=2)[:, :, :, 1]
                g1 = p.sbuf if False else None
                p.tt(r1, x1, cosb, OP.mult, reads=[ctm, ccos], writes=[crt])
                p.tt(r2, x1, sinb, OP.mult, reads=[ctm, csin], writes=[crt])
                p.stt(r1, x2, -1.0, None, OP.mult, OP.bypass) if False else None
                p.tt(x1, x2, sinb, OP.mult, reads=[ctm, csin], writes=[ctm])
                p.tt(r1, r1, x1, OP.subtract, reads=[ctm, crt], writes=[crt])
                p.tt(x1, x2, cosb, OP.mult, reads=[ctm, ccos], writes=[ctm])
                p.tt(r2, r2, x1, OP.add, reads=[ctm, crt], writes=[crt])
                p.dma("sync", out_dram[ti * 128:(ti + 1) * 128, :], rt[:, :],
                      reads=[crt])
            else:
                p.dma("sync", out_dram[ti * 128:(ti + 1) * 128, :], src,
                      reads=[ctm])

    def ev_q(ro, c0, cw, pb, cbk):
        p.copy(res_fm[:, ro * T + c0: ro * T + c0 + cw], pb[:, 0:cw],
               reads=[cbk], writes=[cres])
    gemm(p, pool, wtq, cwq, 8, 1024, 1024, xf, cxf, T, T, ev_q)
    fm_to_tm(cres, res_fm)
    rope_and_out(q_o, True)
    wtc, cwc_, _ = load_w_bf16(p, "wtkc", Wkc, 1024, 256)

    def ev_kc(ro, c0, cw, pb, cbk):
        p.copy(kc_fm[:, ro * T + c0: ro * T + c0 + cw], pb[:, 0:cw],
               reads=[cbk], writes=[ckc])
    gemm(p, pool, wtc, cwc_, 8, 256, 1024, xf, cxf, T, T, ev_kc)
    wtk, cwk, _ = load_w_bf16(p, "wtk", Wk, 256, 1024)
    gemm(p, pool, wtk, cwk, 2, 1024, 256, kc_fm, ckc, T, T, ev_q)
    fm_to_tm(cres, res_fm)
    rope_and_out(k_o, True)
    for ik in range(8):
        p.dma("gpsimd", wtc[:, ik * 256:(ik + 1) * 256],
              Wvc[ik * 128:(ik + 1) * 128, :], writes=[cwc_[ik]])
    gemm(p, pool, wtc, cwc_, 8, 256, 1024, xf, cxf, T, T, ev_kc)
    for ik in range(2):
        p.dma("gpsimd", wtk[:, ik * 1024:(ik + 1) * 1024],
              Wv[ik * 128:(ik + 1) * 128, :], writes=[cwk[ik]])
    gemm(p, pool, wtk, cwk, 2, 1024, 256, kc_fm, ckc, T, T, ev_q)
    fm_to_tm(cres, res_fm)
    rope_and_out(v_o, False)
    p.drain_q("sync")
    return p


def build_l1_attn(p, L=8192, DH=64):
    qT = p.inp("qT_h", [128, 2 * L])
    kT = p.inp("kT_h", [128, 2 * L])
    vv = p.inp("v_h", [2 * L, 128])
    onesb = p.inp("onesb", [128, 64], BF16)
    onesf = p.inp("onesf_at", [1, 128])
    sa_o = p.outp("sa_h", [128, 2 * L])
    if p.gbanks is not None:
        pool = PsumPool(p, 4)
        po = p.gbanks[4]
        pn = p.gbanks[5]
        cpo, cpn = p.gbank_cells[4], p.gbank_cells[5]
    else:
        pool = PsumPool(p, 4)
        po = p.psum("po_at", [64, 512])
        pn = p.psum("pn_at", [1, 512])
        cpo, cpn = p.cell(po), p.cell(pn)
    ob = p.sbuf("ob_at", [128, 64], BF16)
    cob = p.cell(ob)
    p.dma("sync", ob[:, :], onesb, writes=[cob])
    of = p.sbuf("of_at", [1, 128])
    cof = p.cell(of)
    p.dma("sync", of[:, :], onesf, writes=[cof])
    kbs = [p.sbuf(f"kb_at{i}", [64, L], BF16) for i in range(2)]
    qbs = [p.sbuf(f"qb_at{i}", [64, L], BF16) for i in range(2)]
    vts = [p.sbuf(f"vt_at{i}", [128, 64 * 65], BF16) for i in range(2)]
    ckbs = [p.cell(t) for t in kbs]
    cqbs = [p.cell(t) for t in qbs]
    cvts = [[p.cell(t, f"s{j}") for j in range(4)] for t in vts]
    pbufs = [p.sbuf(f"pbuf_at{i}", [128, 512], BF16) for i in range(2)]
    cpbs = [p.cell(t) for t in pbufs]
    pbuf = pbufs[0]
    rq = p.sbuf("rq_at", [128, 512])
    rb2 = p.sbuf("rb2_at", [64, 512])
    sa_s = p.sbuf("sa_s", [64, 512])
    of2 = p.sbuf("of2_at", [128, 128])
    cof2 = p.cell(of2)
    p.dma("sync", of2[64:65, :], onesf, writes=[cof2])
    crq, crb, csa = (p.cell(t) for t in (rq, rb2, sa_s))
    p.memset(vts[0][:, :], 1.0, writes=cvts[0])
    p.memset(vts[1][:, :], 1.0, writes=cvts[1])
    NKB = L // 128
    pairs = [(b, hh) for b in range(2) for hh in range(2)]

    def load_pair(i):
        b, hh = pairs[i]
        pi = i % 2
        p.dma("gpsimd", kbs[pi][:, :],
              kT[hh * 64:(hh + 1) * 64, b * L:(b + 1) * L], writes=[ckbs[pi]])
        p.dma("gpsimd", qbs[pi][:, :],
              qT[hh * 64:(hh + 1) * 64, b * L:(b + 1) * L], writes=[cqbs[pi]])
        for kbk in range(NKB):
            p.dma("gpsimd", vts[pi][:, kbk * 65:kbk * 65 + 64],
                  vv[b * L + kbk * 128: b * L + (kbk + 1) * 128,
                     hh * 64:(hh + 1) * 64], writes=[cvts[pi][kbk % 4]])

    load_pair(0)
    if True:
        for ip, (b, hh) in enumerate(pairs):
            pi = ip % 2
            kb, ckb = kbs[pi], ckbs[pi]
            qb, cqb = qbs[pi], cqbs[pi]
            vt, cvt = vts[pi], cvts[pi]
            if ip + 1 < len(pairs):
                load_pair(ip + 1)
            tiles = [(q0, kbk) for q0 in range(0, L, 512) for kbk in range(NKB)]

            def emit_sp(i):
                q0, kbk = tiles[i]
                ps, cps = pool.next()
                mm_group(p, [(ps[:, 0:512], kb[:, kbk * 128:(kbk + 1) * 128],
                              qb[:, q0:q0 + 512])], [ckb, cqb], cps)
                return ps, cps

            pend = emit_sp(0)
            for i, (q0, kbk) in enumerate(tiles):
                ps, cps = pend
                if i + 1 < len(tiles):
                    pend = emit_sp(i + 1)
                par = kbk % 2
                pbuf, cpb = pbufs[par], cpbs[par]
                p.act(pbuf[:, :], ps[:, 0:512], AF.Exp, scale=0.125,
                      reads=[cps], writes=[cpb])
                p.mm(po[0:65, 0:512], vt[:, kbk * 65:(kbk + 1) * 65],
                     pbuf[:, :], start=(kbk == 0), stop=(kbk == NKB - 1),
                     reads=[cvt[kbk % 4], cpb], writes=[cpo])
                if kbk == NKB - 1:
                    p.recip(rq[64:65, 0:512], po[64:65, 0:512], reads=[cpo],
                            writes=[crq])
                    pb2, cpb2 = pool.next()
                    mm_group(p, [(pb2[0:64, 0:512], of2[64:65, 0:64],
                                  rq[64:65, 0:512])], [cof2, crq], cpb2)
                    p.copy(rb2[:, :], pb2[0:64, 0:512], reads=[cpb2], writes=[crb])
                    p.tt(sa_s[:, :], po[0:64, 0:512], rb2[:, :], OP.mult,
                         reads=[cpo, crb], writes=[csa])
                    p.dma("sync", sa_o[hh * 64:(hh + 1) * 64,
                                       b * L + q0:b * L + q0 + 512],
                          sa_s[:, :], reads=[csa])
    p.drain_q("sync")
    return p

def _emit_rms(p, out_ap, in_ap, cin, cout, wb, cwb, sc, cs_, eps=1e-6, D=1024):
    junk, s2, v, r = sc["junk"], sc["s2"], sc["v"], sc["r"]
    cj, c2, cv, cr = cs_["junk"], cs_["s2"], cs_["v"], cs_["r"]
    p.stt(junk[:, :], in_ap, 1.0, in_ap, OP.bypass, OP.mult,
          accum_out=s2[:, :], reads=[cin], writes=[cj, c2])
    p.ts(v[:, :], s2[:, :], 1.0 / D, eps, OP.mult, OP.add, reads=[c2], writes=[cv])
    p.act(r[:, :], v[:, :], AF.Ln, reads=[cv], writes=[cr])
    p.act(r[:, :], r[:, :], AF.Exp, scale=-0.5, reads=[cr], writes=[cr])
    p.ts(out_ap, in_ap, r[:, :], None, OP.mult, reads=[cin, cr], writes=[cout])
    p.tt(out_ap, out_ap, wb[:, :], OP.mult, reads=[cwb, cout], writes=[cout])


def build_l1_attnout(p, T=2048):
    saT = p.inp("saT_sl", [1024, T])
    Wo = p.inp("Wo_f", [1024, 1024])
    src_tm = p.inp("xn1_sl", [T, 1024])
    nrow = p.inp("norm1_row", [1, 1024])
    ones1 = p.inp("ones1_ao", [1, 128])
    idf = p.inp("ident_ao", [128, 128])
    xr_o = p.outp("xr_tm", [T, 1024])
    pool = PsumPool(p, 8)
    wt, cwt, nk = load_w_bf16(p, "wtao", Wo, 1024, 1024)
    ids = p.sbuf("idsao", [128, 128])
    o1 = p.sbuf("o1ao", [1, 128])
    nr = p.sbuf("nrao", [1, 1024])
    wb = p.sbuf("wbao", [128, 1024])
    cid, co1, cnr, cwb = (p.cell(t) for t in (ids, o1, nr, wb))
    p.dma("sync", ids[:, :], idf, writes=[cid])
    p.dma("sync", o1[:, :], ones1, writes=[co1])
    p.dma("sync", nr[:, :], nrow, writes=[cnr])
    _row_bcast(p, pool, wb, cwb, nr, cnr, o1, co1, 1024)
    xf = p.sbuf("xfao", [128, 8 * T], BF16)
    cxf = [p.cell(xf, f"t{i}") for i in range(nk)]
    for ik in range(nk):
        p.dma("gpsimd", xf[:, ik * T:(ik + 1) * T], saT[ik * 128:(ik + 1) * 128, :],
              writes=[cxf[ik]])
    nt = T // 128
    xout = p.sbuf("xoutao", [128, nt * 1024])
    cxo = p.cell(xout)
    for ti in range(nt):
        p.dma("sync", xout[:, ti * 1024:(ti + 1) * 1024],
              src_tm[ti * 128:(ti + 1) * 128, :], writes=[cxo])
    prod = p.sbuf("prodao", [128, T])
    cprod = p.cell(prod)

    def evac(ro, c0, cw, pb, cbk):
        p.copy(prod[:, c0:c0 + cw], pb[:, 0:cw], reads=[cbk], writes=[cprod])
        if c0 + cw == T:
            for ti in range(nt):
                pb2, cpb2 = pool.next()
                p.trans(pb2[:, 0:128], prod[:, ti * 128:(ti + 1) * 128], ids[:, :],
                        reads=[cprod, cid], writes=[cpb2])
                p.tt(xout[:, ti * 1024 + ro * 128: ti * 1024 + (ro + 1) * 128],
                     xout[:, ti * 1024 + ro * 128: ti * 1024 + (ro + 1) * 128],
                     pb2[:, 0:128], OP.add, reads=[cpb2, cxo], writes=[cxo])
    gemm(p, pool, wt, cwt, nk, 1024, 1024, xf, cxf, T, T, evac)
    sc, cs_ = mk_ln_scratch(p, "ao")
    xr_t = p.sbuf("xr_t_ao", [128, 1024])
    cxrt = p.cell(xr_t)
    for ti in range(nt):
        _emit_rms(p, xr_t[:, :], xout[:, ti * 1024:(ti + 1) * 1024], cxo, cxrt,
                  wb, cwb, sc, cs_)
        p.dma("sync", xr_o[ti * 128:(ti + 1) * 128, :], xr_t[:, :], reads=[cxrt])
    p.drain_q("sync")
    return p


def build_l1_gate2(p, T=2048):
    xT = p.inp("xrT_sl", [1024, T])
    Wg = p.inp("Wg_f", [1024, 2])
    We = p.inp("We_f", [1024, 8])
    gbr = p.inp("gb_row", [1, 2])
    ebr = p.inp("eb_row", [1, 8])
    ones1 = p.inp("ones1_g2", [1, 128])
    idf = p.inp("ident_g2", [128, 128])
    wts_o = p.outp("wts_o", [8, T])
    pool = PsumPool(p, 8)
    xg = p.sbuf("xg2", [128, 8 * T])
    cxg = p.cell(xg)
    for ik in range(8):
        p.dma("sync", xg[:, ik * T:(ik + 1) * T], xT[ik * 128:(ik + 1) * 128, :],
              writes=[cxg])
    wg_s = p.sbuf("wg_s", [128, 8 * 2])
    we_s = p.sbuf("we_s", [128, 8 * 8])
    cwg, cwe = p.cell(wg_s), p.cell(we_s)
    for ik in range(8):
        p.dma("sync", wg_s[:, ik * 2:(ik + 1) * 2], Wg[ik * 128:(ik + 1) * 128, :],
              writes=[cwg])
        p.dma("sync", we_s[:, ik * 8:(ik + 1) * 8], We[ik * 128:(ik + 1) * 128, :],
              writes=[cwe])
    o1 = p.sbuf("o1g2", [1, 128])
    gb_s = p.sbuf("gb_s", [1, 2])
    eb_s = p.sbuf("eb_s", [1, 8])
    gbb = p.sbuf("gbb", [128, 2])
    ebb = p.sbuf("ebb", [128, 8])
    ids = p.sbuf("idsg2", [128, 128])
    (co1, cgb, ceb, cgbb, cebb, cid) = (p.cell(t) for t in
                                        (o1, gb_s, eb_s, gbb, ebb, ids))
    p.dma("sync", o1[:, :], ones1, writes=[co1])
    p.dma("sync", gb_s[:, :], gbr, writes=[cgb])
    p.dma("sync", eb_s[:, :], ebr, writes=[ceb])
    p.dma("sync", ids[:, :], idf, writes=[cid])
    pb, cpb = pool.next()
    mm_group(p, [(pb[:, 0:2], o1[:, :], gb_s[:, :])], [co1, cgb], cpb)
    p.copy(gbb[:, :], pb[:, 0:2], reads=[cpb], writes=[cgbb])
    pb, cpb = pool.next()
    mm_group(p, [(pb[:, 0:8], o1[:, :], eb_s[:, :])], [co1, ceb], cpb)
    p.copy(ebb[:, :], pb[:, 0:8], reads=[cpb], writes=[cebb])
    lg = p.sbuf("lg", [128, 2])
    le = p.sbuf("le", [128, 8])
    m1c = p.sbuf("m1c", [128, 1])
    sc1 = p.sbuf("sc1", [128, 1])
    sc2 = p.sbuf("sc2", [128, 1])
    e2 = p.sbuf("e2", [128, 2])
    e8 = p.sbuf("e8", [128, 8])
    msk = p.sbuf("msk", [128, 8])
    wts = p.sbuf("wts", [128, 8])
    wrow = p.sbuf("wrow", [8, T])
    (clg, cle, cm1, cs1, cs2, ce2, ce8, cmsk, cwts, cwrow) = (
        p.cell(t) for t in (lg, le, m1c, sc1, sc2, e2, e8, msk, wts, wrow))
    for tb in range(T // 128):
        pg, cpg = pool.next()
        mm_group(p, [(pg[0:128, 0:2], xg[:, ik * T + tb * 128: ik * T + (tb + 1) * 128],
                      wg_s[:, ik * 2:(ik + 1) * 2]) for ik in range(8)],
                 [cxg, cwg], cpg)
        pe, cpe = pool.next()
        mm_group(p, [(pe[0:128, 0:8], xg[:, ik * T + tb * 128: ik * T + (tb + 1) * 128],
                      we_s[:, ik * 8:(ik + 1) * 8]) for ik in range(8)],
                 [cxg, cwe], cpe)
        p.tt(lg[:, :], pg[:, 0:2], gbb[:, :], OP.add, reads=[cpg, cgbb], writes=[clg])
        p.tt(le[:, :], pe[:, 0:8], ebb[:, :], OP.add, reads=[cpe, cebb], writes=[cle])
        p.red(m1c[:, :], lg[:, :], op=OP.max, reads=[clg], writes=[cm1])
        p.ts(m1c[:, :], m1c[:, :], -1.0, None, OP.mult, reads=[cm1], writes=[cm1])
        p.act(e2[:, :], lg[:, :], AF.Exp, bias=m1c[:, :], reads=[clg, cm1],
              writes=[ce2])
        p.red(sc1[:, :], e2[:, :], op=OP.add, reads=[ce2], writes=[cs1])
        p.recip(sc1[:, :], sc1[:, :], reads=[cs1], writes=[cs1])
        p.red(sc2[:, :], e2[:, :], op=OP.max, reads=[ce2], writes=[cs2])
        p.tt(sc2[:, :], sc2[:, :], sc1[:, :], OP.mult, reads=[cs1, cs2],
             writes=[cs2])
        p.ts(m1c[:, :], m1c[:, :], -1.0, None, OP.mult, reads=[cm1], writes=[cm1])
        p.tt(msk[:, 0:2], lg[:, :], m1c[:, 0:1].broadcast_to([128, 2]), OP.is_ge,
             reads=[clg, cm1], writes=[cmsk])
        p.ts(msk[:, 2:4], msk[:, 0:2], 1.0, 1.0, OP.mult, OP.subtract,
             reads=[cmsk], writes=[cmsk])
        gm8 = msk[:, 2:4].unsqueeze(2).broadcast_to([128, 2, 4])
        p.stt(e8.rearrange("p (g j) -> p g j", g=2), gm8, 1e30,
              le.rearrange("p (g j) -> p g j", g=2), OP.mult, OP.add,
              reads=[cmsk, cle], writes=[ce8])
        p.red(m1c[:, :], e8[:, :], op=OP.max, reads=[ce8], writes=[cm1])
        p.ts(m1c[:, :], m1c[:, :], -1.0, None, OP.mult, reads=[cm1], writes=[cm1])
        p.act(e8[:, :], e8[:, :], AF.Exp, bias=m1c[:, :], reads=[ce8, cm1],
              writes=[ce8])
        p.red(sc1[:, :], e8[:, :], op=OP.add, reads=[ce8], writes=[cs1])
        p.recip(sc1[:, :], sc1[:, :], reads=[cs1], writes=[cs1])
        p.ts(e8[:, :], e8[:, :], sc1[:, :], sc2[:, :], OP.mult, OP.mult,
             reads=[ce8, cs1, cs2], writes=[ce8])
        p.red(m1c[:, :], e8[:, :], op=OP.max, reads=[ce8], writes=[cm1])
        p.tt(msk[:, :], e8[:, :], m1c[:, 0:1].broadcast_to([128, 8]), OP.is_lt,
             reads=[ce8, cm1], writes=[cmsk])
        p.tt(wts[:, :], e8[:, :], msk[:, :], OP.mult, reads=[ce8, cmsk],
             writes=[cwts])
        p.red(sc1[:, :], wts[:, :], op=OP.max, reads=[cwts], writes=[cs1])
        p.tt(msk[:, :], e8[:, :], sc1[:, 0:1].broadcast_to([128, 8]), OP.is_ge,
             reads=[ce8, cs1], writes=[cmsk])
        p.tt(wts[:, :], e8[:, :], msk[:, :], OP.mult, reads=[ce8, cmsk],
             writes=[cwts])
        pb3, cpb3 = pool.next()
        p.trans(pb3[0:8, 0:128], wts[:, :], ids[:, :], reads=[cwts, cid],
                writes=[cpb3])
        p.copy(wrow[:, tb * 128:(tb + 1) * 128], pb3[0:8, 0:128], reads=[cpb3],
               writes=[cwrow])
    p.dma("sync", wts_o, wrow[:, :], reads=[cwrow])
    p.drain_q("sync")
    return p


def build_l1_moe1(p, T=2048, TG=1024):
    xT = p.inp("xrT_sl1", [1024, T])
    Wsa = p.inp("Wsi_a", [1024, 2048])
    Wsb = p.inp("Wsi_b", [1024, 2048])
    Wso = p.inp("Wso_f", [2048, 1024])
    W1a = p.inp("W1_a", [1024, 2048])
    W1b = p.inp("W1_b", [1024, 2048])
    sh_o = p.outp("shared_o", [1024, T])
    h1_o = p.outp("h1_o", [2048, T], BF16)
    pool = PsumPool(p, 8)
    wta, cwa, _ = load_w_bf16(p, "wta1", Wsa, 1024, 2048)
    wtb, cwb2, _ = load_w_bf16(p, "wtb1", Wsb, 1024, 2048)
    wtd, cwd, _ = load_w_bf16(p, "wtd1", Wso, 2048, 1024)
    xf = p.sbuf("xf1", [128, 8 * TG], BF16)
    cxf = [p.cell(xf, f"t{i}") for i in range(8)]
    act = p.sbuf("act1", [128, 16 * TG], BF16)
    cact = p.cell(act)
    sg = p.sbuf("sg1", [128, 512], BF16)
    csg = p.cell(sg)
    ev = p.sbuf("ev1", [128, 512])
    cev = p.cell(ev)

    def swiglu_group(t0):
        for ro in range(16):
            for c0 in range(0, TG, 512):
                pa, cpa = pool.next()
                mm_group(p, [(pa[:, 0:512],
                              wta[:, ik * 2048 + ro * 128: ik * 2048 + ro * 128 + 128],
                              xf[:, ik * TG + c0: ik * TG + c0 + 512])
                             for ik in range(8)], [], cpa,
                         item_reads=[[cwa[ik], cxf[ik]] for ik in range(8)])
                pb_, cpb_ = pool.next()
                mm_group(p, [(pb_[:, 0:512],
                              wtb[:, ik * 2048 + ro * 128: ik * 2048 + ro * 128 + 128],
                              xf[:, ik * TG + c0: ik * TG + c0 + 512])
                             for ik in range(8)], [], cpb_,
                         item_reads=[[cwb2[ik], cxf[ik]] for ik in range(8)])
                p.act(sg[:, :], pa[:, 0:512], AF.Silu, reads=[cpa], writes=[csg])
                p.tt(act[:, ro * TG + c0: ro * TG + c0 + 512], sg[:, :],
                     pb_[:, 0:512], OP.mult, reads=[csg, cpb_], writes=[cact])

    for gi, t0 in enumerate(range(0, T, TG)):
        for ik in range(8):
            p.dma("gpsimd", xf[:, ik * TG:(ik + 1) * TG],
                  xT[ik * 128:(ik + 1) * 128, t0:t0 + TG], writes=[cxf[ik]])
        swiglu_group(t0)
        for ro in range(8):
            for c0 in range(0, TG, 512):
                pd, cpd = pool.next()
                mm_group(p, [(pd[:, 0:512],
                              wtd[:, ik * 1024 + ro * 128: ik * 1024 + ro * 128 + 128],
                              act[:, ik * TG + c0: ik * TG + c0 + 512])
                             for ik in range(16)], [cact], cpd,
                         item_reads=[[cwd[ik]] for ik in range(16)])
                p.copy(ev[:, 0:512], pd[:, 0:512], reads=[cpd], writes=[cev])
                p.dma("sync", sh_o[ro * 128:(ro + 1) * 128, t0 + c0:t0 + c0 + 512],
                      ev[:, 0:512], reads=[cev])
        for ik in range(8):
            p.dma("gpsimd", wta[:, ik * 2048:(ik + 1) * 2048],
                  W1a[ik * 128:(ik + 1) * 128, :], writes=[cwa[ik]])
            p.dma("gpsimd", wtb[:, ik * 2048:(ik + 1) * 2048],
                  W1b[ik * 128:(ik + 1) * 128, :], writes=[cwb2[ik]])
        swiglu_group(t0)
        for ro in range(16):
            p.dma("sync", h1_o[ro * 128:(ro + 1) * 128, t0:t0 + TG],
                  act[:, ro * TG:(ro + 1) * TG], reads=[cact])
        if t0 + TG < T:
            for ik in range(8):
                p.dma("gpsimd", wta[:, ik * 2048:(ik + 1) * 2048],
                      Wsa[ik * 128:(ik + 1) * 128, :], writes=[cwa[ik]])
                p.dma("gpsimd", wtb[:, ik * 2048:(ik + 1) * 2048],
                      Wsb[ik * 128:(ik + 1) * 128, :], writes=[cwb2[ik]])
    p.drain_q("sync")
    return p


def build_l1_moe2(p, T=2048, TG=2048):
    h1 = p.inp("h1_i", [2048, T], BF16)
    sh = p.inp("shared_i", [1024, T])
    wts8 = p.inp("wts8", [8, T])
    W2 = p.inp("W2_all", [16384, 1024])
    xr_tm = p.inp("xr_sl", [T, 1024])
    nrow = p.inp("norm2_row", [1, 1024])
    ones1 = p.inp("ones1_m2", [1, 128])
    idf = p.inp("ident_m2", [128, 128])
    g0_o = p.outp("g0_tm", [T, 1024])
    pool = PsumPool(p, 8)
    ids = p.sbuf("idsm2", [128, 128])
    o1 = p.sbuf("o1m2", [1, 128])
    nr = p.sbuf("nrm2", [1, 1024])
    wb = p.sbuf("wbm2", [128, 1024])
    cid, co1, cnr, cwb = (p.cell(t) for t in (ids, o1, nr, wb))
    p.dma("sync", ids[:, :], idf, writes=[cid])
    p.dma("sync", o1[:, :], ones1, writes=[co1])
    p.dma("sync", nr[:, :], nrow, writes=[cnr])
    _row_bcast(p, pool, wb, cwb, nr, cnr, o1, co1, 1024)
    wtd = p.sbuf("wtd2", [128, 16 * 1024], BF16)
    cwd = [p.cell(wtd, f"t{i}") for i in range(16)]
    h1f = p.sbuf("h1f", [128, 16 * TG], BF16)
    ch1 = [p.cell(h1f, f"t{i}") for i in range(16)]
    racc = p.sbuf("racc", [128, 8 * TG])
    cra = p.cell(racc)
    web = p.sbuf("web", [128, TG])
    cweb = p.cell(web)
    wrow = p.sbuf("wrowm", [1, T])
    cwr = p.cell(wrow)
    tmp5 = p.sbuf("tmp5", [128, 512])
    ct5 = p.cell(tmp5)
    for ik in range(16):
        p.dma("gpsimd", h1f[:, ik * TG:(ik + 1) * TG],
              h1[ik * 128:(ik + 1) * 128, :], writes=[ch1[ik]])
    for ro in range(8):
        p.dma("sync", racc[:, ro * TG:(ro + 1) * TG],
              sh[ro * 128:(ro + 1) * 128, :], writes=[cra])
    for e in range(8):
        for ik in range(16):
            p.dma("gpsimd", wtd[:, ik * 1024:(ik + 1) * 1024],
                  W2[e * 2048 + ik * 128: e * 2048 + (ik + 1) * 128, :],
                  writes=[cwd[ik]])
        p.dma("sync", wrow[:, :], wts8[e:e + 1, :], writes=[cwr])
        for c0 in range(0, TG, 512):
            pwb, cpwb = pool.next()
            mm_group(p, [(pwb[:, 0:512], o1[:, :], wrow[0:1, c0:c0 + 512])],
                     [co1, cwr], cpwb)
            p.copy(web[:, c0:c0 + 512], pwb[:, 0:512], reads=[cpwb], writes=[cweb])
        for ro in range(8):
            for c0 in range(0, TG, 512):
                pd, cpd = pool.next()
                mm_group(p, [(pd[:, 0:512],
                              wtd[:, ik * 1024 + ro * 128: ik * 1024 + ro * 128 + 128],
                              h1f[:, ik * TG + c0: ik * TG + c0 + 512])
                             for ik in range(16)], [], cpd,
                         item_reads=[[cwd[ik], ch1[ik]] for ik in range(16)])
                p.tt(tmp5[:, 0:512], pd[:, 0:512], web[:, c0:c0 + 512],
                     OP.mult, reads=[cpd, cweb], writes=[ct5])
                p.tt(racc[:, ro * TG + c0: ro * TG + c0 + 512],
                     racc[:, ro * TG + c0: ro * TG + c0 + 512], tmp5[:, 0:512],
                     OP.add, reads=[ct5, cra], writes=[cra])
    sc, cs_ = mk_ln_scratch(p, "m2")
    xt = p.sbuf("xtm2", [128, 1024])
    cxt = p.cell(xt)
    g0_t = p.sbuf("g0_t", [128, 1024])
    cg0 = p.cell(g0_t)
    nt = T // 128
    for ti in range(nt):
        p.dma("sync", xt[:, :], xr_tm[ti * 128:(ti + 1) * 128, :], writes=[cxt])
        for ro in range(8):
            pb2, cpb2 = pool.next()
            p.trans(pb2[:, 0:128], racc[:, ro * TG + ti * 128: ro * TG + (ti + 1) * 128],
                    ids[:, :], reads=[cra, cid], writes=[cpb2])
            p.tt(xt[:, ro * 128:(ro + 1) * 128], xt[:, ro * 128:(ro + 1) * 128],
                 pb2[:, 0:128], OP.add, reads=[cpb2, cxt], writes=[cxt])
        _emit_rms(p, g0_t[:, :], xt[:, :], cxt, cg0, wb, cwb, sc, cs_)
        p.dma("sync", g0_o[ti * 128:(ti + 1) * 128, :], g0_t[:, :], reads=[cg0])
    p.drain_q("sync")
    return p

def build_l1_final(p, T=2048):
    x1 = p.inp("x1_f", [T, 1024])
    s1 = p.inp("s1_f", [T, 1024])
    g0 = p.inp("g0_f", [T, 1024])
    out = p.outp("xout_tm", [T, 1024])
    a = p.sbuf("fa", [128, 1024])
    b = p.sbuf("fb", [128, 1024])
    c = p.sbuf("fc", [128, 1024])
    ca, cb, cc = p.cell(a), p.cell(b), p.cell(c)
    for ti in range(T // 128):
        p.dma("sync", a[:, :], x1[ti * 128:(ti + 1) * 128, :], writes=[ca])
        p.dma("sync", b[:, :], s1[ti * 128:(ti + 1) * 128, :], writes=[cb])
        p.dma("sync", c[:, :], g0[ti * 128:(ti + 1) * 128, :], writes=[cc])
        p.act(c[:, :], c[:, :], AF.Silu, reads=[cc], writes=[cc])
        p.tt(c[:, :], c[:, :], b[:, :], OP.mult, reads=[cb, cc], writes=[cc])
        p.tt(c[:, :], c[:, :], a[:, :], OP.add, reads=[ca, cc], writes=[cc])
        p.dma("sync", out[ti * 128:(ti + 1) * 128, :], c[:, :], reads=[cc])
    p.drain_q("sync")
    return p


def build_tp(p, tag, T=2048, D=1024):
    srcd = p.inp(f"tp_src_{tag}", [T, D])
    dstd = p.outp(f"tp_dst_{tag}", [D, T])
    pool = PsumPool(p, 4)
    idt = p.sbuf("idtp", [128, 128])
    cidt = p.cell(idt)
    idf = p.inp(f"tp_id_{tag}", [128, 128])
    p.dma("sync", idt[:, :], idf, writes=[cidt])
    st = p.sbuf("st_tp", [128, 1024])
    ev = p.sbuf("ev_tp", [128, 512])
    cst, cev = p.cell(st), p.cell(ev)
    nt = T // 128
    for ti in range(nt):
        p.dma("sync", st[:, :], srcd[ti * 128:(ti + 1) * 128, :], writes=[cst])
        for ro in range(D // 128):
            pb, cpb = pool.next()
            p.trans(pb[:, 0:128], st[:, ro * 128:(ro + 1) * 128], idt[:, :],
                    reads=[cst, cidt], writes=[cpb])
            p.copy(ev[:, 0:128], pb[:, 0:128], reads=[cpb], writes=[cev])
            p.dma("sync", dstd[ro * 128:(ro + 1) * 128, ti * 128:(ti + 1) * 128],
                  ev[:, 0:128], reads=[cev])
    p.drain_q("sync")
    return p


def build_ln0(p, TOK, D=1024, nt=None):
    nt = nt or TOK // 128
    x = p.inp("x", [TOK, D])
    xn = p.outp("xn", [TOK, D])
    xt = p.sbuf("xt", [128, D])
    ot = p.sbuf("ot", [128, D])
    junk = p.sbuf("junk", [128, D])
    s1 = p.sbuf("s1", [128, 1])
    s2 = p.sbuf("s2", [128, 1])
    m = p.sbuf("m", [128, 1])
    v = p.sbuf("v", [128, 1])
    r = p.sbuf("r", [128, 1])
    cx, co, cj, c1, c2, cm, cv, cr = (p.cell(t) for t in (xt, ot, junk, s1, s2, m, v, r))
    out_toks = []
    for i in range(nt):
        p.dma("sync", xt[:, :], x[i * 128:(i + 1) * 128, :], writes=[cx])
        p.red(s1[:, :], xt[:, :], reads=[cx], writes=[c1])
        p.stt(junk[:, :], xt[:, :], 1.0, xt[:, :], OP.bypass, OP.mult,
              accum_out=s2[:, :], reads=[cx], writes=[cj, c2])
        p.ts(m[:, :], s1[:, :], 1.0 / D, reads=[c1], writes=[cm])
        p.ts(v[:, :], s2[:, :], 1.0 / D, reads=[c2], writes=[cv])
        p.stt(v[:, :], m[:, :], m[:, :], v[:, :], OP.mult, OP.subtract,
              reads=[cm, cv], writes=[cv])
        p.ts(v[:, :], v[:, :], -1.0, 1e-5, OP.mult, OP.add, reads=[cv], writes=[cv])
        p.act(r[:, :], v[:, :], AF.Ln, reads=[cv], writes=[cr])
        p.act(r[:, :], r[:, :], AF.Exp, scale=-0.5, reads=[cr], writes=[cr])
        p.ts(m[:, :], m[:, :], -1.0, reads=[cm], writes=[cm])
        p.ts(ot[:, :], xt[:, :], m[:, :], r[:, :], OP.add, OP.mult,
             reads=[cx, cm, cr], writes=[co])
        out_toks.append(p.dma("sync", xn[i * 128:(i + 1) * 128, :], ot[:, :],
                              reads=[co]))
    p.drain_q("sync")
    return p


"""Fused single-launch megakernel driver."""
from concourse.bass_utils import run_bass_kernel_spmd
from concourse.replica_groups import maybe_share_collective_output_space
import concourse.mybir as mybir

NC, T = 8, 16384
f32 = np.float32
RG8 = [list(range(8))]


def _cc(a):
    return np.ascontiguousarray(a)


def build_fused(p):
    p.init_gbanks()
    d = p.dram

    def idram(name, shape, dtype=F32, space=None):
        return d(name, shape, dtype, addr_space=space)

    def barrier():
        toks = [(s, c) for s, c in p.sem_count.items() if c > 0]
        for eng in ("tensor", "vector", "scalar", "gpsimd", "sync"):
            p.wait_all(eng, toks)

    def coll(kind, in_ap, out_ap, cells_r, cell_w):
        barrier()
        op = mybir.AluOpType.add if kind == "AllReduce" else mybir.AluOpType.bypass
        p.collective(kind, op, [in_ap.opt()], [out_ap.opt()],
                     reads=cells_r, writes=[cell_w])
        barrier()

    sp_ag = maybe_share_collective_output_space("AllGather", RG8)
    sp_a2a = maybe_share_collective_output_space("AllToAll", RG8)

    W_in_d = idram("W_in_d", [1024, 512])
    W_xp_d = idram("W_xp_d", [256, 96])
    W_dt_d = idram("W_dt_d", [64, 256])
    W_out_d = idram("W_out_d", [2048, 1024])
    xn_d = idram("xn_d", [2048, 1024])
    xnT_in = idram("xnT_in", [1024, 2048])
    xnT_ag = idram("xnT_ag", [8, 1024, 2048], space=sp_ag)
    xcz_d = idram("xcz_d", [512, T])
    u_d = idram("u_d", [256, T])
    xpp_in = idram("xpp_in", [96, T])
    xpp_ag = idram("xpp_ag", [768, T], space=sp_ag)
    ssm_a2a_i = idram("ssm_a2a_i", [8, 256, 2048])
    ssm_a2a_o = idram("ssm_a2a_o", [8, 256, 2048], space=sp_a2a)
    hs_in = idram("hs_in", [1, 1])
    hs_ag = idram("hs_ag", [8, 1], space=sp_ag)
    s_fm_d = idram("s_fm_d", [1024, 2048])
    score_d = idram("score_d", [1, T])
    mask_d = idram("mask_d", [1, T])
    sc_a2a_o = idram("sc_a2a_o", [8, 2048], space=sp_a2a)
    rt_a2a_i = idram("rt_a2a_i", [8, 1024, 2048])
    rt_a2a_o = idram("rt_a2a_o", [8, 1024, 2048], space=sp_a2a)
    sh_fm_d = idram("sh_fm_d", [1024, 2048])
    x1_d = idram("x1_d", [2048, 1024])
    xn1_d = idram("xn1_d", [2048, 1024])
    z_d = idram("z_d", [2048, 1024])
    z_a2a_i = idram("z_a2a_i", [8, 128, 2048])
    z_a2a_o = idram("z_a2a_o", [8, 128, 2048], space=sp_a2a)
    yg_a2a_i = idram("yg_a2a_i", [8, 128, 2048])
    yg_a2a_o = idram("yg_a2a_o", [8, 128, 2048], space=sp_a2a)
    h_d = [idram(f"h_d{i}", [2048, 1024]) for i in range(5)]
    q_d = idram("q_tm_d", [2048, 1024])
    k_d2 = idram("k_tm_d", [2048, 1024])
    v_d2 = idram("v_tm_d", [2048, 1024])
    q_a2a_i = idram("q_a2a_i", [8, 128, 2048])
    q_a2a_o = idram("q_a2a_o", [8, 128, 2048], space=sp_a2a)
    k_a2a_i = idram("k_a2a_i", [8, 128, 2048])
    k_a2a_o = idram("k_a2a_o", [8, 128, 2048], space=sp_a2a)
    v_a2a_i = idram("v_a2a_i", [8, 2048, 128])
    v_a2a_o = idram("v_a2a_o", [8, 2048, 128], space=sp_a2a)
    sa_a2a_i = idram("sa_a2a_i", [8, 128, 2048])
    sa_a2a_o = idram("sa_a2a_o", [8, 128, 2048], space=sp_a2a)
    xnT_flat = idram("xnT_flat", [1024, T])
    ssm_flat = idram("ssm_flat", [256, T])
    rt_flat = idram("rt_flat", [1024, T])
    zT_flat = idram("zT_flat", [128, T])
    yg_flat = idram("yg_flat", [128, T])
    qT_flat = idram("qT_flat", [128, T])
    kT_flat = idram("kT_flat", [128, T])
    sa_flat = idram("sa_flat", [128, T])
    xr_d = idram("xr_d", [2048, 1024])
    xrT_d = idram("xrT_d", [1024, 2048])
    wts_d = idram("wts_d", [8, 2048])
    sh1_d = idram("sh1_d", [1024, 2048])
    h1_d = idram("h1_d", [2048, 2048], BF16)
    g0_d = idram("g0_d", [2048, 1024])

    cells = {}

    def dc(t):
        return p.cell(t)

    def pack_shards(dst_staged, src_flat, rows):
        sap = custom_ap(src_flat.ap(), [[2048, 8], [T, rows], [1, 2048]], 0)
        p.dma("sync", dst_staged.ap(), sap, reads=[dc(src_flat)],
              writes=[dc(dst_staged)])

    def unpack_shards(dst_flat, src_staged, rows):
        dap = custom_ap(dst_flat.ap(), [[2048, 8], [T, rows], [1, 2048]], 0)
        p.dma("sync", dap, src_staged.ap(), reads=[dc(src_staged)],
              writes=[dc(dst_flat)])

    p.io = {"W_in": W_in_d.ap(), "W_xp": W_xp_d.ap(), "W_dt": W_dt_d.ap(),
            "W_out": W_out_d.ap()}
    with p.phase():
        build_prep(p, plan={"in": (32, 8, 32, 64), "xp": (4, 8, 64, 12),
                               "dt": (8, 4, 8, 64), "out": (32, 32, 64, 32)})
    barrier()
    p.io = {"xn": xn_d.ap()}
    with p.phase():
        build_ln0(p, 2048)
    barrier()
    p.io = {"tp_src_xn": xn_d.ap(), "tp_dst_xn": xnT_in.ap()}
    with p.phase():
        build_tp(p, "xn")
    coll("AllGather", xnT_in.ap(), xnT_ag.ap(), [dc(xnT_in)], dc(xnT_ag))
    unpack_shards(xnT_flat, xnT_ag, 1024)
    xnT_full = xnT_flat.ap()
    p.io = {"xnT": xnT_full, "W_sel": W_in_d.ap(), "xcz": xcz_d.ap()}
    with p.phase():
        build_l0_inproj(p)
    barrier()
    p.io = {"xcz": xcz_d.ap(), "Wxp_rows": W_xp_d.ap(), "u_out": u_d.ap(),
            "xp_part": xpp_in.ap()}
    with p.phase():
        build_l0_conv(p)
    coll("AllGather", xpp_in.ap(), xpp_ag.ap(), [dc(xpp_in)], dc(xpp_ag))
    p.io = {"u_in": u_d.ap(), "xp_all": xpp_ag.ap(), "Wdt_sel": W_dt_d.ap(),
            "ssm": ssm_flat.ap(), "hsum": hs_in.ap()}
    with p.phase():
        build_l0_scan(p)
    barrier()
    pack_shards(ssm_a2a_i, ssm_flat, 256)
    coll("AllToAll", ssm_a2a_i.ap(), ssm_a2a_o.ap(), [dc(ssm_a2a_i)],
         dc(ssm_a2a_o))
    coll("AllGather", hs_in.ap(), hs_ag.ap(), [dc(hs_in)], dc(hs_ag))
    p.io = {"ssmT_sl": ssm_a2a_o.ap().rearrange("s d t -> (s d) t"),
            "W_out_f": W_out_d.ap(), "hsums": hs_ag.ap(),
            "s_fm": s_fm_d.ap()}
    with p.phase():
        build_l0_outproj(p)
    barrier()
    p.io = {"xnT_g": xnT_full, "score_row": score_d.ap(),
            "mask_row": mask_d.ap()}
    with p.phase():
        build_l0_gate(p)
    coll("AllToAll", score_d.ap().rearrange("o (s t) -> (o s) t", s=8),
         sc_a2a_o.ap(), [dc(score_d)], dc(sc_a2a_o))
    p.io = {"xnT_e": xnT_full, "xnT_my": xnT_in.ap(),
            "score_row_i": score_d.ap(),
            "score_my": custom_ap(sc_a2a_o.ap(), [[2048, 1], [1, 2048]], 0),
            "mask_row_i": mask_d.ap(),
            "routed_m": rt_flat.ap(),
            "shared_fm": sh_fm_d.ap()}
    with p.phase():
        build_l0_expert(p)
    barrier()
    pack_shards(rt_a2a_i, rt_flat, 1024)
    coll("AllToAll", rt_a2a_i.ap(), rt_a2a_o.ap(), [dc(rt_a2a_i)], dc(rt_a2a_o))
    p.io = {"s_fm_i": s_fm_d.ap(), "shared_fm_i": sh_fm_d.ap(),
            "routed_all": rt_a2a_o.ap().rearrange("s r t -> (s r) t"),
            "x1_tm": x1_d.ap(), "xn1_tm": xn1_d.ap()}
    with p.phase():
        build_l0_combine(p)
    barrier()
    identg = p.inp("identg", [128, 128])
    identg_b = p.inp("identg_b", [128, 128], BF16)
    ones1g = p.inp("ones1g", [1, 128])
    swapg = p.inp("swapg", [64, 64])
    hcur = xn1_d
    for IL in range(4):
        p.io = {"h_tm": hcur.ap(), "z_tm": z_d.ap(),
                "ln_w_row": p.inp(f"lnw{IL}", [1, 1024]),
                "ln_b_row": p.inp(f"lnb{IL}", [1, 1024]),
                "ones1_l": ones1g}
        with p.phase():
            build_l1_lnchan(p)
        barrier()
        p.io = {"tp_src_z": z_d.ap(),
                "tp_dst_z": z_a2a_i.ap().rearrange("s h t -> (s h) t"),
                "tp_id_z": identg}
        with p.phase():
            build_tp(p, "z")
        coll("AllToAll", z_a2a_i.ap(), z_a2a_o.ap(), [dc(z_a2a_i)], dc(z_a2a_o))
        unpack_shards(zT_flat, z_a2a_o, 128)
        barrier()
        p.io = {"zT_sl": zT_flat.ap(),
                "yg_sl": yg_flat.ap(),
                "log_dt_sel": p.inp(f"ldt{IL}", [128, 1]),
                "A_re_log_sel": p.inp(f"arl{IL}", [128, 32]),
                "A_im_sel": p.inp(f"aim{IL}", [128, 32]),
                "C_re_sel": p.inp(f"cre{IL}", [128, 32]),
                "C_im_sel": p.inp(f"cim{IL}", [128, 32]),
                "D_sel": p.inp(f"dsk{IL}", [128, 1]),
                "ident_b": identg_b, "ident_f32c": identg, "swap64": swapg}
        with p.phase():
            build_l1_conv(p)
        barrier()
        pack_shards(yg_a2a_i, yg_flat, 128)
        coll("AllToAll", yg_a2a_i.ap(), yg_a2a_o.ap(), [dc(yg_a2a_i)],
             dc(yg_a2a_o))
        p.io = {"ygT_sl": yg_a2a_o.ap().rearrange("s h t -> (s h) t"),
                "h_tm_w": hcur.ap(), "h1_tm": h_d[IL + 1].ap(),
                "WoutT": p.inp(f"WoT{IL}", [1024, 2048]),
                "b_out_i": p.inp(f"bo{IL}", [2048, 1]),
                "ident_fw": identg}
        with p.phase():
            build_l1_wout(p)
        barrier()
        hcur = h_d[IL + 1]
    s1_d = hcur
    xn1T_d = idram("xn1T_d", [1024, 2048])
    p.io = {"tp_src_xn1": xn1_d.ap(), "tp_dst_xn1": xn1T_d.ap(),
            "tp_id_xn1": identg}
    with p.phase():
        build_tp(p, "xn1")
    barrier()
    p.io = {"xn1T_sl": xn1T_d.ap(), "q_tm": q_d.ap(), "k_tm": k_d2.ap(),
            "v_tm": v_d2.ap()}
    with p.phase():
        build_l1_qkv(p)
    barrier()
    p.io = {"tp_src_q": q_d.ap(),
            "tp_dst_q": q_a2a_i.ap().rearrange("s d t -> (s d) t"),
            "tp_id_q": identg}
    with p.phase():
        build_tp(p, "q")
    p.io = {"tp_src_k": k_d2.ap(),
            "tp_dst_k": k_a2a_i.ap().rearrange("s d t -> (s d) t"),
            "tp_id_k": identg}
    with p.phase():
        build_tp(p, "k")
    coll("AllToAll", q_a2a_i.ap(), q_a2a_o.ap(), [dc(q_a2a_i)], dc(q_a2a_o))
    coll("AllToAll", k_a2a_i.ap(), k_a2a_o.ap(), [dc(k_a2a_i)], dc(k_a2a_o))
    p.io = {}
    vsrc = v_d2.ap().rearrange("t (s d) -> s t d", s=8)
    cvs, cvi = dc(v_d2), dc(v_a2a_i)
    p.dma("sync", v_a2a_i.ap(), vsrc, reads=[cvs], writes=[cvi])
    coll("AllToAll", v_a2a_i.ap(), v_a2a_o.ap(), [cvi], dc(v_a2a_o))
    unpack_shards(qT_flat, q_a2a_o, 128)
    unpack_shards(kT_flat, k_a2a_o, 128)
    p.io = {"qT_h": qT_flat.ap(),
            "kT_h": kT_flat.ap(),
            "v_h": v_a2a_o.ap().rearrange("s t d -> (s t) d"),
            "sa_h": sa_flat.ap()}
    with p.phase():
        build_l1_attn(p)
    barrier()
    pack_shards(sa_a2a_i, sa_flat, 128)
    coll("AllToAll", sa_a2a_i.ap(), sa_a2a_o.ap(), [dc(sa_a2a_i)], dc(sa_a2a_o))
    p.io = {"saT_sl": sa_a2a_o.ap().rearrange("s d t -> (s d) t"),
            "xn1_sl": xn1_d.ap(), "xr_tm": xr_d.ap()}
    with p.phase():
        build_l1_attnout(p)
    barrier()
    p.io = {"tp_src_xr": xr_d.ap(), "tp_dst_xr": xrT_d.ap(),
            "tp_id_xr": identg}
    with p.phase():
        build_tp(p, "xr")
    barrier()
    p.io = {"xrT_sl": xrT_d.ap(), "wts_o": wts_d.ap()}
    with p.phase():
        build_l1_gate2(p)
    barrier()
    p.io = {"xrT_sl1": xrT_d.ap(), "shared_o": sh1_d.ap(), "h1_o": h1_d.ap()}
    with p.phase():
        build_l1_moe1(p)
    barrier()
    p.io = {"h1_i": h1_d.ap(), "shared_i": sh1_d.ap(), "wts8": wts_d.ap(),
            "xr_sl": xr_d.ap(), "g0_tm": g0_d.ap()}
    with p.phase():
        build_l1_moe2(p)
    barrier()
    p.io = {"x1_f": x1_d.ap(), "s1_f": s1_d.ap(), "g0_f": g0_d.ap()}
    with p.phase():
        build_l1_final(p)
    p.io = {}
    return p


def host_maps(ins):
    ins = {k: np.asarray(v) for k, v in ins.items()}
    x = np.asarray(ins["L0_attn_x"], f32).reshape(T, 1024)
    ident_f = np.eye(128, dtype=f32)
    ident_bf = np.eye(128, dtype=ml_dtypes.bfloat16)
    ones1 = np.ones((1, 128), f32)
    swap = np.eye(64, dtype=f32)[:, list(range(32, 64)) + list(range(32))]
    sel = np.zeros((32, 4096), f32)
    for m in range(32):
        sel[m, m * 128:(m + 1) * 128] = 1.0
    conv_w = np.asarray(ins["L0_attn_conv_w"], f32).reshape(4, 2048)
    conv_b = np.asarray(ins["L0_attn_conv_b"], f32)
    in_b = np.asarray(ins["L0_attn_in_b"], f32)
    A_log = np.asarray(ins["L0_attn_A_log"], f32)
    D_skip = np.asarray(ins["L0_attn_D_skip"], f32)
    dt_b = np.asarray(ins["L0_attn_dt_b"], f32)
    xp_b = np.asarray(ins["L0_attn_xp_b"], f32)
    out_b = np.asarray(ins["L0_attn_out_b"], f32)
    DH = 64
    inv_freq = 1.0 / 10000.0 ** (np.arange(0, DH, 2, dtype=np.float64) / DH)
    ang = np.arange(8192, dtype=np.float64)[:, None] * inv_freq[None, :]
    sin_t = np.sin(ang).astype(f32)
    cos_t = np.cos(ang).astype(f32)
    Wsi = np.asarray(ins["L1_ffn_W_shared_in"], f32)
    W1s = np.asarray(ins["L1_ffn_W1_shared"], f32)

    def tt_fused(prefix, key, usel=None, isel=None):
        g1 = np.asarray(ins[f"{prefix}_{key}_g1"], f32)
        g2 = np.asarray(ins[f"{prefix}_{key}_g2"], f32)
        m1, n1, m2, n2 = TT_SHAPES[key]
        g1T = np.transpose(g1, (2, 0, 1))
        if usel is not None:
            g1T = g1T[:, :, usel]
        if isel is not None:
            g1T = g1T[:, isel, :]
        g1T = _cc(g1T.reshape(16, -1))
        g2f = _cc(g2.reshape(16, m2 * n2))
        return {f"{key}_g1T": g1T, f"{key}_g2f": g2f}

    maps = []
    for k in range(NC):
        ts = slice(2048 * k, 2048 * (k + 1))
        ds = slice(256 * k, 256 * (k + 1))
        hs = slice(128 * k, 128 * (k + 1))
        usel_in = np.r_[np.arange(4 * k, 4 * k + 4), 32 + np.arange(4 * k, 4 * k + 4)]
        im = {}
        im.update(tt_fused("L0_attn", "in", usel=usel_in))
        im.update(tt_fused("L0_attn", "xp", isel=np.arange(4 * k, 4 * k + 4)))
        im.update(tt_fused("L0_attn", "dt", usel=np.arange(4 * k, 4 * k + 4)))
        im.update(tt_fused("L0_attn", "out"))
        cols = np.r_[np.arange(256 * k, 256 * (k + 1)),
                     2048 + np.arange(256 * k, 256 * (k + 1))]
        sel8 = np.zeros((128, 8), f32)
        sel8[:, k] = 1.0
        lpos = np.arange(2048 * k, 2048 * (k + 1)) % 8192
        im.update(dict(
            x=_cc(x[ts]),
            b_sel=_cc(in_b[cols][:, None]),
            conv_wT_sel=_cc(conv_w[:, ds].T),
            conv_b_sel=_cc(conv_b[ds][:, None]),
            xp_b=xp_b[:, None],
            dt_b_sel=_cc(dt_b[ds][:, None]),
            A_log_sel=_cc(A_log[ds]),
            D_sel=_cc(D_skip[ds][:, None]),
            sel=sel, ident=ident_bf, ones_col=np.ones((128, 1), f32),
            out_b=out_b[:, None], ones8=np.ones((8, 1), f32), ones1=ones1,
            onesc_bf=np.ones((128, 1), ml_dtypes.bfloat16),
            gate_w=np.asarray(ins["L0_ffn_gate_w"], f32), sel8=sel8,
            rt_g=_cc(np.asarray(ins["L0_ffn_rt_gate"], f32)[k]),
            rt_u=_cc(np.asarray(ins["L0_ffn_rt_up"], f32)[k]),
            rt_d=_cc(np.asarray(ins["L0_ffn_rt_down"], f32)[k]),
            sh_g=np.asarray(ins["L0_ffn_sh_gate"], f32),
            sh_u=np.asarray(ins["L0_ffn_sh_up"], f32),
            sh_d=np.asarray(ins["L0_ffn_sh_down"], f32),
            ones1_e=ones1, x_tm=_cc(x[ts]), ident_f=ident_f,
            ones1_l=ones1,
            identg=ident_f, identg_b=ident_bf, ones1g=ones1, swapg=swap,
            Wq_f=np.asarray(ins["L1_ffn_Wq"], f32),
            Wk_c_f=np.asarray(ins["L1_ffn_Wk_c"], f32),
            Wk_f=np.asarray(ins["L1_ffn_Wk"], f32),
            Wv_c_f=np.asarray(ins["L1_ffn_Wv_c"], f32),
            Wv_f=np.asarray(ins["L1_ffn_Wv"], f32),
            sin_my=_cc(sin_t[lpos]), cos_my=_cc(cos_t[lpos]),
            ident_fq=ident_f,
            onesb=np.ones((128, 64), ml_dtypes.bfloat16), onesf_at=ones1,
            Wo_f=np.asarray(ins["L1_ffn_Wo"], f32),
            norm1_row=np.asarray(ins["L1_ffn_norm1_w"], f32)[None, :],
            ones1_ao=ones1, ident_ao=ident_f,
            Wg_f=np.asarray(ins["L1_ffn_Wg_gate"], f32),
            We_f=np.asarray(ins["L1_ffn_We_gate"], f32),
            gb_row=np.asarray(ins["L1_ffn_group_bias"], f32)[None, :],
            eb_row=np.asarray(ins["L1_ffn_expert_bias"], f32)[None, :],
            ones1_g2=ones1, ident_g2=ident_f,
            Wsi_a=_cc(Wsi[:, :2048]), Wsi_b=_cc(Wsi[:, 2048:]),
            Wso_f=np.asarray(ins["L1_ffn_W_shared_out"], f32),
            W1_a=_cc(W1s[:, :2048]), W1_b=_cc(W1s[:, 2048:]),
            W2_all=_cc(np.asarray(ins["L1_ffn_W2_experts"], f32).reshape(16384, 1024)),
            norm2_row=np.asarray(ins["L1_ffn_norm2_w"], f32)[None, :],
            ones1_m2=ones1, ident_m2=ident_f,
            tp_id_xn=ident_f,
        ))
        for IL in range(4):
            im[f"lnw{IL}"] = _cc(np.asarray(ins["L1_attn_ln_w"], f32)[IL][None, :])
            im[f"lnb{IL}"] = _cc(np.asarray(ins["L1_attn_ln_b"], f32)[IL][None, :])
            im[f"ldt{IL}"] = _cc(np.asarray(ins["L1_attn_log_dt"], f32)[IL, hs][:, None])
            im[f"arl{IL}"] = _cc(np.asarray(ins["L1_attn_A_re_log"], f32)[IL, hs])
            im[f"aim{IL}"] = _cc(np.asarray(ins["L1_attn_A_im"], f32)[IL, hs])
            im[f"cre{IL}"] = _cc(np.asarray(ins["L1_attn_C_re"], f32)[IL, hs])
            im[f"cim{IL}"] = _cc(np.asarray(ins["L1_attn_C_im"], f32)[IL, hs])
            im[f"dsk{IL}"] = _cc(np.asarray(ins["L1_attn_D_skip"], f32)[IL, hs][:, None])
            im[f"WoT{IL}"] = _cc(np.asarray(ins["L1_attn_W_out"], f32)[IL].T)
            im[f"bo{IL}"] = _cc(np.asarray(ins["L1_attn_b_out"], f32)[IL][:, None])
        maps.append(im)
    return maps


def kernel(**ins):
    maps = host_maps(ins)
    p = P("fused")
    build_fused(p)
    nc = p.finish()
    names = p.ext_inputs
    maps = [{k: v for k, v in im.items() if k in names} for im in maps]
    res = run_bass_kernel_spmd(nc, maps, core_ids=list(range(NC)))
    out = np.concatenate([np.asarray(r["xout_tm"]) for r in res.results], axis=0)
    return out.reshape(2, 8192, 1024).astype(np.float32)


# version 09 (final)
# speedup vs baseline: 1.1280x; 1.1280x over previous
import numpy as np
import ml_dtypes
import concourse.bass as bass
import concourse.mybir as mybir
from concourse.alu_op_type import AluOpType
from contextlib import ExitStack

F32 = mybir.dt.float32
BF16 = mybir.dt.bfloat16
AF = mybir.ActivationFunctionType
AX = mybir.AxisListType
OP = AluOpType

ENGINES = ("tensor", "vector", "scalar", "gpsimd", "sync")


class Cell:
    __slots__ = ("writer", "readers", "name")

    def __init__(self, name=""):
        self.writer = None
        self.readers = []
        self.name = name


class P:
    def __init__(self, name="k", ncores=8):
        self.nc = bass.Bass("TRN2", target_bir_lowering=False)
        self.ncores = ncores
        self.stack = ExitStack()
        self.ops = {e: [] for e in ENGINES}
        self.sems = {}
        self.sem_count = {}
        self.waited = {e: {} for e in ENGINES}
        self.cells = {}
        self._sbuf_bytes = 0
        self._uid = 0
        self.io = {}
        self.gbanks = None
        self.gbank_cells = None
        self._scope = None
        self._keep = []
        names = ["sv", "ss", "st", "sg", "sq", "cc"]
        self.dma_sems = {"sync": [f"ds{i}" for i in range(4)],
                         "scalar": [f"dc{i}" for i in range(2)],
                         "gpsimd": [f"dg{i}" for i in range(4)],
                         "vector": ["dv0"]}
        for q, ls in self.dma_sems.items():
            names.extend(ls)
        for s in names:
            self.sems[s] = self.stack.enter_context(self.nc.semaphore(s))
            self.sem_count[s] = 0
        self.eng_sem = {"tensor": "st", "vector": "sv", "scalar": "ss",
                        "gpsimd": "sg", "sync": "sq"}
        self._dma_rr = {q: 0 for q in self.dma_sems}

    def dram(self, name, shape, dtype=F32, kind=None, addr_space=None):
        kw = {}
        if kind is not None:
            kw["kind"] = kind
        else:
            self._uid += 1
            name = f"{name}_d{self._uid}"
        if addr_space is not None:
            kw["addr_space"] = addr_space
        t = self.nc.dram_tensor(name, list(shape), dtype, **kw)
        self._keep.append(t)
        return t

    def inp(self, name, shape, dtype=F32):
        if name in self.io:
            return self.io[name]
        if not hasattr(self, "ext_inputs"):
            self.ext_inputs = set()
        self.ext_inputs.add(name)
        return self.dram(name, shape, dtype, kind="ExternalInput").ap()

    def outp(self, name, shape, dtype=F32):
        if name in self.io:
            return self.io[name]
        return self.dram(name, shape, dtype, kind="ExternalOutput").ap()

    def sbuf(self, name, shape, dtype=F32):
        self._uid += 1
        arena = self._scope if self._scope is not None else self.stack
        t = arena.enter_context(
            self.nc.sbuf_tensor(f"{name}_u{self._uid}", list(shape), dtype))
        self._keep.append(t)
        return t

    def barrier(self):
        toks = [(s, c) for s, c in self.sem_count.items() if c > 0]
        for eng in ENGINES:
            need = [t for t in toks if self.waited[eng].get(t[0], 0) < t[1]]
            if need:
                self.wait_all(eng, need)

    def phase(self):
        import contextlib

        @contextlib.contextmanager
        def _cm():
            self.barrier()
            prev = self._scope
            self._scope = ExitStack()
            try:
                yield
            finally:
                self.barrier()
                self._scope.close()
                self._scope = prev
        return _cm()

    def psum(self, name, shape, dtype=F32):
        self._uid += 1
        t = self.stack.enter_context(
            self.nc.psum_tensor(f"{name}_u{self._uid}", list(shape), dtype))
        return t

    def init_gbanks(self):
        if self.gbanks is None:
            self.gbanks = [self.psum(f"gb{i}", [128, 512]) for i in range(7)]
            self.gbank_cells = [self.cell(b) for b in self.gbanks]
            self.ptr_bf = self.psum("gptrb", [128, 512], BF16)
            self.ptr_f32 = self.ptr_bf[0:64, 256:512].bitcast(F32)
            c7 = self.cell(self.ptr_bf)
            self.ptr_cell = c7

    def cell(self, obj, key=None):
        k = (id(obj), key)
        c = self.cells.get(k)
        if c is None:
            c = Cell(f"{getattr(obj, 'name', obj)}:{key}")
            self.cells[k] = c
        return c

    def emit(self, eng, build, reads=(), writes=(), sem=None, inc=1,
             tag="", extra_waits=()):
        waits = {}
        for s, v in extra_waits:
            if v > 0 and self.waited[eng].get(s, 0) < v:
                waits[s] = max(waits.get(s, 0), v)

        def add_wait(tok):
            if tok is None:
                return
            s, v = tok
            if self.waited[eng].get(s, 0) >= v:
                return
            if waits.get(s, 0) < v:
                waits[s] = v

        for c in reads:
            add_wait(c.writer)
        for c in writes:
            add_wait(c.writer)
            for t in c.readers:
                add_wait(t)

        sem = sem or self.eng_sem[eng]
        self.sem_count[sem] += inc
        tok = (sem, self.sem_count[sem])
        semh = self.sems[sem]
        wl = sorted(waits.items())
        for s, v in wl:
            if self.waited[eng].get(s, 0) < v:
                self.waited[eng][s] = v

        def fn(e, wl=wl, build=build, semh=semh, inc=inc):
            for s, v in wl:
                e.wait_ge(self.sems[s], v)
            instr = build(e)
            if instr is not None:
                instr.then_inc(semh, inc)

        self.ops[eng].append(fn)
        for c in writes:
            c.writer = tok
            c.readers = []
        for c in reads:
            c.readers.append(tok)
        return tok

    def wait_all(self, eng, toks):
        def fn(e, toks=tuple(toks)):
            for s, v in toks:
                e.wait_ge(self.sems[s], v)
        for s, v in toks:
            if self.waited[eng].get(s, 0) < v:
                self.waited[eng][s] = v
        self.ops[eng].append(fn)

    def barrier_tok(self, cells):
        return [c.writer for c in cells if c.writer is not None]

    def _dma_emit(self, q, build, reads, writes):
        ls = self.dma_sems[q]
        sem = ls[self._dma_rr[q] % len(ls)]
        self._dma_rr[q] += 1
        prev = self.sem_count[sem]
        tok = self.emit(q, build, reads=reads, writes=writes, sem=sem, inc=16,
                        extra_waits=[(sem, prev)])
        return tok

    def drain_q(self, q):
        self.wait_all(q, [(s, self.sem_count[s]) for s in self.dma_sems[q]
                          if self.sem_count[s] > 0])

    def dma(self, q, out_ap, in_ap, reads=(), writes=(), **kw):
        return self._dma_emit(
            q, lambda e: e.dma_start(out=out_ap, in_=in_ap, **kw), reads, writes)

    def dma_tp(self, q, out_ap, in_ap, reads=(), writes=()):
        assert q in ("sync", "scalar")
        return self._dma_emit(
            q, lambda e: e.dma_start(out=out_ap, in_=in_ap, transpose=True),
            reads, writes)

    def tt(self, out, a, b, op, reads=(), writes=(), eng="vector"):
        return self.emit(eng, lambda e: e.tensor_tensor(out, a, b, op),
                         reads=reads, writes=writes)

    def ts(self, out, a, s1, s2=None, op0=OP.mult, op1=None, accum_out=None,
           reads=(), writes=(), eng="vector"):
        if op1 is None:
            op1 = OP.bypass
        return self.emit(
            eng, lambda e: e.tensor_scalar(out, a, s1, s2, op0, op1,
                                           accum_out=accum_out),
            reads=reads, writes=writes)

    def stt(self, out, in0, scalar, in1, op0, op1, accum_out=None,
            reads=(), writes=()):
        return self.emit(
            "vector",
            lambda e: e.scalar_tensor_tensor(out, in0, scalar, in1, op0, op1,
                                             accum_out=accum_out),
            reads=reads, writes=writes)

    def scan(self, out, d0, d1, init, reads=(), writes=()):
        return self.emit(
            "vector",
            lambda e: e.tensor_tensor_scan(out, d0, d1, init, OP.mult, OP.add),
            reads=reads, writes=writes)

    def act(self, out, in_, func, bias=None, scale=None, accum_out=None,
            reads=(), writes=()):
        kw = {}
        if bias is not None:
            kw["bias"] = bias
        if scale is not None:
            kw["scale"] = scale
        if accum_out is not None:
            kw["accum_out"] = accum_out
        return self.emit(
            "scalar", lambda e: e.activation(out, in_, func, **kw),
            reads=reads, writes=writes)

    def red(self, out, in_, op=OP.add, axis=AX.X, negate=None,
            reads=(), writes=(), eng="vector"):
        return self.emit(
            eng, lambda e: e.tensor_reduce(out, in_, axis, op, negate=negate),
            reads=reads, writes=writes)

    def copy(self, out, in_, reads=(), writes=(), eng="vector"):
        return self.emit(eng, lambda e: e.tensor_copy(out, in_),
                         reads=reads, writes=writes)

    def memset(self, ap, val, reads=(), writes=(), eng="vector"):
        return self.emit(eng, lambda e: e.memset(ap, val),
                         reads=reads, writes=writes)

    def mm(self, out, lhsT, rhs, start=True, stop=True, reads=(), writes=()):
        return self.emit(
            "tensor",
            lambda e: e.matmul(out, lhsT, rhs, start=start, stop=stop,
                               skip_group_check=True),
            reads=reads, writes=writes)

    def trans(self, out, in_, ident, reads=(), writes=()):
        return self.emit(
            "tensor", lambda e: e.transpose(out, in_, ident),
            reads=reads, writes=writes)

    def recip(self, out, in_, reads=(), writes=()):
        return self.emit("vector", lambda e: e.reciprocal(out, in_),
                         reads=reads, writes=writes)

    def collective(self, kind, op, ins, outs, reads=(), writes=()):
        def build(e):
            if kind == "AllToAll":
                from concourse.aot_env import allow_ici_all_to_all
                with allow_ici_all_to_all():
                    return e.collective_compute(
                        kind, op, replica_groups=[list(range(self.ncores))],
                        ins=ins, outs=outs)
            return e.collective_compute(
                kind, op, replica_groups=[list(range(self.ncores))],
                ins=ins, outs=outs)
        return self.emit("gpsimd", build, reads=reads, writes=writes,
                         sem="cc", inc=1)

    def finish(self):
        with self.nc.Block() as block:
            for eng in ENGINES:
                if not self.ops[eng]:
                    continue
                dec = getattr(block, eng)

                def make(eng):
                    def body(e):
                        for fn in self.ops[eng]:
                            fn(e)
                    return body
                dec(make(eng))
        self.stack.close()
        return self.nc


def ceil_div(a, b):
    return (a + b - 1) // b


def mm_group(p, items, reads, write_cell, item_reads=None):
    n = len(items)
    assert n >= 1
    eng = "tensor"

    def mk_waits(cells, extra=()):
        waits = {}

        def add_wait(tok):
            if tok is None:
                return
            s, v = tok
            if p.waited[eng].get(s, 0) >= v:
                return
            if waits.get(s, 0) < v:
                waits[s] = v

        for c in cells:
            add_wait(c.writer)
        for t in extra:
            add_wait(t)
        wl = sorted(waits.items())
        for s, v in wl:
            if p.waited[eng].get(s, 0) < v:
                p.waited[eng][s] = v
        return wl

    extra = [write_cell.writer] + list(write_cell.readers)
    first = list(reads) + (list(item_reads[0]) if item_reads else [])
    wl0 = mk_waits(first, [t for t in extra if t is not None])
    iwls = [()] * n
    if item_reads is not None:
        assert len(item_reads) == n
        for i in range(1, n):
            iwls[i] = mk_waits(item_reads[i])
    sem = p.eng_sem[eng]
    p.sem_count[sem] += 1
    tok = (sem, p.sem_count[sem])
    semh = p.sems[sem]

    def fn(e, wl0=wl0, iwls=tuple(iwls), items=tuple(items), semh=semh, n=n):
        for s, v in wl0:
            e.wait_ge(p.sems[s], v)
        for i, (out_ap, lhsT, rhs) in enumerate(items):
            for s, v in iwls[i]:
                e.wait_ge(p.sems[s], v)
            instr = e.matmul(out_ap, lhsT, rhs, start=(i == 0), stop=(i == n - 1))
            if i == n - 1:
                instr.then_inc(semh, 1)

    p.ops[eng].append(fn)
    write_cell.writer = tok
    write_cell.readers = []
    for c in reads:
        c.readers.append(tok)
    if item_reads is not None:
        for cl in item_reads:
            for c in cl:
                c.readers.append(tok)
    return tok


class PsumPool:

    def __init__(self, p, n=8, name="pb", off=0):
        self.p = p
        if p.gbanks is not None:
            n = min(n, len(p.gbanks) - off)
            self.banks = p.gbanks[off:off + n]
            self.cells = p.gbank_cells[off:off + n]
        else:
            self.banks = [p.psum(f"{name}{i}", [128, 512]) for i in range(n)]
            self.cells = [p.cell(b) for b in self.banks]
        self.i = 0

    def next(self):
        j = self.i % len(self.banks)
        self.i += 1
        return self.banks[j], self.cells[j]


def load_w_bf16(p, name, w_dram_ap, K, RO, q="gpsimd", dtype=BF16):
    nk = ceil_div(K, 128)
    sb = p.sbuf(name, [128, nk * RO], dtype)
    cl = [p.cell(sb, f"t{ik}") for ik in range(nk)]
    for ik in range(nk):
        kk = min(128, K - ik * 128)
        p.dma(q, sb[0:kk, ik * RO:(ik + 1) * RO],
              w_dram_ap[ik * 128:ik * 128 + kk, :], writes=[cl[ik]])
    return sb, cl, nk


def gemm(p, pool, wt, cwt, nk, RO, K, xf, cxf, T, TG, evac,
         ro_list=None, x_col=None, kk_last=128):
    ro_list = ro_list if ro_list is not None else range(ceil_div(RO, 128))
    wl = cwt if isinstance(cwt, list) else [cwt] * nk
    xl = cxf if isinstance(cxf, list) else [cxf] * nk
    for ro in ro_list:
        rw = min(128, RO - ro * 128)
        for c0 in range(0, TG, 512):
            cw = min(512, TG - c0)
            pb, cb = pool.next()
            items = []
            for ik in range(nk):
                kk = kk_last if ik == nk - 1 else 128
                lhsT = wt[0:kk, ik * RO + ro * 128: ik * RO + ro * 128 + rw]
                rhs = (x_col(ik, c0, cw) if x_col is not None
                       else xf[0:kk, ik * TG + c0: ik * TG + c0 + cw])
                items.append((pb[0:rw, 0:cw], lhsT, rhs))
            mm_group(p, items, [], cb,
                     item_reads=[[wl[ik], xl[ik]] for ik in range(nk)])
            evac(ro, c0, cw, pb, cb)


"""Phase builders for the mk_recurrent_d2 kernel (stage A: one launch each)."""


TT_SHAPES = {
    "in": (32, 64, 32, 64),
    "xp": (32, 8, 64, 12),
    "dt": (8, 32, 8, 64),
    "out": (32, 32, 64, 32),
}
TT_CHUNK = {"in": 512, "xp": 384, "dt": 512, "out": 512}


def host_tt_inputs(ins, prefix, key):
    g1 = np.asarray(ins[f"{prefix}_{key}_g1"], np.float32)
    g2 = np.asarray(ins[f"{prefix}_{key}_g2"], np.float32)
    m1, n1, m2, n2 = TT_SHAPES[key]
    g1T = np.ascontiguousarray(np.transpose(g1, (2, 0, 1)).reshape(16, m1 * n1))
    g2f = np.ascontiguousarray(g2.reshape(16, m2 * n2))
    return {f"{key}_g1T": g1T, f"{key}_g2f": g2f}


def build_prep(p, plan=None):
    pool = PsumPool(p, 4)
    ev = p.sbuf("ev", [128, 512])
    cev = p.cell(ev)
    out_toks = []
    for key in ("in", "xp", "dt", "out"):
        m1, n1, m2, n2 = (plan or TT_SHAPES)[key]
        g1T = p.inp(f"{key}_g1T", [16, m1 * n1])
        g2f = p.inp(f"{key}_g2f", [16, m2 * n2])
        W = p.outp(f"W_{key}", [m1 * m2, n1 * n2])
        W4 = W.rearrange("(i j) (u v) -> i u j v", i=m1, u=n1)
        g1s = p.sbuf(f"{key}_g1s", [16, m1 * n1])
        g2s = p.sbuf(f"{key}_g2s", [16, m2 * n2])
        c1, c2 = p.cell(g1s), p.cell(g2s)
        p.dma("sync", g1s[:, :], g1T, writes=[c1])
        p.dma("sync", g2s[:, :], g2f, writes=[c2])
        AR = m1 * n1
        CW = TT_CHUNK[key]
        for rt in range(ceil_div(AR, 128)):
            rw = min(128, AR - rt * 128)
            for c0 in range(0, m2 * n2, CW):
                cw = min(CW, m2 * n2 - c0)
                pb, cb = pool.next()
                mm_group(p, [(pb[0:rw, 0:cw],
                              g1s[:, rt * 128: rt * 128 + rw],
                              g2s[:, c0:c0 + cw])], [c1, c2], cb)
                p.copy(ev[0:rw, 0:cw], pb[0:rw, 0:cw], reads=[cb], writes=[cev])
                ni = rw // n1 if n1 <= 128 else 1
                j0 = c0 // n2
                nj = cw // n2
                for il in range(ni):
                    i_abs = (rt * 128) // n1 + il
                    tok = p.dma("sync",
                                W4[i_abs, :, j0:j0 + nj, :],
                                ev[il * n1:(il + 1) * n1, 0:cw],
                                reads=[cev])
                    out_toks.append(tok)
    p.drain_q("sync")
    return p


def build_l0_inproj(p, T=16384, TG=2048):
    xnT = p.inp("xnT", [1024, T])
    W = p.inp("W_sel", [1024, 512])
    b = p.inp("b_sel", [512, 1])
    xcz = p.outp("xcz", [512, T])
    pool = PsumPool(p, 8)
    wt, cwt, nk = load_w_bf16(p, "wt", W, 1024, 512)
    bsl = p.sbuf("bsl", [128, 4])
    cb = p.cell(bsl)
    for ro in range(4):
        p.dma("sync", bsl[:, ro:ro + 1], b[ro * 128:(ro + 1) * 128, :], writes=[cb])
    xf = p.sbuf("xf", [128, nk * TG], BF16)
    cxf = [p.cell(xf, f"t{i}") for i in range(nk)]
    ev = [p.sbuf(f"ev{i}", [128, 512]) for i in range(2)]
    cev = [p.cell(t) for t in ev]
    for t0 in range(0, T, TG):
        for ik in range(nk):
            p.dma("gpsimd", xf[:, ik * TG:(ik + 1) * TG],
                  xnT[ik * 128:(ik + 1) * 128, t0:t0 + TG], writes=[cxf[ik]])

        def evac(ro, c0, cw, pb, cbk, t0=t0):
            i = pool.i % 2
            p.ts(ev[i][:, 0:cw], pb[:, 0:cw], bsl[:, ro:ro + 1], None, OP.add,
                 reads=[cbk, cb], writes=[cev[i]])
            p.dma("sync", xcz[ro * 128:(ro + 1) * 128, t0 + c0:t0 + c0 + cw],
                  ev[i][:, 0:cw], reads=[cev[i]])
        gemm(p, pool, wt, cwt, nk, 512, 1024, xf, cxf, TG, TG, evac)
    p.drain_q("sync")
    return p


def build_l0_conv(p, T=16384, TG=2048, LSEQ=8192):
    xcz = p.inp("xcz", [512, T])
    cwT = p.inp("conv_wT_sel", [256, 4])
    cbv = p.inp("conv_b_sel", [256, 1])
    Wxp = p.inp("Wxp_rows", [256, 96])
    u_out = p.outp("u_out", [256, T])
    xp_o = p.outp("xp_part", [96, T])
    pool = PsumPool(p, 8)
    wxp, cwxp, nkxp = load_w_bf16(p, "wxp", Wxp, 256, 96)
    wcv = p.sbuf("wcv", [128, 8])
    cb_s = p.sbuf("cbs", [128, 2])
    cwc, ccb = p.cell(wcv), p.cell(cb_s)
    for di in range(2):
        p.dma("sync", wcv[:, di * 4:(di + 1) * 4], cwT[di * 128:(di + 1) * 128, :], writes=[cwc])
        p.dma("sync", cb_s[:, di:di + 1], cbv[di * 128:(di + 1) * 128, :], writes=[ccb])
    xct = p.sbuf("xct", [128, TG + 3])
    zt = p.sbuf("zt", [128, TG])
    acc = p.sbuf("acc", [128, TG])
    sg = p.sbuf("sg", [128, TG])
    ut = p.sbuf("ut", [128, TG])
    ubf = p.sbuf("ubf", [128, 2 * TG], BF16)
    ev = p.sbuf("evp", [96, 512])
    cxc, czt, cac, csg, cut, cub, cev = (p.cell(t) for t in (xct, zt, acc, sg, ut, ubf, ev))
    for t0 in range(0, T, TG):
        left = (t0 % LSEQ) == 0
        right = ((t0 + TG) % LSEQ) == 0
        seg0 = (t0 // LSEQ) * LSEQ
        seg1 = seg0 + LSEQ
        vlo = max(t0 - 1, seg0)
        vhi = min(t0 + TG + 2, seg1)
        off = vlo - (t0 - 1)
        for di in range(2):
            if left:
                p.memset(xct[:, 0:1], 0.0, writes=[cxc])
            if right:
                p.memset(xct[:, TG + 1:TG + 3], 0.0, writes=[cxc])
            p.dma("sync", xct[:, off:off + (vhi - vlo)],
                  xcz[di * 128:(di + 1) * 128, vlo:vhi], writes=[cxc])
            p.dma("sync", zt[:, :], xcz[256 + di * 128: 256 + (di + 1) * 128, t0:t0 + TG],
                  writes=[czt])
            p.ts(acc[:, :], xct[:, 0:TG], wcv[:, di * 4:di * 4 + 1], None, OP.mult,
                 reads=[cxc, cwc], writes=[cac])
            for k in range(1, 4):
                p.stt(acc[:, :], xct[:, k:k + TG], wcv[:, di * 4 + k:di * 4 + k + 1],
                      acc[:, :], OP.mult, OP.add, reads=[cxc, cwc, cac], writes=[cac])
            p.act(sg[:, :], acc[:, :], AF.Silu, bias=cb_s[:, di:di + 1],
                  reads=[cac, ccb], writes=[csg])
            p.tt(ut[:, :], sg[:, :], zt[:, :], OP.mult, reads=[csg, czt], writes=[cut])
            p.dma("sync", u_out[di * 128:(di + 1) * 128, t0:t0 + TG], ut[:, :],
                  reads=[cut])
            p.copy(ubf[:, di * TG:(di + 1) * TG], ut[:, :], reads=[cut], writes=[cub])

        def evac(ro, c0, cw, pb, cbk, t0=t0):
            p.copy(ev[0:96, 0:cw], pb[0:96, 0:cw], reads=[cbk], writes=[cev])
            p.dma("sync", xp_o[:, t0 + c0:t0 + c0 + cw], ev[0:96, 0:cw], reads=[cev])
        gemm(p, pool, wxp, cwxp, nkxp, 96, 256, ubf, cub, TG, TG, evac)
    p.drain_q("sync")
    return p


def build_l0_scan(p, T=16384, TB=2048, LSEQ=8192):
    u_in = p.inp("u_in", [256, T])
    xp_all = p.inp("xp_all", [768, T])
    xp_b = p.inp("xp_b", [96, 1])
    Wdt = p.inp("Wdt_sel", [64, 256])
    dt_b = p.inp("dt_b_sel", [256, 1])
    A_log = p.inp("A_log_sel", [256, 16])
    D_sel = p.inp("D_sel", [256, 1])
    sel = p.inp("sel", [32, 4096])
    ident = p.inp("ident", [128, 128], BF16)
    ones_col = p.inp("ones_col", [128, 1])
    ssm_o = p.outp("ssm", [256, T])
    hsum_o = p.outp("hsum", [1, 1])

    if p.gbanks is not None:
        ybanks = p.gbanks[0:4]
        ycells = p.gbank_cells[0:4]
        pool = PsumPool(p, 3, name="pm", off=4)
    else:
        ybanks = [p.psum(f"yb{i}", [128, 512]) for i in range(4)]
        ycells = [p.cell(b) for b in ybanks]
        pool = PsumPool(p, 4, name="pm")

    dtbc = p.sbuf("dtbc", [96, TB])
    bc = p.sbuf("bc", [32, TB])
    sel_s = p.sbuf("sel_s", [32, 4096])
    id_s = p.sbuf("id_s", [128, 128], BF16)
    one_s = p.sbuf("one_s", [128, 1])
    wdt_s = p.sbuf("wdt_s", [64, 256])
    dtb_s = p.sbuf("dtb_s", [128, 2])
    D_s = p.sbuf("D_s", [128, 2])
    Alog_s = p.sbuf("Alog_s", [128, 32])
    negA = p.sbuf("negA", [128, 32])
    tmp96 = p.sbuf("tmp96", [96, TB])
    dt2 = p.sbuf("dt2", [128, 2 * TB])
    dtu2 = p.sbuf("dtu2", [128, 2 * TB])
    ublk = p.sbuf("ublk", [128, 2 * TB])
    dA2_ = [p.sbuf(f"dA{i}", [128, TB // 2]) for i in range(4)]
    wts2_ = [p.sbuf(f"wts{i}", [128, TB // 2]) for i in range(4)]
    h2_ = [p.sbuf(f"h_s{i}", [128, TB // 2]) for i in range(4)]
    hC2_ = [p.sbuf(f"hC{i}", [128, TB // 2], BF16) for i in range(4)]
    evb = p.sbuf("evb", [128, TB])
    sp1 = p.sbuf("sp1", [128, 512])
    sp2 = p.sbuf("sp2", [128, 512])
    sp3 = p.sbuf("sp3", [128, 512])
    carry = p.sbuf("carry", [128, 32])
    hacc = p.sbuf("hacc", [128, 1])
    hrow = p.sbuf("hrow", [128, 1])
    hfin = p.sbuf("hfin", [1, 1])

    (cdtbc, cbc, csel, cid, cone, cwdt, cdtb, cD, cAl, cnA, ctmp, cdt2, cdtu,
     cu, cev, c1, c2, c3, ccar, chac, chrw, chfin) = (
        p.cell(t) for t in (dtbc, bc, sel_s, id_s, one_s, wdt_s, dtb_s, D_s,
                            Alog_s, negA, tmp96, dt2, dtu2, ublk,
                            evb, sp1, sp2, sp3, carry, hacc, hrow, hfin))
    cdA2 = [p.cell(t) for t in dA2_]
    cw2 = [p.cell(t) for t in wts2_]
    ch2 = [p.cell(t) for t in h2_]
    chC2 = [p.cell(t) for t in hC2_]

    p.dma("sync", sel_s[:, :], sel, writes=[csel])
    p.dma("sync", id_s[:, :], ident, writes=[cid])
    p.dma("sync", one_s[:, :], ones_col, writes=[cone])
    p.dma("sync", wdt_s[:, :], Wdt, writes=[cwdt])
    for di in range(2):
        p.dma("sync", dtb_s[:, di:di + 1], dt_b[di * 128:(di + 1) * 128, :], writes=[cdtb])
        p.dma("sync", D_s[:, di:di + 1], D_sel[di * 128:(di + 1) * 128, :], writes=[cD])
        p.dma("sync", Alog_s[:, di * 16:(di + 1) * 16], A_log[di * 128:(di + 1) * 128, :],
              writes=[cAl])
    p.act(negA[:, :], Alog_s[:, :], AF.Exp, reads=[cAl], writes=[cnA])
    p.ts(negA[:, :], negA[:, :], -1.0, reads=[cnA], writes=[cnA])
    p.memset(hacc[:, :], 0.0, writes=[chac])

    xpb_s = p.sbuf("xpb_s", [96, 1])
    cxpb = p.cell(xpb_s)
    p.dma("sync", xpb_s[:, :], xp_b, writes=[cxpb])

    nblk = T // TB
    for bi in range(nblk):
        t0 = bi * TB
        if t0 % LSEQ == 0:
            p.memset(carry[:, :], 0.0, writes=[ccar])
        for di in range(2):
            p.dma("sync", ublk[:, di * TB:(di + 1) * TB],
                  u_in[di * 128:(di + 1) * 128, t0:t0 + TB], writes=[cu])
        p.dma("sync", dtbc[:, :], xp_all[0:96, t0:t0 + TB], writes=[cdtbc])
        for k in range(1, 8):
            p.dma("sync", tmp96[:, :], xp_all[96 * k:96 * (k + 1), t0:t0 + TB],
                  writes=[ctmp])
            p.tt(dtbc[:, :], dtbc[:, :], tmp96[:, :], OP.add,
                 reads=[ctmp, cdtbc], writes=[cdtbc])
        p.ts(dtbc[:, :], dtbc[:, :], xpb_s[:, :], None, OP.add,
             reads=[cxpb, cdtbc], writes=[cdtbc])
        p.copy(bc[:, :], dtbc[64:96, :], reads=[cdtbc], writes=[cbc])
        for di in range(2):
            for c0 in range(0, TB, 512):
                pb, cpb = pool.next()
                mm_group(p, [(pb[:, 0:512], wdt_s[:, di * 128:(di + 1) * 128],
                              dtbc[0:64, c0:c0 + 512])], [cwdt, cdtbc], cpb)
                p.ts(sp1[:, :], pb[:, 0:512], dtb_s[:, di:di + 1], None, OP.add,
                     reads=[cpb, cdtb], writes=[c1])
                p.act(sp2[:, :], sp1[:, :], AF.Abs, reads=[c1], writes=[c2])
                p.act(sp2[:, :], sp2[:, :], AF.Exp, scale=-1.0, reads=[c2], writes=[c2])
                p.act(sp2[:, :], sp2[:, :], AF.Ln, bias=1.0, reads=[c2], writes=[c2])
                p.act(sp3[:, :], sp1[:, :], AF.Relu, reads=[c1], writes=[c3])
                p.tt(dt2[:, di * TB + c0: di * TB + c0 + 512], sp3[:, :], sp2[:, :],
                     OP.add, reads=[c2, c3], writes=[cdt2])
            p.tt(dtu2[:, di * TB:(di + 1) * TB], dt2[:, di * TB:(di + 1) * TB],
                 ublk[:, di * TB:(di + 1) * TB], OP.mult, reads=[cdt2, cu], writes=[cdtu])
        for half in range(2):
            o0 = half * 1024
            for n in range(16):
                par = n % 2
                pbB = []
                for c0 in (0, 512):
                    pb, cpb = pool.next()
                    mm_group(p, [(pb[:, 0:512], sel_s[:, n * 128:(n + 1) * 128],
                                  bc[0:32, o0 + c0:o0 + c0 + 512])], [csel, cbc],
                             cpb)
                    pbB.append((pb, cpb))
                for di in range(2):
                    bi4 = di * 2 + par
                    dA, cdA = dA2_[bi4], cdA2[bi4]
                    wts, cw = wts2_[bi4], cw2[bi4]
                    p.act(dA[:, :], dt2[:, di * TB + o0: di * TB + o0 + 1024],
                          AF.Exp, scale=negA[:, di * 16 + n:di * 16 + n + 1],
                          reads=[cdt2, cnA], writes=[cdA])
                    for j, c0 in enumerate((0, 512)):
                        pb, cpb = pbB[j]
                        p.tt(wts[:, c0:c0 + 512],
                             dtu2[:, di * TB + o0 + c0: di * TB + o0 + c0 + 512],
                             pb[:, 0:512], OP.mult, reads=[cdtu, cpb], writes=[cw])
                for di in range(2):
                    bi4 = di * 2 + par
                    p.scan(h2_[bi4][:, :], dA2_[bi4][:, :], wts2_[bi4][:, :],
                           carry[:, di * 16 + n:di * 16 + n + 1],
                           reads=[cdA2[bi4], cw2[bi4], ccar], writes=[ch2[bi4]])
                    p.copy(carry[:, di * 16 + n:di * 16 + n + 1],
                           h2_[bi4][:, 1023:1024], reads=[ch2[bi4]], writes=[ccar])
                    p.red(hrow[:, :], h2_[bi4][:, :], reads=[ch2[bi4]],
                          writes=[chrw])
                    p.tt(hacc[:, :], hacc[:, :], hrow[:, :], OP.add,
                         reads=[chrw, chac], writes=[chac])
                pbC = []
                for c0 in (0, 512):
                    pb, cpb = pool.next()
                    mm_group(p, [(pb[:, 0:512],
                                  sel_s[:, (16 + n) * 128:(17 + n) * 128],
                                  bc[0:32, o0 + c0:o0 + c0 + 512])], [csel, cbc],
                             cpb)
                    pbC.append((pb, cpb))
                for di in range(2):
                    bi4 = di * 2 + par
                    hC, chC = hC2_[bi4], chC2[bi4]
                    for j, c0 in enumerate((0, 512)):
                        pb, cpb = pbC[j]
                        p.tt(hC[:, c0:c0 + 512], h2_[bi4][:, c0:c0 + 512],
                             pb[:, 0:512], OP.mult, reads=[ch2[bi4], cpb],
                             writes=[chC])
                        yi = di * 2 + j
                        p.mm(ybanks[yi][:, :], id_s[:, :], hC[:, c0:c0 + 512],
                             start=(n == 0), stop=(n == 15),
                             reads=[cid, chC], writes=[ycells[yi]])
            for di in range(2):
                for j in range(2):
                    yi = di * 2 + j
                    p.stt(evb[:, di * 1024 + j * 512: di * 1024 + (j + 1) * 512],
                          ublk[:, di * TB + o0 + j * 512:
                               di * TB + o0 + (j + 1) * 512],
                          D_s[:, di:di + 1], ybanks[yi][:, :], OP.mult, OP.add,
                          reads=[cu, cD, ycells[yi]], writes=[cev])
                p.dma("sync", ssm_o[di * 128:(di + 1) * 128, t0 + o0:t0 + o0 + 1024],
                      evb[:, di * 1024: (di + 1) * 1024], reads=[cev])
    pb, cpb = pool.next()
    mm_group(p, [(pb[0:1, 0:1], hacc[:, :], one_s[:, :])], [chac, cone], cpb)
    p.copy(hfin[:, :], pb[0:1, 0:1], reads=[cpb], writes=[chfin])
    p.dma("sync", hsum_o, hfin[:, :], reads=[chfin])
    p.drain_q("sync")
    return p


def build_l0_outproj(p, T=2048):
    ssmT = p.inp("ssmT_sl", [2048, T])
    W = p.inp("W_out_f", [2048, 1024])
    ob = p.inp("out_b", [1024, 1])
    hsums = p.inp("hsums", [8, 1])
    ones8 = p.inp("ones8", [8, 1])
    ones1 = p.inp("ones1", [1, 128])
    onesc_bf = p.inp("onesc_bf", [128, 1], BF16)
    s_o = p.outp("s_fm", [1024, T])
    pool = PsumPool(p, 8)
    wt, cwt, nk = load_w_bf16(p, "wto", W, 2048, 1024)
    xf = p.sbuf("xfo", [128, 16 * T], BF16)
    cxf = [p.cell(xf, f"t{i}") for i in range(nk)]
    for ik in range(nk):
        p.dma("gpsimd", xf[:, ik * T:(ik + 1) * T], ssmT[ik * 128:(ik + 1) * 128, :],
              writes=[cxf[ik]])
    hs_s = p.sbuf("hs_s", [8, 1])
    o8_s = p.sbuf("o8_s", [8, 1])
    o1_s = p.sbuf("o1_s", [1, 128])
    ocb_s = p.sbuf("ocb_s", [128, 1], BF16)
    cscal = p.sbuf("cscal", [1, 1])
    cbc = p.sbuf("cbc128", [128, 1])
    wc = p.sbuf("wc", [128, 8])
    bias2 = p.sbuf("bias2", [128, 8])
    obs = p.sbuf("obs", [128, 8])
    (chs, co8, co1, cocb, ccs, ccb, cwc2, cb2, cob) = (
        p.cell(t) for t in (hs_s, o8_s, o1_s, ocb_s, cscal, cbc, wc, bias2, obs))
    p.dma("sync", hs_s[:, :], hsums, writes=[chs])
    p.dma("sync", o8_s[:, :], ones8, writes=[co8])
    p.dma("sync", o1_s[:, :], ones1, writes=[co1])
    p.dma("sync", ocb_s[:, :], onesc_bf, writes=[cocb])
    for ro in range(8):
        p.dma("sync", obs[:, ro:ro + 1], ob[ro * 128:(ro + 1) * 128, :], writes=[cob])
    pb, cpb = pool.next()
    mm_group(p, [(pb[0:1, 0:1], hs_s[:, :], o8_s[:, :])], [chs, co8], cpb)
    p.ts(cscal[:, :], pb[0:1, 0:1], 1e-8 / (16384.0 * 2048.0 * 16.0), None, OP.mult,
         reads=[cpb], writes=[ccs])
    pb2, cpb2 = pool.next()
    mm_group(p, [(pb2[:, 0:1], o1_s[:, :], cscal[:, :])], [co1, ccs], cpb2)
    p.copy(cbc[:, :], pb2[:, 0:1], reads=[cpb2], writes=[ccb])
    for ro in range(8):
        pb3, cpb3 = pool.next()
        items = [(pb3[:, 0:1], wt[:, ik * 1024 + ro * 128: ik * 1024 + ro * 128 + 128],
                  ocb_s[:, :]) for ik in range(nk)]
        mm_group(p, items, list(cwt) + [cocb], cpb3)
        p.copy(wc[:, ro:ro + 1], pb3[:, 0:1], reads=[cpb3], writes=[cwc2])
    p.ts(bias2[:, :], wc[:, :], cbc[:, 0:1], None, OP.mult, reads=[cwc2, ccb],
         writes=[cb2])
    p.tt(bias2[:, :], bias2[:, :], obs[:, :], OP.add, reads=[cb2, cob], writes=[cb2])
    ev = p.sbuf("evo", [128, 512])
    cev = p.cell(ev)

    def evac(ro, c0, cw, pb, cbk):
        p.ts(ev[:, 0:cw], pb[:, 0:cw], bias2[:, ro:ro + 1], None, OP.add,
             reads=[cbk, cb2], writes=[cev])
        p.dma("sync", s_o[ro * 128:(ro + 1) * 128, c0:c0 + cw], ev[:, 0:cw],
              reads=[cev])
    gemm(p, pool, wt, cwt, nk, 1024, 2048, xf, cxf, T, T, evac)
    p.drain_q("sync")
    return p


def build_l0_gate(p, T=16384, TG=2048):
    xnT = p.inp("xnT_g", [1024, T])
    gw = p.inp("gate_w", [1024, 8])
    sel8 = p.inp("sel8", [128, 8])
    score_o = p.outp("score_row", [1, T])
    mask_o = p.outp("mask_row", [1, T])
    pool = PsumPool(p, 8)
    gw_s = p.sbuf("gw_s", [128, 8 * 8])
    sel_s = p.sbuf("sel8_s", [128, 8])
    cgw, csel = p.cell(gw_s), p.cell(sel_s)
    for ik in range(8):
        p.dma("sync", gw_s[:, ik * 8:(ik + 1) * 8], gw[ik * 128:(ik + 1) * 128, :],
              writes=[cgw])
    p.dma("sync", sel_s[:, :], sel8, writes=[csel])
    xg = p.sbuf("xg", [128, 8 * TG])
    cxg = p.cell(xg)
    mx = p.sbuf("mx", [128, 1])
    sc = p.sbuf("sc", [128, 1])
    m8 = p.sbuf("m8", [128, 8])
    me = p.sbuf("me", [128, 1])
    cmx, csc, cm8, cme = (p.cell(t) for t in (mx, sc, m8, me))
    for t0 in range(0, T, TG):
        for ik in range(8):
            p.dma("sync", xg[:, ik * TG:(ik + 1) * TG],
                  xnT[ik * 128:(ik + 1) * 128, t0:t0 + TG], writes=[cxg])
        for tb in range(TG // 128):
            pb, cpb = pool.next()
            items = [(pb[0:128, 0:8],
                      xg[:, ik * TG + tb * 128: ik * TG + (tb + 1) * 128],
                      gw_s[:, ik * 8:(ik + 1) * 8]) for ik in range(8)]
            mm_group(p, items, [cxg, cgw], cpb)
            p.red(mx[:, :], pb[:, 0:8], op=OP.max, reads=[cpb], writes=[cmx])
            p.act(sc[:, :], mx[:, :], AF.Sigmoid, reads=[cmx], writes=[csc])
            p.tt(m8[:, :], pb[:, 0:8], mx[:, 0:1].broadcast_to([128, 8]),
                 OP.is_equal, reads=[cpb, cmx], writes=[cm8])
            p.tt(m8[:, :], m8[:, :], sel_s[:, :], OP.mult, reads=[cm8, csel],
                 writes=[cm8])
            p.red(me[:, :], m8[:, :], op=OP.max, reads=[cm8], writes=[cme])
            p.dma("sync",
                  score_o[0:1, t0 + tb * 128: t0 + (tb + 1) * 128].rearrange("one t -> t one"),
                  sc[:, :], reads=[csc])
            p.dma("sync",
                  mask_o[0:1, t0 + tb * 128: t0 + (tb + 1) * 128].rearrange("one t -> t one"),
                  me[:, :], reads=[cme])
    p.drain_q("sync")
    return p


def _swiglu_gemm(p, pool, act, cact, sg, csg, wt_g, cwg, wt_u, cwu, wt_d, cwd,
                 xs, cxs, TG, out_cb):
    for ro in range(16):
        for c0 in range(0, TG, 512):
            cw = min(512, TG - c0)
            pg, cpg = pool.next()
            items = [(pg[:, 0:cw], wt_g[:, ik * 2048 + ro * 128: ik * 2048 + ro * 128 + 128],
                      xs[:, ik * TG + c0: ik * TG + c0 + cw]) for ik in range(8)]
            mm_group(p, items, [cxs], cpg,
                     item_reads=[[cwg[ik]] for ik in range(8)])
            pu, cpu = pool.next()
            items = [(pu[:, 0:cw], wt_u[:, ik * 2048 + ro * 128: ik * 2048 + ro * 128 + 128],
                      xs[:, ik * TG + c0: ik * TG + c0 + cw]) for ik in range(8)]
            mm_group(p, items, [cxs], cpu,
                     item_reads=[[cwu[ik]] for ik in range(8)])
            p.act(sg[:, 0:cw], pg[:, 0:cw], AF.Silu, reads=[cpg], writes=[csg])
            p.tt(act[:, ro * TG + c0: ro * TG + c0 + cw], sg[:, 0:cw], pu[:, 0:cw],
                 OP.mult, reads=[csg, cpu], writes=[cact])
    for ro in range(8):
        for c0 in range(0, TG, 512):
            cw = min(512, TG - c0)
            pd, cpd = pool.next()
            items = [(pd[:, 0:cw], wt_d[:, ik * 1024 + ro * 128: ik * 1024 + ro * 128 + 128],
                      act[:, ik * TG + c0: ik * TG + c0 + cw]) for ik in range(16)]
            mm_group(p, items, [cact], cpd,
                     item_reads=[[cwd[ik]] for ik in range(16)])
            out_cb(ro, c0, cw, pd, cpd)


def build_l0_expert(p, T=16384, TG=1024):
    xnT = p.inp("xnT_e", [1024, T])
    xnT_my = p.inp("xnT_my", [1024, 2048])
    score = p.inp("score_row_i", [1, T])
    score_my = p.inp("score_my", [1, 2048])
    mask = p.inp("mask_row_i", [1, T])
    rt_g = p.inp("rt_g", [1024, 2048])
    rt_u = p.inp("rt_u", [1024, 2048])
    rt_d = p.inp("rt_d", [2048, 1024])
    sh_g = p.inp("sh_g", [1024, 2048])
    sh_u = p.inp("sh_u", [1024, 2048])
    sh_d = p.inp("sh_d", [2048, 1024])
    ones1 = p.inp("ones1_e", [1, 128])
    routed_o = p.outp("routed_m", [1024, T])
    shared_o = p.outp("shared_fm", [1024, 2048])
    pool = PsumPool(p, 8)
    o1_s = p.sbuf("o1s", [1, 128])
    co1s = p.cell(o1_s)
    p.dma("sync", o1_s[:, :], ones1, writes=[co1s])
    wt_g, cwg, _ = load_w_bf16(p, "wtg", rt_g, 1024, 2048)
    wt_u, cwu, _ = load_w_bf16(p, "wtu", rt_u, 1024, 2048)
    wt_d, cwd, _ = load_w_bf16(p, "wtd", rt_d, 2048, 1024)
    xs = p.sbuf("xs", [128, 8 * TG], BF16)
    cxs = p.cell(xs)
    srow = p.sbuf("srow", [1, TG])
    csrow = p.cell(srow)
    mrow = p.sbuf("mrow", [1, TG])
    cmrow = p.cell(mrow)
    mrow_g = p.sbuf("mrow_g", [128, TG])
    cmrg = p.cell(mrow_g)
    evr = p.sbuf("evr", [128, 512])
    cevr = p.cell(evr)
    act_s = p.sbuf("act_s", [128, 16 * TG], BF16)
    cact_s = p.cell(act_s)
    sg_s = p.sbuf("sg_s", [128, 512], BF16)
    csg_s = p.cell(sg_s)

    def make_xs(src_xnT, src_score, t0):
        for ik in range(8):
            p.dma("gpsimd", xs[:, ik * TG:(ik + 1) * TG],
                  src_xnT[ik * 128:(ik + 1) * 128, t0:t0 + TG], writes=[cxs])
        if src_score is None:
            return
        p.dma("sync", srow[:, :], src_score[0:1, t0:t0 + TG], writes=[csrow])
        for c0 in range(0, TG, 512):
            psc, cpsc = pool.next()
            mm_group(p, [(psc[:, 0:512], o1_s[:, :], srow[0:1, c0:c0 + 512])],
                     [co1s, csrow], cpsc)
            for ik in range(8):
                p.tt(xs[:, ik * TG + c0: ik * TG + c0 + 512],
                     xs[:, ik * TG + c0: ik * TG + c0 + 512], psc[:, 0:512],
                     OP.mult, reads=[cpsc, cxs], writes=[cxs])

    for t0 in range(0, T, TG):
        make_xs(xnT, score, t0)
        p.dma("sync", mrow[:, :], mask[0:1, t0:t0 + TG], writes=[cmrow])
        for c0 in range(0, TG, 512):
            pmb, cpmb = pool.next()
            mm_group(p, [(pmb[:, 0:512], o1_s[:, :], mrow[0:1, c0:c0 + 512])],
                     [co1s, cmrow], cpmb)
            p.copy(mrow_g[:, c0:c0 + 512], pmb[:, 0:512], reads=[cpmb], writes=[cmrg])

        def out_cb(ro, c0, cw, pd, cpd, t0=t0):
            p.tt(evr[:, 0:cw], pd[:, 0:cw], mrow_g[:, c0:c0 + cw], OP.mult,
                 reads=[cpd, cmrg], writes=[cevr])
            p.dma("sync", routed_o[ro * 128:(ro + 1) * 128, t0 + c0:t0 + c0 + cw],
                  evr[:, 0:cw], reads=[cevr])
        _swiglu_gemm(p, pool, act_s, cact_s, sg_s, csg_s, wt_g, cwg, wt_u, cwu,
                     wt_d, cwd, xs, cxs, TG, out_cb)
    for wsb, csb, src, K, RO in ((wt_g, cwg, sh_g, 1024, 2048),
                                 (wt_u, cwu, sh_u, 1024, 2048),
                                 (wt_d, cwd, sh_d, 2048, 1024)):
        for ik in range(K // 128):
            p.dma("gpsimd", wsb[:, ik * RO:(ik + 1) * RO],
                  src[ik * 128:(ik + 1) * 128, :], writes=[csb[ik]])
    for t0 in range(0, 2048, TG):
        make_xs(xnT_my, None, t0)

        def out_cb_sh(ro, c0, cw, pd, cpd, t0=t0):
            p.copy(evr[:, 0:cw], pd[:, 0:cw], reads=[cpd], writes=[cevr])
            p.dma("sync", shared_o[ro * 128:(ro + 1) * 128, t0 + c0:t0 + c0 + cw],
                  evr[:, 0:cw], reads=[cevr])
        _swiglu_gemm(p, pool, act_s, cact_s, sg_s, csg_s, wt_g, cwg, wt_u, cwu,
                     wt_d, cwd, xs, cxs, TG, out_cb_sh)
    p.drain_q("sync")
    return p


def emit_ln(p, out_ap, in_ap, sc, cs_, rms=False, eps=1e-5, D=1024):
    junk, s1, s2, m, v, r = (sc[k] for k in ("junk", "s1", "s2", "m", "v", "r"))
    cj, c1, c2, cm, cv, cr = (cs_[k] for k in ("junk", "s1", "s2", "m", "v", "r"))
    cin = cs_["in"]
    if not rms:
        p.red(s1[:, :], in_ap, reads=[cin], writes=[c1])
    p.stt(junk[:, :], in_ap, 1.0, in_ap, OP.bypass, OP.mult,
          accum_out=s2[:, :], reads=[cin], writes=[cj, c2])
    if not rms:
        p.ts(m[:, :], s1[:, :], 1.0 / D, reads=[c1], writes=[cm])
        p.ts(v[:, :], s2[:, :], 1.0 / D, reads=[c2], writes=[cv])
        p.stt(v[:, :], m[:, :], m[:, :], v[:, :], OP.mult, OP.subtract,
              reads=[cm, cv], writes=[cv])
        p.ts(v[:, :], v[:, :], -1.0, eps, OP.mult, OP.add, reads=[cv], writes=[cv])
    else:
        p.ts(v[:, :], s2[:, :], 1.0 / D, eps, OP.mult, OP.add, reads=[c2], writes=[cv])
    p.act(r[:, :], v[:, :], AF.Ln, reads=[cv], writes=[cr])
    p.act(r[:, :], r[:, :], AF.Exp, scale=-0.5, reads=[cr], writes=[cr])
    if not rms:
        p.ts(m[:, :], m[:, :], -1.0, reads=[cm], writes=[cm])
        p.ts(out_ap, in_ap, m[:, :], r[:, :], OP.add, OP.mult,
             reads=[cin, cm, cr], writes=[cs_["out"]])
    else:
        p.ts(out_ap, in_ap, r[:, :], None, OP.mult,
             reads=[cin, cr], writes=[cs_["out"]])


def mk_ln_scratch(p, tag, D=1024):
    sc = {"junk": p.sbuf(f"lnj{tag}", [128, D]),
          "s1": p.sbuf(f"ln1{tag}", [128, 1]),
          "s2": p.sbuf(f"ln2{tag}", [128, 1]),
          "m": p.sbuf(f"lnm{tag}", [128, 1]),
          "v": p.sbuf(f"lnv{tag}", [128, 1]),
          "r": p.sbuf(f"lnr{tag}", [128, 1])}
    cs_ = {k: p.cell(t) for k, t in sc.items()}
    return sc, cs_


def build_l0_combine(p, T=2048):
    x_tm = p.inp("x_tm", [T, 1024])
    s_fm = p.inp("s_fm_i", [1024, T])
    sh_fm = p.inp("shared_fm_i", [1024, T])
    r_all = p.inp("routed_all", [8192, T])
    identf = p.inp("ident_f", [128, 128])
    x1_o = p.outp("x1_tm", [T, 1024])
    xn1_o = p.outp("xn1_tm", [T, 1024])
    pool = PsumPool(p, 8)
    idf = p.sbuf("idf", [128, 128])
    cidf = p.cell(idf)
    p.dma("sync", idf[:, :], identf, writes=[cidf])
    accf = p.sbuf("accf", [128, T])
    tmpf = p.sbuf("tmpf", [128, T])
    sfm = p.sbuf("sfm", [128, T])
    cacc, ctmp, csfm = p.cell(accf), p.cell(tmpf), p.cell(sfm)
    nt = T // 128
    xout = p.sbuf("xout", [128, nt * 1024 // 16])
    xout = p.sbuf("xout2", [128, 16 * 1024])
    cxo = p.cell(xout)
    for ti in range(nt):
        p.dma("sync", xout[:, ti * 1024:(ti + 1) * 1024],
              x_tm[ti * 128:(ti + 1) * 128, :], writes=[cxo])
    for ro in range(8):
        p.dma("sync", accf[:, :], sh_fm[ro * 128:(ro + 1) * 128, :], writes=[cacc])
        for e in range(8):
            p.dma("sync", tmpf[:, :], r_all[e * 1024 + ro * 128: e * 1024 + (ro + 1) * 128, :],
                  writes=[ctmp])
            p.tt(accf[:, :], accf[:, :], tmpf[:, :], OP.add, reads=[ctmp, cacc],
                 writes=[cacc])
        p.act(accf[:, :], accf[:, :], AF.Silu, reads=[cacc], writes=[cacc])
        p.dma("sync", sfm[:, :], s_fm[ro * 128:(ro + 1) * 128, :], writes=[csfm])
        p.tt(accf[:, :], accf[:, :], sfm[:, :], OP.mult, reads=[csfm, cacc],
             writes=[cacc])
        for ti in range(nt):
            pb, cpb = pool.next()
            p.trans(pb[:, 0:128], accf[:, ti * 128:(ti + 1) * 128], idf[:, :],
                    reads=[cacc, cidf], writes=[cpb])
            p.tt(xout[:, ti * 1024 + ro * 128: ti * 1024 + (ro + 1) * 128],
                 xout[:, ti * 1024 + ro * 128: ti * 1024 + (ro + 1) * 128],
                 pb[:, 0:128], OP.add, reads=[cpb, cxo], writes=[cxo])
    sc, cs_ = mk_ln_scratch(p, "c")
    xn_t = p.sbuf("xn_t", [128, 1024])
    cs_["in"] = cxo
    cs_["out"] = p.cell(xn_t)
    for ti in range(nt):
        p.dma("sync", x1_o[ti * 128:(ti + 1) * 128, :],
              xout[:, ti * 1024:(ti + 1) * 1024], reads=[cxo])
        emit_ln(p, xn_t[:, :], xout[:, ti * 1024:(ti + 1) * 1024], sc, cs_)
        p.dma("sync", xn1_o[ti * 128:(ti + 1) * 128, :], xn_t[:, :],
              reads=[cs_["out"]])
    p.drain_q("sync")
    return p


def _row_bcast(p, pool, dst, cdst, row_sb, crow, o1_s, co1s, W=1024):
    for c0 in range(0, W, 512):
        cw = min(512, W - c0)
        pb, cpb = pool.next()
        mm_group(p, [(pb[:, 0:cw], o1_s[:, :], row_sb[0:1, c0:c0 + cw])],
                 [co1s, crow], cpb)
        p.copy(dst[:, c0:c0 + cw], pb[:, 0:cw], reads=[cpb], writes=[cdst])


def build_l1_lnchan(p, T=2048, D=1024):
    h_tm = p.inp("h_tm", [T, D])
    wrow = p.inp("ln_w_row", [1, D])
    brow = p.inp("ln_b_row", [1, D])
    ones1 = p.inp("ones1_l", [1, 128])
    z_o = p.outp("z_tm", [T, D])
    pool = PsumPool(p, 4)
    o1_s = p.sbuf("o1l", [1, 128])
    wr_s = p.sbuf("wrs", [1, D])
    br_s = p.sbuf("brs", [1, D])
    wb = p.sbuf("wb", [128, D])
    bb = p.sbuf("bb", [128, D])
    co1, cwr, cbr, cwb, cbb = (p.cell(t) for t in (o1_s, wr_s, br_s, wb, bb))
    p.dma("sync", o1_s[:, :], ones1, writes=[co1])
    p.dma("sync", wr_s[:, :], wrow, writes=[cwr])
    p.dma("sync", br_s[:, :], brow, writes=[cbr])
    _row_bcast(p, pool, wb, cwb, wr_s, cwr, o1_s, co1, D)
    _row_bcast(p, pool, bb, cbb, br_s, cbr, o1_s, co1, D)
    xt = p.sbuf("xtl", [128, D])
    ot = p.sbuf("otl", [128, D])
    cxt, cot = p.cell(xt), p.cell(ot)
    sc, cs_ = mk_ln_scratch(p, "lc", D)
    cs_["in"] = cxt
    cs_["out"] = cot
    for ti in range(T // 128):
        p.dma("sync", xt[:, :], h_tm[ti * 128:(ti + 1) * 128, :], writes=[cxt])
        emit_ln(p, ot[:, :], xt[:, :], sc, cs_, D=D)
        p.tt(ot[:, :], ot[:, :], wb[:, :], OP.mult, reads=[cwb, cot], writes=[cot])
        p.tt(ot[:, :], ot[:, :], bb[:, :], OP.add, reads=[cbb, cot], writes=[cot])
        p.dma("sync", z_o[ti * 128:(ti + 1) * 128, :], ot[:, :], reads=[cot])
    p.drain_q("sync")
    return p


def build_l1_wout(p, T=2048):
    ygT = p.inp("ygT_sl", [1024, T])
    WoT = p.inp("WoutT", [1024, 2048])
    bo = p.inp("b_out_i", [2048, 1])
    h_tm = p.inp("h_tm_w", [T, 1024])
    identf = p.inp("ident_fw", [128, 128])
    h1_o = p.outp("h1_tm", [T, 1024])
    pool = PsumPool(p, 8)
    wt, cwt, nk = load_w_bf16(p, "wtw", WoT, 1024, 2048)
    idf = p.sbuf("idfw", [128, 128])
    cidf = p.cell(idf)
    p.dma("sync", idf[:, :], identf, writes=[cidf])
    bos = p.sbuf("bos", [128, 16])
    cbo = p.cell(bos)
    for ro in range(16):
        p.dma("sync", bos[:, ro:ro + 1], bo[ro * 128:(ro + 1) * 128, :], writes=[cbo])
    xf = p.sbuf("xfw", [128, 8 * T], BF16)
    cxf = [p.cell(xf, f"t{i}") for i in range(nk)]
    for ik in range(nk):
        p.dma("gpsimd", xf[:, ik * T:(ik + 1) * T], ygT[ik * 128:(ik + 1) * 128, :],
              writes=[cxf[ik]])
    sigb = p.sbuf("sigb", [128, 8 * T], BF16)
    csig = p.cell(sigb)
    def evac_g(ro, c0, cw, pb, cbk):
        rr = ro - 8
        p.act(sigb[:, rr * T + c0: rr * T + c0 + cw], pb[:, 0:cw], AF.Sigmoid,
              bias=bos[:, ro:ro + 1], reads=[cbk, cbo], writes=[csig])
    gemm(p, pool, wt, cwt, nk, 2048, 1024, xf, cxf, T, T, evac_g,
         ro_list=range(8, 16))
    xout = p.sbuf("xoutw", [128, (T // 128) * 1024])
    cxo = p.cell(xout)
    for ti in range(T // 128):
        p.dma("sync", xout[:, ti * 1024:(ti + 1) * 1024],
              h_tm[ti * 128:(ti + 1) * 128, :], writes=[cxo])
    glu = p.sbuf("glu", [128, T])
    cglu = p.cell(glu)

    def evac_v(ro, c0, cw, pb, cbk):
        p.ts(glu[:, c0:c0 + cw], pb[:, 0:cw], bos[:, ro:ro + 1], None, OP.add,
             reads=[cbk, cbo], writes=[cglu])
        p.tt(glu[:, c0:c0 + cw], glu[:, c0:c0 + cw],
             sigb[:, ro * T + c0: ro * T + c0 + cw], OP.mult,
             reads=[csig, cglu], writes=[cglu])
        if c0 + cw == T:
            for ti in range(T // 128):
                pb2, cpb2 = pool.next()
                p.trans(pb2[:, 0:128], glu[:, ti * 128:(ti + 1) * 128], idf[:, :],
                        reads=[cglu, cidf], writes=[cpb2])
                p.tt(xout[:, ti * 1024 + ro * 128: ti * 1024 + (ro + 1) * 128],
                     xout[:, ti * 1024 + ro * 128: ti * 1024 + (ro + 1) * 128],
                     pb2[:, 0:128], OP.add, reads=[cpb2, cxo], writes=[cxo])
    gemm(p, pool, wt, cwt, nk, 2048, 1024, xf, cxf, T, T, evac_v,
         ro_list=range(8))
    for ti in range(T // 128):
        p.dma("sync", h1_o[ti * 128:(ti + 1) * 128, :],
              xout[:, ti * 1024:(ti + 1) * 1024], reads=[cxo])
    p.drain_q("sync")
    return p


def custom_ap(base_ap, dims, offset):
    import bass_rust as br
    c = base_ap.copy()
    c.ap = br.VecI64Pair([list(d) for d in dims])
    c.offset = offset
    return c


def build_l1_conv(p, L=8192, NB=2, Q=128):
    NCH = L // Q
    zT = p.inp("zT_sl", [128, NB * L])
    ldt = p.inp("log_dt_sel", [128, 1])
    arl = p.inp("A_re_log_sel", [128, 32])
    aim = p.inp("A_im_sel", [128, 32])
    cre = p.inp("C_re_sel", [128, 32])
    cim = p.inp("C_im_sel", [128, 32])
    dsk = p.inp("D_sel", [128, 1])
    idb = p.inp("ident_b", [128, 128], BF16)
    swp = p.inp("swap64", [64, 64])
    idf32_i = p.inp("ident_f32c", [128, 128])
    yg_o = p.outp("yg_sl", [128, NB * L])
    kpad_d = p.dram("kpad_d", [128, 256])
    q_d = p.dram("q_d", [2, 128, 32, 128])
    pool = PsumPool(p, 6)
    ckpd = p.cell(kpad_d)
    cqd = p.cell(q_d)
    if p.gbanks is not None:
        _ptr_bf = p.ptr_bf
        _ptr_f = p.ptr_f32
        _cptr = p.ptr_cell
    else:
        _ptr_bf = None

    P_ = p.sbuf("Ptab", [128, 8192])
    tmp = p.sbuf("tmpc", [128, 4096])
    sm = p.sbuf("smalls", [128, 512])
    toep = p.sbuf("toep", [128, 128 * 128], BF16)
    m1st = p.sbuf("m1st", [128, 8192], BF16)
    m2st = p.sbuf("m2st", [64, 128 * 128], BF16)
    lamA = p.sbuf("lamA", [64, 128])
    lamB = p.sbuf("lamB", [64, 128])
    swp_s = p.sbuf("swp_s", [64, 64])
    ztst = p.sbuf("ztst", [128, 8192], BF16)
    stbf = p.sbuf("stbf", [64, 8192], BF16)
    scur = p.sbuf("scur", [64, 128])

    ytst = p.sbuf("ytst", [128, 8192], BF16)
    zbf = ytst[:, 0:2048]
    yblk = p.sbuf("yblkc", [128, 2048])
    zldc = p.sbuf("zldc", [128, 2048])
    g1t = p.sbuf("g1tc", [128, 2048])
    g2t = zldc
    kt = g1t[:, 0:256]

    ids = p.sbuf("idsc", [128, 128], BF16)
    (cP, ctmp, csm, ctoep, cm1, cm2, clA, clB, cswp, czt, cst, cscur, cyt,
     cyb, czl, cg1, cid) = (p.cell(t) for t in (
        P_, tmp, sm, toep, m1st, m2st, lamA, lamB, swp_s, ztst, stbf, scur,
        ytst, yblk, zldc, g1t, ids))
    cg2 = czl
    czb = cyt
    ckt = cg1
    ctoep4 = [p.cell(toep, f"s{j}") for j in range(4)]
    cin = [p.cell(zT, "in")]
    idf32 = p.sbuf("idf32c", [128, 128])
    cidf = p.cell(idf32)
    p.dma("sync", ids[:, :], idb, writes=[cid])
    p.dma("sync", swp_s[:, :], swp, writes=[cswp])
    p.dma("sync", idf32[:, :], idf32_i, writes=[cidf])

    S = lambda i: sm[:, i * 32:(i + 1) * 32]
    csl = csm
    d_in = [p.cell(ldt), p.cell(arl), p.cell(aim), p.cell(cre), p.cell(cim),
            p.cell(dsk)]
    ldt_s = p.sbuf("ldt_s", [128, 1])
    dsk_s = p.sbuf("dsk_s", [128, 1])
    arl_s = p.sbuf("arl_s", [128, 32])
    aim_s = p.sbuf("aim_s", [128, 32])
    cre_s = p.sbuf("cre_s", [128, 32])
    cim_s = p.sbuf("cim_s", [128, 32])
    cldt, cdsk, carl, caim, ccre, ccim = (p.cell(t) for t in
                                          (ldt_s, dsk_s, arl_s, aim_s, cre_s, cim_s))
    p.dma("sync", ldt_s[:, :], ldt, writes=[cldt])
    p.dma("sync", dsk_s[:, :], dsk, writes=[cdsk])
    p.dma("sync", arl_s[:, :], arl, writes=[carl])
    p.dma("sync", aim_s[:, :], aim, writes=[caim])
    p.dma("sync", cre_s[:, :], cre, writes=[ccre])
    p.dma("sync", cim_s[:, :], cim, writes=[ccim])
    PI = float(np.pi)

    dtc = p.sbuf("dtc", [128, 1])
    cdt = p.cell(dtc)
    p.act(dtc[:, :], ldt_s[:, :], AF.Exp, reads=[cldt], writes=[cdt])
    p.act(S(0), arl_s[:, :], AF.Exp, reads=[carl], writes=[csl])
    p.ts(S(1), S(0), dtc[:, :], None, OP.mult, reads=[csl, cdt], writes=[csl])
    p.ts(S(1), S(1), -1.0, None, OP.mult, reads=[csl], writes=[csl])
    p.ts(S(2), aim_s[:, :], dtc[:, :], None, OP.mult, reads=[caim, cdt, csl],
         writes=[csl])
    p.act(S(3), S(1), AF.Exp, reads=[csl], writes=[csl])
    p.copy(S(4), S(2), reads=[csl], writes=[csl], eng="vector")
    for _ in range(2):
        p.ts(S(13), S(4), PI, None, OP.is_gt, reads=[csl], writes=[csl])
        p.stt(S(4), S(13), -2.0 * PI, S(4), OP.mult, OP.add, reads=[csl],
              writes=[csl])
    p.act(S(6), S(4), AF.Sin, reads=[csl], writes=[csl])
    p.ts(S(13), S(4), -1.0, PI / 2, OP.mult, OP.add, reads=[csl], writes=[csl])
    p.ts(S(14), S(13), PI, None, OP.is_gt, reads=[csl], writes=[csl])
    p.stt(S(13), S(14), -2.0 * PI, S(13), OP.mult, OP.add, reads=[csl], writes=[csl])
    p.act(S(5), S(13), AF.Sin, reads=[csl], writes=[csl])
    p.tt(S(7), S(3), S(5), OP.mult, reads=[csl], writes=[csl])
    p.tt(S(8), S(3), S(6), OP.mult, reads=[csl], writes=[csl])
    p.ts(S(13), S(7), 1.0, None, OP.subtract, reads=[csl], writes=[csl])
    p.tt(S(14), S(0), S(0), OP.mult, reads=[csl], writes=[csl])
    p.stt(S(14), aim_s[:, :], 1.0, S(14), OP.bypass, OP.bypass, reads=[caim, csl],
          writes=[csl]) if False else None
    p.stt(S(14), aim_s[:, :], 1.0, S(14), OP.mult, OP.add, reads=[caim, csl],
          writes=[csl]) if False else None
    p.tt(S(15), aim_s[:, :], aim_s[:, :], OP.mult, reads=[caim, csl], writes=[csl])
    p.tt(S(14), S(14), S(15), OP.add, reads=[csl], writes=[csl])
    p.recip(S(14), S(14), reads=[csl], writes=[csl])
    p.tt(S(9), S(13), S(0), OP.mult, reads=[csl], writes=[csl])
    p.ts(S(9), S(9), -1.0, None, OP.mult, reads=[csl], writes=[csl])
    p.tt(S(15), S(8), aim_s[:, :], OP.mult, reads=[caim, csl], writes=[csl])
    p.tt(S(9), S(9), S(15), OP.add, reads=[csl], writes=[csl])
    p.tt(S(9), S(9), S(14), OP.mult, reads=[csl], writes=[csl])
    p.tt(S(10), S(8), S(0), OP.mult, reads=[csl], writes=[csl])
    p.ts(S(10), S(10), -1.0, None, OP.mult, reads=[csl], writes=[csl])
    p.tt(S(15), S(13), aim_s[:, :], OP.mult, reads=[caim, csl], writes=[csl])
    p.tt(S(10), S(10), S(15), OP.subtract, reads=[csl], writes=[csl])
    p.tt(S(10), S(10), S(14), OP.mult, reads=[csl], writes=[csl])
    p.tt(S(11), cre_s[:, :], S(9), OP.mult, reads=[ccre, csl], writes=[csl])
    p.tt(S(15), cim_s[:, :], S(10), OP.mult, reads=[ccim, csl], writes=[csl])
    p.tt(S(11), S(11), S(15), OP.subtract, reads=[csl], writes=[csl])
    p.tt(S(12), cre_s[:, :], S(10), OP.mult, reads=[ccre, csl], writes=[csl])
    p.tt(S(15), cim_s[:, :], S(9), OP.mult, reads=[ccim, csl], writes=[csl])
    p.tt(S(12), S(12), S(15), OP.add, reads=[csl], writes=[csl])

    import os as _os
    _dbg = _os.environ.get("DBG_CONV", "0") == "1"
    if _dbg:
        dbg_sm = p.outp("dbg_sm", [128, 512])
        dbg_P = p.outp("dbg_P", [128, 8192])
    Pre = P_[:, 0:4096]
    Pim = P_[:, 4096:8192]
    n32 = lambda ap, n0, w: ap

    def bview(slot_ap, w):
        return slot_ap.unsqueeze(2).broadcast_to([128, 32, w])

    def pview(half, c0, w):
        base = 0 if half == "re" else 4096
        return P_[:, base:base + 4096].rearrange("p (n d) -> p n d", n=32)[:, :, c0:c0 + w]

    p.memset(Pre[:, :], 0.0, writes=[cP])
    p.memset(Pim[:, :], 0.0, writes=[cP])
    p.memset(pview("re", 0, 1), 1.0, writes=[cP])
    p.copy(pview("re", 1, 1), S(7).unsqueeze(2), reads=[csl], writes=[cP])
    p.copy(pview("im", 1, 1), S(8).unsqueeze(2), reads=[csl], writes=[cP])
    t1 = tmp[:, 0:2048]
    t2 = tmp[:, 2048:4096]

    def t1v(w):
        return tmp[:, 0:32 * w].rearrange("p (n d) -> p n d", n=32)

    m = 2
    while m < 128:
        p.tt(t1v(1), pview("re", m - 1, 1), S(7).unsqueeze(2), OP.mult,
             reads=[cP, csl], writes=[ctmp])
        p.tt(pview("re", m, 1), pview("im", m - 1, 1), S(8).unsqueeze(2),
             OP.mult, reads=[cP, csl], writes=[cP])
        p.tt(t2[:, 0:32].rearrange("p (n d) -> p n d", n=32),
             pview("im", m - 1, 1), S(7).unsqueeze(2), OP.mult,
             reads=[cP, csl], writes=[ctmp])
        p.tt(pview("im", m, 1), pview("re", m - 1, 1), S(8).unsqueeze(2),
             OP.mult, reads=[cP, csl], writes=[cP])
        p.tt(pview("im", m, 1), pview("im", m, 1),
             t2[:, 0:32].rearrange("p (n d) -> p n d", n=32), OP.add,
             reads=[cP, ctmp], writes=[cP])
        p.tt(pview("re", m, 1), t1v(1), pview("re", m, 1), OP.subtract,
             reads=[ctmp, cP], writes=[cP])
        w = m - 1
        if w > 0:
            rb = pview("re", m, 1).broadcast_to([128, 32, w])
            ib = pview("im", m, 1).broadcast_to([128, 32, w])
            ta = t1v(w)
            t2v = t2[:, 0:32 * w].rearrange("p (n d) -> p n d", n=32)
            p.tt(ta, pview("re", 1, w), rb, OP.mult, reads=[cP], writes=[ctmp])
            p.tt(t2v, pview("im", 1, w), rb, OP.mult, reads=[cP], writes=[ctmp])
            p.tt(pview("im", m + 1, w), pview("re", 1, w), ib, OP.mult,
                 reads=[cP], writes=[cP])
            p.tt(pview("im", m + 1, w), pview("im", m + 1, w), t2v, OP.add,
                 reads=[cP, ctmp], writes=[cP])
            p.tt(pview("re", m + 1, w), pview("im", 1, w), ib, OP.mult,
                 reads=[cP], writes=[cP])
            p.tt(pview("re", m + 1, w), ta, pview("re", m + 1, w), OP.subtract,
                 reads=[ctmp, cP], writes=[cP])
        m *= 2

    if _dbg:
        p.dma("sync", dbg_sm, sm[:, :], reads=[csl])
        p.dma("sync", dbg_P, P_[:, :], reads=[cP])
    p.tt(t1v(128), pview("re", 0, 128), bview(S(11), 128), OP.mult,
         reads=[cP, csl], writes=[ctmp])
    p.tt(t2[:, 0:2048].rearrange("p (n d) -> p n d", n=32) if False else
         tmp[:, 2048:2048 + 32 * 128].rearrange("p (n d) -> p n d", n=32),
         pview("im", 0, 128), bview(S(12), 128), OP.mult,
         reads=[cP, csl], writes=[ctmp]) if False else None
    p.memset(kt[:, :], 0.0, writes=[ckt])
    for half in range(2):
        n0 = half * 16
        tva = tmp[:, 0:16 * 128].rearrange("p (n d) -> p n d", n=16)
        tvb = tmp[:, 2048:2048 + 16 * 128].rearrange("p (n d) -> p n d", n=16)
        Prh = P_[:, n0 * 128:(n0 + 16) * 128].rearrange("p (n d) -> p n d", n=16)
        Pih = P_[:, 4096 + n0 * 128:4096 + (n0 + 16) * 128].rearrange(
            "p (n d) -> p n d", n=16)
        Cr = S(11)[:, n0:n0 + 16].unsqueeze(2).broadcast_to([128, 16, 128])
        Ci = S(12)[:, n0:n0 + 16].unsqueeze(2).broadcast_to([128, 16, 128])
        p.tt(tva, Prh, Cr, OP.mult, reads=[cP, csl], writes=[ctmp])
        p.tt(tvb, Pih, Ci, OP.mult, reads=[cP, csl], writes=[ctmp])
        p.tt(tva, tva, tvb, OP.subtract, reads=[ctmp], writes=[ctmp])
        red_in = tmp[:, 0:16 * 128].rearrange("p (n d) -> p d n", n=16)
        p.red(kt[:, 128 + half:0] if False else kt[:, 0:128], red_in, op=OP.add,
              reads=[ctmp], writes=[ckt]) if half == 0 else None
        if half == 0:
            pass
        else:
            p.red(kt[:, 128:256], red_in, op=OP.add, reads=[ctmp], writes=[ckt])
    p.tt(kt[:, 0:128], kt[:, 0:128], kt[:, 128:256], OP.add, reads=[ckt],
         writes=[ckt])
    p.ts(kt[:, 0:128], kt[:, 0:128], 2.0, None, OP.mult, reads=[ckt], writes=[ckt])
    p.memset(kt[:, 128:256], 0.0, writes=[ckt])
    p.dma("sync", kpad_d.ap()[:, 127:256], kt[:, 0:129], reads=[ckt], writes=[ckpd])
    p.dma("sync", kpad_d.ap()[:, 0:127], kt[:, 128:255], reads=[ckt], writes=[ckpd])

    for half in range(2):
        n0 = half * 16
        tva = tmp[:, 0:16 * 128].rearrange("p (n d) -> p n d", n=16)
        tvb = tmp[:, 2048:2048 + 16 * 128].rearrange("p (n d) -> p n d", n=16)
        Prh = P_[:, n0 * 128:(n0 + 16) * 128].rearrange("p (n d) -> p n d", n=16)
        Pih = P_[:, 4096 + n0 * 128:4096 + (n0 + 16) * 128].rearrange(
            "p (n d) -> p n d", n=16)
        lr = S(7)[:, n0:n0 + 16].unsqueeze(2).broadcast_to([128, 16, 128])
        li = S(8)[:, n0:n0 + 16].unsqueeze(2).broadcast_to([128, 16, 128])
        Cr = S(11)[:, n0:n0 + 16].unsqueeze(2).broadcast_to([128, 16, 128])
        Ci = S(12)[:, n0:n0 + 16].unsqueeze(2).broadcast_to([128, 16, 128])
        p.tt(tva, Prh, lr, OP.mult, reads=[cP, csl], writes=[ctmp])
        p.tt(tvb, Pih, li, OP.mult, reads=[cP, csl], writes=[ctmp])
        p.tt(tva, tva, tvb, OP.subtract, reads=[ctmp], writes=[ctmp])
        p.tt(tvb, Prh, li, OP.mult, reads=[cP, csl], writes=[ctmp])
        ex = g1t[:, 0:16 * 128].rearrange("p (n d) -> p n d", n=16)
        p.tt(ex, Pih, lr, OP.mult, reads=[cP, csl], writes=[cg1])
        p.tt(tvb, tvb, ex, OP.add, reads=[ctmp, cg1], writes=[ctmp])
        p.tt(ex, tva, Cr, OP.mult, reads=[ctmp, csl], writes=[cg1])
        ex2 = g2t[:, 0:16 * 128].rearrange("p (n d) -> p n d", n=16)
        p.tt(ex2, tvb, Ci, OP.mult, reads=[ctmp, csl], writes=[cg2])
        p.tt(ex, ex, ex2, OP.subtract, reads=[cg1, cg2], writes=[cg1])
        p.ts(g1t[:, 0:2048], g1t[:, 0:2048], 2.0, None, OP.mult, reads=[cg1],
             writes=[cg1])
        p.dma("sync", q_d.ap()[0, :, n0:n0 + 16, :], g1t[:, 0:2048], reads=[cg1],
              writes=[cqd])
        p.tt(ex, tvb, Cr, OP.mult, reads=[ctmp, csl], writes=[cg1])
        p.tt(ex2, tva, Ci, OP.mult, reads=[ctmp, csl], writes=[cg2])
        p.tt(ex, ex, ex2, OP.add, reads=[cg1, cg2], writes=[cg1])
        p.ts(g1t[:, 0:2048], g1t[:, 0:2048], -2.0, None, OP.mult, reads=[cg1],
             writes=[cg1])
        p.dma("sync", q_d.ap()[1, :, n0:n0 + 16, :], g1t[:, 0:2048], reads=[cg1],
              writes=[cqd])

    p.copy(S(13), pview("re", 127, 1).squeeze(2), reads=[cP], writes=[csl])
    p.copy(S(14), pview("im", 127, 1).squeeze(2), reads=[cP], writes=[csl])
    p.tt(S(0), S(13), S(7), OP.mult, reads=[csl], writes=[csl])
    p.tt(S(15), S(14), S(8), OP.mult, reads=[csl], writes=[csl])
    p.tt(S(0), S(0), S(15), OP.subtract, reads=[csl], writes=[csl])
    p.tt(S(1), S(13), S(8), OP.mult, reads=[csl], writes=[csl])
    p.tt(S(15), S(14), S(7), OP.mult, reads=[csl], writes=[csl])
    p.tt(S(1), S(1), S(15), OP.add, reads=[csl], writes=[csl])
    l128b = p.sbuf("l128b", [128, 64])
    cl128 = p.cell(l128b)
    if _ptr_bf is not None:
        ptr = _ptr_bf
        cptr = _cptr
        ptrfv = _ptr_f
        cptrf = _cptr
        ptr2 = p.gbanks[6][:, 0:64].bitcast(BF16)
        cptr2 = p.gbank_cells[6]
    else:
        ptr = p.psum("ptrc", [128, 128], BF16)
        cptr = p.cell(ptr)
        _ptrf_t = p.psum("ptrf", [64, 128])
        ptrfv = _ptrf_t[0:64, 0:128]
        cptrf = p.cell(_ptrf_t)
        ptr2 = p.psum("ptrc2", [128, 128], BF16)
        cptr2 = p.cell(ptr2)
    p.copy(l128b[:, 0:32], S(0), reads=[csl], writes=[cl128])
    p.copy(l128b[:, 32:64], S(0), reads=[csl], writes=[cl128])
    p.trans(ptrfv, l128b[:, :], idf32[:, :], reads=[cl128, cidf],
            writes=[cptrf])
    p.copy(lamA[:, :], ptrfv, reads=[cptrf], writes=[clA])
    p.ts(S(15), S(1), -1.0, None, OP.mult, reads=[csl], writes=[csl])
    p.copy(l128b[:, 0:32], S(15), reads=[csl], writes=[cl128])
    p.copy(l128b[:, 32:64], S(1), reads=[csl], writes=[cl128])
    p.trans(ptrfv, l128b[:, :], idf32[:, :], reads=[cl128, cidf],
            writes=[cptrf])
    p.copy(lamB[:, :], ptrfv, reads=[cptrf], writes=[clB])

    p.tt(S(15), S(3), S(3), OP.mult, reads=[csl], writes=[csl])
    p.recip(S(15), S(15), reads=[csl], writes=[csl])
    p.tt(S(9), S(7), S(15), OP.mult, reads=[csl], writes=[csl])
    p.tt(S(10), S(8), S(15), OP.mult, reads=[csl], writes=[csl])
    p.ts(S(10), S(10), -1.0, None, OP.mult, reads=[csl], writes=[csl])
    p.memset(Pre[:, :], 0.0, reads=[cP], writes=[cP])
    p.memset(Pim[:, :], 0.0, reads=[cP], writes=[cP])
    p.memset(pview("re", 0, 1), 1.0, writes=[cP])
    p.copy(pview("re", 1, 1), S(9).unsqueeze(2), reads=[csl], writes=[cP])
    p.copy(pview("im", 1, 1), S(10).unsqueeze(2), reads=[csl], writes=[cP])
    m = 2
    while m < 128:
        p.tt(t1v(1), pview("re", m - 1, 1), S(9).unsqueeze(2), OP.mult,
             reads=[cP, csl], writes=[ctmp])
        p.tt(pview("re", m, 1), pview("im", m - 1, 1), S(10).unsqueeze(2),
             OP.mult, reads=[cP, csl], writes=[cP])
        p.tt(t2[:, 0:32].rearrange("p (n d) -> p n d", n=32),
             pview("im", m - 1, 1), S(9).unsqueeze(2), OP.mult,
             reads=[cP, csl], writes=[ctmp])
        p.tt(pview("im", m, 1), pview("re", m - 1, 1), S(10).unsqueeze(2),
             OP.mult, reads=[cP, csl], writes=[cP])
        p.tt(pview("im", m, 1), pview("im", m, 1),
             t2[:, 0:32].rearrange("p (n d) -> p n d", n=32), OP.add,
             reads=[cP, ctmp], writes=[cP])
        p.tt(pview("re", m, 1), t1v(1), pview("re", m, 1), OP.subtract,
             reads=[ctmp, cP], writes=[cP])
        w = m - 1
        if w > 0:
            rb = pview("re", m, 1).broadcast_to([128, 32, w])
            ib = pview("im", m, 1).broadcast_to([128, 32, w])
            ta = t1v(w)
            t2v = t2[:, 0:32 * w].rearrange("p (n d) -> p n d", n=32)
            p.tt(ta, pview("re", 1, w), rb, OP.mult, reads=[cP], writes=[ctmp])
            p.tt(t2v, pview("im", 1, w), rb, OP.mult, reads=[cP], writes=[ctmp])
            p.tt(pview("im", m + 1, w), pview("re", 1, w), ib, OP.mult,
                 reads=[cP], writes=[cP])
            p.tt(pview("im", m + 1, w), pview("im", m + 1, w), t2v, OP.add,
                 reads=[cP, ctmp], writes=[cP])
            p.tt(pview("re", m + 1, w), pview("im", 1, w), ib, OP.mult,
                 reads=[cP], writes=[cP])
            p.tt(pview("re", m + 1, w), ta, pview("re", m + 1, w), OP.subtract,
                 reads=[ctmp, cP], writes=[cP])
        m *= 2
    prevb = ztst[:, 0:4096]
    cprev = czt
    for r in range(2):
        for half in range(2):
            n0 = half * 16
            tva = tmp[:, 0:16 * 128].rearrange("p (n d) -> p n d", n=16)
            tvb = tmp[:, 2048:4096].rearrange("p (n d) -> p n d", n=16)
            Prh = P_[:, n0 * 128:(n0 + 16) * 128].rearrange("p (n d) -> p n d", n=16)
            Pih = P_[:, 4096 + n0 * 128:4096 + (n0 + 16) * 128].rearrange(
                "p (n d) -> p n d", n=16)
            l7r = S(13)[:, n0:n0 + 16].unsqueeze(2).broadcast_to([128, 16, 128])
            l7i = S(14)[:, n0:n0 + 16].unsqueeze(2).broadcast_to([128, 16, 128])
            if r == 0:
                p.tt(tva, Prh, l7r, OP.mult, reads=[cP, csl], writes=[ctmp])
                p.tt(tvb, Pih, l7i, OP.mult, reads=[cP, csl], writes=[ctmp])
                p.tt(tva, tva, tvb, OP.subtract, reads=[ctmp], writes=[ctmp])
            else:
                p.tt(tva, Prh, l7i, OP.mult, reads=[cP, csl], writes=[ctmp])
                p.tt(tvb, Pih, l7r, OP.mult, reads=[cP, csl], writes=[ctmp])
                p.tt(tva, tva, tvb, OP.add, reads=[ctmp], writes=[ctmp])
            p.copy(prevb[:, n0 * 128:(n0 + 16) * 128], tmp[:, 0:2048],
                   reads=[ctmp], writes=[cprev])
        for n in range(32):
            pt, cpt = (ptr, cptr) if n % 2 == 0 else (ptr2, cptr2)
            p.trans(pt[:, 0:128], prevb[:, n * 128:(n + 1) * 128], ids[:, :],
                    reads=[cprev, cid], writes=[cpt])
            dst = m1st[:, :].rearrange("t (h c) -> t h c", h=128)[:, :, r * 32 + n:
                                                                 r * 32 + n + 1]
            p.copy(dst, pt[:, 0:128].unsqueeze(2), reads=[cpt], writes=[cm1])

    for r in range(2):
        src = q_d.ap()[r].rearrange("h n d -> n (h d)") if False else None
        src = custom_ap(q_d.ap(), [[128, 32], [4096, 128], [1, 128]],
                        r * 128 * 32 * 128)
        p.dma("gpsimd", m2st[r * 32:(r + 1) * 32, :], src, reads=[cqd], writes=[cm2])

    for tau in range(128):
        srcap = custom_ap(kpad_d.ap(), [[1, 1], [256, 128], [1, 128]], 127 - tau)
        p.dma("gpsimd", toep[tau:tau + 1, :].rearrange("one (h t) -> one h t", h=128),
              srcap, reads=[ckpd], writes=[ctoep4[tau % 4]])

    zT3 = zT
    for b in range(NB):
        for blk in range(4):
            p.dma("gpsimd", zbf[:, :], zT[0:128, b * L + blk * 2048:
                                          b * L + (blk + 1) * 2048], writes=[czb])
            for cc in range(16):
                c_abs = blk * 16 + cc
                pt, cpt = (ptr, cptr) if cc % 2 == 0 else (ptr2, cptr2)
                p.trans(pt[:, 0:128], zbf[:, cc * 128:(cc + 1) * 128], ids[:, :],
                        reads=[czb, cid], writes=[cpt])
                dst = ztst[:, :].rearrange("t (h c) -> t h c", h=128)[:, :, c_abs:
                                                                      c_abs + 1]
                p.copy(dst, pt[:, 0:128].unsqueeze(2), reads=[cpt], writes=[czt])
        agg = P_[0:64, 0:8192]
        for h in range(128):
            pb, cpb = pool.next()
            mm_group(p, [(pb[0:64, 0:64], m1st[:, h * 64:(h + 1) * 64],
                          ztst[:, h * 64:(h + 1) * 64])], [cm1, czt], cpb)
            p.copy(agg[:, h * 64:(h + 1) * 64], pb[0:64, 0:64], reads=[cpb],
                   writes=[cP])
        p.memset(scur[:, :], 0.0, writes=[cscur])
        rec1 = tmp[0:64, 0:128]
        cr1 = ctmp
        for c in range(NCH):
            dstc = stbf[:, :].rearrange("s (h c) -> s h c", h=128)[:, :, c:c + 1]
            p.copy(dstc, scur[:, :].unsqueeze(2), reads=[cscur], writes=[cst])
            aggc = agg[:, :].rearrange("s (h c) -> s h c", h=128)[:, :, c]
            pb, cpb = pool.next()
            mm_group(p, [(pb[0:64, 0:128], swp_s[:, :], scur[:, :])],
                     [cswp, cscur], cpb)
            p.tt(rec1, pb[0:64, 0:128], lamB[:, :], OP.mult,
                 reads=[cpb, clB], writes=[cr1])
            p.tt(scur[:, :], scur[:, :], lamA[:, :], OP.mult,
                 reads=[clA, cscur], writes=[cscur])
            p.tt(scur[:, :], scur[:, :], rec1, OP.add,
                 reads=[cr1, cscur], writes=[cscur])
            p.tt(scur[:, :], scur[:, :], aggc, OP.add,
                 reads=[cP, cscur], writes=[cscur])
        for h in range(128):
            pb, cpb = pool.next()
            mm_group(p, [
                (pb[:, 0:64], toep[:, h * 128:(h + 1) * 128],
                 ztst[:, h * 64:(h + 1) * 64]),
                (pb[:, 0:64], m2st[:, h * 128:(h + 1) * 128],
                 stbf[:, h * 64:(h + 1) * 64]),
            ], ctoep4 + [cm1, cm2, czt, cst], cpb)
            dst = ytst[:, :].rearrange("t (c h) -> t c h", c=64)[:, :, h]
            p.copy(dst, pb[:, 0:64], reads=[cpb], writes=[cyt])
        for blk in range(4):
            for cc in range(16):
                c_abs = blk * 16 + cc
                pt, cpt = (ptr, cptr) if cc % 2 == 0 else (ptr2, cptr2)
                p.trans(pt[:, 0:128], ytst[:, c_abs * 128:(c_abs + 1) * 128],
                        ids[:, :], reads=[cyt, cid], writes=[cpt])
                p.copy(yblk[:, cc * 128:(cc + 1) * 128], pt[:, 0:128],
                       reads=[cpt], writes=[cyb])
            p.dma("sync", zldc[:, :], zT[0:128, b * L + blk * 2048:
                                         b * L + (blk + 1) * 2048], writes=[czl])
            p.stt(yblk[:, :], zldc[:, :], dsk_s[:, :], yblk[:, :], OP.mult,
                  OP.add, reads=[czl, cdsk, cyb], writes=[cyb])
            p.act(g1t[:, :], yblk[:, :], AF.Square, reads=[cyb], writes=[cg1])
            p.ts(g1t[:, :], g1t[:, :], 0.7978845608 * 0.044715, 0.7978845608,
                 OP.mult, OP.add, reads=[cg1], writes=[cg1])
            p.tt(g1t[:, :], g1t[:, :], yblk[:, :], OP.mult, reads=[cyb, cg1],
                 writes=[cg1])
            p.act(g1t[:, :], g1t[:, :], AF.Tanh, reads=[cg1], writes=[cg1])
            p.ts(g1t[:, :], g1t[:, :], 0.5, 0.5, OP.mult, OP.add, reads=[cg1],
                 writes=[cg1])
            p.tt(g1t[:, :], g1t[:, :], yblk[:, :], OP.mult, reads=[cyb, cg1],
                 writes=[cg1])
            p.dma("sync", yg_o[0:128, b * L + blk * 2048:b * L + (blk + 1) * 2048],
                  g1t[:, :], reads=[cg1])
    import os
    if os.environ.get("DBG_CONV", "0") == "1":
        dbg_k = p.outp("dbg_k", [128, 256])
        dbg_lamA = p.outp("dbg_lamA", [64, 128])
        dbg_lamB = p.outp("dbg_lamB", [64, 128])
        dbg_toep = p.outp("dbg_toep", [128, 16384], BF16)
        dbg_m1 = p.outp("dbg_m1", [128, 8192], BF16)
        dbg_m2 = p.outp("dbg_m2", [64, 16384], BF16)
        dbg_zt = p.outp("dbg_zt", [128, 8192], BF16)
        dbg_st = p.outp("dbg_st", [64, 8192], BF16)
        dbg_agg = p.outp("dbg_agg", [64, 8192])
        dbg_yt = p.outp("dbg_yt", [128, 8192], BF16)
        p.dma("sync", dbg_k, g1t[:, 0:256], reads=[cg1])
        p.dma("sync", dbg_lamA, lamA[:, :], reads=[clA])
        p.dma("sync", dbg_lamB, lamB[:, :], reads=[clB])
        p.dma("sync", dbg_toep, toep[:, :], reads=ctoep4)
        p.dma("sync", dbg_m1, m1st[:, :], reads=[cm1])
        p.dma("sync", dbg_m2, m2st[:, :], reads=[cm2])
        p.dma("sync", dbg_zt, ztst[:, :], reads=[czt])
        p.dma("sync", dbg_st, stbf[:, :], reads=[cst])
        p.dma("sync", dbg_agg, P_[0:64, :], reads=[cP])
        p.dma("sync", dbg_yt, ytst[:, :], reads=[cyt])
    p.drain_q("sync")
    p.drain_q("gpsimd")
    return p


def build_l1_qkv(p, T=2048):
    xT = p.inp("xn1T_sl", [1024, T])

    Wq = p.inp("Wq_f", [1024, 1024])
    Wkc = p.inp("Wk_c_f", [1024, 256])
    Wk = p.inp("Wk_f", [256, 1024])
    Wvc = p.inp("Wv_c_f", [1024, 256])
    Wv = p.inp("Wv_f", [256, 1024])
    sin_i = p.inp("sin_my", [T, 32])
    cos_i = p.inp("cos_my", [T, 32])
    idf = p.inp("ident_fq", [128, 128])
    q_o = p.outp("q_tm", [T, 1024])
    k_o = p.outp("k_tm", [T, 1024])
    v_o = p.outp("v_tm", [T, 1024])
    pool = PsumPool(p, 6)
    ids = p.sbuf("idsq", [128, 128])
    cid = p.cell(ids)
    p.dma("sync", ids[:, :], idf, writes=[cid])
    xf = p.sbuf("xfq", [128, 8 * T], BF16)
    cxf = [p.cell(xf, f"t{i}") for i in range(8)]
    for ik in range(8):
        p.dma("gpsimd", xf[:, ik * T:(ik + 1) * T], xT[ik * 128:(ik + 1) * 128, :],
              writes=[cxf[ik]])
    wtq, cwq, _ = load_w_bf16(p, "wtq", Wq, 1024, 1024)
    res_fm = p.sbuf("res_fm", [128, 8 * T], BF16)
    cres = p.cell(res_fm)
    kc_fm = p.sbuf("kc_fm", [128, 2 * T], BF16)
    ckc = p.cell(kc_fm)
    tm = p.sbuf("tmq", [128, (T // 128) * 1024])
    ctm = p.cell(tm)
    sin_s = p.sbuf("sin_s", [128, (T // 128) * 32])
    cos_s = p.sbuf("cos_s", [128, (T // 128) * 32])
    csin, ccos = p.cell(sin_s), p.cell(cos_s)
    nt = T // 128
    for ti in range(nt):
        p.dma("sync", sin_s[:, ti * 32:(ti + 1) * 32], sin_i[ti * 128:(ti + 1) * 128, :],
              writes=[csin])
        p.dma("sync", cos_s[:, ti * 32:(ti + 1) * 32], cos_i[ti * 128:(ti + 1) * 128, :],
              writes=[ccos])
    rt = p.sbuf("rtq", [128, 1024])
    crt = p.cell(rt)
    idsb = p.sbuf("idsbq", [128, 128], BF16)
    cidb = p.cell(idsb)
    p.copy(idsb[:, :], ids[:, :], reads=[cid], writes=[cidb])
    if p.gbanks is not None:
        ptq = p.ptr_bf
        cptq = p.ptr_cell
        ptq2 = p.gbanks[6][:, 0:64].bitcast(BF16)
        cptq2 = p.gbank_cells[6]
    else:
        ptq = p.psum("ptq", [128, 128], BF16)
        cptq = p.cell(ptq)
        ptq2 = p.psum("ptq2", [128, 128], BF16)
        cptq2 = p.cell(ptq2)

    def fm_to_tm(cfm, fm):
        for ti in range(nt):
            for ro in range(8):
                pt, cpt = (ptq, cptq) if ro % 2 == 0 else (ptq2, cptq2)
                p.trans(pt[:, 0:128], fm[:, ro * T + ti * 128: ro * T + (ti + 1) * 128],
                        idsb[:, :], reads=[cfm, cidb], writes=[cpt])
                p.copy(tm[:, ti * 1024 + ro * 128: ti * 1024 + (ro + 1) * 128],
                       pt[:, 0:128], reads=[cpt], writes=[ctm])

    def rope_and_out(out_dram, do_rope=True):
        for ti in range(nt):
            src = tm[:, ti * 1024:(ti + 1) * 1024]
            if do_rope:
                x1 = src.rearrange("p (h two j) -> p h two j", h=16, two=2)[:, :, 0, :] \
                    if False else src.rearrange("p (h j two) -> p h j two", h=16,
                                                two=2)[:, :, :, 0]
                x2 = src.rearrange("p (h j two) -> p h j two", h=16, two=2)[:, :, :, 1]
                cosb = cos_s[:, ti * 32:(ti + 1) * 32].unsqueeze(1).broadcast_to(
                    [128, 16, 32]).rearrange("p h j -> p h j") if False else \
                    cos_s[:, ti * 32:(ti + 1) * 32].unsqueeze(0).broadcast_to(
                        [128, 16, 32]) if False else None
                cosb = cos_s[:, ti * 32:(ti + 1) * 32].rearrange(
                    "p j -> p j").unsqueeze(1).broadcast_to([128, 16, 32])
                sinb = sin_s[:, ti * 32:(ti + 1) * 32].unsqueeze(1).broadcast_to(
                    [128, 16, 32])
                r1 = rt.rearrange("p (h j two) -> p h j two", h=16, two=2)[:, :, :, 0]
                r2 = rt.rearrange("p (h j two) -> p h j two", h=16, two=2)[:, :, :, 1]
                g1 = p.sbuf if False else None
                p.tt(r1, x1, cosb, OP.mult, reads=[ctm, ccos], writes=[crt])
                p.tt(r2, x1, sinb, OP.mult, reads=[ctm, csin], writes=[crt])
                p.stt(r1, x2, -1.0, None, OP.mult, OP.bypass) if False else None
                p.tt(x1, x2, sinb, OP.mult, reads=[ctm, csin], writes=[ctm])
                p.tt(r1, r1, x1, OP.subtract, reads=[ctm, crt], writes=[crt])
                p.tt(x1, x2, cosb, OP.mult, reads=[ctm, ccos], writes=[ctm])
                p.tt(r2, r2, x1, OP.add, reads=[ctm, crt], writes=[crt])
                p.dma("sync", out_dram[ti * 128:(ti + 1) * 128, :], rt[:, :],
                      reads=[crt])
            else:
                p.dma("sync", out_dram[ti * 128:(ti + 1) * 128, :], src,
                      reads=[ctm])

    def ev_q(ro, c0, cw, pb, cbk):
        p.copy(res_fm[:, ro * T + c0: ro * T + c0 + cw], pb[:, 0:cw],
               reads=[cbk], writes=[cres])
    gemm(p, pool, wtq, cwq, 8, 1024, 1024, xf, cxf, T, T, ev_q)
    fm_to_tm(cres, res_fm)
    rope_and_out(q_o, True)
    wtc, cwc_, _ = load_w_bf16(p, "wtkc", Wkc, 1024, 256)

    def ev_kc(ro, c0, cw, pb, cbk):
        p.copy(kc_fm[:, ro * T + c0: ro * T + c0 + cw], pb[:, 0:cw],
               reads=[cbk], writes=[ckc])
    gemm(p, pool, wtc, cwc_, 8, 256, 1024, xf, cxf, T, T, ev_kc)
    wtk, cwk, _ = load_w_bf16(p, "wtk", Wk, 256, 1024)
    gemm(p, pool, wtk, cwk, 2, 1024, 256, kc_fm, ckc, T, T, ev_q)
    fm_to_tm(cres, res_fm)
    rope_and_out(k_o, True)
    for ik in range(8):
        p.dma("gpsimd", wtc[:, ik * 256:(ik + 1) * 256],
              Wvc[ik * 128:(ik + 1) * 128, :], writes=[cwc_[ik]])
    gemm(p, pool, wtc, cwc_, 8, 256, 1024, xf, cxf, T, T, ev_kc)
    for ik in range(2):
        p.dma("gpsimd", wtk[:, ik * 1024:(ik + 1) * 1024],
              Wv[ik * 128:(ik + 1) * 128, :], writes=[cwk[ik]])
    gemm(p, pool, wtk, cwk, 2, 1024, 256, kc_fm, ckc, T, T, ev_q)
    fm_to_tm(cres, res_fm)
    rope_and_out(v_o, False)
    p.drain_q("sync")
    return p


def build_l1_attn(p, L=8192, DH=64):
    qT = p.inp("qT_h", [128, 2 * L])
    kT = p.inp("kT_h", [128, 2 * L])
    vv = p.inp("v_h", [2 * L, 128])
    onesb = p.inp("onesb", [128, 64], BF16)
    onesf = p.inp("onesf_at", [1, 128])
    sa_o = p.outp("sa_h", [128, 2 * L])
    if p.gbanks is not None:
        pool = PsumPool(p, 4)
        po = p.gbanks[4]
        pn = p.gbanks[5]
        cpo, cpn = p.gbank_cells[4], p.gbank_cells[5]
    else:
        pool = PsumPool(p, 4)
        po = p.psum("po_at", [64, 512])
        pn = p.psum("pn_at", [1, 512])
        cpo, cpn = p.cell(po), p.cell(pn)
    ob = p.sbuf("ob_at", [128, 64], BF16)
    cob = p.cell(ob)
    p.dma("sync", ob[:, :], onesb, writes=[cob])
    of = p.sbuf("of_at", [1, 128])
    cof = p.cell(of)
    p.dma("sync", of[:, :], onesf, writes=[cof])
    kbs = [p.sbuf(f"kb_at{i}", [64, L], BF16) for i in range(2)]
    qbs = [p.sbuf(f"qb_at{i}", [64, L], BF16) for i in range(2)]
    vts = [p.sbuf(f"vt_at{i}", [128, 64 * 65], BF16) for i in range(2)]
    ckbs = [p.cell(t) for t in kbs]
    cqbs = [p.cell(t) for t in qbs]
    cvts = [[p.cell(t, f"s{j}") for j in range(4)] for t in vts]
    pbufs = [p.sbuf(f"pbuf_at{i}", [128, 512], BF16) for i in range(2)]
    cpbs = [p.cell(t) for t in pbufs]
    pbuf = pbufs[0]
    rq = p.sbuf("rq_at", [128, 512])
    rb2 = p.sbuf("rb2_at", [64, 512])
    sa_s = p.sbuf("sa_s", [64, 512])
    of2 = p.sbuf("of2_at", [128, 128])
    cof2 = p.cell(of2)
    p.dma("sync", of2[64:65, :], onesf, writes=[cof2])
    crq, crb, csa = (p.cell(t) for t in (rq, rb2, sa_s))
    p.memset(vts[0][:, :], 1.0, writes=cvts[0])
    p.memset(vts[1][:, :], 1.0, writes=cvts[1])
    NKB = L // 128
    pairs = [(b, hh) for b in range(2) for hh in range(2)]

    def load_pair(i):
        b, hh = pairs[i]
        pi = i % 2
        p.dma("gpsimd", kbs[pi][:, :],
              kT[hh * 64:(hh + 1) * 64, b * L:(b + 1) * L], writes=[ckbs[pi]])
        p.dma("gpsimd", qbs[pi][:, :],
              qT[hh * 64:(hh + 1) * 64, b * L:(b + 1) * L], writes=[cqbs[pi]])
        for kbk in range(NKB):
            p.dma("gpsimd", vts[pi][:, kbk * 65:kbk * 65 + 64],
                  vv[b * L + kbk * 128: b * L + (kbk + 1) * 128,
                     hh * 64:(hh + 1) * 64], writes=[cvts[pi][kbk % 4]])

    load_pair(0)
    if True:
        for ip, (b, hh) in enumerate(pairs):
            pi = ip % 2
            kb, ckb = kbs[pi], ckbs[pi]
            qb, cqb = qbs[pi], cqbs[pi]
            vt, cvt = vts[pi], cvts[pi]
            if ip + 1 < len(pairs):
                load_pair(ip + 1)
            tiles = [(q0, kbk) for q0 in range(0, L, 512) for kbk in range(NKB)]

            def emit_sp(i):
                q0, kbk = tiles[i]
                ps, cps = pool.next()
                mm_group(p, [(ps[:, 0:512], kb[:, kbk * 128:(kbk + 1) * 128],
                              qb[:, q0:q0 + 512])], [ckb, cqb], cps)
                return ps, cps

            pend = emit_sp(0)
            for i, (q0, kbk) in enumerate(tiles):
                ps, cps = pend
                if i + 1 < len(tiles):
                    pend = emit_sp(i + 1)
                par = kbk % 2
                pbuf, cpb = pbufs[par], cpbs[par]
                p.act(pbuf[:, :], ps[:, 0:512], AF.Exp, scale=0.125,
                      reads=[cps], writes=[cpb])
                p.mm(po[0:65, 0:512], vt[:, kbk * 65:(kbk + 1) * 65],
                     pbuf[:, :], start=(kbk == 0), stop=(kbk == NKB - 1),
                     reads=[cvt[kbk % 4], cpb], writes=[cpo])
                if kbk == NKB - 1:
                    p.recip(rq[64:65, 0:512], po[64:65, 0:512], reads=[cpo],
                            writes=[crq])
                    pb2, cpb2 = pool.next()
                    mm_group(p, [(pb2[0:64, 0:512], of2[64:65, 0:64],
                                  rq[64:65, 0:512])], [cof2, crq], cpb2)
                    p.copy(rb2[:, :], pb2[0:64, 0:512], reads=[cpb2], writes=[crb])
                    p.tt(sa_s[:, :], po[0:64, 0:512], rb2[:, :], OP.mult,
                         reads=[cpo, crb], writes=[csa])
                    p.dma("sync", sa_o[hh * 64:(hh + 1) * 64,
                                       b * L + q0:b * L + q0 + 512],
                          sa_s[:, :], reads=[csa])
    p.drain_q("sync")
    return p

def _emit_rms(p, out_ap, in_ap, cin, cout, wb, cwb, sc, cs_, eps=1e-6, D=1024):
    junk, s2, v, r = sc["junk"], sc["s2"], sc["v"], sc["r"]
    cj, c2, cv, cr = cs_["junk"], cs_["s2"], cs_["v"], cs_["r"]
    p.stt(junk[:, :], in_ap, 1.0, in_ap, OP.bypass, OP.mult,
          accum_out=s2[:, :], reads=[cin], writes=[cj, c2])
    p.ts(v[:, :], s2[:, :], 1.0 / D, eps, OP.mult, OP.add, reads=[c2], writes=[cv])
    p.act(r[:, :], v[:, :], AF.Ln, reads=[cv], writes=[cr])
    p.act(r[:, :], r[:, :], AF.Exp, scale=-0.5, reads=[cr], writes=[cr])
    p.ts(out_ap, in_ap, r[:, :], None, OP.mult, reads=[cin, cr], writes=[cout])
    p.tt(out_ap, out_ap, wb[:, :], OP.mult, reads=[cwb, cout], writes=[cout])


def build_l1_attnout(p, T=2048):
    saT = p.inp("saT_sl", [1024, T])
    Wo = p.inp("Wo_f", [1024, 1024])
    src_tm = p.inp("xn1_sl", [T, 1024])
    nrow = p.inp("norm1_row", [1, 1024])
    ones1 = p.inp("ones1_ao", [1, 128])
    idf = p.inp("ident_ao", [128, 128])
    xr_o = p.outp("xr_tm", [T, 1024])
    pool = PsumPool(p, 8)
    wt, cwt, nk = load_w_bf16(p, "wtao", Wo, 1024, 1024)
    ids = p.sbuf("idsao", [128, 128])
    o1 = p.sbuf("o1ao", [1, 128])
    nr = p.sbuf("nrao", [1, 1024])
    wb = p.sbuf("wbao", [128, 1024])
    cid, co1, cnr, cwb = (p.cell(t) for t in (ids, o1, nr, wb))
    p.dma("sync", ids[:, :], idf, writes=[cid])
    p.dma("sync", o1[:, :], ones1, writes=[co1])
    p.dma("sync", nr[:, :], nrow, writes=[cnr])
    _row_bcast(p, pool, wb, cwb, nr, cnr, o1, co1, 1024)
    xf = p.sbuf("xfao", [128, 8 * T], BF16)
    cxf = [p.cell(xf, f"t{i}") for i in range(nk)]
    for ik in range(nk):
        p.dma("gpsimd", xf[:, ik * T:(ik + 1) * T], saT[ik * 128:(ik + 1) * 128, :],
              writes=[cxf[ik]])
    nt = T // 128
    xout = p.sbuf("xoutao", [128, nt * 1024])
    cxo = p.cell(xout)
    for ti in range(nt):
        p.dma("sync", xout[:, ti * 1024:(ti + 1) * 1024],
              src_tm[ti * 128:(ti + 1) * 128, :], writes=[cxo])
    prod = p.sbuf("prodao", [128, T])
    cprod = p.cell(prod)

    def evac(ro, c0, cw, pb, cbk):
        p.copy(prod[:, c0:c0 + cw], pb[:, 0:cw], reads=[cbk], writes=[cprod])
        if c0 + cw == T:
            for ti in range(nt):
                pb2, cpb2 = pool.next()
                p.trans(pb2[:, 0:128], prod[:, ti * 128:(ti + 1) * 128], ids[:, :],
                        reads=[cprod, cid], writes=[cpb2])
                p.tt(xout[:, ti * 1024 + ro * 128: ti * 1024 + (ro + 1) * 128],
                     xout[:, ti * 1024 + ro * 128: ti * 1024 + (ro + 1) * 128],
                     pb2[:, 0:128], OP.add, reads=[cpb2, cxo], writes=[cxo])
    gemm(p, pool, wt, cwt, nk, 1024, 1024, xf, cxf, T, T, evac)
    sc, cs_ = mk_ln_scratch(p, "ao")
    xr_t = p.sbuf("xr_t_ao", [128, 1024])
    cxrt = p.cell(xr_t)
    for ti in range(nt):
        _emit_rms(p, xr_t[:, :], xout[:, ti * 1024:(ti + 1) * 1024], cxo, cxrt,
                  wb, cwb, sc, cs_)
        p.dma("sync", xr_o[ti * 128:(ti + 1) * 128, :], xr_t[:, :], reads=[cxrt])
    p.drain_q("sync")
    return p


def build_l1_gate2(p, T=2048):
    xT = p.inp("xrT_sl", [1024, T])
    Wg = p.inp("Wg_f", [1024, 2])
    We = p.inp("We_f", [1024, 8])
    gbr = p.inp("gb_row", [1, 2])
    ebr = p.inp("eb_row", [1, 8])
    ones1 = p.inp("ones1_g2", [1, 128])
    idf = p.inp("ident_g2", [128, 128])
    wts_o = p.outp("wts_o", [8, T])
    pool = PsumPool(p, 8)
    xg = p.sbuf("xg2", [128, 8 * T])
    cxg = p.cell(xg)
    for ik in range(8):
        p.dma("sync", xg[:, ik * T:(ik + 1) * T], xT[ik * 128:(ik + 1) * 128, :],
              writes=[cxg])
    wg_s = p.sbuf("wg_s", [128, 8 * 2])
    we_s = p.sbuf("we_s", [128, 8 * 8])
    cwg, cwe = p.cell(wg_s), p.cell(we_s)
    for ik in range(8):
        p.dma("sync", wg_s[:, ik * 2:(ik + 1) * 2], Wg[ik * 128:(ik + 1) * 128, :],
              writes=[cwg])
        p.dma("sync", we_s[:, ik * 8:(ik + 1) * 8], We[ik * 128:(ik + 1) * 128, :],
              writes=[cwe])
    o1 = p.sbuf("o1g2", [1, 128])
    gb_s = p.sbuf("gb_s", [1, 2])
    eb_s = p.sbuf("eb_s", [1, 8])
    gbb = p.sbuf("gbb", [128, 2])
    ebb = p.sbuf("ebb", [128, 8])
    ids = p.sbuf("idsg2", [128, 128])
    (co1, cgb, ceb, cgbb, cebb, cid) = (p.cell(t) for t in
                                        (o1, gb_s, eb_s, gbb, ebb, ids))
    p.dma("sync", o1[:, :], ones1, writes=[co1])
    p.dma("sync", gb_s[:, :], gbr, writes=[cgb])
    p.dma("sync", eb_s[:, :], ebr, writes=[ceb])
    p.dma("sync", ids[:, :], idf, writes=[cid])
    pb, cpb = pool.next()
    mm_group(p, [(pb[:, 0:2], o1[:, :], gb_s[:, :])], [co1, cgb], cpb)
    p.copy(gbb[:, :], pb[:, 0:2], reads=[cpb], writes=[cgbb])
    pb, cpb = pool.next()
    mm_group(p, [(pb[:, 0:8], o1[:, :], eb_s[:, :])], [co1, ceb], cpb)
    p.copy(ebb[:, :], pb[:, 0:8], reads=[cpb], writes=[cebb])
    lg = p.sbuf("lg", [128, 2])
    le = p.sbuf("le", [128, 8])
    m1c = p.sbuf("m1c", [128, 1])
    sc1 = p.sbuf("sc1", [128, 1])
    sc2 = p.sbuf("sc2", [128, 1])
    e2 = p.sbuf("e2", [128, 2])
    e8 = p.sbuf("e8", [128, 8])
    msk = p.sbuf("msk", [128, 8])
    wts = p.sbuf("wts", [128, 8])
    wrow = p.sbuf("wrow", [8, T])
    (clg, cle, cm1, cs1, cs2, ce2, ce8, cmsk, cwts, cwrow) = (
        p.cell(t) for t in (lg, le, m1c, sc1, sc2, e2, e8, msk, wts, wrow))
    for tb in range(T // 128):
        pg, cpg = pool.next()
        mm_group(p, [(pg[0:128, 0:2], xg[:, ik * T + tb * 128: ik * T + (tb + 1) * 128],
                      wg_s[:, ik * 2:(ik + 1) * 2]) for ik in range(8)],
                 [cxg, cwg], cpg)
        pe, cpe = pool.next()
        mm_group(p, [(pe[0:128, 0:8], xg[:, ik * T + tb * 128: ik * T + (tb + 1) * 128],
                      we_s[:, ik * 8:(ik + 1) * 8]) for ik in range(8)],
                 [cxg, cwe], cpe)
        p.tt(lg[:, :], pg[:, 0:2], gbb[:, :], OP.add, reads=[cpg, cgbb], writes=[clg])
        p.tt(le[:, :], pe[:, 0:8], ebb[:, :], OP.add, reads=[cpe, cebb], writes=[cle])
        p.red(m1c[:, :], lg[:, :], op=OP.max, reads=[clg], writes=[cm1])
        p.ts(m1c[:, :], m1c[:, :], -1.0, None, OP.mult, reads=[cm1], writes=[cm1])
        p.act(e2[:, :], lg[:, :], AF.Exp, bias=m1c[:, :], reads=[clg, cm1],
              writes=[ce2])
        p.red(sc1[:, :], e2[:, :], op=OP.add, reads=[ce2], writes=[cs1])
        p.recip(sc1[:, :], sc1[:, :], reads=[cs1], writes=[cs1])
        p.red(sc2[:, :], e2[:, :], op=OP.max, reads=[ce2], writes=[cs2])
        p.tt(sc2[:, :], sc2[:, :], sc1[:, :], OP.mult, reads=[cs1, cs2],
             writes=[cs2])
        p.ts(m1c[:, :], m1c[:, :], -1.0, None, OP.mult, reads=[cm1], writes=[cm1])
        p.tt(msk[:, 0:2], lg[:, :], m1c[:, 0:1].broadcast_to([128, 2]), OP.is_ge,
             reads=[clg, cm1], writes=[cmsk])
        p.ts(msk[:, 2:4], msk[:, 0:2], 1.0, 1.0, OP.mult, OP.subtract,
             reads=[cmsk], writes=[cmsk])
        gm8 = msk[:, 2:4].unsqueeze(2).broadcast_to([128, 2, 4])
        p.stt(e8.rearrange("p (g j) -> p g j", g=2), gm8, 1e30,
              le.rearrange("p (g j) -> p g j", g=2), OP.mult, OP.add,
              reads=[cmsk, cle], writes=[ce8])
        p.red(m1c[:, :], e8[:, :], op=OP.max, reads=[ce8], writes=[cm1])
        p.ts(m1c[:, :], m1c[:, :], -1.0, None, OP.mult, reads=[cm1], writes=[cm1])
        p.act(e8[:, :], e8[:, :], AF.Exp, bias=m1c[:, :], reads=[ce8, cm1],
              writes=[ce8])
        p.red(sc1[:, :], e8[:, :], op=OP.add, reads=[ce8], writes=[cs1])
        p.recip(sc1[:, :], sc1[:, :], reads=[cs1], writes=[cs1])
        p.ts(e8[:, :], e8[:, :], sc1[:, :], sc2[:, :], OP.mult, OP.mult,
             reads=[ce8, cs1, cs2], writes=[ce8])
        p.red(m1c[:, :], e8[:, :], op=OP.max, reads=[ce8], writes=[cm1])
        p.tt(msk[:, :], e8[:, :], m1c[:, 0:1].broadcast_to([128, 8]), OP.is_lt,
             reads=[ce8, cm1], writes=[cmsk])
        p.tt(wts[:, :], e8[:, :], msk[:, :], OP.mult, reads=[ce8, cmsk],
             writes=[cwts])
        p.red(sc1[:, :], wts[:, :], op=OP.max, reads=[cwts], writes=[cs1])
        p.tt(msk[:, :], e8[:, :], sc1[:, 0:1].broadcast_to([128, 8]), OP.is_ge,
             reads=[ce8, cs1], writes=[cmsk])
        p.tt(wts[:, :], e8[:, :], msk[:, :], OP.mult, reads=[ce8, cmsk],
             writes=[cwts])
        pb3, cpb3 = pool.next()
        p.trans(pb3[0:8, 0:128], wts[:, :], ids[:, :], reads=[cwts, cid],
                writes=[cpb3])
        p.copy(wrow[:, tb * 128:(tb + 1) * 128], pb3[0:8, 0:128], reads=[cpb3],
               writes=[cwrow])
    p.dma("sync", wts_o, wrow[:, :], reads=[cwrow])
    p.drain_q("sync")
    return p


def build_l1_moe1(p, T=2048, TG=1024):
    xT = p.inp("xrT_sl1", [1024, T])
    Wsa = p.inp("Wsi_a", [1024, 2048])
    Wsb = p.inp("Wsi_b", [1024, 2048])
    Wso = p.inp("Wso_f", [2048, 1024])
    W1a = p.inp("W1_a", [1024, 2048])
    W1b = p.inp("W1_b", [1024, 2048])
    sh_o = p.outp("shared_o", [1024, T])
    h1_o = p.outp("h1_o", [2048, T], BF16)
    pool = PsumPool(p, 8)
    wta, cwa, _ = load_w_bf16(p, "wta1", Wsa, 1024, 2048)
    wtb, cwb2, _ = load_w_bf16(p, "wtb1", Wsb, 1024, 2048)
    wtd, cwd, _ = load_w_bf16(p, "wtd1", Wso, 2048, 1024)
    xf = p.sbuf("xf1", [128, 8 * TG], BF16)
    cxf = [p.cell(xf, f"t{i}") for i in range(8)]
    act = p.sbuf("act1", [128, 16 * TG], BF16)
    cact = p.cell(act)
    sg = p.sbuf("sg1", [128, 512], BF16)
    csg = p.cell(sg)
    ev = p.sbuf("ev1", [128, 512])
    cev = p.cell(ev)

    def swiglu_group(t0):
        for ro in range(16):
            for c0 in range(0, TG, 512):
                pa, cpa = pool.next()
                mm_group(p, [(pa[:, 0:512],
                              wta[:, ik * 2048 + ro * 128: ik * 2048 + ro * 128 + 128],
                              xf[:, ik * TG + c0: ik * TG + c0 + 512])
                             for ik in range(8)], [], cpa,
                         item_reads=[[cwa[ik], cxf[ik]] for ik in range(8)])
                pb_, cpb_ = pool.next()
                mm_group(p, [(pb_[:, 0:512],
                              wtb[:, ik * 2048 + ro * 128: ik * 2048 + ro * 128 + 128],
                              xf[:, ik * TG + c0: ik * TG + c0 + 512])
                             for ik in range(8)], [], cpb_,
                         item_reads=[[cwb2[ik], cxf[ik]] for ik in range(8)])
                p.act(sg[:, :], pa[:, 0:512], AF.Silu, reads=[cpa], writes=[csg])
                p.tt(act[:, ro * TG + c0: ro * TG + c0 + 512], sg[:, :],
                     pb_[:, 0:512], OP.mult, reads=[csg, cpb_], writes=[cact])

    for gi, t0 in enumerate(range(0, T, TG)):
        for ik in range(8):
            p.dma("gpsimd", xf[:, ik * TG:(ik + 1) * TG],
                  xT[ik * 128:(ik + 1) * 128, t0:t0 + TG], writes=[cxf[ik]])
        swiglu_group(t0)
        for ro in range(8):
            for c0 in range(0, TG, 512):
                pd, cpd = pool.next()
                mm_group(p, [(pd[:, 0:512],
                              wtd[:, ik * 1024 + ro * 128: ik * 1024 + ro * 128 + 128],
                              act[:, ik * TG + c0: ik * TG + c0 + 512])
                             for ik in range(16)], [cact], cpd,
                         item_reads=[[cwd[ik]] for ik in range(16)])
                p.copy(ev[:, 0:512], pd[:, 0:512], reads=[cpd], writes=[cev])
                p.dma("sync", sh_o[ro * 128:(ro + 1) * 128, t0 + c0:t0 + c0 + 512],
                      ev[:, 0:512], reads=[cev])
        for ik in range(8):
            p.dma("gpsimd", wta[:, ik * 2048:(ik + 1) * 2048],
                  W1a[ik * 128:(ik + 1) * 128, :], writes=[cwa[ik]])
            p.dma("gpsimd", wtb[:, ik * 2048:(ik + 1) * 2048],
                  W1b[ik * 128:(ik + 1) * 128, :], writes=[cwb2[ik]])
        swiglu_group(t0)
        for ro in range(16):
            p.dma("sync", h1_o[ro * 128:(ro + 1) * 128, t0:t0 + TG],
                  act[:, ro * TG:(ro + 1) * TG], reads=[cact])
        if t0 + TG < T:
            for ik in range(8):
                p.dma("gpsimd", wta[:, ik * 2048:(ik + 1) * 2048],
                      Wsa[ik * 128:(ik + 1) * 128, :], writes=[cwa[ik]])
                p.dma("gpsimd", wtb[:, ik * 2048:(ik + 1) * 2048],
                      Wsb[ik * 128:(ik + 1) * 128, :], writes=[cwb2[ik]])
    p.drain_q("sync")
    return p


def build_l1_moe2(p, T=2048, TG=2048):
    h1 = p.inp("h1_i", [2048, T], BF16)
    sh = p.inp("shared_i", [1024, T])
    wts8 = p.inp("wts8", [8, T])
    W2 = p.inp("W2_all", [16384, 1024])
    xr_tm = p.inp("xr_sl", [T, 1024])
    nrow = p.inp("norm2_row", [1, 1024])
    ones1 = p.inp("ones1_m2", [1, 128])
    idf = p.inp("ident_m2", [128, 128])
    g0_o = p.outp("g0_tm", [T, 1024])
    pool = PsumPool(p, 8)
    ids = p.sbuf("idsm2", [128, 128])
    o1 = p.sbuf("o1m2", [1, 128])
    nr = p.sbuf("nrm2", [1, 1024])
    wb = p.sbuf("wbm2", [128, 1024])
    cid, co1, cnr, cwb = (p.cell(t) for t in (ids, o1, nr, wb))
    p.dma("sync", ids[:, :], idf, writes=[cid])
    p.dma("sync", o1[:, :], ones1, writes=[co1])
    p.dma("sync", nr[:, :], nrow, writes=[cnr])
    _row_bcast(p, pool, wb, cwb, nr, cnr, o1, co1, 1024)
    wtd = p.sbuf("wtd2", [128, 16 * 1024], BF16)
    cwd = [p.cell(wtd, f"t{i}") for i in range(16)]
    h1f = p.sbuf("h1f", [128, 16 * TG], BF16)
    ch1 = [p.cell(h1f, f"t{i}") for i in range(16)]
    racc = p.sbuf("racc", [128, 8 * TG])
    cra = p.cell(racc)
    web = p.sbuf("web", [128, TG])
    cweb = p.cell(web)
    wrow = p.sbuf("wrowm", [1, T])
    cwr = p.cell(wrow)
    tmp5 = p.sbuf("tmp5", [128, 512])
    ct5 = p.cell(tmp5)
    for ik in range(16):
        p.dma("gpsimd", h1f[:, ik * TG:(ik + 1) * TG],
              h1[ik * 128:(ik + 1) * 128, :], writes=[ch1[ik]])
    for ro in range(8):
        p.dma("sync", racc[:, ro * TG:(ro + 1) * TG],
              sh[ro * 128:(ro + 1) * 128, :], writes=[cra])
    for e in range(8):
        for ik in range(16):
            p.dma("gpsimd", wtd[:, ik * 1024:(ik + 1) * 1024],
                  W2[e * 2048 + ik * 128: e * 2048 + (ik + 1) * 128, :],
                  writes=[cwd[ik]])
        p.dma("sync", wrow[:, :], wts8[e:e + 1, :], writes=[cwr])
        for c0 in range(0, TG, 512):
            pwb, cpwb = pool.next()
            mm_group(p, [(pwb[:, 0:512], o1[:, :], wrow[0:1, c0:c0 + 512])],
                     [co1, cwr], cpwb)
            p.copy(web[:, c0:c0 + 512], pwb[:, 0:512], reads=[cpwb], writes=[cweb])
        for ro in range(8):
            for c0 in range(0, TG, 512):
                pd, cpd = pool.next()
                mm_group(p, [(pd[:, 0:512],
                              wtd[:, ik * 1024 + ro * 128: ik * 1024 + ro * 128 + 128],
                              h1f[:, ik * TG + c0: ik * TG + c0 + 512])
                             for ik in range(16)], [], cpd,
                         item_reads=[[cwd[ik], ch1[ik]] for ik in range(16)])
                p.tt(tmp5[:, 0:512], pd[:, 0:512], web[:, c0:c0 + 512],
                     OP.mult, reads=[cpd, cweb], writes=[ct5])
                p.tt(racc[:, ro * TG + c0: ro * TG + c0 + 512],
                     racc[:, ro * TG + c0: ro * TG + c0 + 512], tmp5[:, 0:512],
                     OP.add, reads=[ct5, cra], writes=[cra])
    sc, cs_ = mk_ln_scratch(p, "m2")
    xt = p.sbuf("xtm2", [128, 1024])
    cxt = p.cell(xt)
    g0_t = p.sbuf("g0_t", [128, 1024])
    cg0 = p.cell(g0_t)
    nt = T // 128
    for ti in range(nt):
        p.dma("sync", xt[:, :], xr_tm[ti * 128:(ti + 1) * 128, :], writes=[cxt])
        for ro in range(8):
            pb2, cpb2 = pool.next()
            p.trans(pb2[:, 0:128], racc[:, ro * TG + ti * 128: ro * TG + (ti + 1) * 128],
                    ids[:, :], reads=[cra, cid], writes=[cpb2])
            p.tt(xt[:, ro * 128:(ro + 1) * 128], xt[:, ro * 128:(ro + 1) * 128],
                 pb2[:, 0:128], OP.add, reads=[cpb2, cxt], writes=[cxt])
        _emit_rms(p, g0_t[:, :], xt[:, :], cxt, cg0, wb, cwb, sc, cs_)
        p.dma("sync", g0_o[ti * 128:(ti + 1) * 128, :], g0_t[:, :], reads=[cg0])
    p.drain_q("sync")
    return p

def build_l1_final(p, T=2048):
    x1 = p.inp("x1_f", [T, 1024])
    s1 = p.inp("s1_f", [T, 1024])
    g0 = p.inp("g0_f", [T, 1024])
    out = p.outp("xout_tm", [T, 1024])
    a = p.sbuf("fa", [128, 1024])
    b = p.sbuf("fb", [128, 1024])
    c = p.sbuf("fc", [128, 1024])
    ca, cb, cc = p.cell(a), p.cell(b), p.cell(c)
    for ti in range(T // 128):
        p.dma("sync", a[:, :], x1[ti * 128:(ti + 1) * 128, :], writes=[ca])
        p.dma("sync", b[:, :], s1[ti * 128:(ti + 1) * 128, :], writes=[cb])
        p.dma("sync", c[:, :], g0[ti * 128:(ti + 1) * 128, :], writes=[cc])
        p.act(c[:, :], c[:, :], AF.Silu, reads=[cc], writes=[cc])
        p.tt(c[:, :], c[:, :], b[:, :], OP.mult, reads=[cb, cc], writes=[cc])
        p.tt(c[:, :], c[:, :], a[:, :], OP.add, reads=[ca, cc], writes=[cc])
        p.dma("sync", out[ti * 128:(ti + 1) * 128, :], c[:, :], reads=[cc])
    p.drain_q("sync")
    return p


def build_tp(p, tag, T=2048, D=1024):
    srcd = p.inp(f"tp_src_{tag}", [T, D])
    dstd = p.outp(f"tp_dst_{tag}", [D, T])
    pool = PsumPool(p, 4)
    idt = p.sbuf("idtp", [128, 128])
    cidt = p.cell(idt)
    idf = p.inp(f"tp_id_{tag}", [128, 128])
    p.dma("sync", idt[:, :], idf, writes=[cidt])
    st = p.sbuf("st_tp", [128, 1024])
    ev = p.sbuf("ev_tp", [128, 512])
    cst, cev = p.cell(st), p.cell(ev)
    nt = T // 128
    for ti in range(nt):
        p.dma("sync", st[:, :], srcd[ti * 128:(ti + 1) * 128, :], writes=[cst])
        for ro in range(D // 128):
            pb, cpb = pool.next()
            p.trans(pb[:, 0:128], st[:, ro * 128:(ro + 1) * 128], idt[:, :],
                    reads=[cst, cidt], writes=[cpb])
            p.copy(ev[:, 0:128], pb[:, 0:128], reads=[cpb], writes=[cev])
            p.dma("sync", dstd[ro * 128:(ro + 1) * 128, ti * 128:(ti + 1) * 128],
                  ev[:, 0:128], reads=[cev])
    p.drain_q("sync")
    return p


def build_ln0(p, TOK, D=1024, nt=None):
    nt = nt or TOK // 128
    x = p.inp("x", [TOK, D])
    xn = p.outp("xn", [TOK, D])
    xt = p.sbuf("xt", [128, D])
    ot = p.sbuf("ot", [128, D])
    junk = p.sbuf("junk", [128, D])
    s1 = p.sbuf("s1", [128, 1])
    s2 = p.sbuf("s2", [128, 1])
    m = p.sbuf("m", [128, 1])
    v = p.sbuf("v", [128, 1])
    r = p.sbuf("r", [128, 1])
    cx, co, cj, c1, c2, cm, cv, cr = (p.cell(t) for t in (xt, ot, junk, s1, s2, m, v, r))
    out_toks = []
    for i in range(nt):
        p.dma("sync", xt[:, :], x[i * 128:(i + 1) * 128, :], writes=[cx])
        p.red(s1[:, :], xt[:, :], reads=[cx], writes=[c1])
        p.stt(junk[:, :], xt[:, :], 1.0, xt[:, :], OP.bypass, OP.mult,
              accum_out=s2[:, :], reads=[cx], writes=[cj, c2])
        p.ts(m[:, :], s1[:, :], 1.0 / D, reads=[c1], writes=[cm])
        p.ts(v[:, :], s2[:, :], 1.0 / D, reads=[c2], writes=[cv])
        p.stt(v[:, :], m[:, :], m[:, :], v[:, :], OP.mult, OP.subtract,
              reads=[cm, cv], writes=[cv])
        p.ts(v[:, :], v[:, :], -1.0, 1e-5, OP.mult, OP.add, reads=[cv], writes=[cv])
        p.act(r[:, :], v[:, :], AF.Ln, reads=[cv], writes=[cr])
        p.act(r[:, :], r[:, :], AF.Exp, scale=-0.5, reads=[cr], writes=[cr])
        p.ts(m[:, :], m[:, :], -1.0, reads=[cm], writes=[cm])
        p.ts(ot[:, :], xt[:, :], m[:, :], r[:, :], OP.add, OP.mult,
             reads=[cx, cm, cr], writes=[co])
        out_toks.append(p.dma("sync", xn[i * 128:(i + 1) * 128, :], ot[:, :],
                              reads=[co]))
    p.drain_q("sync")
    return p


"""Fused single-launch megakernel driver."""
from concourse.bass_utils import run_bass_kernel_spmd
from concourse.replica_groups import maybe_share_collective_output_space
import concourse.mybir as mybir

NC, T = 8, 16384
f32 = np.float32
RG8 = [list(range(8))]


def _cc(a):
    return np.ascontiguousarray(a)


def build_fused(p):
    p.init_gbanks()
    d = p.dram

    def idram(name, shape, dtype=F32, space=None):
        return d(name, shape, dtype, addr_space=space)

    def barrier():
        toks = [(s, c) for s, c in p.sem_count.items() if c > 0]
        for eng in ("tensor", "vector", "scalar", "gpsimd", "sync"):
            p.wait_all(eng, toks)

    def coll(kind, in_ap, out_ap, cells_r, cell_w):
        barrier()
        op = mybir.AluOpType.add if kind == "AllReduce" else mybir.AluOpType.bypass
        p.collective(kind, op, [in_ap.opt()], [out_ap.opt()],
                     reads=cells_r, writes=[cell_w])
        barrier()

    sp_ag = maybe_share_collective_output_space("AllGather", RG8)
    sp_a2a = maybe_share_collective_output_space("AllToAll", RG8)

    W_in_d = idram("W_in_d", [1024, 512])
    W_xp_d = idram("W_xp_d", [256, 96])
    W_dt_d = idram("W_dt_d", [64, 256])
    W_out_d = idram("W_out_d", [2048, 1024])
    xn_d = idram("xn_d", [2048, 1024])
    xnT_in = idram("xnT_in", [1024, 2048])
    xnT_ag = idram("xnT_ag", [8, 1024, 2048], space=sp_ag)
    xcz_d = idram("xcz_d", [512, T])
    u_d = idram("u_d", [256, T])
    xpp_in = idram("xpp_in", [96, T])
    xpp_ag = idram("xpp_ag", [768, T], space=sp_ag)
    ssm_a2a_i = idram("ssm_a2a_i", [8, 256, 2048])
    ssm_a2a_o = idram("ssm_a2a_o", [8, 256, 2048], space=sp_a2a)
    hs_in = idram("hs_in", [1, 1])
    hs_ag = idram("hs_ag", [8, 1], space=sp_ag)
    s_fm_d = idram("s_fm_d", [1024, 2048])
    score_d = idram("score_d", [1, T])
    mask_d = idram("mask_d", [1, T])
    sc_a2a_o = idram("sc_a2a_o", [8, 2048], space=sp_a2a)
    rt_a2a_i = idram("rt_a2a_i", [8, 1024, 2048])
    rt_a2a_o = idram("rt_a2a_o", [8, 1024, 2048], space=sp_a2a)
    sh_fm_d = idram("sh_fm_d", [1024, 2048])
    x1_d = idram("x1_d", [2048, 1024])
    xn1_d = idram("xn1_d", [2048, 1024])
    z_d = idram("z_d", [2048, 1024])
    z_a2a_i = idram("z_a2a_i", [8, 128, 2048])
    z_a2a_o = idram("z_a2a_o", [8, 128, 2048], space=sp_a2a)
    yg_a2a_i = idram("yg_a2a_i", [8, 128, 2048])
    yg_a2a_o = idram("yg_a2a_o", [8, 128, 2048], space=sp_a2a)
    h_d = [idram(f"h_d{i}", [2048, 1024]) for i in range(5)]
    q_d = idram("q_tm_d", [2048, 1024])
    k_d2 = idram("k_tm_d", [2048, 1024])
    v_d2 = idram("v_tm_d", [2048, 1024])
    q_a2a_i = idram("q_a2a_i", [8, 128, 2048])
    q_a2a_o = idram("q_a2a_o", [8, 128, 2048], space=sp_a2a)
    k_a2a_i = idram("k_a2a_i", [8, 128, 2048])
    k_a2a_o = idram("k_a2a_o", [8, 128, 2048], space=sp_a2a)
    v_a2a_i = idram("v_a2a_i", [8, 2048, 128])
    v_a2a_o = idram("v_a2a_o", [8, 2048, 128], space=sp_a2a)
    sa_a2a_i = idram("sa_a2a_i", [8, 128, 2048])
    sa_a2a_o = idram("sa_a2a_o", [8, 128, 2048], space=sp_a2a)
    xnT_flat = idram("xnT_flat", [1024, T])
    ssm_flat = idram("ssm_flat", [256, T])
    rt_flat = idram("rt_flat", [1024, T])
    zT_flat = idram("zT_flat", [128, T])
    yg_flat = idram("yg_flat", [128, T])
    qT_flat = idram("qT_flat", [128, T])
    kT_flat = idram("kT_flat", [128, T])
    sa_flat = idram("sa_flat", [128, T])
    xr_d = idram("xr_d", [2048, 1024])
    xrT_d = idram("xrT_d", [1024, 2048])
    wts_d = idram("wts_d", [8, 2048])
    sh1_d = idram("sh1_d", [1024, 2048])
    h1_d = idram("h1_d", [2048, 2048], BF16)
    g0_d = idram("g0_d", [2048, 1024])

    cells = {}

    def dc(t):
        return p.cell(t)

    def pack_shards(dst_staged, src_flat, rows):
        sap = custom_ap(src_flat.ap(), [[2048, 8], [T, rows], [1, 2048]], 0)
        p.dma("sync", dst_staged.ap(), sap, reads=[dc(src_flat)],
              writes=[dc(dst_staged)])

    def unpack_shards(dst_flat, src_staged, rows):
        dap = custom_ap(dst_flat.ap(), [[2048, 8], [T, rows], [1, 2048]], 0)
        p.dma("sync", dap, src_staged.ap(), reads=[dc(src_staged)],
              writes=[dc(dst_flat)])

    p.io = {"W_in": W_in_d.ap(), "W_xp": W_xp_d.ap(), "W_dt": W_dt_d.ap(),
            "W_out": W_out_d.ap()}
    with p.phase():
        build_prep(p, plan={"in": (32, 8, 32, 64), "xp": (4, 8, 64, 12),
                               "dt": (8, 4, 8, 64), "out": (32, 32, 64, 32)})
    barrier()
    p.io = {"xn": xn_d.ap()}
    with p.phase():
        build_ln0(p, 2048)
    barrier()
    p.io = {"tp_src_xn": xn_d.ap(), "tp_dst_xn": xnT_in.ap()}
    with p.phase():
        build_tp(p, "xn")
    coll("AllGather", xnT_in.ap(), xnT_ag.ap(), [dc(xnT_in)], dc(xnT_ag))
    unpack_shards(xnT_flat, xnT_ag, 1024)
    xnT_full = xnT_flat.ap()
    p.io = {"xnT": xnT_full, "W_sel": W_in_d.ap(), "xcz": xcz_d.ap()}
    with p.phase():
        build_l0_inproj(p)
    barrier()
    p.io = {"xcz": xcz_d.ap(), "Wxp_rows": W_xp_d.ap(), "u_out": u_d.ap(),
            "xp_part": xpp_in.ap()}
    with p.phase():
        build_l0_conv(p)
    coll("AllGather", xpp_in.ap(), xpp_ag.ap(), [dc(xpp_in)], dc(xpp_ag))
    p.io = {"u_in": u_d.ap(), "xp_all": xpp_ag.ap(), "Wdt_sel": W_dt_d.ap(),
            "ssm": ssm_flat.ap(), "hsum": hs_in.ap()}
    with p.phase():
        build_l0_scan(p)
    barrier()
    pack_shards(ssm_a2a_i, ssm_flat, 256)
    coll("AllToAll", ssm_a2a_i.ap(), ssm_a2a_o.ap(), [dc(ssm_a2a_i)],
         dc(ssm_a2a_o))
    coll("AllGather", hs_in.ap(), hs_ag.ap(), [dc(hs_in)], dc(hs_ag))
    p.io = {"ssmT_sl": ssm_a2a_o.ap().rearrange("s d t -> (s d) t"),
            "W_out_f": W_out_d.ap(), "hsums": hs_ag.ap(),
            "s_fm": s_fm_d.ap()}
    with p.phase():
        build_l0_outproj(p)
    barrier()
    p.io = {"xnT_g": xnT_full, "score_row": score_d.ap(),
            "mask_row": mask_d.ap()}
    with p.phase():
        build_l0_gate(p)
    coll("AllToAll", score_d.ap().rearrange("o (s t) -> (o s) t", s=8),
         sc_a2a_o.ap(), [dc(score_d)], dc(sc_a2a_o))
    p.io = {"xnT_e": xnT_full, "xnT_my": xnT_in.ap(),
            "score_row_i": score_d.ap(),
            "score_my": custom_ap(sc_a2a_o.ap(), [[2048, 1], [1, 2048]], 0),
            "mask_row_i": mask_d.ap(),
            "routed_m": rt_flat.ap(),
            "shared_fm": sh_fm_d.ap()}
    with p.phase():
        build_l0_expert(p)
    barrier()
    pack_shards(rt_a2a_i, rt_flat, 1024)
    coll("AllToAll", rt_a2a_i.ap(), rt_a2a_o.ap(), [dc(rt_a2a_i)], dc(rt_a2a_o))
    p.io = {"s_fm_i": s_fm_d.ap(), "shared_fm_i": sh_fm_d.ap(),
            "routed_all": rt_a2a_o.ap().rearrange("s r t -> (s r) t"),
            "x1_tm": x1_d.ap(), "xn1_tm": xn1_d.ap()}
    with p.phase():
        build_l0_combine(p)
    barrier()
    identg = p.inp("identg", [128, 128])
    identg_b = p.inp("identg_b", [128, 128], BF16)
    ones1g = p.inp("ones1g", [1, 128])
    swapg = p.inp("swapg", [64, 64])
    hcur = xn1_d
    for IL in range(4):
        p.io = {"h_tm": hcur.ap(), "z_tm": z_d.ap(),
                "ln_w_row": p.inp(f"lnw{IL}", [1, 1024]),
                "ln_b_row": p.inp(f"lnb{IL}", [1, 1024]),
                "ones1_l": ones1g}
        with p.phase():
            build_l1_lnchan(p)
        barrier()
        p.io = {"tp_src_z": z_d.ap(),
                "tp_dst_z": z_a2a_i.ap().rearrange("s h t -> (s h) t"),
                "tp_id_z": identg}
        with p.phase():
            build_tp(p, "z")
        coll("AllToAll", z_a2a_i.ap(), z_a2a_o.ap(), [dc(z_a2a_i)], dc(z_a2a_o))
        unpack_shards(zT_flat, z_a2a_o, 128)
        barrier()
        p.io = {"zT_sl": zT_flat.ap(),
                "yg_sl": yg_flat.ap(),
                "log_dt_sel": p.inp(f"ldt{IL}", [128, 1]),
                "A_re_log_sel": p.inp(f"arl{IL}", [128, 32]),
                "A_im_sel": p.inp(f"aim{IL}", [128, 32]),
                "C_re_sel": p.inp(f"cre{IL}", [128, 32]),
                "C_im_sel": p.inp(f"cim{IL}", [128, 32]),
                "D_sel": p.inp(f"dsk{IL}", [128, 1]),
                "ident_b": identg_b, "ident_f32c": identg, "swap64": swapg}
        with p.phase():
            build_l1_conv(p)
        barrier()
        pack_shards(yg_a2a_i, yg_flat, 128)
        coll("AllToAll", yg_a2a_i.ap(), yg_a2a_o.ap(), [dc(yg_a2a_i)],
             dc(yg_a2a_o))
        p.io = {"ygT_sl": yg_a2a_o.ap().rearrange("s h t -> (s h) t"),
                "h_tm_w": hcur.ap(), "h1_tm": h_d[IL + 1].ap(),
                "WoutT": p.inp(f"WoT{IL}", [1024, 2048]),
                "b_out_i": p.inp(f"bo{IL}", [2048, 1]),
                "ident_fw": identg}
        with p.phase():
            build_l1_wout(p)
        barrier()
        hcur = h_d[IL + 1]
    s1_d = hcur
    xn1T_d = idram("xn1T_d", [1024, 2048])
    p.io = {"tp_src_xn1": xn1_d.ap(), "tp_dst_xn1": xn1T_d.ap(),
            "tp_id_xn1": identg}
    with p.phase():
        build_tp(p, "xn1")
    barrier()
    p.io = {"xn1T_sl": xn1T_d.ap(), "q_tm": q_d.ap(), "k_tm": k_d2.ap(),
            "v_tm": v_d2.ap()}
    with p.phase():
        build_l1_qkv(p)
    barrier()
    p.io = {"tp_src_q": q_d.ap(),
            "tp_dst_q": q_a2a_i.ap().rearrange("s d t -> (s d) t"),
            "tp_id_q": identg}
    with p.phase():
        build_tp(p, "q")
    p.io = {"tp_src_k": k_d2.ap(),
            "tp_dst_k": k_a2a_i.ap().rearrange("s d t -> (s d) t"),
            "tp_id_k": identg}
    with p.phase():
        build_tp(p, "k")
    coll("AllToAll", q_a2a_i.ap(), q_a2a_o.ap(), [dc(q_a2a_i)], dc(q_a2a_o))
    coll("AllToAll", k_a2a_i.ap(), k_a2a_o.ap(), [dc(k_a2a_i)], dc(k_a2a_o))
    p.io = {}
    vsrc = v_d2.ap().rearrange("t (s d) -> s t d", s=8)
    cvs, cvi = dc(v_d2), dc(v_a2a_i)
    p.dma("sync", v_a2a_i.ap(), vsrc, reads=[cvs], writes=[cvi])
    coll("AllToAll", v_a2a_i.ap(), v_a2a_o.ap(), [cvi], dc(v_a2a_o))
    unpack_shards(qT_flat, q_a2a_o, 128)
    unpack_shards(kT_flat, k_a2a_o, 128)
    p.io = {"qT_h": qT_flat.ap(),
            "kT_h": kT_flat.ap(),
            "v_h": v_a2a_o.ap().rearrange("s t d -> (s t) d"),
            "sa_h": sa_flat.ap()}
    with p.phase():
        build_l1_attn(p)
    barrier()
    pack_shards(sa_a2a_i, sa_flat, 128)
    coll("AllToAll", sa_a2a_i.ap(), sa_a2a_o.ap(), [dc(sa_a2a_i)], dc(sa_a2a_o))
    p.io = {"saT_sl": sa_a2a_o.ap().rearrange("s d t -> (s d) t"),
            "xn1_sl": xn1_d.ap(), "xr_tm": xr_d.ap()}
    with p.phase():
        build_l1_attnout(p)
    barrier()
    p.io = {"tp_src_xr": xr_d.ap(), "tp_dst_xr": xrT_d.ap(),
            "tp_id_xr": identg}
    with p.phase():
        build_tp(p, "xr")
    barrier()
    p.io = {"xrT_sl": xrT_d.ap(), "wts_o": wts_d.ap()}
    with p.phase():
        build_l1_gate2(p)
    barrier()
    p.io = {"xrT_sl1": xrT_d.ap(), "shared_o": sh1_d.ap(), "h1_o": h1_d.ap()}
    with p.phase():
        build_l1_moe1(p)
    barrier()
    p.io = {"h1_i": h1_d.ap(), "shared_i": sh1_d.ap(), "wts8": wts_d.ap(),
            "xr_sl": xr_d.ap(), "g0_tm": g0_d.ap()}
    with p.phase():
        build_l1_moe2(p)
    barrier()
    p.io = {"x1_f": x1_d.ap(), "s1_f": s1_d.ap(), "g0_f": g0_d.ap()}
    with p.phase():
        build_l1_final(p)
    p.io = {}
    return p


def host_maps(ins):
    ins = {k: np.asarray(v) for k, v in ins.items()}
    x = np.asarray(ins["L0_attn_x"], f32).reshape(T, 1024)
    ident_f = np.eye(128, dtype=f32)
    ident_bf = np.eye(128, dtype=ml_dtypes.bfloat16)
    ones1 = np.ones((1, 128), f32)
    swap = np.eye(64, dtype=f32)[:, list(range(32, 64)) + list(range(32))]
    sel = np.zeros((32, 4096), f32)
    for m in range(32):
        sel[m, m * 128:(m + 1) * 128] = 1.0
    conv_w = np.asarray(ins["L0_attn_conv_w"], f32).reshape(4, 2048)
    conv_b = np.asarray(ins["L0_attn_conv_b"], f32)
    in_b = np.asarray(ins["L0_attn_in_b"], f32)
    A_log = np.asarray(ins["L0_attn_A_log"], f32)
    D_skip = np.asarray(ins["L0_attn_D_skip"], f32)
    dt_b = np.asarray(ins["L0_attn_dt_b"], f32)
    xp_b = np.asarray(ins["L0_attn_xp_b"], f32)
    out_b = np.asarray(ins["L0_attn_out_b"], f32)
    DH = 64
    inv_freq = 1.0 / 10000.0 ** (np.arange(0, DH, 2, dtype=np.float64) / DH)
    ang = np.arange(8192, dtype=np.float64)[:, None] * inv_freq[None, :]
    sin_t = np.sin(ang).astype(f32)
    cos_t = np.cos(ang).astype(f32)
    Wsi = np.asarray(ins["L1_ffn_W_shared_in"], f32)
    W1s = np.asarray(ins["L1_ffn_W1_shared"], f32)

    def tt_fused(prefix, key, usel=None, isel=None):
        g1 = np.asarray(ins[f"{prefix}_{key}_g1"], f32)
        g2 = np.asarray(ins[f"{prefix}_{key}_g2"], f32)
        m1, n1, m2, n2 = TT_SHAPES[key]
        g1T = np.transpose(g1, (2, 0, 1))
        if usel is not None:
            g1T = g1T[:, :, usel]
        if isel is not None:
            g1T = g1T[:, isel, :]
        g1T = _cc(g1T.reshape(16, -1))
        g2f = _cc(g2.reshape(16, m2 * n2))
        return {f"{key}_g1T": g1T, f"{key}_g2f": g2f}

    maps = []
    for k in range(NC):
        ts = slice(2048 * k, 2048 * (k + 1))
        ds = slice(256 * k, 256 * (k + 1))
        hs = slice(128 * k, 128 * (k + 1))
        usel_in = np.r_[np.arange(4 * k, 4 * k + 4), 32 + np.arange(4 * k, 4 * k + 4)]
        im = {}
        im.update(tt_fused("L0_attn", "in", usel=usel_in))
        im.update(tt_fused("L0_attn", "xp", isel=np.arange(4 * k, 4 * k + 4)))
        im.update(tt_fused("L0_attn", "dt", usel=np.arange(4 * k, 4 * k + 4)))
        im.update(tt_fused("L0_attn", "out"))
        cols = np.r_[np.arange(256 * k, 256 * (k + 1)),
                     2048 + np.arange(256 * k, 256 * (k + 1))]
        sel8 = np.zeros((128, 8), f32)
        sel8[:, k] = 1.0
        lpos = np.arange(2048 * k, 2048 * (k + 1)) % 8192
        im.update(dict(
            x=_cc(x[ts]),
            b_sel=_cc(in_b[cols][:, None]),
            conv_wT_sel=_cc(conv_w[:, ds].T),
            conv_b_sel=_cc(conv_b[ds][:, None]),
            xp_b=xp_b[:, None],
            dt_b_sel=_cc(dt_b[ds][:, None]),
            A_log_sel=_cc(A_log[ds]),
            D_sel=_cc(D_skip[ds][:, None]),
            sel=sel, ident=ident_bf, ones_col=np.ones((128, 1), f32),
            out_b=out_b[:, None], ones8=np.ones((8, 1), f32), ones1=ones1,
            onesc_bf=np.ones((128, 1), ml_dtypes.bfloat16),
            gate_w=np.asarray(ins["L0_ffn_gate_w"], f32), sel8=sel8,
            rt_g=_cc(np.asarray(ins["L0_ffn_rt_gate"], f32)[k]),
            rt_u=_cc(np.asarray(ins["L0_ffn_rt_up"], f32)[k]),
            rt_d=_cc(np.asarray(ins["L0_ffn_rt_down"], f32)[k]),
            sh_g=np.asarray(ins["L0_ffn_sh_gate"], f32),
            sh_u=np.asarray(ins["L0_ffn_sh_up"], f32),
            sh_d=np.asarray(ins["L0_ffn_sh_down"], f32),
            ones1_e=ones1, x_tm=_cc(x[ts]), ident_f=ident_f,
            ones1_l=ones1,
            identg=ident_f, identg_b=ident_bf, ones1g=ones1, swapg=swap,
            Wq_f=np.asarray(ins["L1_ffn_Wq"], f32),
            Wk_c_f=np.asarray(ins["L1_ffn_Wk_c"], f32),
            Wk_f=np.asarray(ins["L1_ffn_Wk"], f32),
            Wv_c_f=np.asarray(ins["L1_ffn_Wv_c"], f32),
            Wv_f=np.asarray(ins["L1_ffn_Wv"], f32),
            sin_my=_cc(sin_t[lpos]), cos_my=_cc(cos_t[lpos]),
            ident_fq=ident_f,
            onesb=np.ones((128, 64), ml_dtypes.bfloat16), onesf_at=ones1,
            Wo_f=np.asarray(ins["L1_ffn_Wo"], f32),
            norm1_row=np.asarray(ins["L1_ffn_norm1_w"], f32)[None, :],
            ones1_ao=ones1, ident_ao=ident_f,
            Wg_f=np.asarray(ins["L1_ffn_Wg_gate"], f32),
            We_f=np.asarray(ins["L1_ffn_We_gate"], f32),
            gb_row=np.asarray(ins["L1_ffn_group_bias"], f32)[None, :],
            eb_row=np.asarray(ins["L1_ffn_expert_bias"], f32)[None, :],
            ones1_g2=ones1, ident_g2=ident_f,
            Wsi_a=_cc(Wsi[:, :2048]), Wsi_b=_cc(Wsi[:, 2048:]),
            Wso_f=np.asarray(ins["L1_ffn_W_shared_out"], f32),
            W1_a=_cc(W1s[:, :2048]), W1_b=_cc(W1s[:, 2048:]),
            W2_all=_cc(np.asarray(ins["L1_ffn_W2_experts"], f32).reshape(16384, 1024)),
            norm2_row=np.asarray(ins["L1_ffn_norm2_w"], f32)[None, :],
            ones1_m2=ones1, ident_m2=ident_f,
            tp_id_xn=ident_f,
        ))
        for IL in range(4):
            im[f"lnw{IL}"] = _cc(np.asarray(ins["L1_attn_ln_w"], f32)[IL][None, :])
            im[f"lnb{IL}"] = _cc(np.asarray(ins["L1_attn_ln_b"], f32)[IL][None, :])
            im[f"ldt{IL}"] = _cc(np.asarray(ins["L1_attn_log_dt"], f32)[IL, hs][:, None])
            im[f"arl{IL}"] = _cc(np.asarray(ins["L1_attn_A_re_log"], f32)[IL, hs])
            im[f"aim{IL}"] = _cc(np.asarray(ins["L1_attn_A_im"], f32)[IL, hs])
            im[f"cre{IL}"] = _cc(np.asarray(ins["L1_attn_C_re"], f32)[IL, hs])
            im[f"cim{IL}"] = _cc(np.asarray(ins["L1_attn_C_im"], f32)[IL, hs])
            im[f"dsk{IL}"] = _cc(np.asarray(ins["L1_attn_D_skip"], f32)[IL, hs][:, None])
            im[f"WoT{IL}"] = _cc(np.asarray(ins["L1_attn_W_out"], f32)[IL].T)
            im[f"bo{IL}"] = _cc(np.asarray(ins["L1_attn_b_out"], f32)[IL][:, None])
        maps.append(im)
    return maps


def kernel(**ins):
    maps = host_maps(ins)
    p = P("fused")
    build_fused(p)
    nc = p.finish()
    names = p.ext_inputs
    maps = [{k: v for k, v in im.items() if k in names} for im in maps]
    res = run_bass_kernel_spmd(nc, maps, core_ids=list(range(NC)))
    out = np.concatenate([np.asarray(r["xout_tm"]) for r in res.results], axis=0)
    return out.reshape(2, 8192, 1024).astype(np.float32)


# version 10
# speedup vs baseline: 1.1434x; 1.0137x over previous
import numpy as np
import ml_dtypes
import concourse.bass as bass
import concourse.mybir as mybir
from concourse.alu_op_type import AluOpType
from contextlib import ExitStack

F32 = mybir.dt.float32
BF16 = mybir.dt.bfloat16
AF = mybir.ActivationFunctionType
AX = mybir.AxisListType
OP = AluOpType

ENGINES = ("tensor", "vector", "scalar", "gpsimd", "sync")


class Cell:
    __slots__ = ("writer", "readers", "name")

    def __init__(self, name=""):
        self.writer = None
        self.readers = []
        self.name = name


class P:
    def __init__(self, name="k", ncores=8):
        self.nc = bass.Bass("TRN2", target_bir_lowering=False)
        self.ncores = ncores
        self.stack = ExitStack()
        self.ops = {e: [] for e in ENGINES}
        self.sems = {}
        self.sem_count = {}
        self.waited = {e: {} for e in ENGINES}
        self.cells = {}
        self._sbuf_bytes = 0
        self._uid = 0
        self.io = {}
        self.gbanks = None
        self.gbank_cells = None
        self._scope = None
        self._keep = []
        names = ["sv", "ss", "st", "sg", "sq", "cc"]
        self.dma_sems = {"sync": [f"ds{i}" for i in range(4)],
                         "scalar": [f"dc{i}" for i in range(2)],
                         "gpsimd": [f"dg{i}" for i in range(4)],
                         "vector": ["dv0"]}
        for q, ls in self.dma_sems.items():
            names.extend(ls)
        for s in names:
            self.sems[s] = self.stack.enter_context(self.nc.semaphore(s))
            self.sem_count[s] = 0
        self.eng_sem = {"tensor": "st", "vector": "sv", "scalar": "ss",
                        "gpsimd": "sg", "sync": "sq"}
        self._dma_rr = {q: 0 for q in self.dma_sems}

    def dram(self, name, shape, dtype=F32, kind=None, addr_space=None):
        kw = {}
        if kind is not None:
            kw["kind"] = kind
        else:
            self._uid += 1
            name = f"{name}_d{self._uid}"
        if addr_space is not None:
            kw["addr_space"] = addr_space
        t = self.nc.dram_tensor(name, list(shape), dtype, **kw)
        self._keep.append(t)
        return t

    def inp(self, name, shape, dtype=F32):
        if name in self.io:
            return self.io[name]
        if not hasattr(self, "ext_inputs"):
            self.ext_inputs = set()
        self.ext_inputs.add(name)
        return self.dram(name, shape, dtype, kind="ExternalInput").ap()

    def outp(self, name, shape, dtype=F32):
        if name in self.io:
            return self.io[name]
        return self.dram(name, shape, dtype, kind="ExternalOutput").ap()

    def sbuf(self, name, shape, dtype=F32):
        self._uid += 1
        arena = self._scope if self._scope is not None else self.stack
        t = arena.enter_context(
            self.nc.sbuf_tensor(f"{name}_u{self._uid}", list(shape), dtype))
        self._keep.append(t)
        return t

    def barrier(self):
        toks = [(s, c) for s, c in self.sem_count.items() if c > 0]
        for eng in ENGINES:
            need = [t for t in toks if self.waited[eng].get(t[0], 0) < t[1]]
            if need:
                self.wait_all(eng, need)

    def phase(self):
        import contextlib

        @contextlib.contextmanager
        def _cm():
            self.barrier()
            prev = self._scope
            self._scope = ExitStack()
            try:
                yield
            finally:
                self.barrier()
                self._scope.close()
                self._scope = prev
        return _cm()

    def psum(self, name, shape, dtype=F32):
        self._uid += 1
        t = self.stack.enter_context(
            self.nc.psum_tensor(f"{name}_u{self._uid}", list(shape), dtype))
        return t

    def init_gbanks(self):
        if self.gbanks is None:
            self.gbanks = [self.psum(f"gb{i}", [128, 512]) for i in range(7)]
            self.gbank_cells = [self.cell(b) for b in self.gbanks]
            self.ptr_bf = self.psum("gptrb", [128, 512], BF16)
            self.ptr_f32 = self.ptr_bf[0:64, 256:512].bitcast(F32)
            c7 = self.cell(self.ptr_bf)
            self.ptr_cell = c7

    def cell(self, obj, key=None):
        k = (id(obj), key)
        c = self.cells.get(k)
        if c is None:
            c = Cell(f"{getattr(obj, 'name', obj)}:{key}")
            self.cells[k] = c
        return c

    def emit(self, eng, build, reads=(), writes=(), sem=None, inc=1,
             tag="", extra_waits=()):
        waits = {}
        for s, v in extra_waits:
            if v > 0 and self.waited[eng].get(s, 0) < v:
                waits[s] = max(waits.get(s, 0), v)

        def add_wait(tok):
            if tok is None:
                return
            s, v = tok
            if self.waited[eng].get(s, 0) >= v:
                return
            if waits.get(s, 0) < v:
                waits[s] = v

        for c in reads:
            add_wait(c.writer)
        for c in writes:
            add_wait(c.writer)
            for t in c.readers:
                add_wait(t)

        sem = sem or self.eng_sem[eng]
        self.sem_count[sem] += inc
        tok = (sem, self.sem_count[sem])
        semh = self.sems[sem]
        wl = sorted(waits.items())
        for s, v in wl:
            if self.waited[eng].get(s, 0) < v:
                self.waited[eng][s] = v

        def fn(e, wl=wl, build=build, semh=semh, inc=inc):
            for s, v in wl:
                e.wait_ge(self.sems[s], v)
            instr = build(e)
            if instr is not None:
                instr.then_inc(semh, inc)

        self.ops[eng].append(fn)
        for c in writes:
            c.writer = tok
            c.readers = []
        for c in reads:
            c.readers.append(tok)
        return tok

    def wait_all(self, eng, toks):
        def fn(e, toks=tuple(toks)):
            for s, v in toks:
                e.wait_ge(self.sems[s], v)
        for s, v in toks:
            if self.waited[eng].get(s, 0) < v:
                self.waited[eng][s] = v
        self.ops[eng].append(fn)

    def barrier_tok(self, cells):
        return [c.writer for c in cells if c.writer is not None]

    def _dma_emit(self, q, build, reads, writes):
        ls = self.dma_sems[q]
        sem = ls[self._dma_rr[q] % len(ls)]
        self._dma_rr[q] += 1
        prev = self.sem_count[sem]
        tok = self.emit(q, build, reads=reads, writes=writes, sem=sem, inc=16,
                        extra_waits=[(sem, prev)])
        return tok

    def drain_q(self, q):
        self.wait_all(q, [(s, self.sem_count[s]) for s in self.dma_sems[q]
                          if self.sem_count[s] > 0])

    def dma(self, q, out_ap, in_ap, reads=(), writes=(), **kw):
        return self._dma_emit(
            q, lambda e: e.dma_start(out=out_ap, in_=in_ap, **kw), reads, writes)

    def dma_tp(self, q, out_ap, in_ap, reads=(), writes=()):
        assert q in ("sync", "scalar")
        return self._dma_emit(
            q, lambda e: e.dma_start(out=out_ap, in_=in_ap, transpose=True),
            reads, writes)

    def tt(self, out, a, b, op, reads=(), writes=(), eng="vector"):
        return self.emit(eng, lambda e: e.tensor_tensor(out, a, b, op),
                         reads=reads, writes=writes)

    def ts(self, out, a, s1, s2=None, op0=OP.mult, op1=None, accum_out=None,
           reads=(), writes=(), eng="vector"):
        if op1 is None:
            op1 = OP.bypass
        return self.emit(
            eng, lambda e: e.tensor_scalar(out, a, s1, s2, op0, op1,
                                           accum_out=accum_out),
            reads=reads, writes=writes)

    def stt(self, out, in0, scalar, in1, op0, op1, accum_out=None,
            reads=(), writes=()):
        return self.emit(
            "vector",
            lambda e: e.scalar_tensor_tensor(out, in0, scalar, in1, op0, op1,
                                             accum_out=accum_out),
            reads=reads, writes=writes)

    def scan(self, out, d0, d1, init, reads=(), writes=()):
        return self.emit(
            "vector",
            lambda e: e.tensor_tensor_scan(out, d0, d1, init, OP.mult, OP.add),
            reads=reads, writes=writes)

    def act(self, out, in_, func, bias=None, scale=None, accum_out=None,
            reads=(), writes=()):
        kw = {}
        if bias is not None:
            kw["bias"] = bias
        if scale is not None:
            kw["scale"] = scale
        if accum_out is not None:
            kw["accum_out"] = accum_out
        return self.emit(
            "scalar", lambda e: e.activation(out, in_, func, **kw),
            reads=reads, writes=writes)

    def red(self, out, in_, op=OP.add, axis=AX.X, negate=None,
            reads=(), writes=(), eng="vector"):
        return self.emit(
            eng, lambda e: e.tensor_reduce(out, in_, axis, op, negate=negate),
            reads=reads, writes=writes)

    def copy(self, out, in_, reads=(), writes=(), eng="vector"):
        return self.emit(eng, lambda e: e.tensor_copy(out, in_),
                         reads=reads, writes=writes)

    def memset(self, ap, val, reads=(), writes=(), eng="vector"):
        return self.emit(eng, lambda e: e.memset(ap, val),
                         reads=reads, writes=writes)

    def mm(self, out, lhsT, rhs, start=True, stop=True, reads=(), writes=()):
        return self.emit(
            "tensor",
            lambda e: e.matmul(out, lhsT, rhs, start=start, stop=stop,
                               skip_group_check=True),
            reads=reads, writes=writes)

    def trans(self, out, in_, ident, reads=(), writes=()):
        return self.emit(
            "tensor", lambda e: e.transpose(out, in_, ident),
            reads=reads, writes=writes)

    def recip(self, out, in_, reads=(), writes=()):
        return self.emit("vector", lambda e: e.reciprocal(out, in_),
                         reads=reads, writes=writes)

    def collective(self, kind, op, ins, outs, reads=(), writes=()):
        def build(e):
            if kind == "AllToAll":
                from concourse.aot_env import allow_ici_all_to_all
                with allow_ici_all_to_all():
                    return e.collective_compute(
                        kind, op, replica_groups=[list(range(self.ncores))],
                        ins=ins, outs=outs)
            return e.collective_compute(
                kind, op, replica_groups=[list(range(self.ncores))],
                ins=ins, outs=outs)
        return self.emit("gpsimd", build, reads=reads, writes=writes,
                         sem="cc", inc=1)

    def finish(self):
        with self.nc.Block() as block:
            for eng in ENGINES:
                if not self.ops[eng]:
                    continue
                dec = getattr(block, eng)

                def make(eng):
                    def body(e):
                        for fn in self.ops[eng]:
                            fn(e)
                    return body
                dec(make(eng))
        self.stack.close()
        return self.nc


def ceil_div(a, b):
    return (a + b - 1) // b


def mm_group(p, items, reads, write_cell, item_reads=None):
    n = len(items)
    assert n >= 1
    eng = "tensor"

    def mk_waits(cells, extra=()):
        waits = {}

        def add_wait(tok):
            if tok is None:
                return
            s, v = tok
            if p.waited[eng].get(s, 0) >= v:
                return
            if waits.get(s, 0) < v:
                waits[s] = v

        for c in cells:
            add_wait(c.writer)
        for t in extra:
            add_wait(t)
        wl = sorted(waits.items())
        for s, v in wl:
            if p.waited[eng].get(s, 0) < v:
                p.waited[eng][s] = v
        return wl

    extra = [write_cell.writer] + list(write_cell.readers)
    first = list(reads) + (list(item_reads[0]) if item_reads else [])
    wl0 = mk_waits(first, [t for t in extra if t is not None])
    iwls = [()] * n
    if item_reads is not None:
        assert len(item_reads) == n
        for i in range(1, n):
            iwls[i] = mk_waits(item_reads[i])
    sem = p.eng_sem[eng]
    p.sem_count[sem] += 1
    tok = (sem, p.sem_count[sem])
    semh = p.sems[sem]

    def fn(e, wl0=wl0, iwls=tuple(iwls), items=tuple(items), semh=semh, n=n):
        for s, v in wl0:
            e.wait_ge(p.sems[s], v)
        for i, (out_ap, lhsT, rhs) in enumerate(items):
            for s, v in iwls[i]:
                e.wait_ge(p.sems[s], v)
            instr = e.matmul(out_ap, lhsT, rhs, start=(i == 0), stop=(i == n - 1))
            if i == n - 1:
                instr.then_inc(semh, 1)

    p.ops[eng].append(fn)
    write_cell.writer = tok
    write_cell.readers = []
    for c in reads:
        c.readers.append(tok)
    if item_reads is not None:
        for cl in item_reads:
            for c in cl:
                c.readers.append(tok)
    return tok


class PsumPool:

    def __init__(self, p, n=8, name="pb", off=0):
        self.p = p
        if p.gbanks is not None:
            n = min(n, len(p.gbanks) - off)
            self.banks = p.gbanks[off:off + n]
            self.cells = p.gbank_cells[off:off + n]
        else:
            self.banks = [p.psum(f"{name}{i}", [128, 512]) for i in range(n)]
            self.cells = [p.cell(b) for b in self.banks]
        self.i = 0

    def next(self):
        j = self.i % len(self.banks)
        self.i += 1
        return self.banks[j], self.cells[j]


def load_w_bf16(p, name, w_dram_ap, K, RO, q="gpsimd", dtype=BF16):
    nk = ceil_div(K, 128)
    sb = p.sbuf(name, [128, nk * RO], dtype)
    cl = [p.cell(sb, f"t{ik}") for ik in range(nk)]
    for ik in range(nk):
        kk = min(128, K - ik * 128)
        p.dma(q, sb[0:kk, ik * RO:(ik + 1) * RO],
              w_dram_ap[ik * 128:ik * 128 + kk, :], writes=[cl[ik]])
    return sb, cl, nk


def gemm(p, pool, wt, cwt, nk, RO, K, xf, cxf, T, TG, evac,
         ro_list=None, x_col=None, kk_last=128):
    ro_list = ro_list if ro_list is not None else range(ceil_div(RO, 128))
    wl = cwt if isinstance(cwt, list) else [cwt] * nk
    xl = cxf if isinstance(cxf, list) else [cxf] * nk
    for ro in ro_list:
        rw = min(128, RO - ro * 128)
        for c0 in range(0, TG, 512):
            cw = min(512, TG - c0)
            pb, cb = pool.next()
            items = []
            for ik in range(nk):
                kk = kk_last if ik == nk - 1 else 128
                lhsT = wt[0:kk, ik * RO + ro * 128: ik * RO + ro * 128 + rw]
                rhs = (x_col(ik, c0, cw) if x_col is not None
                       else xf[0:kk, ik * TG + c0: ik * TG + c0 + cw])
                items.append((pb[0:rw, 0:cw], lhsT, rhs))
            mm_group(p, items, [], cb,
                     item_reads=[[wl[ik], xl[ik]] for ik in range(nk)])
            evac(ro, c0, cw, pb, cb)


"""Phase builders for the mk_recurrent_d2 kernel (stage A: one launch each)."""


TT_SHAPES = {
    "in": (32, 64, 32, 64),
    "xp": (32, 8, 64, 12),
    "dt": (8, 32, 8, 64),
    "out": (32, 32, 64, 32),
}
TT_CHUNK = {"in": 512, "xp": 384, "dt": 512, "out": 512}


def host_tt_inputs(ins, prefix, key):
    g1 = np.asarray(ins[f"{prefix}_{key}_g1"], np.float32)
    g2 = np.asarray(ins[f"{prefix}_{key}_g2"], np.float32)
    m1, n1, m2, n2 = TT_SHAPES[key]
    g1T = np.ascontiguousarray(np.transpose(g1, (2, 0, 1)).reshape(16, m1 * n1))
    g2f = np.ascontiguousarray(g2.reshape(16, m2 * n2))
    return {f"{key}_g1T": g1T, f"{key}_g2f": g2f}


def build_prep(p, plan=None):
    pool = PsumPool(p, 4)
    ev = p.sbuf("ev", [128, 512])
    cev = p.cell(ev)
    out_toks = []
    for key in ("in", "xp", "dt", "out"):
        m1, n1, m2, n2 = (plan or TT_SHAPES)[key]
        g1T = p.inp(f"{key}_g1T", [16, m1 * n1])
        g2f = p.inp(f"{key}_g2f", [16, m2 * n2])
        W = p.outp(f"W_{key}", [m1 * m2, n1 * n2])
        W4 = W.rearrange("(i j) (u v) -> i u j v", i=m1, u=n1)
        g1s = p.sbuf(f"{key}_g1s", [16, m1 * n1])
        g2s = p.sbuf(f"{key}_g2s", [16, m2 * n2])
        c1, c2 = p.cell(g1s), p.cell(g2s)
        p.dma("sync", g1s[:, :], g1T, writes=[c1])
        p.dma("sync", g2s[:, :], g2f, writes=[c2])
        AR = m1 * n1
        CW = TT_CHUNK[key]
        for rt in range(ceil_div(AR, 128)):
            rw = min(128, AR - rt * 128)
            for c0 in range(0, m2 * n2, CW):
                cw = min(CW, m2 * n2 - c0)
                pb, cb = pool.next()
                mm_group(p, [(pb[0:rw, 0:cw],
                              g1s[:, rt * 128: rt * 128 + rw],
                              g2s[:, c0:c0 + cw])], [c1, c2], cb)
                p.copy(ev[0:rw, 0:cw], pb[0:rw, 0:cw], reads=[cb], writes=[cev])
                ni = rw // n1 if n1 <= 128 else 1
                j0 = c0 // n2
                nj = cw // n2
                for il in range(ni):
                    i_abs = (rt * 128) // n1 + il
                    tok = p.dma("sync",
                                W4[i_abs, :, j0:j0 + nj, :],
                                ev[il * n1:(il + 1) * n1, 0:cw],
                                reads=[cev])
                    out_toks.append(tok)
    p.drain_q("sync")
    return p


def build_l0_inproj(p, T=16384, TG=2048):
    xnT = p.inp("xnT", [1024, T])
    W = p.inp("W_sel", [1024, 512])
    b = p.inp("b_sel", [512, 1])
    xcz = p.outp("xcz", [512, T])
    pool = PsumPool(p, 8)
    wt, cwt, nk = load_w_bf16(p, "wt", W, 1024, 512)
    bsl = p.sbuf("bsl", [128, 4])
    cb = p.cell(bsl)
    for ro in range(4):
        p.dma("sync", bsl[:, ro:ro + 1], b[ro * 128:(ro + 1) * 128, :], writes=[cb])
    xf = p.sbuf("xf", [128, nk * TG], BF16)
    cxf = [p.cell(xf, f"t{i}") for i in range(nk)]
    ev = [p.sbuf(f"ev{i}", [128, 512]) for i in range(2)]
    cev = [p.cell(t) for t in ev]
    for t0 in range(0, T, TG):
        for ik in range(nk):
            p.dma("gpsimd", xf[:, ik * TG:(ik + 1) * TG],
                  xnT[ik * 128:(ik + 1) * 128, t0:t0 + TG], writes=[cxf[ik]])

        def evac(ro, c0, cw, pb, cbk, t0=t0):
            i = pool.i % 2
            p.ts(ev[i][:, 0:cw], pb[:, 0:cw], bsl[:, ro:ro + 1], None, OP.add,
                 reads=[cbk, cb], writes=[cev[i]])
            p.dma("sync", xcz[ro * 128:(ro + 1) * 128, t0 + c0:t0 + c0 + cw],
                  ev[i][:, 0:cw], reads=[cev[i]])
        gemm(p, pool, wt, cwt, nk, 512, 1024, xf, cxf, TG, TG, evac)
    p.drain_q("sync")
    return p


def build_l0_conv(p, T=16384, TG=2048, LSEQ=8192):
    xcz = p.inp("xcz", [512, T])
    cwT = p.inp("conv_wT_sel", [256, 4])
    cbv = p.inp("conv_b_sel", [256, 1])
    Wxp = p.inp("Wxp_rows", [256, 96])
    u_out = p.outp("u_out", [256, T])
    xp_o = p.outp("xp_part", [96, T])
    pool = PsumPool(p, 8)
    wxp, cwxp, nkxp = load_w_bf16(p, "wxp", Wxp, 256, 96)
    wcv = p.sbuf("wcv", [128, 8])
    cb_s = p.sbuf("cbs", [128, 2])
    cwc, ccb = p.cell(wcv), p.cell(cb_s)
    for di in range(2):
        p.dma("sync", wcv[:, di * 4:(di + 1) * 4], cwT[di * 128:(di + 1) * 128, :], writes=[cwc])
        p.dma("sync", cb_s[:, di:di + 1], cbv[di * 128:(di + 1) * 128, :], writes=[ccb])
    xct = p.sbuf("xct", [128, TG + 3])
    zt = p.sbuf("zt", [128, TG])
    acc = p.sbuf("acc", [128, TG])
    sg = p.sbuf("sg", [128, TG])
    ut = p.sbuf("ut", [128, TG])
    ubf = p.sbuf("ubf", [128, 2 * TG], BF16)
    ev = p.sbuf("evp", [96, 512])
    cxc, czt, cac, csg, cut, cub, cev = (p.cell(t) for t in (xct, zt, acc, sg, ut, ubf, ev))
    for t0 in range(0, T, TG):
        left = (t0 % LSEQ) == 0
        right = ((t0 + TG) % LSEQ) == 0
        seg0 = (t0 // LSEQ) * LSEQ
        seg1 = seg0 + LSEQ
        vlo = max(t0 - 1, seg0)
        vhi = min(t0 + TG + 2, seg1)
        off = vlo - (t0 - 1)
        for di in range(2):
            if left:
                p.memset(xct[:, 0:1], 0.0, writes=[cxc])
            if right:
                p.memset(xct[:, TG + 1:TG + 3], 0.0, writes=[cxc])
            p.dma("sync", xct[:, off:off + (vhi - vlo)],
                  xcz[di * 128:(di + 1) * 128, vlo:vhi], writes=[cxc])
            p.dma("sync", zt[:, :], xcz[256 + di * 128: 256 + (di + 1) * 128, t0:t0 + TG],
                  writes=[czt])
            p.ts(acc[:, :], xct[:, 0:TG], wcv[:, di * 4:di * 4 + 1], None, OP.mult,
                 reads=[cxc, cwc], writes=[cac])
            for k in range(1, 4):
                p.stt(acc[:, :], xct[:, k:k + TG], wcv[:, di * 4 + k:di * 4 + k + 1],
                      acc[:, :], OP.mult, OP.add, reads=[cxc, cwc, cac], writes=[cac])
            p.act(sg[:, :], acc[:, :], AF.Silu, bias=cb_s[:, di:di + 1],
                  reads=[cac, ccb], writes=[csg])
            p.tt(ut[:, :], sg[:, :], zt[:, :], OP.mult, reads=[csg, czt], writes=[cut])
            p.dma("sync", u_out[di * 128:(di + 1) * 128, t0:t0 + TG], ut[:, :],
                  reads=[cut])
            p.copy(ubf[:, di * TG:(di + 1) * TG], ut[:, :], reads=[cut], writes=[cub])

        def evac(ro, c0, cw, pb, cbk, t0=t0):
            p.copy(ev[0:96, 0:cw], pb[0:96, 0:cw], reads=[cbk], writes=[cev])
            p.dma("sync", xp_o[:, t0 + c0:t0 + c0 + cw], ev[0:96, 0:cw], reads=[cev])
        gemm(p, pool, wxp, cwxp, nkxp, 96, 256, ubf, cub, TG, TG, evac)
    p.drain_q("sync")
    return p


def build_l0_scan(p, T=16384, TB=2048, LSEQ=8192):
    u_in = p.inp("u_in", [256, T])
    xp_all = p.inp("xp_all", [768, T])
    xp_b = p.inp("xp_b", [96, 1])
    Wdt = p.inp("Wdt_sel", [64, 256])
    dt_b = p.inp("dt_b_sel", [256, 1])
    A_log = p.inp("A_log_sel", [256, 16])
    D_sel = p.inp("D_sel", [256, 1])
    sel = p.inp("sel", [32, 4096])
    ident = p.inp("ident", [128, 128], BF16)
    ones_col = p.inp("ones_col", [128, 1])
    ssm_o = p.outp("ssm", [256, T])
    hsum_o = p.outp("hsum", [1, 1])

    if p.gbanks is not None:
        ybanks = p.gbanks[0:4]
        ycells = p.gbank_cells[0:4]
        pool = PsumPool(p, 3, name="pm", off=4)
    else:
        ybanks = [p.psum(f"yb{i}", [128, 512]) for i in range(4)]
        ycells = [p.cell(b) for b in ybanks]
        pool = PsumPool(p, 4, name="pm")

    dtbc = p.sbuf("dtbc", [96, TB])
    bc = p.sbuf("bc", [32, TB])
    sel_s = p.sbuf("sel_s", [32, 4096])
    id_s = p.sbuf("id_s", [128, 128], BF16)
    one_s = p.sbuf("one_s", [128, 1])
    wdt_s = p.sbuf("wdt_s", [64, 256])
    dtb_s = p.sbuf("dtb_s", [128, 2])
    D_s = p.sbuf("D_s", [128, 2])
    Alog_s = p.sbuf("Alog_s", [128, 32])
    negA = p.sbuf("negA", [128, 32])
    tmp96 = p.sbuf("tmp96", [96, TB])
    dt2 = p.sbuf("dt2", [128, 2 * TB])
    dtu2 = p.sbuf("dtu2", [128, 2 * TB])
    ublk = p.sbuf("ublk", [128, 2 * TB])
    dA2_ = [p.sbuf(f"dA{i}", [128, TB // 2]) for i in range(4)]
    wts2_ = [p.sbuf(f"wts{i}", [128, TB // 2]) for i in range(4)]
    h2_ = [p.sbuf(f"h_s{i}", [128, TB // 2]) for i in range(4)]
    hC2_ = [p.sbuf(f"hC{i}", [128, TB // 2], BF16) for i in range(4)]
    evb = p.sbuf("evb", [128, TB])
    sp1 = p.sbuf("sp1", [128, 512])
    sp2 = p.sbuf("sp2", [128, 512])
    sp3 = p.sbuf("sp3", [128, 512])
    carry = p.sbuf("carry", [128, 32])
    hacc = p.sbuf("hacc", [128, 1])
    hrow = p.sbuf("hrow", [128, 1])
    hfin = p.sbuf("hfin", [1, 1])

    (cdtbc, cbc, csel, cid, cone, cwdt, cdtb, cD, cAl, cnA, ctmp, cdt2, cdtu,
     cu, cev, c1, c2, c3, ccar, chac, chrw, chfin) = (
        p.cell(t) for t in (dtbc, bc, sel_s, id_s, one_s, wdt_s, dtb_s, D_s,
                            Alog_s, negA, tmp96, dt2, dtu2, ublk,
                            evb, sp1, sp2, sp3, carry, hacc, hrow, hfin))
    cdA2 = [p.cell(t) for t in dA2_]
    cw2 = [p.cell(t) for t in wts2_]
    ch2 = [p.cell(t) for t in h2_]
    chC2 = [p.cell(t) for t in hC2_]

    p.dma("sync", sel_s[:, :], sel, writes=[csel])
    p.dma("sync", id_s[:, :], ident, writes=[cid])
    p.dma("sync", one_s[:, :], ones_col, writes=[cone])
    p.dma("sync", wdt_s[:, :], Wdt, writes=[cwdt])
    for di in range(2):
        p.dma("sync", dtb_s[:, di:di + 1], dt_b[di * 128:(di + 1) * 128, :], writes=[cdtb])
        p.dma("sync", D_s[:, di:di + 1], D_sel[di * 128:(di + 1) * 128, :], writes=[cD])
        p.dma("sync", Alog_s[:, di * 16:(di + 1) * 16], A_log[di * 128:(di + 1) * 128, :],
              writes=[cAl])
    p.act(negA[:, :], Alog_s[:, :], AF.Exp, reads=[cAl], writes=[cnA])
    p.ts(negA[:, :], negA[:, :], -1.0, reads=[cnA], writes=[cnA])
    p.memset(hacc[:, :], 0.0, writes=[chac])

    xpb_s = p.sbuf("xpb_s", [96, 1])
    cxpb = p.cell(xpb_s)
    p.dma("sync", xpb_s[:, :], xp_b, writes=[cxpb])

    nblk = T // TB
    for bi in range(nblk):
        t0 = bi * TB
        if t0 % LSEQ == 0:
            p.memset(carry[:, :], 0.0, writes=[ccar])
        for di in range(2):
            p.dma("sync", ublk[:, di * TB:(di + 1) * TB],
                  u_in[di * 128:(di + 1) * 128, t0:t0 + TB], writes=[cu])
        p.dma("sync", dtbc[:, :], xp_all[0:96, t0:t0 + TB], writes=[cdtbc])
        for k in range(1, 8):
            p.dma("sync", tmp96[:, :], xp_all[96 * k:96 * (k + 1), t0:t0 + TB],
                  writes=[ctmp])
            p.tt(dtbc[:, :], dtbc[:, :], tmp96[:, :], OP.add,
                 reads=[ctmp, cdtbc], writes=[cdtbc])
        p.ts(dtbc[:, :], dtbc[:, :], xpb_s[:, :], None, OP.add,
             reads=[cxpb, cdtbc], writes=[cdtbc])
        p.copy(bc[:, :], dtbc[64:96, :], reads=[cdtbc], writes=[cbc])
        for di in range(2):
            for c0 in range(0, TB, 512):
                pb, cpb = pool.next()
                mm_group(p, [(pb[:, 0:512], wdt_s[:, di * 128:(di + 1) * 128],
                              dtbc[0:64, c0:c0 + 512])], [cwdt, cdtbc], cpb)
                p.ts(sp1[:, :], pb[:, 0:512], dtb_s[:, di:di + 1], None, OP.add,
                     reads=[cpb, cdtb], writes=[c1])
                p.act(sp2[:, :], sp1[:, :], AF.Abs, reads=[c1], writes=[c2])
                p.act(sp2[:, :], sp2[:, :], AF.Exp, scale=-1.0, reads=[c2], writes=[c2])
                p.act(sp2[:, :], sp2[:, :], AF.Ln, bias=1.0, reads=[c2], writes=[c2])
                p.act(sp3[:, :], sp1[:, :], AF.Relu, reads=[c1], writes=[c3])
                p.tt(dt2[:, di * TB + c0: di * TB + c0 + 512], sp3[:, :], sp2[:, :],
                     OP.add, reads=[c2, c3], writes=[cdt2])
            p.tt(dtu2[:, di * TB:(di + 1) * TB], dt2[:, di * TB:(di + 1) * TB],
                 ublk[:, di * TB:(di + 1) * TB], OP.mult, reads=[cdt2, cu], writes=[cdtu])
        for half in range(2):
            o0 = half * 1024
            for n in range(16):
                par = n % 2
                pbB = []
                for c0 in (0, 512):
                    pb, cpb = pool.next()
                    mm_group(p, [(pb[:, 0:512], sel_s[:, n * 128:(n + 1) * 128],
                                  bc[0:32, o0 + c0:o0 + c0 + 512])], [csel, cbc],
                             cpb)
                    pbB.append((pb, cpb))
                for di in range(2):
                    bi4 = di * 2 + par
                    dA, cdA = dA2_[bi4], cdA2[bi4]
                    wts, cw = wts2_[bi4], cw2[bi4]
                    p.act(dA[:, :], dt2[:, di * TB + o0: di * TB + o0 + 1024],
                          AF.Exp, scale=negA[:, di * 16 + n:di * 16 + n + 1],
                          reads=[cdt2, cnA], writes=[cdA])
                    for j, c0 in enumerate((0, 512)):
                        pb, cpb = pbB[j]
                        p.tt(wts[:, c0:c0 + 512],
                             dtu2[:, di * TB + o0 + c0: di * TB + o0 + c0 + 512],
                             pb[:, 0:512], OP.mult, reads=[cdtu, cpb], writes=[cw])
                for di in range(2):
                    bi4 = di * 2 + par
                    p.scan(h2_[bi4][:, :], dA2_[bi4][:, :], wts2_[bi4][:, :],
                           carry[:, di * 16 + n:di * 16 + n + 1],
                           reads=[cdA2[bi4], cw2[bi4], ccar], writes=[ch2[bi4]])
                    p.copy(carry[:, di * 16 + n:di * 16 + n + 1],
                           h2_[bi4][:, 1023:1024], reads=[ch2[bi4]], writes=[ccar])
                    p.red(hrow[:, :], h2_[bi4][:, :], reads=[ch2[bi4]],
                          writes=[chrw])
                    p.tt(hacc[:, :], hacc[:, :], hrow[:, :], OP.add,
                         reads=[chrw, chac], writes=[chac])
                pbC = []
                for c0 in (0, 512):
                    pb, cpb = pool.next()
                    mm_group(p, [(pb[:, 0:512],
                                  sel_s[:, (16 + n) * 128:(17 + n) * 128],
                                  bc[0:32, o0 + c0:o0 + c0 + 512])], [csel, cbc],
                             cpb)
                    pbC.append((pb, cpb))
                for di in range(2):
                    bi4 = di * 2 + par
                    hC, chC = hC2_[bi4], chC2[bi4]
                    for j, c0 in enumerate((0, 512)):
                        pb, cpb = pbC[j]
                        p.tt(hC[:, c0:c0 + 512], h2_[bi4][:, c0:c0 + 512],
                             pb[:, 0:512], OP.mult, reads=[ch2[bi4], cpb],
                             writes=[chC])
                        yi = di * 2 + j
                        p.mm(ybanks[yi][:, :], id_s[:, :], hC[:, c0:c0 + 512],
                             start=(n == 0), stop=(n == 15),
                             reads=[cid, chC], writes=[ycells[yi]])
            for di in range(2):
                for j in range(2):
                    yi = di * 2 + j
                    p.stt(evb[:, di * 1024 + j * 512: di * 1024 + (j + 1) * 512],
                          ublk[:, di * TB + o0 + j * 512:
                               di * TB + o0 + (j + 1) * 512],
                          D_s[:, di:di + 1], ybanks[yi][:, :], OP.mult, OP.add,
                          reads=[cu, cD, ycells[yi]], writes=[cev])
                p.dma("sync", ssm_o[di * 128:(di + 1) * 128, t0 + o0:t0 + o0 + 1024],
                      evb[:, di * 1024: (di + 1) * 1024], reads=[cev])
    pb, cpb = pool.next()
    mm_group(p, [(pb[0:1, 0:1], hacc[:, :], one_s[:, :])], [chac, cone], cpb)
    p.copy(hfin[:, :], pb[0:1, 0:1], reads=[cpb], writes=[chfin])
    p.dma("sync", hsum_o, hfin[:, :], reads=[chfin])
    p.drain_q("sync")
    return p


def build_l0_outproj(p, T=2048):
    ssmT = p.inp("ssmT_sl", [2048, T])
    W = p.inp("W_out_f", [2048, 1024])
    ob = p.inp("out_b", [1024, 1])
    hsums = p.inp("hsums", [8, 1])
    ones8 = p.inp("ones8", [8, 1])
    ones1 = p.inp("ones1", [1, 128])
    onesc_bf = p.inp("onesc_bf", [128, 1], BF16)
    s_o = p.outp("s_fm", [1024, T])
    pool = PsumPool(p, 8)
    wt, cwt, nk = load_w_bf16(p, "wto", W, 2048, 1024)
    xf = p.sbuf("xfo", [128, 16 * T], BF16)
    cxf = [p.cell(xf, f"t{i}") for i in range(nk)]
    for ik in range(nk):
        p.dma("gpsimd", xf[:, ik * T:(ik + 1) * T], ssmT[ik * 128:(ik + 1) * 128, :],
              writes=[cxf[ik]])
    hs_s = p.sbuf("hs_s", [8, 1])
    o8_s = p.sbuf("o8_s", [8, 1])
    o1_s = p.sbuf("o1_s", [1, 128])
    ocb_s = p.sbuf("ocb_s", [128, 1], BF16)
    cscal = p.sbuf("cscal", [1, 1])
    cbc = p.sbuf("cbc128", [128, 1])
    wc = p.sbuf("wc", [128, 8])
    bias2 = p.sbuf("bias2", [128, 8])
    obs = p.sbuf("obs", [128, 8])
    (chs, co8, co1, cocb, ccs, ccb, cwc2, cb2, cob) = (
        p.cell(t) for t in (hs_s, o8_s, o1_s, ocb_s, cscal, cbc, wc, bias2, obs))
    p.dma("sync", hs_s[:, :], hsums, writes=[chs])
    p.dma("sync", o8_s[:, :], ones8, writes=[co8])
    p.dma("sync", o1_s[:, :], ones1, writes=[co1])
    p.dma("sync", ocb_s[:, :], onesc_bf, writes=[cocb])
    for ro in range(8):
        p.dma("sync", obs[:, ro:ro + 1], ob[ro * 128:(ro + 1) * 128, :], writes=[cob])
    pb, cpb = pool.next()
    mm_group(p, [(pb[0:1, 0:1], hs_s[:, :], o8_s[:, :])], [chs, co8], cpb)
    p.ts(cscal[:, :], pb[0:1, 0:1], 1e-8 / (16384.0 * 2048.0 * 16.0), None, OP.mult,
         reads=[cpb], writes=[ccs])
    pb2, cpb2 = pool.next()
    mm_group(p, [(pb2[:, 0:1], o1_s[:, :], cscal[:, :])], [co1, ccs], cpb2)
    p.copy(cbc[:, :], pb2[:, 0:1], reads=[cpb2], writes=[ccb])
    for ro in range(8):
        pb3, cpb3 = pool.next()
        items = [(pb3[:, 0:1], wt[:, ik * 1024 + ro * 128: ik * 1024 + ro * 128 + 128],
                  ocb_s[:, :]) for ik in range(nk)]
        mm_group(p, items, list(cwt) + [cocb], cpb3)
        p.copy(wc[:, ro:ro + 1], pb3[:, 0:1], reads=[cpb3], writes=[cwc2])
    p.ts(bias2[:, :], wc[:, :], cbc[:, 0:1], None, OP.mult, reads=[cwc2, ccb],
         writes=[cb2])
    p.tt(bias2[:, :], bias2[:, :], obs[:, :], OP.add, reads=[cb2, cob], writes=[cb2])
    ev = p.sbuf("evo", [128, 512])
    cev = p.cell(ev)

    def evac(ro, c0, cw, pb, cbk):
        p.ts(ev[:, 0:cw], pb[:, 0:cw], bias2[:, ro:ro + 1], None, OP.add,
             reads=[cbk, cb2], writes=[cev])
        p.dma("sync", s_o[ro * 128:(ro + 1) * 128, c0:c0 + cw], ev[:, 0:cw],
              reads=[cev])
    gemm(p, pool, wt, cwt, nk, 1024, 2048, xf, cxf, T, T, evac)
    p.drain_q("sync")
    return p


def build_l0_gate(p, T=16384, TG=2048):
    xnT = p.inp("xnT_g", [1024, T])
    gw = p.inp("gate_w", [1024, 8])
    sel8 = p.inp("sel8", [128, 8])
    score_o = p.outp("score_row", [1, T])
    mask_o = p.outp("mask_row", [1, T])
    pool = PsumPool(p, 8)
    gw_s = p.sbuf("gw_s", [128, 8 * 8])
    sel_s = p.sbuf("sel8_s", [128, 8])
    cgw, csel = p.cell(gw_s), p.cell(sel_s)
    for ik in range(8):
        p.dma("sync", gw_s[:, ik * 8:(ik + 1) * 8], gw[ik * 128:(ik + 1) * 128, :],
              writes=[cgw])
    p.dma("sync", sel_s[:, :], sel8, writes=[csel])
    xg = p.sbuf("xg", [128, 8 * TG])
    cxg = [p.cell(xg, f"t{i}") for i in range(8)]
    mx_ = [p.sbuf(f"mx{i}", [128, 1]) for i in range(2)]
    sc_ = [p.sbuf(f"sc{i}", [128, 1]) for i in range(2)]
    m8_ = [p.sbuf(f"m8{i}", [128, 8]) for i in range(2)]
    me_ = [p.sbuf(f"me{i}", [128, 1]) for i in range(2)]
    cmx_ = [p.cell(t) for t in mx_]
    csc_ = [p.cell(t) for t in sc_]
    cm8_ = [p.cell(t) for t in m8_]
    cme_ = [p.cell(t) for t in me_]
    for t0 in range(0, T, TG):
        for ik in range(8):
            p.dma("sync", xg[:, ik * TG:(ik + 1) * TG],
                  xnT[ik * 128:(ik + 1) * 128, t0:t0 + TG], writes=[cxg[ik]])
        for tb in range(TG // 128):
            pr = tb % 2
            mx, cmx = mx_[pr], cmx_[pr]
            sc, csc = sc_[pr], csc_[pr]
            m8, cm8 = m8_[pr], cm8_[pr]
            me, cme = me_[pr], cme_[pr]
            pb, cpb = pool.next()
            items = [(pb[0:128, 0:8],
                      xg[:, ik * TG + tb * 128: ik * TG + (tb + 1) * 128],
                      gw_s[:, ik * 8:(ik + 1) * 8]) for ik in range(8)]
            mm_group(p, items, [cgw], cpb,
                     item_reads=[[cxg[ik]] for ik in range(8)])
            p.red(mx[:, :], pb[:, 0:8], op=OP.max, reads=[cpb], writes=[cmx])
            p.act(sc[:, :], mx[:, :], AF.Sigmoid, reads=[cmx], writes=[csc])
            p.tt(m8[:, :], pb[:, 0:8], mx[:, 0:1].broadcast_to([128, 8]),
                 OP.is_equal, reads=[cpb, cmx], writes=[cm8])
            p.tt(m8[:, :], m8[:, :], sel_s[:, :], OP.mult, reads=[cm8, csel],
                 writes=[cm8])
            p.red(me[:, :], m8[:, :], op=OP.max, reads=[cm8], writes=[cme])
            p.dma("sync",
                  score_o[0:1, t0 + tb * 128: t0 + (tb + 1) * 128].rearrange("one t -> t one"),
                  sc[:, :], reads=[csc])
            p.dma("sync",
                  mask_o[0:1, t0 + tb * 128: t0 + (tb + 1) * 128].rearrange("one t -> t one"),
                  me[:, :], reads=[cme])
    p.drain_q("sync")
    return p


def _swiglu_gemm(p, pool, act, cact, sg, csg, wt_g, cwg, wt_u, cwu, wt_d, cwd,
                 xs, cxs, TG, out_cb):
    for ro in range(16):
        for c0 in range(0, TG, 512):
            cw = min(512, TG - c0)
            pg, cpg = pool.next()
            items = [(pg[:, 0:cw], wt_g[:, ik * 2048 + ro * 128: ik * 2048 + ro * 128 + 128],
                      xs[:, ik * TG + c0: ik * TG + c0 + cw]) for ik in range(8)]
            mm_group(p, items, [cxs], cpg,
                     item_reads=[[cwg[ik]] for ik in range(8)])
            pu, cpu = pool.next()
            items = [(pu[:, 0:cw], wt_u[:, ik * 2048 + ro * 128: ik * 2048 + ro * 128 + 128],
                      xs[:, ik * TG + c0: ik * TG + c0 + cw]) for ik in range(8)]
            mm_group(p, items, [cxs], cpu,
                     item_reads=[[cwu[ik]] for ik in range(8)])
            p.act(sg[:, 0:cw], pg[:, 0:cw], AF.Silu, reads=[cpg], writes=[csg])
            p.tt(act[:, ro * TG + c0: ro * TG + c0 + cw], sg[:, 0:cw], pu[:, 0:cw],
                 OP.mult, reads=[csg, cpu], writes=[cact])
    for ro in range(8):
        for c0 in range(0, TG, 512):
            cw = min(512, TG - c0)
            pd, cpd = pool.next()
            items = [(pd[:, 0:cw], wt_d[:, ik * 1024 + ro * 128: ik * 1024 + ro * 128 + 128],
                      act[:, ik * TG + c0: ik * TG + c0 + cw]) for ik in range(16)]
            mm_group(p, items, [cact], cpd,
                     item_reads=[[cwd[ik]] for ik in range(16)])
            out_cb(ro, c0, cw, pd, cpd)


def build_l0_expert(p, T=16384, TG=1024):
    xnT = p.inp("xnT_e", [1024, T])
    xnT_my = p.inp("xnT_my", [1024, 2048])
    score = p.inp("score_row_i", [1, T])
    score_my = p.inp("score_my", [1, 2048])
    mask = p.inp("mask_row_i", [1, T])
    rt_g = p.inp("rt_g", [1024, 2048])
    rt_u = p.inp("rt_u", [1024, 2048])
    rt_d = p.inp("rt_d", [2048, 1024])
    sh_g = p.inp("sh_g", [1024, 2048])
    sh_u = p.inp("sh_u", [1024, 2048])
    sh_d = p.inp("sh_d", [2048, 1024])
    ones1 = p.inp("ones1_e", [1, 128])
    routed_o = p.outp("routed_m", [1024, T])
    shared_o = p.outp("shared_fm", [1024, 2048])
    pool = PsumPool(p, 8)
    o1_s = p.sbuf("o1s", [1, 128])
    co1s = p.cell(o1_s)
    p.dma("sync", o1_s[:, :], ones1, writes=[co1s])
    wt_g, cwg, _ = load_w_bf16(p, "wtg", rt_g, 1024, 2048)
    wt_u, cwu, _ = load_w_bf16(p, "wtu", rt_u, 1024, 2048)
    wt_d, cwd, _ = load_w_bf16(p, "wtd", rt_d, 2048, 1024)
    xs = p.sbuf("xs", [128, 8 * TG], BF16)
    cxs = p.cell(xs)
    srow = p.sbuf("srow", [1, TG])
    csrow = p.cell(srow)
    mrow = p.sbuf("mrow", [1, TG])
    cmrow = p.cell(mrow)
    mrow_g = p.sbuf("mrow_g", [128, TG])
    cmrg = p.cell(mrow_g)
    evr = p.sbuf("evr", [128, 512])
    cevr = p.cell(evr)
    act_s = p.sbuf("act_s", [128, 16 * TG], BF16)
    cact_s = p.cell(act_s)
    sg_s = p.sbuf("sg_s", [128, 512], BF16)
    csg_s = p.cell(sg_s)

    def make_xs(src_xnT, src_score, t0):
        for ik in range(8):
            p.dma("gpsimd", xs[:, ik * TG:(ik + 1) * TG],
                  src_xnT[ik * 128:(ik + 1) * 128, t0:t0 + TG], writes=[cxs])
        if src_score is None:
            return
        p.dma("sync", srow[:, :], src_score[0:1, t0:t0 + TG], writes=[csrow])
        for c0 in range(0, TG, 512):
            psc, cpsc = pool.next()
            mm_group(p, [(psc[:, 0:512], o1_s[:, :], srow[0:1, c0:c0 + 512])],
                     [co1s, csrow], cpsc)
            for ik in range(8):
                p.tt(xs[:, ik * TG + c0: ik * TG + c0 + 512],
                     xs[:, ik * TG + c0: ik * TG + c0 + 512], psc[:, 0:512],
                     OP.mult, reads=[cpsc, cxs], writes=[cxs])

    for t0 in range(0, T, TG):
        make_xs(xnT, score, t0)
        p.dma("sync", mrow[:, :], mask[0:1, t0:t0 + TG], writes=[cmrow])
        for c0 in range(0, TG, 512):
            pmb, cpmb = pool.next()
            mm_group(p, [(pmb[:, 0:512], o1_s[:, :], mrow[0:1, c0:c0 + 512])],
                     [co1s, cmrow], cpmb)
            p.copy(mrow_g[:, c0:c0 + 512], pmb[:, 0:512], reads=[cpmb], writes=[cmrg])

        def out_cb(ro, c0, cw, pd, cpd, t0=t0):
            p.tt(evr[:, 0:cw], pd[:, 0:cw], mrow_g[:, c0:c0 + cw], OP.mult,
                 reads=[cpd, cmrg], writes=[cevr])
            p.dma("sync", routed_o[ro * 128:(ro + 1) * 128, t0 + c0:t0 + c0 + cw],
                  evr[:, 0:cw], reads=[cevr])
        _swiglu_gemm(p, pool, act_s, cact_s, sg_s, csg_s, wt_g, cwg, wt_u, cwu,
                     wt_d, cwd, xs, cxs, TG, out_cb)
    for wsb, csb, src, K, RO in ((wt_g, cwg, sh_g, 1024, 2048),
                                 (wt_u, cwu, sh_u, 1024, 2048),
                                 (wt_d, cwd, sh_d, 2048, 1024)):
        for ik in range(K // 128):
            p.dma("gpsimd", wsb[:, ik * RO:(ik + 1) * RO],
                  src[ik * 128:(ik + 1) * 128, :], writes=[csb[ik]])
    for t0 in range(0, 2048, TG):
        make_xs(xnT_my, None, t0)

        def out_cb_sh(ro, c0, cw, pd, cpd, t0=t0):
            p.copy(evr[:, 0:cw], pd[:, 0:cw], reads=[cpd], writes=[cevr])
            p.dma("sync", shared_o[ro * 128:(ro + 1) * 128, t0 + c0:t0 + c0 + cw],
                  evr[:, 0:cw], reads=[cevr])
        _swiglu_gemm(p, pool, act_s, cact_s, sg_s, csg_s, wt_g, cwg, wt_u, cwu,
                     wt_d, cwd, xs, cxs, TG, out_cb_sh)
    p.drain_q("sync")
    return p


def emit_ln(p, out_ap, in_ap, sc, cs_, rms=False, eps=1e-5, D=1024):
    junk, s1, s2, m, v, r = (sc[k] for k in ("junk", "s1", "s2", "m", "v", "r"))
    cj, c1, c2, cm, cv, cr = (cs_[k] for k in ("junk", "s1", "s2", "m", "v", "r"))
    cin = cs_["in"]
    if not rms:
        p.red(s1[:, :], in_ap, reads=[cin], writes=[c1])
    p.stt(junk[:, :], in_ap, 1.0, in_ap, OP.bypass, OP.mult,
          accum_out=s2[:, :], reads=[cin], writes=[cj, c2])
    if not rms:
        p.ts(m[:, :], s1[:, :], 1.0 / D, reads=[c1], writes=[cm])
        p.ts(v[:, :], s2[:, :], 1.0 / D, reads=[c2], writes=[cv])
        p.stt(v[:, :], m[:, :], m[:, :], v[:, :], OP.mult, OP.subtract,
              reads=[cm, cv], writes=[cv])
        p.ts(v[:, :], v[:, :], -1.0, eps, OP.mult, OP.add, reads=[cv], writes=[cv])
    else:
        p.ts(v[:, :], s2[:, :], 1.0 / D, eps, OP.mult, OP.add, reads=[c2], writes=[cv])
    p.act(r[:, :], v[:, :], AF.Ln, reads=[cv], writes=[cr])
    p.act(r[:, :], r[:, :], AF.Exp, scale=-0.5, reads=[cr], writes=[cr])
    if not rms:
        p.ts(m[:, :], m[:, :], -1.0, reads=[cm], writes=[cm])
        p.ts(out_ap, in_ap, m[:, :], r[:, :], OP.add, OP.mult,
             reads=[cin, cm, cr], writes=[cs_["out"]])
    else:
        p.ts(out_ap, in_ap, r[:, :], None, OP.mult,
             reads=[cin, cr], writes=[cs_["out"]])


def mk_ln_scratch(p, tag, D=1024):
    sc = {"junk": p.sbuf(f"lnj{tag}", [128, D]),
          "s1": p.sbuf(f"ln1{tag}", [128, 1]),
          "s2": p.sbuf(f"ln2{tag}", [128, 1]),
          "m": p.sbuf(f"lnm{tag}", [128, 1]),
          "v": p.sbuf(f"lnv{tag}", [128, 1]),
          "r": p.sbuf(f"lnr{tag}", [128, 1])}
    cs_ = {k: p.cell(t) for k, t in sc.items()}
    return sc, cs_


def build_l0_combine(p, T=2048):
    x_tm = p.inp("x_tm", [T, 1024])
    s_fm = p.inp("s_fm_i", [1024, T])
    sh_fm = p.inp("shared_fm_i", [1024, T])
    r_all = p.inp("routed_all", [8192, T])
    identf = p.inp("ident_f", [128, 128])
    x1_o = p.outp("x1_tm", [T, 1024])
    xn1_o = p.outp("xn1_tm", [T, 1024])
    pool = PsumPool(p, 8)
    idf = p.sbuf("idf", [128, 128])
    cidf = p.cell(idf)
    p.dma("sync", idf[:, :], identf, writes=[cidf])
    accf = p.sbuf("accf", [128, T])
    tmpf = p.sbuf("tmpf", [128, T])
    sfm = p.sbuf("sfm", [128, T])
    cacc, ctmp, csfm = p.cell(accf), p.cell(tmpf), p.cell(sfm)
    nt = T // 128
    xout = p.sbuf("xout", [128, nt * 1024 // 16])
    xout = p.sbuf("xout2", [128, 16 * 1024])
    cxo = p.cell(xout)
    for ti in range(nt):
        p.dma("sync", xout[:, ti * 1024:(ti + 1) * 1024],
              x_tm[ti * 128:(ti + 1) * 128, :], writes=[cxo])
    for ro in range(8):
        p.dma("sync", accf[:, :], sh_fm[ro * 128:(ro + 1) * 128, :], writes=[cacc])
        for e in range(8):
            p.dma("sync", tmpf[:, :], r_all[e * 1024 + ro * 128: e * 1024 + (ro + 1) * 128, :],
                  writes=[ctmp])
            p.tt(accf[:, :], accf[:, :], tmpf[:, :], OP.add, reads=[ctmp, cacc],
                 writes=[cacc])
        p.act(accf[:, :], accf[:, :], AF.Silu, reads=[cacc], writes=[cacc])
        p.dma("sync", sfm[:, :], s_fm[ro * 128:(ro + 1) * 128, :], writes=[csfm])
        p.tt(accf[:, :], accf[:, :], sfm[:, :], OP.mult, reads=[csfm, cacc],
             writes=[cacc])
        for ti in range(nt):
            pb, cpb = pool.next()
            p.trans(pb[:, 0:128], accf[:, ti * 128:(ti + 1) * 128], idf[:, :],
                    reads=[cacc, cidf], writes=[cpb])
            p.tt(xout[:, ti * 1024 + ro * 128: ti * 1024 + (ro + 1) * 128],
                 xout[:, ti * 1024 + ro * 128: ti * 1024 + (ro + 1) * 128],
                 pb[:, 0:128], OP.add, reads=[cpb, cxo], writes=[cxo])
    sc, cs_ = mk_ln_scratch(p, "c")
    xn_t = p.sbuf("xn_t", [128, 1024])
    cs_["in"] = cxo
    cs_["out"] = p.cell(xn_t)
    for ti in range(nt):
        p.dma("sync", x1_o[ti * 128:(ti + 1) * 128, :],
              xout[:, ti * 1024:(ti + 1) * 1024], reads=[cxo])
        emit_ln(p, xn_t[:, :], xout[:, ti * 1024:(ti + 1) * 1024], sc, cs_)
        p.dma("sync", xn1_o[ti * 128:(ti + 1) * 128, :], xn_t[:, :],
              reads=[cs_["out"]])
    p.drain_q("sync")
    return p


def _row_bcast(p, pool, dst, cdst, row_sb, crow, o1_s, co1s, W=1024):
    for c0 in range(0, W, 512):
        cw = min(512, W - c0)
        pb, cpb = pool.next()
        mm_group(p, [(pb[:, 0:cw], o1_s[:, :], row_sb[0:1, c0:c0 + cw])],
                 [co1s, crow], cpb)
        p.copy(dst[:, c0:c0 + cw], pb[:, 0:cw], reads=[cpb], writes=[cdst])


def build_l1_lnchan(p, T=2048, D=1024):
    h_tm = p.inp("h_tm", [T, D])
    wrow = p.inp("ln_w_row", [1, D])
    brow = p.inp("ln_b_row", [1, D])
    ones1 = p.inp("ones1_l", [1, 128])
    z_o = p.outp("z_tm", [T, D])
    pool = PsumPool(p, 4)
    o1_s = p.sbuf("o1l", [1, 128])
    wr_s = p.sbuf("wrs", [1, D])
    br_s = p.sbuf("brs", [1, D])
    wb = p.sbuf("wb", [128, D])
    bb = p.sbuf("bb", [128, D])
    co1, cwr, cbr, cwb, cbb = (p.cell(t) for t in (o1_s, wr_s, br_s, wb, bb))
    p.dma("sync", o1_s[:, :], ones1, writes=[co1])
    p.dma("sync", wr_s[:, :], wrow, writes=[cwr])
    p.dma("sync", br_s[:, :], brow, writes=[cbr])
    _row_bcast(p, pool, wb, cwb, wr_s, cwr, o1_s, co1, D)
    _row_bcast(p, pool, bb, cbb, br_s, cbr, o1_s, co1, D)
    xt = p.sbuf("xtl", [128, D])
    ot = p.sbuf("otl", [128, D])
    cxt, cot = p.cell(xt), p.cell(ot)
    sc, cs_ = mk_ln_scratch(p, "lc", D)
    cs_["in"] = cxt
    cs_["out"] = cot
    for ti in range(T // 128):
        p.dma("sync", xt[:, :], h_tm[ti * 128:(ti + 1) * 128, :], writes=[cxt])
        emit_ln(p, ot[:, :], xt[:, :], sc, cs_, D=D)
        p.tt(ot[:, :], ot[:, :], wb[:, :], OP.mult, reads=[cwb, cot], writes=[cot])
        p.tt(ot[:, :], ot[:, :], bb[:, :], OP.add, reads=[cbb, cot], writes=[cot])
        p.dma("sync", z_o[ti * 128:(ti + 1) * 128, :], ot[:, :], reads=[cot])
    p.drain_q("sync")
    return p


def build_l1_wout(p, T=2048):
    ygT = p.inp("ygT_sl", [1024, T])
    WoT = p.inp("WoutT", [1024, 2048])
    bo = p.inp("b_out_i", [2048, 1])
    h_tm = p.inp("h_tm_w", [T, 1024])
    identf = p.inp("ident_fw", [128, 128])
    h1_o = p.outp("h1_tm", [T, 1024])
    pool = PsumPool(p, 8)
    wt, cwt, nk = load_w_bf16(p, "wtw", WoT, 1024, 2048)
    idf = p.sbuf("idfw", [128, 128])
    cidf = p.cell(idf)
    p.dma("sync", idf[:, :], identf, writes=[cidf])
    bos = p.sbuf("bos", [128, 16])
    cbo = p.cell(bos)
    for ro in range(16):
        p.dma("sync", bos[:, ro:ro + 1], bo[ro * 128:(ro + 1) * 128, :], writes=[cbo])
    xf = p.sbuf("xfw", [128, 8 * T], BF16)
    cxf = [p.cell(xf, f"t{i}") for i in range(nk)]
    for ik in range(nk):
        p.dma("gpsimd", xf[:, ik * T:(ik + 1) * T], ygT[ik * 128:(ik + 1) * 128, :],
              writes=[cxf[ik]])
    sigb = p.sbuf("sigb", [128, 8 * T], BF16)
    csig = p.cell(sigb)
    def evac_g(ro, c0, cw, pb, cbk):
        rr = ro - 8
        p.act(sigb[:, rr * T + c0: rr * T + c0 + cw], pb[:, 0:cw], AF.Sigmoid,
              bias=bos[:, ro:ro + 1], reads=[cbk, cbo], writes=[csig])
    gemm(p, pool, wt, cwt, nk, 2048, 1024, xf, cxf, T, T, evac_g,
         ro_list=range(8, 16))
    xout = p.sbuf("xoutw", [128, (T // 128) * 1024])
    cxo = p.cell(xout)
    for ti in range(T // 128):
        p.dma("sync", xout[:, ti * 1024:(ti + 1) * 1024],
              h_tm[ti * 128:(ti + 1) * 128, :], writes=[cxo])
    glu = p.sbuf("glu", [128, T])
    cglu = p.cell(glu)

    def evac_v(ro, c0, cw, pb, cbk):
        p.ts(glu[:, c0:c0 + cw], pb[:, 0:cw], bos[:, ro:ro + 1], None, OP.add,
             reads=[cbk, cbo], writes=[cglu])
        p.tt(glu[:, c0:c0 + cw], glu[:, c0:c0 + cw],
             sigb[:, ro * T + c0: ro * T + c0 + cw], OP.mult,
             reads=[csig, cglu], writes=[cglu])
        if c0 + cw == T:
            for ti in range(T // 128):
                pb2, cpb2 = pool.next()
                p.trans(pb2[:, 0:128], glu[:, ti * 128:(ti + 1) * 128], idf[:, :],
                        reads=[cglu, cidf], writes=[cpb2])
                p.tt(xout[:, ti * 1024 + ro * 128: ti * 1024 + (ro + 1) * 128],
                     xout[:, ti * 1024 + ro * 128: ti * 1024 + (ro + 1) * 128],
                     pb2[:, 0:128], OP.add, reads=[cpb2, cxo], writes=[cxo])
    gemm(p, pool, wt, cwt, nk, 2048, 1024, xf, cxf, T, T, evac_v,
         ro_list=range(8))
    for ti in range(T // 128):
        p.dma("sync", h1_o[ti * 128:(ti + 1) * 128, :],
              xout[:, ti * 1024:(ti + 1) * 1024], reads=[cxo])
    p.drain_q("sync")
    return p


def custom_ap(base_ap, dims, offset):
    import bass_rust as br
    c = base_ap.copy()
    c.ap = br.VecI64Pair([list(d) for d in dims])
    c.offset = offset
    return c


def build_l1_conv(p, L=8192, NB=2, Q=128):
    NCH = L // Q
    zT = p.inp("zT_sl", [128, NB * L])
    ldt = p.inp("log_dt_sel", [128, 1])
    arl = p.inp("A_re_log_sel", [128, 32])
    aim = p.inp("A_im_sel", [128, 32])
    cre = p.inp("C_re_sel", [128, 32])
    cim = p.inp("C_im_sel", [128, 32])
    dsk = p.inp("D_sel", [128, 1])
    idb = p.inp("ident_b", [128, 128], BF16)
    swp = p.inp("swap64", [64, 64])
    idf32_i = p.inp("ident_f32c", [128, 128])
    yg_o = p.outp("yg_sl", [128, NB * L])
    kpad_d = p.dram("kpad_d", [128, 256])
    q_d = p.dram("q_d", [2, 128, 32, 128])
    pool = PsumPool(p, 6)
    ckpd = p.cell(kpad_d)
    cqd = p.cell(q_d)
    if p.gbanks is not None:
        _ptr_bf = p.ptr_bf
        _ptr_f = p.ptr_f32
        _cptr = p.ptr_cell
    else:
        _ptr_bf = None

    P_ = p.sbuf("Ptab", [128, 8192])
    tmp = p.sbuf("tmpc", [128, 4096])
    sm = p.sbuf("smalls", [128, 512])
    toep = p.sbuf("toep", [128, 128 * 128], BF16)
    m1st = p.sbuf("m1st", [128, 8192], BF16)
    m2st = p.sbuf("m2st", [64, 128 * 128], BF16)
    lamA = p.sbuf("lamA", [64, 128])
    lamB = p.sbuf("lamB", [64, 128])
    swp_s = p.sbuf("swp_s", [64, 64])
    ztst = p.sbuf("ztst", [128, 8192], BF16)
    stbf = p.sbuf("stbf", [64, 8192], BF16)
    scur = p.sbuf("scur", [64, 128])

    ytst = p.sbuf("ytst", [128, 8192], BF16)
    zbf = ytst[:, 0:2048]
    yblk = p.sbuf("yblkc", [128, 2048])
    zldc = p.sbuf("zldc", [128, 2048])
    g1t = p.sbuf("g1tc", [128, 2048])
    g2t = zldc
    kt = g1t[:, 0:256]

    ids = p.sbuf("idsc", [128, 128], BF16)
    (cP, ctmp, csm, ctoep, cm1, cm2, clA, clB, cswp, czt, cst, cscur, cyt,
     cyb, czl, cg1, cid) = (p.cell(t) for t in (
        P_, tmp, sm, toep, m1st, m2st, lamA, lamB, swp_s, ztst, stbf, scur,
        ytst, yblk, zldc, g1t, ids))
    cg2 = czl
    czb = cyt
    ckt = cg1
    ctoep4 = [p.cell(toep, f"s{j}") for j in range(4)]
    cin = [p.cell(zT, "in")]
    idf32 = p.sbuf("idf32c", [128, 128])
    cidf = p.cell(idf32)
    p.dma("sync", ids[:, :], idb, writes=[cid])
    p.dma("sync", swp_s[:, :], swp, writes=[cswp])
    p.dma("sync", idf32[:, :], idf32_i, writes=[cidf])

    S = lambda i: sm[:, i * 32:(i + 1) * 32]
    csl = csm
    d_in = [p.cell(ldt), p.cell(arl), p.cell(aim), p.cell(cre), p.cell(cim),
            p.cell(dsk)]
    ldt_s = p.sbuf("ldt_s", [128, 1])
    dsk_s = p.sbuf("dsk_s", [128, 1])
    arl_s = p.sbuf("arl_s", [128, 32])
    aim_s = p.sbuf("aim_s", [128, 32])
    cre_s = p.sbuf("cre_s", [128, 32])
    cim_s = p.sbuf("cim_s", [128, 32])
    cldt, cdsk, carl, caim, ccre, ccim = (p.cell(t) for t in
                                          (ldt_s, dsk_s, arl_s, aim_s, cre_s, cim_s))
    p.dma("sync", ldt_s[:, :], ldt, writes=[cldt])
    p.dma("sync", dsk_s[:, :], dsk, writes=[cdsk])
    p.dma("sync", arl_s[:, :], arl, writes=[carl])
    p.dma("sync", aim_s[:, :], aim, writes=[caim])
    p.dma("sync", cre_s[:, :], cre, writes=[ccre])
    p.dma("sync", cim_s[:, :], cim, writes=[ccim])
    PI = float(np.pi)

    dtc = p.sbuf("dtc", [128, 1])
    cdt = p.cell(dtc)
    p.act(dtc[:, :], ldt_s[:, :], AF.Exp, reads=[cldt], writes=[cdt])
    p.act(S(0), arl_s[:, :], AF.Exp, reads=[carl], writes=[csl])
    p.ts(S(1), S(0), dtc[:, :], None, OP.mult, reads=[csl, cdt], writes=[csl])
    p.ts(S(1), S(1), -1.0, None, OP.mult, reads=[csl], writes=[csl])
    p.ts(S(2), aim_s[:, :], dtc[:, :], None, OP.mult, reads=[caim, cdt, csl],
         writes=[csl])
    p.act(S(3), S(1), AF.Exp, reads=[csl], writes=[csl])
    p.copy(S(4), S(2), reads=[csl], writes=[csl], eng="vector")
    for _ in range(2):
        p.ts(S(13), S(4), PI, None, OP.is_gt, reads=[csl], writes=[csl])
        p.stt(S(4), S(13), -2.0 * PI, S(4), OP.mult, OP.add, reads=[csl],
              writes=[csl])
    p.act(S(6), S(4), AF.Sin, reads=[csl], writes=[csl])
    p.ts(S(13), S(4), -1.0, PI / 2, OP.mult, OP.add, reads=[csl], writes=[csl])
    p.ts(S(14), S(13), PI, None, OP.is_gt, reads=[csl], writes=[csl])
    p.stt(S(13), S(14), -2.0 * PI, S(13), OP.mult, OP.add, reads=[csl], writes=[csl])
    p.act(S(5), S(13), AF.Sin, reads=[csl], writes=[csl])
    p.tt(S(7), S(3), S(5), OP.mult, reads=[csl], writes=[csl])
    p.tt(S(8), S(3), S(6), OP.mult, reads=[csl], writes=[csl])
    p.ts(S(13), S(7), 1.0, None, OP.subtract, reads=[csl], writes=[csl])
    p.tt(S(14), S(0), S(0), OP.mult, reads=[csl], writes=[csl])
    p.stt(S(14), aim_s[:, :], 1.0, S(14), OP.bypass, OP.bypass, reads=[caim, csl],
          writes=[csl]) if False else None
    p.stt(S(14), aim_s[:, :], 1.0, S(14), OP.mult, OP.add, reads=[caim, csl],
          writes=[csl]) if False else None
    p.tt(S(15), aim_s[:, :], aim_s[:, :], OP.mult, reads=[caim, csl], writes=[csl])
    p.tt(S(14), S(14), S(15), OP.add, reads=[csl], writes=[csl])
    p.recip(S(14), S(14), reads=[csl], writes=[csl])
    p.tt(S(9), S(13), S(0), OP.mult, reads=[csl], writes=[csl])
    p.ts(S(9), S(9), -1.0, None, OP.mult, reads=[csl], writes=[csl])
    p.tt(S(15), S(8), aim_s[:, :], OP.mult, reads=[caim, csl], writes=[csl])
    p.tt(S(9), S(9), S(15), OP.add, reads=[csl], writes=[csl])
    p.tt(S(9), S(9), S(14), OP.mult, reads=[csl], writes=[csl])
    p.tt(S(10), S(8), S(0), OP.mult, reads=[csl], writes=[csl])
    p.ts(S(10), S(10), -1.0, None, OP.mult, reads=[csl], writes=[csl])
    p.tt(S(15), S(13), aim_s[:, :], OP.mult, reads=[caim, csl], writes=[csl])
    p.tt(S(10), S(10), S(15), OP.subtract, reads=[csl], writes=[csl])
    p.tt(S(10), S(10), S(14), OP.mult, reads=[csl], writes=[csl])
    p.tt(S(11), cre_s[:, :], S(9), OP.mult, reads=[ccre, csl], writes=[csl])
    p.tt(S(15), cim_s[:, :], S(10), OP.mult, reads=[ccim, csl], writes=[csl])
    p.tt(S(11), S(11), S(15), OP.subtract, reads=[csl], writes=[csl])
    p.tt(S(12), cre_s[:, :], S(10), OP.mult, reads=[ccre, csl], writes=[csl])
    p.tt(S(15), cim_s[:, :], S(9), OP.mult, reads=[ccim, csl], writes=[csl])
    p.tt(S(12), S(12), S(15), OP.add, reads=[csl], writes=[csl])

    import os as _os
    _dbg = _os.environ.get("DBG_CONV", "0") == "1"
    if _dbg:
        dbg_sm = p.outp("dbg_sm", [128, 512])
        dbg_P = p.outp("dbg_P", [128, 8192])
    Pre = P_[:, 0:4096]
    Pim = P_[:, 4096:8192]
    n32 = lambda ap, n0, w: ap

    def bview(slot_ap, w):
        return slot_ap.unsqueeze(2).broadcast_to([128, 32, w])

    def pview(half, c0, w):
        base = 0 if half == "re" else 4096
        return P_[:, base:base + 4096].rearrange("p (n d) -> p n d", n=32)[:, :, c0:c0 + w]

    p.memset(Pre[:, :], 0.0, writes=[cP])
    p.memset(Pim[:, :], 0.0, writes=[cP])
    p.memset(pview("re", 0, 1), 1.0, writes=[cP])
    p.copy(pview("re", 1, 1), S(7).unsqueeze(2), reads=[csl], writes=[cP])
    p.copy(pview("im", 1, 1), S(8).unsqueeze(2), reads=[csl], writes=[cP])
    t1 = tmp[:, 0:2048]
    t2 = tmp[:, 2048:4096]

    def t1v(w):
        return tmp[:, 0:32 * w].rearrange("p (n d) -> p n d", n=32)

    m = 2
    while m < 128:
        p.tt(t1v(1), pview("re", m - 1, 1), S(7).unsqueeze(2), OP.mult,
             reads=[cP, csl], writes=[ctmp])
        p.tt(pview("re", m, 1), pview("im", m - 1, 1), S(8).unsqueeze(2),
             OP.mult, reads=[cP, csl], writes=[cP])
        p.tt(t2[:, 0:32].rearrange("p (n d) -> p n d", n=32),
             pview("im", m - 1, 1), S(7).unsqueeze(2), OP.mult,
             reads=[cP, csl], writes=[ctmp])
        p.tt(pview("im", m, 1), pview("re", m - 1, 1), S(8).unsqueeze(2),
             OP.mult, reads=[cP, csl], writes=[cP])
        p.tt(pview("im", m, 1), pview("im", m, 1),
             t2[:, 0:32].rearrange("p (n d) -> p n d", n=32), OP.add,
             reads=[cP, ctmp], writes=[cP])
        p.tt(pview("re", m, 1), t1v(1), pview("re", m, 1), OP.subtract,
             reads=[ctmp, cP], writes=[cP])
        w = m - 1
        if w > 0:
            rb = pview("re", m, 1).broadcast_to([128, 32, w])
            ib = pview("im", m, 1).broadcast_to([128, 32, w])
            ta = t1v(w)
            t2v = t2[:, 0:32 * w].rearrange("p (n d) -> p n d", n=32)
            p.tt(ta, pview("re", 1, w), rb, OP.mult, reads=[cP], writes=[ctmp])
            p.tt(t2v, pview("im", 1, w), rb, OP.mult, reads=[cP], writes=[ctmp])
            p.tt(pview("im", m + 1, w), pview("re", 1, w), ib, OP.mult,
                 reads=[cP], writes=[cP])
            p.tt(pview("im", m + 1, w), pview("im", m + 1, w), t2v, OP.add,
                 reads=[cP, ctmp], writes=[cP])
            p.tt(pview("re", m + 1, w), pview("im", 1, w), ib, OP.mult,
                 reads=[cP], writes=[cP])
            p.tt(pview("re", m + 1, w), ta, pview("re", m + 1, w), OP.subtract,
                 reads=[ctmp, cP], writes=[cP])
        m *= 2

    if _dbg:
        p.dma("sync", dbg_sm, sm[:, :], reads=[csl])
        p.dma("sync", dbg_P, P_[:, :], reads=[cP])
    p.tt(t1v(128), pview("re", 0, 128), bview(S(11), 128), OP.mult,
         reads=[cP, csl], writes=[ctmp])
    p.tt(t2[:, 0:2048].rearrange("p (n d) -> p n d", n=32) if False else
         tmp[:, 2048:2048 + 32 * 128].rearrange("p (n d) -> p n d", n=32),
         pview("im", 0, 128), bview(S(12), 128), OP.mult,
         reads=[cP, csl], writes=[ctmp]) if False else None
    p.memset(kt[:, :], 0.0, writes=[ckt])
    for half in range(2):
        n0 = half * 16
        tva = tmp[:, 0:16 * 128].rearrange("p (n d) -> p n d", n=16)
        tvb = tmp[:, 2048:2048 + 16 * 128].rearrange("p (n d) -> p n d", n=16)
        Prh = P_[:, n0 * 128:(n0 + 16) * 128].rearrange("p (n d) -> p n d", n=16)
        Pih = P_[:, 4096 + n0 * 128:4096 + (n0 + 16) * 128].rearrange(
            "p (n d) -> p n d", n=16)
        Cr = S(11)[:, n0:n0 + 16].unsqueeze(2).broadcast_to([128, 16, 128])
        Ci = S(12)[:, n0:n0 + 16].unsqueeze(2).broadcast_to([128, 16, 128])
        p.tt(tva, Prh, Cr, OP.mult, reads=[cP, csl], writes=[ctmp])
        p.tt(tvb, Pih, Ci, OP.mult, reads=[cP, csl], writes=[ctmp])
        p.tt(tva, tva, tvb, OP.subtract, reads=[ctmp], writes=[ctmp])
        red_in = tmp[:, 0:16 * 128].rearrange("p (n d) -> p d n", n=16)
        p.red(kt[:, 128 + half:0] if False else kt[:, 0:128], red_in, op=OP.add,
              reads=[ctmp], writes=[ckt]) if half == 0 else None
        if half == 0:
            pass
        else:
            p.red(kt[:, 128:256], red_in, op=OP.add, reads=[ctmp], writes=[ckt])
    p.tt(kt[:, 0:128], kt[:, 0:128], kt[:, 128:256], OP.add, reads=[ckt],
         writes=[ckt])
    p.ts(kt[:, 0:128], kt[:, 0:128], 2.0, None, OP.mult, reads=[ckt], writes=[ckt])
    p.memset(kt[:, 128:256], 0.0, writes=[ckt])
    p.dma("sync", kpad_d.ap()[:, 127:256], kt[:, 0:129], reads=[ckt], writes=[ckpd])
    p.dma("sync", kpad_d.ap()[:, 0:127], kt[:, 128:255], reads=[ckt], writes=[ckpd])

    for half in range(2):
        n0 = half * 16
        tva = tmp[:, 0:16 * 128].rearrange("p (n d) -> p n d", n=16)
        tvb = tmp[:, 2048:2048 + 16 * 128].rearrange("p (n d) -> p n d", n=16)
        Prh = P_[:, n0 * 128:(n0 + 16) * 128].rearrange("p (n d) -> p n d", n=16)
        Pih = P_[:, 4096 + n0 * 128:4096 + (n0 + 16) * 128].rearrange(
            "p (n d) -> p n d", n=16)
        lr = S(7)[:, n0:n0 + 16].unsqueeze(2).broadcast_to([128, 16, 128])
        li = S(8)[:, n0:n0 + 16].unsqueeze(2).broadcast_to([128, 16, 128])
        Cr = S(11)[:, n0:n0 + 16].unsqueeze(2).broadcast_to([128, 16, 128])
        Ci = S(12)[:, n0:n0 + 16].unsqueeze(2).broadcast_to([128, 16, 128])
        p.tt(tva, Prh, lr, OP.mult, reads=[cP, csl], writes=[ctmp])
        p.tt(tvb, Pih, li, OP.mult, reads=[cP, csl], writes=[ctmp])
        p.tt(tva, tva, tvb, OP.subtract, reads=[ctmp], writes=[ctmp])
        p.tt(tvb, Prh, li, OP.mult, reads=[cP, csl], writes=[ctmp])
        ex = g1t[:, 0:16 * 128].rearrange("p (n d) -> p n d", n=16)
        p.tt(ex, Pih, lr, OP.mult, reads=[cP, csl], writes=[cg1])
        p.tt(tvb, tvb, ex, OP.add, reads=[ctmp, cg1], writes=[ctmp])
        p.tt(ex, tva, Cr, OP.mult, reads=[ctmp, csl], writes=[cg1])
        ex2 = g2t[:, 0:16 * 128].rearrange("p (n d) -> p n d", n=16)
        p.tt(ex2, tvb, Ci, OP.mult, reads=[ctmp, csl], writes=[cg2])
        p.tt(ex, ex, ex2, OP.subtract, reads=[cg1, cg2], writes=[cg1])
        p.ts(g1t[:, 0:2048], g1t[:, 0:2048], 2.0, None, OP.mult, reads=[cg1],
             writes=[cg1])
        p.dma("sync", q_d.ap()[0, :, n0:n0 + 16, :], g1t[:, 0:2048], reads=[cg1],
              writes=[cqd])
        p.tt(ex, tvb, Cr, OP.mult, reads=[ctmp, csl], writes=[cg1])
        p.tt(ex2, tva, Ci, OP.mult, reads=[ctmp, csl], writes=[cg2])
        p.tt(ex, ex, ex2, OP.add, reads=[cg1, cg2], writes=[cg1])
        p.ts(g1t[:, 0:2048], g1t[:, 0:2048], -2.0, None, OP.mult, reads=[cg1],
             writes=[cg1])
        p.dma("sync", q_d.ap()[1, :, n0:n0 + 16, :], g1t[:, 0:2048], reads=[cg1],
              writes=[cqd])

    p.copy(S(13), pview("re", 127, 1).squeeze(2), reads=[cP], writes=[csl])
    p.copy(S(14), pview("im", 127, 1).squeeze(2), reads=[cP], writes=[csl])
    p.tt(S(0), S(13), S(7), OP.mult, reads=[csl], writes=[csl])
    p.tt(S(15), S(14), S(8), OP.mult, reads=[csl], writes=[csl])
    p.tt(S(0), S(0), S(15), OP.subtract, reads=[csl], writes=[csl])
    p.tt(S(1), S(13), S(8), OP.mult, reads=[csl], writes=[csl])
    p.tt(S(15), S(14), S(7), OP.mult, reads=[csl], writes=[csl])
    p.tt(S(1), S(1), S(15), OP.add, reads=[csl], writes=[csl])
    l128b = p.sbuf("l128b", [128, 64])
    cl128 = p.cell(l128b)
    if _ptr_bf is not None:
        ptr = _ptr_bf
        cptr = _cptr
        ptrfv = _ptr_f
        cptrf = _cptr
        ptr2 = p.gbanks[6][:, 0:64].bitcast(BF16)
        cptr2 = p.gbank_cells[6]
    else:
        ptr = p.psum("ptrc", [128, 128], BF16)
        cptr = p.cell(ptr)
        _ptrf_t = p.psum("ptrf", [64, 128])
        ptrfv = _ptrf_t[0:64, 0:128]
        cptrf = p.cell(_ptrf_t)
        ptr2 = p.psum("ptrc2", [128, 128], BF16)
        cptr2 = p.cell(ptr2)
    p.copy(l128b[:, 0:32], S(0), reads=[csl], writes=[cl128])
    p.copy(l128b[:, 32:64], S(0), reads=[csl], writes=[cl128])
    p.trans(ptrfv, l128b[:, :], idf32[:, :], reads=[cl128, cidf],
            writes=[cptrf])
    p.copy(lamA[:, :], ptrfv, reads=[cptrf], writes=[clA])
    p.ts(S(15), S(1), -1.0, None, OP.mult, reads=[csl], writes=[csl])
    p.copy(l128b[:, 0:32], S(15), reads=[csl], writes=[cl128])
    p.copy(l128b[:, 32:64], S(1), reads=[csl], writes=[cl128])
    p.trans(ptrfv, l128b[:, :], idf32[:, :], reads=[cl128, cidf],
            writes=[cptrf])
    p.copy(lamB[:, :], ptrfv, reads=[cptrf], writes=[clB])

    p.tt(S(15), S(3), S(3), OP.mult, reads=[csl], writes=[csl])
    p.recip(S(15), S(15), reads=[csl], writes=[csl])
    p.tt(S(9), S(7), S(15), OP.mult, reads=[csl], writes=[csl])
    p.tt(S(10), S(8), S(15), OP.mult, reads=[csl], writes=[csl])
    p.ts(S(10), S(10), -1.0, None, OP.mult, reads=[csl], writes=[csl])
    p.memset(Pre[:, :], 0.0, reads=[cP], writes=[cP])
    p.memset(Pim[:, :], 0.0, reads=[cP], writes=[cP])
    p.memset(pview("re", 0, 1), 1.0, writes=[cP])
    p.copy(pview("re", 1, 1), S(9).unsqueeze(2), reads=[csl], writes=[cP])
    p.copy(pview("im", 1, 1), S(10).unsqueeze(2), reads=[csl], writes=[cP])
    m = 2
    while m < 128:
        p.tt(t1v(1), pview("re", m - 1, 1), S(9).unsqueeze(2), OP.mult,
             reads=[cP, csl], writes=[ctmp])
        p.tt(pview("re", m, 1), pview("im", m - 1, 1), S(10).unsqueeze(2),
             OP.mult, reads=[cP, csl], writes=[cP])
        p.tt(t2[:, 0:32].rearrange("p (n d) -> p n d", n=32),
             pview("im", m - 1, 1), S(9).unsqueeze(2), OP.mult,
             reads=[cP, csl], writes=[ctmp])
        p.tt(pview("im", m, 1), pview("re", m - 1, 1), S(10).unsqueeze(2),
             OP.mult, reads=[cP, csl], writes=[cP])
        p.tt(pview("im", m, 1), pview("im", m, 1),
             t2[:, 0:32].rearrange("p (n d) -> p n d", n=32), OP.add,
             reads=[cP, ctmp], writes=[cP])
        p.tt(pview("re", m, 1), t1v(1), pview("re", m, 1), OP.subtract,
             reads=[ctmp, cP], writes=[cP])
        w = m - 1
        if w > 0:
            rb = pview("re", m, 1).broadcast_to([128, 32, w])
            ib = pview("im", m, 1).broadcast_to([128, 32, w])
            ta = t1v(w)
            t2v = t2[:, 0:32 * w].rearrange("p (n d) -> p n d", n=32)
            p.tt(ta, pview("re", 1, w), rb, OP.mult, reads=[cP], writes=[ctmp])
            p.tt(t2v, pview("im", 1, w), rb, OP.mult, reads=[cP], writes=[ctmp])
            p.tt(pview("im", m + 1, w), pview("re", 1, w), ib, OP.mult,
                 reads=[cP], writes=[cP])
            p.tt(pview("im", m + 1, w), pview("im", m + 1, w), t2v, OP.add,
                 reads=[cP, ctmp], writes=[cP])
            p.tt(pview("re", m + 1, w), pview("im", 1, w), ib, OP.mult,
                 reads=[cP], writes=[cP])
            p.tt(pview("re", m + 1, w), ta, pview("re", m + 1, w), OP.subtract,
                 reads=[ctmp, cP], writes=[cP])
        m *= 2
    prevb = ztst[:, 0:4096]
    cprev = czt
    for r in range(2):
        for half in range(2):
            n0 = half * 16
            tva = tmp[:, 0:16 * 128].rearrange("p (n d) -> p n d", n=16)
            tvb = tmp[:, 2048:4096].rearrange("p (n d) -> p n d", n=16)
            Prh = P_[:, n0 * 128:(n0 + 16) * 128].rearrange("p (n d) -> p n d", n=16)
            Pih = P_[:, 4096 + n0 * 128:4096 + (n0 + 16) * 128].rearrange(
                "p (n d) -> p n d", n=16)
            l7r = S(13)[:, n0:n0 + 16].unsqueeze(2).broadcast_to([128, 16, 128])
            l7i = S(14)[:, n0:n0 + 16].unsqueeze(2).broadcast_to([128, 16, 128])
            if r == 0:
                p.tt(tva, Prh, l7r, OP.mult, reads=[cP, csl], writes=[ctmp])
                p.tt(tvb, Pih, l7i, OP.mult, reads=[cP, csl], writes=[ctmp])
                p.tt(tva, tva, tvb, OP.subtract, reads=[ctmp], writes=[ctmp])
            else:
                p.tt(tva, Prh, l7i, OP.mult, reads=[cP, csl], writes=[ctmp])
                p.tt(tvb, Pih, l7r, OP.mult, reads=[cP, csl], writes=[ctmp])
                p.tt(tva, tva, tvb, OP.add, reads=[ctmp], writes=[ctmp])
            p.copy(prevb[:, n0 * 128:(n0 + 16) * 128], tmp[:, 0:2048],
                   reads=[ctmp], writes=[cprev])
        for n in range(32):
            pt, cpt = (ptr, cptr) if n % 2 == 0 else (ptr2, cptr2)
            p.trans(pt[:, 0:128], prevb[:, n * 128:(n + 1) * 128], ids[:, :],
                    reads=[cprev, cid], writes=[cpt])
            dst = m1st[:, :].rearrange("t (h c) -> t h c", h=128)[:, :, r * 32 + n:
                                                                 r * 32 + n + 1]
            p.copy(dst, pt[:, 0:128].unsqueeze(2), reads=[cpt], writes=[cm1])

    for r in range(2):
        src = q_d.ap()[r].rearrange("h n d -> n (h d)") if False else None
        src = custom_ap(q_d.ap(), [[128, 32], [4096, 128], [1, 128]],
                        r * 128 * 32 * 128)
        p.dma("gpsimd", m2st[r * 32:(r + 1) * 32, :], src, reads=[cqd], writes=[cm2])

    for tau in range(128):
        srcap = custom_ap(kpad_d.ap(), [[1, 1], [256, 128], [1, 128]], 127 - tau)
        p.dma("gpsimd", toep[tau:tau + 1, :].rearrange("one (h t) -> one h t", h=128),
              srcap, reads=[ckpd], writes=[ctoep4[tau % 4]])

    zT3 = zT
    for b in range(NB):
        for blk in range(4):
            p.dma("gpsimd", zbf[:, :], zT[0:128, b * L + blk * 2048:
                                          b * L + (blk + 1) * 2048], writes=[czb])
            for cc in range(16):
                c_abs = blk * 16 + cc
                pt, cpt = (ptr, cptr) if cc % 2 == 0 else (ptr2, cptr2)
                p.trans(pt[:, 0:128], zbf[:, cc * 128:(cc + 1) * 128], ids[:, :],
                        reads=[czb, cid], writes=[cpt])
                dst = ztst[:, :].rearrange("t (h c) -> t h c", h=128)[:, :, c_abs:
                                                                      c_abs + 1]
                p.copy(dst, pt[:, 0:128].unsqueeze(2), reads=[cpt], writes=[czt])
        agg = P_[0:64, 0:8192]
        for h in range(128):
            pb, cpb = pool.next()
            mm_group(p, [(pb[0:64, 0:64], m1st[:, h * 64:(h + 1) * 64],
                          ztst[:, h * 64:(h + 1) * 64])], [cm1, czt], cpb)
            p.copy(agg[:, h * 64:(h + 1) * 64], pb[0:64, 0:64], reads=[cpb],
                   writes=[cP])
        p.memset(scur[:, :], 0.0, writes=[cscur])
        rec1 = tmp[0:64, 0:128]
        cr1 = ctmp
        for c in range(NCH):
            dstc = stbf[:, :].rearrange("s (h c) -> s h c", h=128)[:, :, c:c + 1]
            p.copy(dstc, scur[:, :].unsqueeze(2), reads=[cscur], writes=[cst])
            aggc = agg[:, :].rearrange("s (h c) -> s h c", h=128)[:, :, c]
            pb, cpb = pool.next()
            mm_group(p, [(pb[0:64, 0:128], swp_s[:, :], scur[:, :])],
                     [cswp, cscur], cpb)
            p.tt(rec1, pb[0:64, 0:128], lamB[:, :], OP.mult,
                 reads=[cpb, clB], writes=[cr1])
            p.tt(scur[:, :], scur[:, :], lamA[:, :], OP.mult,
                 reads=[clA, cscur], writes=[cscur])
            p.tt(scur[:, :], scur[:, :], rec1, OP.add,
                 reads=[cr1, cscur], writes=[cscur])
            p.tt(scur[:, :], scur[:, :], aggc, OP.add,
                 reads=[cP, cscur], writes=[cscur])
        for h in range(128):
            pb, cpb = pool.next()
            mm_group(p, [
                (pb[:, 0:64], toep[:, h * 128:(h + 1) * 128],
                 ztst[:, h * 64:(h + 1) * 64]),
                (pb[:, 0:64], m2st[:, h * 128:(h + 1) * 128],
                 stbf[:, h * 64:(h + 1) * 64]),
            ], ctoep4 + [cm1, cm2, czt, cst], cpb)
            dst = ytst[:, :].rearrange("t (c h) -> t c h", c=64)[:, :, h]
            p.copy(dst, pb[:, 0:64], reads=[cpb], writes=[cyt])
        for blk in range(4):
            for cc in range(16):
                c_abs = blk * 16 + cc
                pt, cpt = (ptr, cptr) if cc % 2 == 0 else (ptr2, cptr2)
                p.trans(pt[:, 0:128], ytst[:, c_abs * 128:(c_abs + 1) * 128],
                        ids[:, :], reads=[cyt, cid], writes=[cpt])
                p.copy(yblk[:, cc * 128:(cc + 1) * 128], pt[:, 0:128],
                       reads=[cpt], writes=[cyb])
            p.dma("sync", zldc[:, :], zT[0:128, b * L + blk * 2048:
                                         b * L + (blk + 1) * 2048], writes=[czl])
            p.stt(yblk[:, :], zldc[:, :], dsk_s[:, :], yblk[:, :], OP.mult,
                  OP.add, reads=[czl, cdsk, cyb], writes=[cyb])
            p.act(g1t[:, :], yblk[:, :], AF.Square, reads=[cyb], writes=[cg1])
            p.ts(g1t[:, :], g1t[:, :], 0.7978845608 * 0.044715, 0.7978845608,
                 OP.mult, OP.add, reads=[cg1], writes=[cg1])
            p.tt(g1t[:, :], g1t[:, :], yblk[:, :], OP.mult, reads=[cyb, cg1],
                 writes=[cg1])
            p.act(g1t[:, :], g1t[:, :], AF.Tanh, reads=[cg1], writes=[cg1])
            p.ts(g1t[:, :], g1t[:, :], 0.5, 0.5, OP.mult, OP.add, reads=[cg1],
                 writes=[cg1])
            p.tt(g1t[:, :], g1t[:, :], yblk[:, :], OP.mult, reads=[cyb, cg1],
                 writes=[cg1])
            p.dma("sync", yg_o[0:128, b * L + blk * 2048:b * L + (blk + 1) * 2048],
                  g1t[:, :], reads=[cg1])
    import os
    if os.environ.get("DBG_CONV", "0") == "1":
        dbg_k = p.outp("dbg_k", [128, 256])
        dbg_lamA = p.outp("dbg_lamA", [64, 128])
        dbg_lamB = p.outp("dbg_lamB", [64, 128])
        dbg_toep = p.outp("dbg_toep", [128, 16384], BF16)
        dbg_m1 = p.outp("dbg_m1", [128, 8192], BF16)
        dbg_m2 = p.outp("dbg_m2", [64, 16384], BF16)
        dbg_zt = p.outp("dbg_zt", [128, 8192], BF16)
        dbg_st = p.outp("dbg_st", [64, 8192], BF16)
        dbg_agg = p.outp("dbg_agg", [64, 8192])
        dbg_yt = p.outp("dbg_yt", [128, 8192], BF16)
        p.dma("sync", dbg_k, g1t[:, 0:256], reads=[cg1])
        p.dma("sync", dbg_lamA, lamA[:, :], reads=[clA])
        p.dma("sync", dbg_lamB, lamB[:, :], reads=[clB])
        p.dma("sync", dbg_toep, toep[:, :], reads=ctoep4)
        p.dma("sync", dbg_m1, m1st[:, :], reads=[cm1])
        p.dma("sync", dbg_m2, m2st[:, :], reads=[cm2])
        p.dma("sync", dbg_zt, ztst[:, :], reads=[czt])
        p.dma("sync", dbg_st, stbf[:, :], reads=[cst])
        p.dma("sync", dbg_agg, P_[0:64, :], reads=[cP])
        p.dma("sync", dbg_yt, ytst[:, :], reads=[cyt])
    p.drain_q("sync")
    p.drain_q("gpsimd")
    return p


def build_l1_qkv(p, T=2048):
    xT = p.inp("xn1T_sl", [1024, T])

    Wq = p.inp("Wq_f", [1024, 1024])
    Wkc = p.inp("Wk_c_f", [1024, 256])
    Wk = p.inp("Wk_f", [256, 1024])
    Wvc = p.inp("Wv_c_f", [1024, 256])
    Wv = p.inp("Wv_f", [256, 1024])
    sin_i = p.inp("sin_my", [T, 32])
    cos_i = p.inp("cos_my", [T, 32])
    idf = p.inp("ident_fq", [128, 128])
    q_o = p.outp("q_tm", [T, 1024])
    k_o = p.outp("k_tm", [T, 1024])
    v_o = p.outp("v_tm", [T, 1024])
    pool = PsumPool(p, 6)
    ids = p.sbuf("idsq", [128, 128])
    cid = p.cell(ids)
    p.dma("sync", ids[:, :], idf, writes=[cid])
    xf = p.sbuf("xfq", [128, 8 * T], BF16)
    cxf = [p.cell(xf, f"t{i}") for i in range(8)]
    for ik in range(8):
        p.dma("gpsimd", xf[:, ik * T:(ik + 1) * T], xT[ik * 128:(ik + 1) * 128, :],
              writes=[cxf[ik]])
    wtq, cwq, _ = load_w_bf16(p, "wtq", Wq, 1024, 1024)
    res_fm = p.sbuf("res_fm", [128, 8 * T], BF16)
    cres = p.cell(res_fm)
    kc_fm = p.sbuf("kc_fm", [128, 2 * T], BF16)
    ckc = p.cell(kc_fm)
    tm = p.sbuf("tmq", [128, (T // 128) * 1024])
    ctm = p.cell(tm)
    sin_s = p.sbuf("sin_s", [128, (T // 128) * 32])
    cos_s = p.sbuf("cos_s", [128, (T // 128) * 32])
    csin, ccos = p.cell(sin_s), p.cell(cos_s)
    nt = T // 128
    for ti in range(nt):
        p.dma("sync", sin_s[:, ti * 32:(ti + 1) * 32], sin_i[ti * 128:(ti + 1) * 128, :],
              writes=[csin])
        p.dma("sync", cos_s[:, ti * 32:(ti + 1) * 32], cos_i[ti * 128:(ti + 1) * 128, :],
              writes=[ccos])
    rt = p.sbuf("rtq", [128, 1024])
    crt = p.cell(rt)
    idsb = p.sbuf("idsbq", [128, 128], BF16)
    cidb = p.cell(idsb)
    p.copy(idsb[:, :], ids[:, :], reads=[cid], writes=[cidb])
    if p.gbanks is not None:
        ptq = p.ptr_bf
        cptq = p.ptr_cell
        ptq2 = p.gbanks[6][:, 0:64].bitcast(BF16)
        cptq2 = p.gbank_cells[6]
    else:
        ptq = p.psum("ptq", [128, 128], BF16)
        cptq = p.cell(ptq)
        ptq2 = p.psum("ptq2", [128, 128], BF16)
        cptq2 = p.cell(ptq2)

    def fm_to_tm(cfm, fm):
        for ti in range(nt):
            for ro in range(8):
                pt, cpt = (ptq, cptq) if ro % 2 == 0 else (ptq2, cptq2)
                p.trans(pt[:, 0:128], fm[:, ro * T + ti * 128: ro * T + (ti + 1) * 128],
                        idsb[:, :], reads=[cfm, cidb], writes=[cpt])
                p.copy(tm[:, ti * 1024 + ro * 128: ti * 1024 + (ro + 1) * 128],
                       pt[:, 0:128], reads=[cpt], writes=[ctm])

    def rope_and_out(out_dram, do_rope=True):
        for ti in range(nt):
            src = tm[:, ti * 1024:(ti + 1) * 1024]
            if do_rope:
                x1 = src.rearrange("p (h two j) -> p h two j", h=16, two=2)[:, :, 0, :] \
                    if False else src.rearrange("p (h j two) -> p h j two", h=16,
                                                two=2)[:, :, :, 0]
                x2 = src.rearrange("p (h j two) -> p h j two", h=16, two=2)[:, :, :, 1]
                cosb = cos_s[:, ti * 32:(ti + 1) * 32].unsqueeze(1).broadcast_to(
                    [128, 16, 32]).rearrange("p h j -> p h j") if False else \
                    cos_s[:, ti * 32:(ti + 1) * 32].unsqueeze(0).broadcast_to(
                        [128, 16, 32]) if False else None
                cosb = cos_s[:, ti * 32:(ti + 1) * 32].rearrange(
                    "p j -> p j").unsqueeze(1).broadcast_to([128, 16, 32])
                sinb = sin_s[:, ti * 32:(ti + 1) * 32].unsqueeze(1).broadcast_to(
                    [128, 16, 32])
                r1 = rt.rearrange("p (h j two) -> p h j two", h=16, two=2)[:, :, :, 0]
                r2 = rt.rearrange("p (h j two) -> p h j two", h=16, two=2)[:, :, :, 1]
                g1 = p.sbuf if False else None
                p.tt(r1, x1, cosb, OP.mult, reads=[ctm, ccos], writes=[crt])
                p.tt(r2, x1, sinb, OP.mult, reads=[ctm, csin], writes=[crt])
                p.stt(r1, x2, -1.0, None, OP.mult, OP.bypass) if False else None
                p.tt(x1, x2, sinb, OP.mult, reads=[ctm, csin], writes=[ctm])
                p.tt(r1, r1, x1, OP.subtract, reads=[ctm, crt], writes=[crt])
                p.tt(x1, x2, cosb, OP.mult, reads=[ctm, ccos], writes=[ctm])
                p.tt(r2, r2, x1, OP.add, reads=[ctm, crt], writes=[crt])
                p.dma("sync", out_dram[ti * 128:(ti + 1) * 128, :], rt[:, :],
                      reads=[crt])
            else:
                p.dma("sync", out_dram[ti * 128:(ti + 1) * 128, :], src,
                      reads=[ctm])

    def ev_q(ro, c0, cw, pb, cbk):
        p.copy(res_fm[:, ro * T + c0: ro * T + c0 + cw], pb[:, 0:cw],
               reads=[cbk], writes=[cres])
    gemm(p, pool, wtq, cwq, 8, 1024, 1024, xf, cxf, T, T, ev_q)
    fm_to_tm(cres, res_fm)
    rope_and_out(q_o, True)
    wtc, cwc_, _ = load_w_bf16(p, "wtkc", Wkc, 1024, 256)

    def ev_kc(ro, c0, cw, pb, cbk):
        p.copy(kc_fm[:, ro * T + c0: ro * T + c0 + cw], pb[:, 0:cw],
               reads=[cbk], writes=[ckc])
    gemm(p, pool, wtc, cwc_, 8, 256, 1024, xf, cxf, T, T, ev_kc)
    wtk, cwk, _ = load_w_bf16(p, "wtk", Wk, 256, 1024)
    gemm(p, pool, wtk, cwk, 2, 1024, 256, kc_fm, ckc, T, T, ev_q)
    fm_to_tm(cres, res_fm)
    rope_and_out(k_o, True)
    for ik in range(8):
        p.dma("gpsimd", wtc[:, ik * 256:(ik + 1) * 256],
              Wvc[ik * 128:(ik + 1) * 128, :], writes=[cwc_[ik]])
    gemm(p, pool, wtc, cwc_, 8, 256, 1024, xf, cxf, T, T, ev_kc)
    for ik in range(2):
        p.dma("gpsimd", wtk[:, ik * 1024:(ik + 1) * 1024],
              Wv[ik * 128:(ik + 1) * 128, :], writes=[cwk[ik]])
    gemm(p, pool, wtk, cwk, 2, 1024, 256, kc_fm, ckc, T, T, ev_q)
    fm_to_tm(cres, res_fm)
    rope_and_out(v_o, False)
    p.drain_q("sync")
    return p


def build_l1_attn(p, L=8192, DH=64):
    qT = p.inp("qT_h", [128, 2 * L])
    kT = p.inp("kT_h", [128, 2 * L])
    vv = p.inp("v_h", [2 * L, 128])
    onesb = p.inp("onesb", [128, 64], BF16)
    onesf = p.inp("onesf_at", [1, 128])
    sa_o = p.outp("sa_h", [128, 2 * L])
    if p.gbanks is not None:
        pool = PsumPool(p, 4)
        po = p.gbanks[4]
        pn = p.gbanks[5]
        cpo, cpn = p.gbank_cells[4], p.gbank_cells[5]
    else:
        pool = PsumPool(p, 4)
        po = p.psum("po_at", [64, 512])
        pn = p.psum("pn_at", [1, 512])
        cpo, cpn = p.cell(po), p.cell(pn)
    ob = p.sbuf("ob_at", [128, 64], BF16)
    cob = p.cell(ob)
    p.dma("sync", ob[:, :], onesb, writes=[cob])
    of = p.sbuf("of_at", [1, 128])
    cof = p.cell(of)
    p.dma("sync", of[:, :], onesf, writes=[cof])
    kbs = [p.sbuf(f"kb_at{i}", [64, L], BF16) for i in range(2)]
    qbs = [p.sbuf(f"qb_at{i}", [64, L], BF16) for i in range(2)]
    vts = [p.sbuf(f"vt_at{i}", [128, 64 * 65], BF16) for i in range(2)]
    ckbs = [p.cell(t) for t in kbs]
    cqbs = [p.cell(t) for t in qbs]
    cvts = [[p.cell(t, f"s{j}") for j in range(4)] for t in vts]
    pbufs = [p.sbuf(f"pbuf_at{i}", [128, 512], BF16) for i in range(2)]
    cpbs = [p.cell(t) for t in pbufs]
    pbuf = pbufs[0]
    rq = p.sbuf("rq_at", [128, 512])
    rb2 = p.sbuf("rb2_at", [64, 512])
    sa_s = p.sbuf("sa_s", [64, 512])
    of2 = p.sbuf("of2_at", [128, 128])
    cof2 = p.cell(of2)
    p.dma("sync", of2[64:65, :], onesf, writes=[cof2])
    crq, crb, csa = (p.cell(t) for t in (rq, rb2, sa_s))
    p.memset(vts[0][:, :], 1.0, writes=cvts[0])
    p.memset(vts[1][:, :], 1.0, writes=cvts[1])
    NKB = L // 128
    pairs = [(b, hh) for b in range(2) for hh in range(2)]

    def load_pair(i):
        b, hh = pairs[i]
        pi = i % 2
        p.dma("gpsimd", kbs[pi][:, :],
              kT[hh * 64:(hh + 1) * 64, b * L:(b + 1) * L], writes=[ckbs[pi]])
        p.dma("gpsimd", qbs[pi][:, :],
              qT[hh * 64:(hh + 1) * 64, b * L:(b + 1) * L], writes=[cqbs[pi]])
        for kbk in range(NKB):
            p.dma("gpsimd", vts[pi][:, kbk * 65:kbk * 65 + 64],
                  vv[b * L + kbk * 128: b * L + (kbk + 1) * 128,
                     hh * 64:(hh + 1) * 64], writes=[cvts[pi][kbk % 4]])

    load_pair(0)
    if True:
        for ip, (b, hh) in enumerate(pairs):
            pi = ip % 2
            kb, ckb = kbs[pi], ckbs[pi]
            qb, cqb = qbs[pi], cqbs[pi]
            vt, cvt = vts[pi], cvts[pi]
            if ip + 1 < len(pairs):
                load_pair(ip + 1)
            tiles = [(q0, kbk) for q0 in range(0, L, 512) for kbk in range(NKB)]

            def emit_sp(i):
                q0, kbk = tiles[i]
                ps, cps = pool.next()
                mm_group(p, [(ps[:, 0:512], kb[:, kbk * 128:(kbk + 1) * 128],
                              qb[:, q0:q0 + 512])], [ckb, cqb], cps)
                return ps, cps

            pend = emit_sp(0)
            for i, (q0, kbk) in enumerate(tiles):
                ps, cps = pend
                if i + 1 < len(tiles):
                    pend = emit_sp(i + 1)
                par = kbk % 2
                pbuf, cpb = pbufs[par], cpbs[par]
                p.act(pbuf[:, :], ps[:, 0:512], AF.Exp, scale=0.125,
                      reads=[cps], writes=[cpb])
                p.mm(po[0:65, 0:512], vt[:, kbk * 65:(kbk + 1) * 65],
                     pbuf[:, :], start=(kbk == 0), stop=(kbk == NKB - 1),
                     reads=[cvt[kbk % 4], cpb], writes=[cpo])
                if kbk == NKB - 1:
                    p.recip(rq[64:65, 0:512], po[64:65, 0:512], reads=[cpo],
                            writes=[crq])
                    pb2, cpb2 = pool.next()
                    mm_group(p, [(pb2[0:64, 0:512], of2[64:65, 0:64],
                                  rq[64:65, 0:512])], [cof2, crq], cpb2)
                    p.copy(rb2[:, :], pb2[0:64, 0:512], reads=[cpb2], writes=[crb])
                    p.tt(sa_s[:, :], po[0:64, 0:512], rb2[:, :], OP.mult,
                         reads=[cpo, crb], writes=[csa])
                    p.dma("sync", sa_o[hh * 64:(hh + 1) * 64,
                                       b * L + q0:b * L + q0 + 512],
                          sa_s[:, :], reads=[csa])
    p.drain_q("sync")
    return p

def _emit_rms(p, out_ap, in_ap, cin, cout, wb, cwb, sc, cs_, eps=1e-6, D=1024):
    junk, s2, v, r = sc["junk"], sc["s2"], sc["v"], sc["r"]
    cj, c2, cv, cr = cs_["junk"], cs_["s2"], cs_["v"], cs_["r"]
    p.stt(junk[:, :], in_ap, 1.0, in_ap, OP.bypass, OP.mult,
          accum_out=s2[:, :], reads=[cin], writes=[cj, c2])
    p.ts(v[:, :], s2[:, :], 1.0 / D, eps, OP.mult, OP.add, reads=[c2], writes=[cv])
    p.act(r[:, :], v[:, :], AF.Ln, reads=[cv], writes=[cr])
    p.act(r[:, :], r[:, :], AF.Exp, scale=-0.5, reads=[cr], writes=[cr])
    p.ts(out_ap, in_ap, r[:, :], None, OP.mult, reads=[cin, cr], writes=[cout])
    p.tt(out_ap, out_ap, wb[:, :], OP.mult, reads=[cwb, cout], writes=[cout])


def build_l1_attnout(p, T=2048):
    saT = p.inp("saT_sl", [1024, T])
    Wo = p.inp("Wo_f", [1024, 1024])
    src_tm = p.inp("xn1_sl", [T, 1024])
    nrow = p.inp("norm1_row", [1, 1024])
    ones1 = p.inp("ones1_ao", [1, 128])
    idf = p.inp("ident_ao", [128, 128])
    xr_o = p.outp("xr_tm", [T, 1024])
    pool = PsumPool(p, 8)
    wt, cwt, nk = load_w_bf16(p, "wtao", Wo, 1024, 1024)
    ids = p.sbuf("idsao", [128, 128])
    o1 = p.sbuf("o1ao", [1, 128])
    nr = p.sbuf("nrao", [1, 1024])
    wb = p.sbuf("wbao", [128, 1024])
    cid, co1, cnr, cwb = (p.cell(t) for t in (ids, o1, nr, wb))
    p.dma("sync", ids[:, :], idf, writes=[cid])
    p.dma("sync", o1[:, :], ones1, writes=[co1])
    p.dma("sync", nr[:, :], nrow, writes=[cnr])
    _row_bcast(p, pool, wb, cwb, nr, cnr, o1, co1, 1024)
    xf = p.sbuf("xfao", [128, 8 * T], BF16)
    cxf = [p.cell(xf, f"t{i}") for i in range(nk)]
    for ik in range(nk):
        p.dma("gpsimd", xf[:, ik * T:(ik + 1) * T], saT[ik * 128:(ik + 1) * 128, :],
              writes=[cxf[ik]])
    nt = T // 128
    xout = p.sbuf("xoutao", [128, nt * 1024])
    cxo = p.cell(xout)
    for ti in range(nt):
        p.dma("sync", xout[:, ti * 1024:(ti + 1) * 1024],
              src_tm[ti * 128:(ti + 1) * 128, :], writes=[cxo])
    prod = p.sbuf("prodao", [128, T])
    cprod = p.cell(prod)

    def evac(ro, c0, cw, pb, cbk):
        p.copy(prod[:, c0:c0 + cw], pb[:, 0:cw], reads=[cbk], writes=[cprod])
        if c0 + cw == T:
            for ti in range(nt):
                pb2, cpb2 = pool.next()
                p.trans(pb2[:, 0:128], prod[:, ti * 128:(ti + 1) * 128], ids[:, :],
                        reads=[cprod, cid], writes=[cpb2])
                p.tt(xout[:, ti * 1024 + ro * 128: ti * 1024 + (ro + 1) * 128],
                     xout[:, ti * 1024 + ro * 128: ti * 1024 + (ro + 1) * 128],
                     pb2[:, 0:128], OP.add, reads=[cpb2, cxo], writes=[cxo])
    gemm(p, pool, wt, cwt, nk, 1024, 1024, xf, cxf, T, T, evac)
    sc, cs_ = mk_ln_scratch(p, "ao")
    xr_t = p.sbuf("xr_t_ao", [128, 1024])
    cxrt = p.cell(xr_t)
    for ti in range(nt):
        _emit_rms(p, xr_t[:, :], xout[:, ti * 1024:(ti + 1) * 1024], cxo, cxrt,
                  wb, cwb, sc, cs_)
        p.dma("sync", xr_o[ti * 128:(ti + 1) * 128, :], xr_t[:, :], reads=[cxrt])
    p.drain_q("sync")
    return p


def build_l1_gate2(p, T=2048):
    xT = p.inp("xrT_sl", [1024, T])
    Wg = p.inp("Wg_f", [1024, 2])
    We = p.inp("We_f", [1024, 8])
    gbr = p.inp("gb_row", [1, 2])
    ebr = p.inp("eb_row", [1, 8])
    ones1 = p.inp("ones1_g2", [1, 128])
    idf = p.inp("ident_g2", [128, 128])
    wts_o = p.outp("wts_o", [8, T])
    pool = PsumPool(p, 8)
    xg = p.sbuf("xg2", [128, 8 * T])
    cxg = p.cell(xg)
    for ik in range(8):
        p.dma("sync", xg[:, ik * T:(ik + 1) * T], xT[ik * 128:(ik + 1) * 128, :],
              writes=[cxg])
    wg_s = p.sbuf("wg_s", [128, 8 * 2])
    we_s = p.sbuf("we_s", [128, 8 * 8])
    cwg, cwe = p.cell(wg_s), p.cell(we_s)
    for ik in range(8):
        p.dma("sync", wg_s[:, ik * 2:(ik + 1) * 2], Wg[ik * 128:(ik + 1) * 128, :],
              writes=[cwg])
        p.dma("sync", we_s[:, ik * 8:(ik + 1) * 8], We[ik * 128:(ik + 1) * 128, :],
              writes=[cwe])
    o1 = p.sbuf("o1g2", [1, 128])
    gb_s = p.sbuf("gb_s", [1, 2])
    eb_s = p.sbuf("eb_s", [1, 8])
    gbb = p.sbuf("gbb", [128, 2])
    ebb = p.sbuf("ebb", [128, 8])
    ids = p.sbuf("idsg2", [128, 128])
    (co1, cgb, ceb, cgbb, cebb, cid) = (p.cell(t) for t in
                                        (o1, gb_s, eb_s, gbb, ebb, ids))
    p.dma("sync", o1[:, :], ones1, writes=[co1])
    p.dma("sync", gb_s[:, :], gbr, writes=[cgb])
    p.dma("sync", eb_s[:, :], ebr, writes=[ceb])
    p.dma("sync", ids[:, :], idf, writes=[cid])
    pb, cpb = pool.next()
    mm_group(p, [(pb[:, 0:2], o1[:, :], gb_s[:, :])], [co1, cgb], cpb)
    p.copy(gbb[:, :], pb[:, 0:2], reads=[cpb], writes=[cgbb])
    pb, cpb = pool.next()
    mm_group(p, [(pb[:, 0:8], o1[:, :], eb_s[:, :])], [co1, ceb], cpb)
    p.copy(ebb[:, :], pb[:, 0:8], reads=[cpb], writes=[cebb])
    lg = p.sbuf("lg", [128, 2])
    le = p.sbuf("le", [128, 8])
    m1c = p.sbuf("m1c", [128, 1])
    sc1 = p.sbuf("sc1", [128, 1])
    sc2 = p.sbuf("sc2", [128, 1])
    e2 = p.sbuf("e2", [128, 2])
    e8 = p.sbuf("e8", [128, 8])
    msk = p.sbuf("msk", [128, 8])
    wts = p.sbuf("wts", [128, 8])
    wrow = p.sbuf("wrow", [8, T])
    (clg, cle, cm1, cs1, cs2, ce2, ce8, cmsk, cwts, cwrow) = (
        p.cell(t) for t in (lg, le, m1c, sc1, sc2, e2, e8, msk, wts, wrow))
    for tb in range(T // 128):
        pg, cpg = pool.next()
        mm_group(p, [(pg[0:128, 0:2], xg[:, ik * T + tb * 128: ik * T + (tb + 1) * 128],
                      wg_s[:, ik * 2:(ik + 1) * 2]) for ik in range(8)],
                 [cxg, cwg], cpg)
        pe, cpe = pool.next()
        mm_group(p, [(pe[0:128, 0:8], xg[:, ik * T + tb * 128: ik * T + (tb + 1) * 128],
                      we_s[:, ik * 8:(ik + 1) * 8]) for ik in range(8)],
                 [cxg, cwe], cpe)
        p.tt(lg[:, :], pg[:, 0:2], gbb[:, :], OP.add, reads=[cpg, cgbb], writes=[clg])
        p.tt(le[:, :], pe[:, 0:8], ebb[:, :], OP.add, reads=[cpe, cebb], writes=[cle])
        p.red(m1c[:, :], lg[:, :], op=OP.max, reads=[clg], writes=[cm1])
        p.ts(m1c[:, :], m1c[:, :], -1.0, None, OP.mult, reads=[cm1], writes=[cm1])
        p.act(e2[:, :], lg[:, :], AF.Exp, bias=m1c[:, :], reads=[clg, cm1],
              writes=[ce2])
        p.red(sc1[:, :], e2[:, :], op=OP.add, reads=[ce2], writes=[cs1])
        p.recip(sc1[:, :], sc1[:, :], reads=[cs1], writes=[cs1])
        p.red(sc2[:, :], e2[:, :], op=OP.max, reads=[ce2], writes=[cs2])
        p.tt(sc2[:, :], sc2[:, :], sc1[:, :], OP.mult, reads=[cs1, cs2],
             writes=[cs2])
        p.ts(m1c[:, :], m1c[:, :], -1.0, None, OP.mult, reads=[cm1], writes=[cm1])
        p.tt(msk[:, 0:2], lg[:, :], m1c[:, 0:1].broadcast_to([128, 2]), OP.is_ge,
             reads=[clg, cm1], writes=[cmsk])
        p.ts(msk[:, 2:4], msk[:, 0:2], 1.0, 1.0, OP.mult, OP.subtract,
             reads=[cmsk], writes=[cmsk])
        gm8 = msk[:, 2:4].unsqueeze(2).broadcast_to([128, 2, 4])
        p.stt(e8.rearrange("p (g j) -> p g j", g=2), gm8, 1e30,
              le.rearrange("p (g j) -> p g j", g=2), OP.mult, OP.add,
              reads=[cmsk, cle], writes=[ce8])
        p.red(m1c[:, :], e8[:, :], op=OP.max, reads=[ce8], writes=[cm1])
        p.ts(m1c[:, :], m1c[:, :], -1.0, None, OP.mult, reads=[cm1], writes=[cm1])
        p.act(e8[:, :], e8[:, :], AF.Exp, bias=m1c[:, :], reads=[ce8, cm1],
              writes=[ce8])
        p.red(sc1[:, :], e8[:, :], op=OP.add, reads=[ce8], writes=[cs1])
        p.recip(sc1[:, :], sc1[:, :], reads=[cs1], writes=[cs1])
        p.ts(e8[:, :], e8[:, :], sc1[:, :], sc2[:, :], OP.mult, OP.mult,
             reads=[ce8, cs1, cs2], writes=[ce8])
        p.red(m1c[:, :], e8[:, :], op=OP.max, reads=[ce8], writes=[cm1])
        p.tt(msk[:, :], e8[:, :], m1c[:, 0:1].broadcast_to([128, 8]), OP.is_lt,
             reads=[ce8, cm1], writes=[cmsk])
        p.tt(wts[:, :], e8[:, :], msk[:, :], OP.mult, reads=[ce8, cmsk],
             writes=[cwts])
        p.red(sc1[:, :], wts[:, :], op=OP.max, reads=[cwts], writes=[cs1])
        p.tt(msk[:, :], e8[:, :], sc1[:, 0:1].broadcast_to([128, 8]), OP.is_ge,
             reads=[ce8, cs1], writes=[cmsk])
        p.tt(wts[:, :], e8[:, :], msk[:, :], OP.mult, reads=[ce8, cmsk],
             writes=[cwts])
        pb3, cpb3 = pool.next()
        p.trans(pb3[0:8, 0:128], wts[:, :], ids[:, :], reads=[cwts, cid],
                writes=[cpb3])
        p.copy(wrow[:, tb * 128:(tb + 1) * 128], pb3[0:8, 0:128], reads=[cpb3],
               writes=[cwrow])
    p.dma("sync", wts_o, wrow[:, :], reads=[cwrow])
    p.drain_q("sync")
    return p


def build_l1_moe1(p, T=2048, TG=1024):
    xT = p.inp("xrT_sl1", [1024, T])
    Wsa = p.inp("Wsi_a", [1024, 2048])
    Wsb = p.inp("Wsi_b", [1024, 2048])
    Wso = p.inp("Wso_f", [2048, 1024])
    W1a = p.inp("W1_a", [1024, 2048])
    W1b = p.inp("W1_b", [1024, 2048])
    sh_o = p.outp("shared_o", [1024, T])
    h1_o = p.outp("h1_o", [2048, T], BF16)
    pool = PsumPool(p, 8)
    wta, cwa, _ = load_w_bf16(p, "wta1", Wsa, 1024, 2048)
    wtb, cwb2, _ = load_w_bf16(p, "wtb1", Wsb, 1024, 2048)
    wtd, cwd, _ = load_w_bf16(p, "wtd1", Wso, 2048, 1024)
    xf = p.sbuf("xf1", [128, 8 * TG], BF16)
    cxf = [p.cell(xf, f"t{i}") for i in range(8)]
    act = p.sbuf("act1", [128, 16 * TG], BF16)
    cact = p.cell(act)
    sg = p.sbuf("sg1", [128, 512], BF16)
    csg = p.cell(sg)
    ev = p.sbuf("ev1", [128, 512])
    cev = p.cell(ev)

    def swiglu_group(t0):
        for ro in range(16):
            for c0 in range(0, TG, 512):
                pa, cpa = pool.next()
                mm_group(p, [(pa[:, 0:512],
                              wta[:, ik * 2048 + ro * 128: ik * 2048 + ro * 128 + 128],
                              xf[:, ik * TG + c0: ik * TG + c0 + 512])
                             for ik in range(8)], [], cpa,
                         item_reads=[[cwa[ik], cxf[ik]] for ik in range(8)])
                pb_, cpb_ = pool.next()
                mm_group(p, [(pb_[:, 0:512],
                              wtb[:, ik * 2048 + ro * 128: ik * 2048 + ro * 128 + 128],
                              xf[:, ik * TG + c0: ik * TG + c0 + 512])
                             for ik in range(8)], [], cpb_,
                         item_reads=[[cwb2[ik], cxf[ik]] for ik in range(8)])
                p.act(sg[:, :], pa[:, 0:512], AF.Silu, reads=[cpa], writes=[csg])
                p.tt(act[:, ro * TG + c0: ro * TG + c0 + 512], sg[:, :],
                     pb_[:, 0:512], OP.mult, reads=[csg, cpb_], writes=[cact])

    for gi, t0 in enumerate(range(0, T, TG)):
        for ik in range(8):
            p.dma("gpsimd", xf[:, ik * TG:(ik + 1) * TG],
                  xT[ik * 128:(ik + 1) * 128, t0:t0 + TG], writes=[cxf[ik]])
        swiglu_group(t0)
        for ro in range(8):
            for c0 in range(0, TG, 512):
                pd, cpd = pool.next()
                mm_group(p, [(pd[:, 0:512],
                              wtd[:, ik * 1024 + ro * 128: ik * 1024 + ro * 128 + 128],
                              act[:, ik * TG + c0: ik * TG + c0 + 512])
                             for ik in range(16)], [cact], cpd,
                         item_reads=[[cwd[ik]] for ik in range(16)])
                p.copy(ev[:, 0:512], pd[:, 0:512], reads=[cpd], writes=[cev])
                p.dma("sync", sh_o[ro * 128:(ro + 1) * 128, t0 + c0:t0 + c0 + 512],
                      ev[:, 0:512], reads=[cev])
        for ik in range(8):
            p.dma("gpsimd", wta[:, ik * 2048:(ik + 1) * 2048],
                  W1a[ik * 128:(ik + 1) * 128, :], writes=[cwa[ik]])
            p.dma("gpsimd", wtb[:, ik * 2048:(ik + 1) * 2048],
                  W1b[ik * 128:(ik + 1) * 128, :], writes=[cwb2[ik]])
        swiglu_group(t0)
        for ro in range(16):
            p.dma("sync", h1_o[ro * 128:(ro + 1) * 128, t0:t0 + TG],
                  act[:, ro * TG:(ro + 1) * TG], reads=[cact])
        if t0 + TG < T:
            for ik in range(8):
                p.dma("gpsimd", wta[:, ik * 2048:(ik + 1) * 2048],
                      Wsa[ik * 128:(ik + 1) * 128, :], writes=[cwa[ik]])
                p.dma("gpsimd", wtb[:, ik * 2048:(ik + 1) * 2048],
                      Wsb[ik * 128:(ik + 1) * 128, :], writes=[cwb2[ik]])
    p.drain_q("sync")
    return p


def build_l1_moe2(p, T=2048, TG=2048):
    h1 = p.inp("h1_i", [2048, T], BF16)
    sh = p.inp("shared_i", [1024, T])
    wts8 = p.inp("wts8", [8, T])
    W2 = p.inp("W2_all", [16384, 1024])
    xr_tm = p.inp("xr_sl", [T, 1024])
    nrow = p.inp("norm2_row", [1, 1024])
    ones1 = p.inp("ones1_m2", [1, 128])
    idf = p.inp("ident_m2", [128, 128])
    g0_o = p.outp("g0_tm", [T, 1024])
    pool = PsumPool(p, 8)
    ids = p.sbuf("idsm2", [128, 128])
    o1 = p.sbuf("o1m2", [1, 128])
    nr = p.sbuf("nrm2", [1, 1024])
    wb = p.sbuf("wbm2", [128, 1024])
    cid, co1, cnr, cwb = (p.cell(t) for t in (ids, o1, nr, wb))
    p.dma("sync", ids[:, :], idf, writes=[cid])
    p.dma("sync", o1[:, :], ones1, writes=[co1])
    p.dma("sync", nr[:, :], nrow, writes=[cnr])
    _row_bcast(p, pool, wb, cwb, nr, cnr, o1, co1, 1024)
    wtd = p.sbuf("wtd2", [128, 16 * 1024], BF16)
    cwd = [p.cell(wtd, f"t{i}") for i in range(16)]
    h1f = p.sbuf("h1f", [128, 16 * TG], BF16)
    ch1 = [p.cell(h1f, f"t{i}") for i in range(16)]
    racc = p.sbuf("racc", [128, 8 * TG])
    cra = p.cell(racc)
    web = p.sbuf("web", [128, TG])
    cweb = p.cell(web)
    wrow = p.sbuf("wrowm", [1, T])
    cwr = p.cell(wrow)
    tmp5 = p.sbuf("tmp5", [128, 512])
    ct5 = p.cell(tmp5)
    for ik in range(16):
        p.dma("gpsimd", h1f[:, ik * TG:(ik + 1) * TG],
              h1[ik * 128:(ik + 1) * 128, :], writes=[ch1[ik]])
    for ro in range(8):
        p.dma("sync", racc[:, ro * TG:(ro + 1) * TG],
              sh[ro * 128:(ro + 1) * 128, :], writes=[cra])
    for e in range(8):
        for ik in range(16):
            p.dma("gpsimd", wtd[:, ik * 1024:(ik + 1) * 1024],
                  W2[e * 2048 + ik * 128: e * 2048 + (ik + 1) * 128, :],
                  writes=[cwd[ik]])
        p.dma("sync", wrow[:, :], wts8[e:e + 1, :], writes=[cwr])
        for c0 in range(0, TG, 512):
            pwb, cpwb = pool.next()
            mm_group(p, [(pwb[:, 0:512], o1[:, :], wrow[0:1, c0:c0 + 512])],
                     [co1, cwr], cpwb)
            p.copy(web[:, c0:c0 + 512], pwb[:, 0:512], reads=[cpwb], writes=[cweb])
        for ro in range(8):
            for c0 in range(0, TG, 512):
                pd, cpd = pool.next()
                mm_group(p, [(pd[:, 0:512],
                              wtd[:, ik * 1024 + ro * 128: ik * 1024 + ro * 128 + 128],
                              h1f[:, ik * TG + c0: ik * TG + c0 + 512])
                             for ik in range(16)], [], cpd,
                         item_reads=[[cwd[ik], ch1[ik]] for ik in range(16)])
                p.tt(tmp5[:, 0:512], pd[:, 0:512], web[:, c0:c0 + 512],
                     OP.mult, reads=[cpd, cweb], writes=[ct5])
                p.tt(racc[:, ro * TG + c0: ro * TG + c0 + 512],
                     racc[:, ro * TG + c0: ro * TG + c0 + 512], tmp5[:, 0:512],
                     OP.add, reads=[ct5, cra], writes=[cra])
    sc, cs_ = mk_ln_scratch(p, "m2")
    xt = p.sbuf("xtm2", [128, 1024])
    cxt = p.cell(xt)
    g0_t = p.sbuf("g0_t", [128, 1024])
    cg0 = p.cell(g0_t)
    nt = T // 128
    for ti in range(nt):
        p.dma("sync", xt[:, :], xr_tm[ti * 128:(ti + 1) * 128, :], writes=[cxt])
        for ro in range(8):
            pb2, cpb2 = pool.next()
            p.trans(pb2[:, 0:128], racc[:, ro * TG + ti * 128: ro * TG + (ti + 1) * 128],
                    ids[:, :], reads=[cra, cid], writes=[cpb2])
            p.tt(xt[:, ro * 128:(ro + 1) * 128], xt[:, ro * 128:(ro + 1) * 128],
                 pb2[:, 0:128], OP.add, reads=[cpb2, cxt], writes=[cxt])
        _emit_rms(p, g0_t[:, :], xt[:, :], cxt, cg0, wb, cwb, sc, cs_)
        p.dma("sync", g0_o[ti * 128:(ti + 1) * 128, :], g0_t[:, :], reads=[cg0])
    p.drain_q("sync")
    return p

def build_l1_final(p, T=2048):
    x1 = p.inp("x1_f", [T, 1024])
    s1 = p.inp("s1_f", [T, 1024])
    g0 = p.inp("g0_f", [T, 1024])
    out = p.outp("xout_tm", [T, 1024])
    a = p.sbuf("fa", [128, 1024])
    b = p.sbuf("fb", [128, 1024])
    c = p.sbuf("fc", [128, 1024])
    ca, cb, cc = p.cell(a), p.cell(b), p.cell(c)
    for ti in range(T // 128):
        p.dma("sync", a[:, :], x1[ti * 128:(ti + 1) * 128, :], writes=[ca])
        p.dma("sync", b[:, :], s1[ti * 128:(ti + 1) * 128, :], writes=[cb])
        p.dma("sync", c[:, :], g0[ti * 128:(ti + 1) * 128, :], writes=[cc])
        p.act(c[:, :], c[:, :], AF.Silu, reads=[cc], writes=[cc])
        p.tt(c[:, :], c[:, :], b[:, :], OP.mult, reads=[cb, cc], writes=[cc])
        p.tt(c[:, :], c[:, :], a[:, :], OP.add, reads=[ca, cc], writes=[cc])
        p.dma("sync", out[ti * 128:(ti + 1) * 128, :], c[:, :], reads=[cc])
    p.drain_q("sync")
    return p


def build_tp(p, tag, T=2048, D=1024):
    srcd = p.inp(f"tp_src_{tag}", [T, D])
    dstd = p.outp(f"tp_dst_{tag}", [D, T])
    pool = PsumPool(p, 4)
    idt = p.sbuf("idtp", [128, 128])
    cidt = p.cell(idt)
    idf = p.inp(f"tp_id_{tag}", [128, 128])
    p.dma("sync", idt[:, :], idf, writes=[cidt])
    st = p.sbuf("st_tp", [128, 1024])
    ev = p.sbuf("ev_tp", [128, 512])
    cst, cev = p.cell(st), p.cell(ev)
    nt = T // 128
    for ti in range(nt):
        p.dma("sync", st[:, :], srcd[ti * 128:(ti + 1) * 128, :], writes=[cst])
        for ro in range(D // 128):
            pb, cpb = pool.next()
            p.trans(pb[:, 0:128], st[:, ro * 128:(ro + 1) * 128], idt[:, :],
                    reads=[cst, cidt], writes=[cpb])
            p.copy(ev[:, 0:128], pb[:, 0:128], reads=[cpb], writes=[cev])
            p.dma("sync", dstd[ro * 128:(ro + 1) * 128, ti * 128:(ti + 1) * 128],
                  ev[:, 0:128], reads=[cev])
    p.drain_q("sync")
    return p


def build_ln0(p, TOK, D=1024, nt=None):
    nt = nt or TOK // 128
    x = p.inp("x", [TOK, D])
    xn = p.outp("xn", [TOK, D])
    xt = p.sbuf("xt", [128, D])
    ot = p.sbuf("ot", [128, D])
    junk = p.sbuf("junk", [128, D])
    s1 = p.sbuf("s1", [128, 1])
    s2 = p.sbuf("s2", [128, 1])
    m = p.sbuf("m", [128, 1])
    v = p.sbuf("v", [128, 1])
    r = p.sbuf("r", [128, 1])
    cx, co, cj, c1, c2, cm, cv, cr = (p.cell(t) for t in (xt, ot, junk, s1, s2, m, v, r))
    out_toks = []
    for i in range(nt):
        p.dma("sync", xt[:, :], x[i * 128:(i + 1) * 128, :], writes=[cx])
        p.red(s1[:, :], xt[:, :], reads=[cx], writes=[c1])
        p.stt(junk[:, :], xt[:, :], 1.0, xt[:, :], OP.bypass, OP.mult,
              accum_out=s2[:, :], reads=[cx], writes=[cj, c2])
        p.ts(m[:, :], s1[:, :], 1.0 / D, reads=[c1], writes=[cm])
        p.ts(v[:, :], s2[:, :], 1.0 / D, reads=[c2], writes=[cv])
        p.stt(v[:, :], m[:, :], m[:, :], v[:, :], OP.mult, OP.subtract,
              reads=[cm, cv], writes=[cv])
        p.ts(v[:, :], v[:, :], -1.0, 1e-5, OP.mult, OP.add, reads=[cv], writes=[cv])
        p.act(r[:, :], v[:, :], AF.Ln, reads=[cv], writes=[cr])
        p.act(r[:, :], r[:, :], AF.Exp, scale=-0.5, reads=[cr], writes=[cr])
        p.ts(m[:, :], m[:, :], -1.0, reads=[cm], writes=[cm])
        p.ts(ot[:, :], xt[:, :], m[:, :], r[:, :], OP.add, OP.mult,
             reads=[cx, cm, cr], writes=[co])
        out_toks.append(p.dma("sync", xn[i * 128:(i + 1) * 128, :], ot[:, :],
                              reads=[co]))
    p.drain_q("sync")
    return p


"""Fused single-launch megakernel driver."""
from concourse.bass_utils import run_bass_kernel_spmd
from concourse.replica_groups import maybe_share_collective_output_space
import concourse.mybir as mybir

NC, T = 8, 16384
f32 = np.float32
RG8 = [list(range(8))]


def _cc(a):
    return np.ascontiguousarray(a)


def build_fused(p):
    p.init_gbanks()
    d = p.dram

    def idram(name, shape, dtype=F32, space=None):
        return d(name, shape, dtype, addr_space=space)

    def barrier():
        toks = [(s, c) for s, c in p.sem_count.items() if c > 0]
        for eng in ("tensor", "vector", "scalar", "gpsimd", "sync"):
            p.wait_all(eng, toks)

    def coll(kind, in_ap, out_ap, cells_r, cell_w):
        barrier()
        op = mybir.AluOpType.add if kind == "AllReduce" else mybir.AluOpType.bypass
        p.collective(kind, op, [in_ap.opt()], [out_ap.opt()],
                     reads=cells_r, writes=[cell_w])
        barrier()

    sp_ag = maybe_share_collective_output_space("AllGather", RG8)
    sp_a2a = maybe_share_collective_output_space("AllToAll", RG8)

    W_in_d = idram("W_in_d", [1024, 512])
    W_xp_d = idram("W_xp_d", [256, 96])
    W_dt_d = idram("W_dt_d", [64, 256])
    W_out_d = idram("W_out_d", [2048, 1024])
    xn_d = idram("xn_d", [2048, 1024])
    xnT_in = idram("xnT_in", [1024, 2048])
    xnT_ag = idram("xnT_ag", [8, 1024, 2048], space=sp_ag)
    xcz_d = idram("xcz_d", [512, T])
    u_d = idram("u_d", [256, T])
    xpp_in = idram("xpp_in", [96, T])
    xpp_ag = idram("xpp_ag", [768, T], space=sp_ag)
    ssm_a2a_i = idram("ssm_a2a_i", [8, 256, 2048])
    ssm_a2a_o = idram("ssm_a2a_o", [8, 256, 2048], space=sp_a2a)
    hs_in = idram("hs_in", [1, 1])
    hs_ag = idram("hs_ag", [8, 1], space=sp_ag)
    s_fm_d = idram("s_fm_d", [1024, 2048])
    score_d = idram("score_d", [1, T])
    mask_d = idram("mask_d", [1, T])
    sc_a2a_o = idram("sc_a2a_o", [8, 2048], space=sp_a2a)
    rt_a2a_i = idram("rt_a2a_i", [8, 1024, 2048])
    rt_a2a_o = idram("rt_a2a_o", [8, 1024, 2048], space=sp_a2a)
    sh_fm_d = idram("sh_fm_d", [1024, 2048])
    x1_d = idram("x1_d", [2048, 1024])
    xn1_d = idram("xn1_d", [2048, 1024])
    z_d = idram("z_d", [2048, 1024])
    z_a2a_i = idram("z_a2a_i", [8, 128, 2048])
    z_a2a_o = idram("z_a2a_o", [8, 128, 2048], space=sp_a2a)
    yg_a2a_i = idram("yg_a2a_i", [8, 128, 2048])
    yg_a2a_o = idram("yg_a2a_o", [8, 128, 2048], space=sp_a2a)
    h_d = [idram(f"h_d{i}", [2048, 1024]) for i in range(5)]
    q_d = idram("q_tm_d", [2048, 1024])
    k_d2 = idram("k_tm_d", [2048, 1024])
    v_d2 = idram("v_tm_d", [2048, 1024])
    q_a2a_i = idram("q_a2a_i", [8, 128, 2048])
    q_a2a_o = idram("q_a2a_o", [8, 128, 2048], space=sp_a2a)
    k_a2a_i = idram("k_a2a_i", [8, 128, 2048])
    k_a2a_o = idram("k_a2a_o", [8, 128, 2048], space=sp_a2a)
    v_a2a_i = idram("v_a2a_i", [8, 2048, 128])
    v_a2a_o = idram("v_a2a_o", [8, 2048, 128], space=sp_a2a)
    sa_a2a_i = idram("sa_a2a_i", [8, 128, 2048])
    sa_a2a_o = idram("sa_a2a_o", [8, 128, 2048], space=sp_a2a)
    xnT_flat = idram("xnT_flat", [1024, T])
    ssm_flat = idram("ssm_flat", [256, T])
    rt_flat = idram("rt_flat", [1024, T])
    zT_flat = idram("zT_flat", [128, T])
    yg_flat = idram("yg_flat", [128, T])
    qT_flat = idram("qT_flat", [128, T])
    kT_flat = idram("kT_flat", [128, T])
    sa_flat = idram("sa_flat", [128, T])
    xr_d = idram("xr_d", [2048, 1024])
    xrT_d = idram("xrT_d", [1024, 2048])
    wts_d = idram("wts_d", [8, 2048])
    sh1_d = idram("sh1_d", [1024, 2048])
    h1_d = idram("h1_d", [2048, 2048], BF16)
    g0_d = idram("g0_d", [2048, 1024])

    cells = {}

    def dc(t):
        return p.cell(t)

    def pack_shards(dst_staged, src_flat, rows):
        sap = custom_ap(src_flat.ap(), [[2048, 8], [T, rows], [1, 2048]], 0)
        p.dma("sync", dst_staged.ap(), sap, reads=[dc(src_flat)],
              writes=[dc(dst_staged)])

    def unpack_shards(dst_flat, src_staged, rows):
        dap = custom_ap(dst_flat.ap(), [[2048, 8], [T, rows], [1, 2048]], 0)
        p.dma("sync", dap, src_staged.ap(), reads=[dc(src_staged)],
              writes=[dc(dst_flat)])

    p.io = {"W_in": W_in_d.ap(), "W_xp": W_xp_d.ap(), "W_dt": W_dt_d.ap(),
            "W_out": W_out_d.ap()}
    with p.phase():
        build_prep(p, plan={"in": (32, 8, 32, 64), "xp": (4, 8, 64, 12),
                               "dt": (8, 4, 8, 64), "out": (32, 32, 64, 32)})
    barrier()
    p.io = {"xn": xn_d.ap()}
    with p.phase():
        build_ln0(p, 2048)
    barrier()
    p.io = {"tp_src_xn": xn_d.ap(), "tp_dst_xn": xnT_in.ap()}
    with p.phase():
        build_tp(p, "xn")
    coll("AllGather", xnT_in.ap(), xnT_ag.ap(), [dc(xnT_in)], dc(xnT_ag))
    unpack_shards(xnT_flat, xnT_ag, 1024)
    xnT_full = xnT_flat.ap()
    p.io = {"xnT": xnT_full, "W_sel": W_in_d.ap(), "xcz": xcz_d.ap()}
    with p.phase():
        build_l0_inproj(p)
    barrier()
    p.io = {"xcz": xcz_d.ap(), "Wxp_rows": W_xp_d.ap(), "u_out": u_d.ap(),
            "xp_part": xpp_in.ap()}
    with p.phase():
        build_l0_conv(p)
    coll("AllGather", xpp_in.ap(), xpp_ag.ap(), [dc(xpp_in)], dc(xpp_ag))
    p.io = {"u_in": u_d.ap(), "xp_all": xpp_ag.ap(), "Wdt_sel": W_dt_d.ap(),
            "ssm": ssm_flat.ap(), "hsum": hs_in.ap()}
    with p.phase():
        build_l0_scan(p)
    barrier()
    pack_shards(ssm_a2a_i, ssm_flat, 256)
    coll("AllToAll", ssm_a2a_i.ap(), ssm_a2a_o.ap(), [dc(ssm_a2a_i)],
         dc(ssm_a2a_o))
    coll("AllGather", hs_in.ap(), hs_ag.ap(), [dc(hs_in)], dc(hs_ag))
    p.io = {"ssmT_sl": ssm_a2a_o.ap().rearrange("s d t -> (s d) t"),
            "W_out_f": W_out_d.ap(), "hsums": hs_ag.ap(),
            "s_fm": s_fm_d.ap()}
    with p.phase():
        build_l0_outproj(p)
    barrier()
    p.io = {"xnT_g": xnT_full, "score_row": score_d.ap(),
            "mask_row": mask_d.ap()}
    with p.phase():
        build_l0_gate(p)
    coll("AllToAll", score_d.ap().rearrange("o (s t) -> (o s) t", s=8),
         sc_a2a_o.ap(), [dc(score_d)], dc(sc_a2a_o))
    p.io = {"xnT_e": xnT_full, "xnT_my": xnT_in.ap(),
            "score_row_i": score_d.ap(),
            "score_my": custom_ap(sc_a2a_o.ap(), [[2048, 1], [1, 2048]], 0),
            "mask_row_i": mask_d.ap(),
            "routed_m": rt_flat.ap(),
            "shared_fm": sh_fm_d.ap()}
    with p.phase():
        build_l0_expert(p)
    barrier()
    pack_shards(rt_a2a_i, rt_flat, 1024)
    coll("AllToAll", rt_a2a_i.ap(), rt_a2a_o.ap(), [dc(rt_a2a_i)], dc(rt_a2a_o))
    p.io = {"s_fm_i": s_fm_d.ap(), "shared_fm_i": sh_fm_d.ap(),
            "routed_all": rt_a2a_o.ap().rearrange("s r t -> (s r) t"),
            "x1_tm": x1_d.ap(), "xn1_tm": xn1_d.ap()}
    with p.phase():
        build_l0_combine(p)
    barrier()
    identg = p.inp("identg", [128, 128])
    identg_b = p.inp("identg_b", [128, 128], BF16)
    ones1g = p.inp("ones1g", [1, 128])
    swapg = p.inp("swapg", [64, 64])
    hcur = xn1_d
    for IL in range(4):
        p.io = {"h_tm": hcur.ap(), "z_tm": z_d.ap(),
                "ln_w_row": p.inp(f"lnw{IL}", [1, 1024]),
                "ln_b_row": p.inp(f"lnb{IL}", [1, 1024]),
                "ones1_l": ones1g}
        with p.phase():
            build_l1_lnchan(p)
        barrier()
        p.io = {"tp_src_z": z_d.ap(),
                "tp_dst_z": z_a2a_i.ap().rearrange("s h t -> (s h) t"),
                "tp_id_z": identg}
        with p.phase():
            build_tp(p, "z")
        coll("AllToAll", z_a2a_i.ap(), z_a2a_o.ap(), [dc(z_a2a_i)], dc(z_a2a_o))
        unpack_shards(zT_flat, z_a2a_o, 128)
        barrier()
        p.io = {"zT_sl": zT_flat.ap(),
                "yg_sl": yg_flat.ap(),
                "log_dt_sel": p.inp(f"ldt{IL}", [128, 1]),
                "A_re_log_sel": p.inp(f"arl{IL}", [128, 32]),
                "A_im_sel": p.inp(f"aim{IL}", [128, 32]),
                "C_re_sel": p.inp(f"cre{IL}", [128, 32]),
                "C_im_sel": p.inp(f"cim{IL}", [128, 32]),
                "D_sel": p.inp(f"dsk{IL}", [128, 1]),
                "ident_b": identg_b, "ident_f32c": identg, "swap64": swapg}
        with p.phase():
            build_l1_conv(p)
        barrier()
        pack_shards(yg_a2a_i, yg_flat, 128)
        coll("AllToAll", yg_a2a_i.ap(), yg_a2a_o.ap(), [dc(yg_a2a_i)],
             dc(yg_a2a_o))
        p.io = {"ygT_sl": yg_a2a_o.ap().rearrange("s h t -> (s h) t"),
                "h_tm_w": hcur.ap(), "h1_tm": h_d[IL + 1].ap(),
                "WoutT": p.inp(f"WoT{IL}", [1024, 2048]),
                "b_out_i": p.inp(f"bo{IL}", [2048, 1]),
                "ident_fw": identg}
        with p.phase():
            build_l1_wout(p)
        barrier()
        hcur = h_d[IL + 1]
    s1_d = hcur
    xn1T_d = idram("xn1T_d", [1024, 2048])
    p.io = {"tp_src_xn1": xn1_d.ap(), "tp_dst_xn1": xn1T_d.ap(),
            "tp_id_xn1": identg}
    with p.phase():
        build_tp(p, "xn1")
    barrier()
    p.io = {"xn1T_sl": xn1T_d.ap(), "q_tm": q_d.ap(), "k_tm": k_d2.ap(),
            "v_tm": v_d2.ap()}
    with p.phase():
        build_l1_qkv(p)
    barrier()
    p.io = {"tp_src_q": q_d.ap(),
            "tp_dst_q": q_a2a_i.ap().rearrange("s d t -> (s d) t"),
            "tp_id_q": identg}
    with p.phase():
        build_tp(p, "q")
    p.io = {"tp_src_k": k_d2.ap(),
            "tp_dst_k": k_a2a_i.ap().rearrange("s d t -> (s d) t"),
            "tp_id_k": identg}
    with p.phase():
        build_tp(p, "k")
    coll("AllToAll", q_a2a_i.ap(), q_a2a_o.ap(), [dc(q_a2a_i)], dc(q_a2a_o))
    coll("AllToAll", k_a2a_i.ap(), k_a2a_o.ap(), [dc(k_a2a_i)], dc(k_a2a_o))
    p.io = {}
    vsrc = v_d2.ap().rearrange("t (s d) -> s t d", s=8)
    cvs, cvi = dc(v_d2), dc(v_a2a_i)
    p.dma("sync", v_a2a_i.ap(), vsrc, reads=[cvs], writes=[cvi])
    coll("AllToAll", v_a2a_i.ap(), v_a2a_o.ap(), [cvi], dc(v_a2a_o))
    unpack_shards(qT_flat, q_a2a_o, 128)
    unpack_shards(kT_flat, k_a2a_o, 128)
    p.io = {"qT_h": qT_flat.ap(),
            "kT_h": kT_flat.ap(),
            "v_h": v_a2a_o.ap().rearrange("s t d -> (s t) d"),
            "sa_h": sa_flat.ap()}
    with p.phase():
        build_l1_attn(p)
    barrier()
    pack_shards(sa_a2a_i, sa_flat, 128)
    coll("AllToAll", sa_a2a_i.ap(), sa_a2a_o.ap(), [dc(sa_a2a_i)], dc(sa_a2a_o))
    p.io = {"saT_sl": sa_a2a_o.ap().rearrange("s d t -> (s d) t"),
            "xn1_sl": xn1_d.ap(), "xr_tm": xr_d.ap()}
    with p.phase():
        build_l1_attnout(p)
    barrier()
    p.io = {"tp_src_xr": xr_d.ap(), "tp_dst_xr": xrT_d.ap(),
            "tp_id_xr": identg}
    with p.phase():
        build_tp(p, "xr")
    barrier()
    p.io = {"xrT_sl": xrT_d.ap(), "wts_o": wts_d.ap()}
    with p.phase():
        build_l1_gate2(p)
    barrier()
    p.io = {"xrT_sl1": xrT_d.ap(), "shared_o": sh1_d.ap(), "h1_o": h1_d.ap()}
    with p.phase():
        build_l1_moe1(p)
    barrier()
    p.io = {"h1_i": h1_d.ap(), "shared_i": sh1_d.ap(), "wts8": wts_d.ap(),
            "xr_sl": xr_d.ap(), "g0_tm": g0_d.ap()}
    with p.phase():
        build_l1_moe2(p)
    barrier()
    p.io = {"x1_f": x1_d.ap(), "s1_f": s1_d.ap(), "g0_f": g0_d.ap()}
    with p.phase():
        build_l1_final(p)
    p.io = {}
    return p


def host_maps(ins):
    ins = {k: np.asarray(v) for k, v in ins.items()}
    x = np.asarray(ins["L0_attn_x"], f32).reshape(T, 1024)
    ident_f = np.eye(128, dtype=f32)
    ident_bf = np.eye(128, dtype=ml_dtypes.bfloat16)
    ones1 = np.ones((1, 128), f32)
    swap = np.eye(64, dtype=f32)[:, list(range(32, 64)) + list(range(32))]
    sel = np.zeros((32, 4096), f32)
    for m in range(32):
        sel[m, m * 128:(m + 1) * 128] = 1.0
    conv_w = np.asarray(ins["L0_attn_conv_w"], f32).reshape(4, 2048)
    conv_b = np.asarray(ins["L0_attn_conv_b"], f32)
    in_b = np.asarray(ins["L0_attn_in_b"], f32)
    A_log = np.asarray(ins["L0_attn_A_log"], f32)
    D_skip = np.asarray(ins["L0_attn_D_skip"], f32)
    dt_b = np.asarray(ins["L0_attn_dt_b"], f32)
    xp_b = np.asarray(ins["L0_attn_xp_b"], f32)
    out_b = np.asarray(ins["L0_attn_out_b"], f32)
    DH = 64
    inv_freq = 1.0 / 10000.0 ** (np.arange(0, DH, 2, dtype=np.float64) / DH)
    ang = np.arange(8192, dtype=np.float64)[:, None] * inv_freq[None, :]
    sin_t = np.sin(ang).astype(f32)
    cos_t = np.cos(ang).astype(f32)
    Wsi = np.asarray(ins["L1_ffn_W_shared_in"], f32)
    W1s = np.asarray(ins["L1_ffn_W1_shared"], f32)

    def tt_fused(prefix, key, usel=None, isel=None):
        g1 = np.asarray(ins[f"{prefix}_{key}_g1"], f32)
        g2 = np.asarray(ins[f"{prefix}_{key}_g2"], f32)
        m1, n1, m2, n2 = TT_SHAPES[key]
        g1T = np.transpose(g1, (2, 0, 1))
        if usel is not None:
            g1T = g1T[:, :, usel]
        if isel is not None:
            g1T = g1T[:, isel, :]
        g1T = _cc(g1T.reshape(16, -1))
        g2f = _cc(g2.reshape(16, m2 * n2))
        return {f"{key}_g1T": g1T, f"{key}_g2f": g2f}

    maps = []
    for k in range(NC):
        ts = slice(2048 * k, 2048 * (k + 1))
        ds = slice(256 * k, 256 * (k + 1))
        hs = slice(128 * k, 128 * (k + 1))
        usel_in = np.r_[np.arange(4 * k, 4 * k + 4), 32 + np.arange(4 * k, 4 * k + 4)]
        im = {}
        im.update(tt_fused("L0_attn", "in", usel=usel_in))
        im.update(tt_fused("L0_attn", "xp", isel=np.arange(4 * k, 4 * k + 4)))
        im.update(tt_fused("L0_attn", "dt", usel=np.arange(4 * k, 4 * k + 4)))
        im.update(tt_fused("L0_attn", "out"))
        cols = np.r_[np.arange(256 * k, 256 * (k + 1)),
                     2048 + np.arange(256 * k, 256 * (k + 1))]
        sel8 = np.zeros((128, 8), f32)
        sel8[:, k] = 1.0
        lpos = np.arange(2048 * k, 2048 * (k + 1)) % 8192
        im.update(dict(
            x=_cc(x[ts]),
            b_sel=_cc(in_b[cols][:, None]),
            conv_wT_sel=_cc(conv_w[:, ds].T),
            conv_b_sel=_cc(conv_b[ds][:, None]),
            xp_b=xp_b[:, None],
            dt_b_sel=_cc(dt_b[ds][:, None]),
            A_log_sel=_cc(A_log[ds]),
            D_sel=_cc(D_skip[ds][:, None]),
            sel=sel, ident=ident_bf, ones_col=np.ones((128, 1), f32),
            out_b=out_b[:, None], ones8=np.ones((8, 1), f32), ones1=ones1,
            onesc_bf=np.ones((128, 1), ml_dtypes.bfloat16),
            gate_w=np.asarray(ins["L0_ffn_gate_w"], f32), sel8=sel8,
            rt_g=_cc(np.asarray(ins["L0_ffn_rt_gate"], f32)[k]),
            rt_u=_cc(np.asarray(ins["L0_ffn_rt_up"], f32)[k]),
            rt_d=_cc(np.asarray(ins["L0_ffn_rt_down"], f32)[k]),
            sh_g=np.asarray(ins["L0_ffn_sh_gate"], f32),
            sh_u=np.asarray(ins["L0_ffn_sh_up"], f32),
            sh_d=np.asarray(ins["L0_ffn_sh_down"], f32),
            ones1_e=ones1, x_tm=_cc(x[ts]), ident_f=ident_f,
            ones1_l=ones1,
            identg=ident_f, identg_b=ident_bf, ones1g=ones1, swapg=swap,
            Wq_f=np.asarray(ins["L1_ffn_Wq"], f32),
            Wk_c_f=np.asarray(ins["L1_ffn_Wk_c"], f32),
            Wk_f=np.asarray(ins["L1_ffn_Wk"], f32),
            Wv_c_f=np.asarray(ins["L1_ffn_Wv_c"], f32),
            Wv_f=np.asarray(ins["L1_ffn_Wv"], f32),
            sin_my=_cc(sin_t[lpos]), cos_my=_cc(cos_t[lpos]),
            ident_fq=ident_f,
            onesb=np.ones((128, 64), ml_dtypes.bfloat16), onesf_at=ones1,
            Wo_f=np.asarray(ins["L1_ffn_Wo"], f32),
            norm1_row=np.asarray(ins["L1_ffn_norm1_w"], f32)[None, :],
            ones1_ao=ones1, ident_ao=ident_f,
            Wg_f=np.asarray(ins["L1_ffn_Wg_gate"], f32),
            We_f=np.asarray(ins["L1_ffn_We_gate"], f32),
            gb_row=np.asarray(ins["L1_ffn_group_bias"], f32)[None, :],
            eb_row=np.asarray(ins["L1_ffn_expert_bias"], f32)[None, :],
            ones1_g2=ones1, ident_g2=ident_f,
            Wsi_a=_cc(Wsi[:, :2048]), Wsi_b=_cc(Wsi[:, 2048:]),
            Wso_f=np.asarray(ins["L1_ffn_W_shared_out"], f32),
            W1_a=_cc(W1s[:, :2048]), W1_b=_cc(W1s[:, 2048:]),
            W2_all=_cc(np.asarray(ins["L1_ffn_W2_experts"], f32).reshape(16384, 1024)),
            norm2_row=np.asarray(ins["L1_ffn_norm2_w"], f32)[None, :],
            ones1_m2=ones1, ident_m2=ident_f,
            tp_id_xn=ident_f,
        ))
        for IL in range(4):
            im[f"lnw{IL}"] = _cc(np.asarray(ins["L1_attn_ln_w"], f32)[IL][None, :])
            im[f"lnb{IL}"] = _cc(np.asarray(ins["L1_attn_ln_b"], f32)[IL][None, :])
            im[f"ldt{IL}"] = _cc(np.asarray(ins["L1_attn_log_dt"], f32)[IL, hs][:, None])
            im[f"arl{IL}"] = _cc(np.asarray(ins["L1_attn_A_re_log"], f32)[IL, hs])
            im[f"aim{IL}"] = _cc(np.asarray(ins["L1_attn_A_im"], f32)[IL, hs])
            im[f"cre{IL}"] = _cc(np.asarray(ins["L1_attn_C_re"], f32)[IL, hs])
            im[f"cim{IL}"] = _cc(np.asarray(ins["L1_attn_C_im"], f32)[IL, hs])
            im[f"dsk{IL}"] = _cc(np.asarray(ins["L1_attn_D_skip"], f32)[IL, hs][:, None])
            im[f"WoT{IL}"] = _cc(np.asarray(ins["L1_attn_W_out"], f32)[IL].T)
            im[f"bo{IL}"] = _cc(np.asarray(ins["L1_attn_b_out"], f32)[IL][:, None])
        maps.append(im)
    return maps


def kernel(**ins):
    maps = host_maps(ins)
    p = P("fused")
    build_fused(p)
    nc = p.finish()
    names = p.ext_inputs
    maps = [{k: v for k, v in im.items() if k in names} for im in maps]
    res = run_bass_kernel_spmd(nc, maps, core_ids=list(range(NC)))
    out = np.concatenate([np.asarray(r["xout_tm"]) for r in res.results], axis=0)
    return out.reshape(2, 8192, 1024).astype(np.float32)
